# version 30
# baseline (speedup 1.0000x reference)
import contextlib
import numpy as np
import ml_dtypes

import concourse.bass as bass
import concourse.mybir as mybir
from concourse.bass_utils import run_bass_kernel_spmd

F32 = mybir.dt.float32
BF16 = mybir.dt.bfloat16
AF = mybir.ActivationFunctionType
ALU = mybir.AluOpType
AX = mybir.AxisListType
BFNP = ml_dtypes.bfloat16

D = 2048
KD = D // 128
NCORE = 8
SCALE = float(np.float64(1.0) / np.sqrt(np.float64(128.0)))
ESHIFT = -20.0
LNEPS = 1e-5
QEPS = 1e-8


class Sem:
    def __init__(self, name):
        self.name = name
        self.v = 0
        self.h = None

    def inc(self, n=1):
        self.v += n
        return self.v


class Plan:
    def __init__(self):
        self.q = {"sync": [], "gpsimd": [], "tensor": [], "vector": [], "scalar": []}

    def wait(self, eng, sem, val):
        if val and val > 0:
            self.q[eng].append(lambda e, s=sem, v=val: e.wait_ge(s.h, v))

    def mm(self, out, lhsT, rhs, start, stop, sem=None):
        v = sem.inc(1) if sem is not None else None

        def f(e, out=out, lhsT=lhsT, rhs=rhs, start=start, stop=stop, sem=sem):
            r = e.matmul(out, lhsT, rhs, start=start, stop=stop)
            if sem is not None:
                r.then_inc(sem.h, 1)
        self.q["tensor"].append(f)
        return v

    def tr(self, out, in_, ident, sem=None):
        v = sem.inc(1) if sem is not None else None

        def f(e, out=out, in_=in_, ident=ident, sem=sem):
            r = e.transpose(out, in_, ident)
            if sem is not None:
                r.then_inc(sem.h, 1)
        self.q["tensor"].append(f)
        return v

    def act(self, out, in_, func, bias=None, scale=None, sem=None):
        v = sem.inc(1) if sem is not None else None

        def f(e, out=out, in_=in_, func=func, bias=bias, scale=scale, sem=sem):
            kw = {}
            if bias is not None:
                kw["bias"] = bias
            if scale is not None:
                kw["scale"] = scale
            r = e.activation(out, in_, func, **kw)
            if sem is not None:
                r.then_inc(sem.h, 1)
        self.q["scalar"].append(f)
        return v

    def vop(self, kind, sem=None, **kw):
        v = sem.inc(1) if sem is not None else None

        def f(e, kind=kind, kw=kw, sem=sem):
            if kind == "tt":
                r = e.tensor_tensor(kw["out"], kw["in0"], kw["in1"], kw["op"])
            elif kind == "ts":
                r = e.tensor_scalar(kw["out"], kw["in0"], kw["s1"], None,
                                    kw["op0"])
            elif kind == "ts2":
                r = e.tensor_scalar(kw["out"], kw["in0"], kw["s1"], kw["s2"],
                                    kw["op0"], kw["op1"])
            elif kind == "stt":
                r = e.scalar_tensor_tensor(kw["out"], kw["in0"], kw["scalar"],
                                           kw["in1"], kw["op0"], kw["op1"])
            elif kind == "red":
                r = e.reduce_sum(kw["out"], kw["in_"], axis=AX.X,
                                 apply_absolute_value=kw.get("absv", False))
            elif kind == "recip":
                r = e.reciprocal(kw["out"], kw["in_"])
            else:
                raise ValueError(kind)
            if sem is not None:
                r.then_inc(sem.h, 1)
        self.q["vector"].append(f)
        return v

    def pop_red(self, out, in_, absv, sem=None):
        v = sem.inc(1) if sem is not None else None

        def f(e, out=out, in_=in_, absv=absv, sem=sem):
            r = e.reduce_sum(out, in_, axis=AX.XYZWC, apply_absolute_value=absv)
            if sem is not None:
                r.then_inc(sem.h, 1)
        self.q["gpsimd"].append(f)
        return v

    def dma(self, eng, out, in_, sem):
        v = sem.inc(16)

        def f(e, out=out, in_=in_, sem=sem):
            e.dma_start(out=out, in_=in_).then_inc(sem.h, 16)
        self.q[eng].append(f)
        return v

    def coll(self, kind, op, groups, ins, outs, sem):
        v = sem.inc(1)

        def f(e, kind=kind, op=op, groups=groups, ins=ins, outs=outs, sem=sem):
            e.collective_compute(kind, op, replica_groups=groups, ins=ins,
                                 outs=outs).then_inc(sem.h, 1)
        self.q["gpsimd"].append(f)
        return v


def build_nc(T, HF):
    NBLK = T // 512
    NTO = T // 4
    NGRP = NTO // 512
    HB = HF // 128
    WMX = max(D, HF)
    assert T % 2048 == 0 and HF % 128 == 0

    nc = bass.Bass("TRN2", target_bir_lowering=False, debug=False)

    xT = nc.dram_tensor("xT", [D, T], F32, kind="ExternalInput")
    xTo = nc.dram_tensor("xTo", [D, NTO], F32, kind="ExternalInput")
    wq_s = nc.dram_tensor("wq_s", [D, 512], F32, kind="ExternalInput")
    wk_s = nc.dram_tensor("wk_s", [D, 128], F32, kind="ExternalInput")
    wv_s = nc.dram_tensor("wv_s", [D, 128], F32, kind="ExternalInput")
    bq4 = nc.dram_tensor("bq4", [4, 128], F32, kind="ExternalInput")
    bk1 = nc.dram_tensor("bk1", [1, 128], F32, kind="ExternalInput")
    bv1 = nc.dram_tensor("bv1", [1, 128], F32, kind="ExternalInput")
    wo_d = nc.dram_tensor("wo", [512, D], F32, kind="ExternalInput")
    bo16 = nc.dram_tensor("bo16", [16, 128], F32, kind="ExternalInput")
    w1T = nc.dram_tensor("w1T", [D, HF], F32, kind="ExternalInput")
    w2T = nc.dram_tensor("w2T", [D, HF], F32, kind="ExternalInput")
    w3T = nc.dram_tensor("w3T", [HF, D], F32, kind="ExternalInput")
    m1t = nc.dram_tensor("m1t", [128, T], F32, kind="ExternalInput")
    m2t = nc.dram_tensor("m2t", [128, T], F32, kind="ExternalInput")
    atm = nc.dram_tensor("atm", [128, 128], BF16, kind="ExternalInput")
    btm = nc.dram_tensor("btm", [128, 128], BF16, kind="ExternalInput")
    idb = nc.dram_tensor("idb", [128, 128], BF16, kind="ExternalInput")
    idf = nc.dram_tensor("idf", [16, 16], F32, kind="ExternalInput")
    oneb = nc.dram_tensor("oneb", [128, 1], BF16, kind="ExternalInput")
    onef = nc.dram_tensor("onef", [128, 1], F32, kind="ExternalInput")
    onerow = nc.dram_tensor("onerow", [1, 128], F32, kind="ExternalInput")
    halfrow = nc.dram_tensor("halfrow", [1, 128], F32, kind="ExternalInput")
    mskd = nc.dram_tensor("mskd", [128, 2048], BF16, kind="ExternalInput")
    msel = nc.dram_tensor("msel", [128, 4], F32, kind="ExternalInput")
    cvals = nc.dram_tensor("cvals", [128, 4], F32, kind="ExternalInput")

    outT = nc.dram_tensor("outT", [D, NTO], F32, kind="ExternalOutput")

    ag_in = nc.dram_tensor("ag_in", [128, T], BF16)
    ag_out = nc.dram_tensor("ag_out", [512, T], BF16)
    w1q_h = nc.dram_tensor("w1q_h", [128, HB * D], BF16)
    w2q_h = nc.dram_tensor("w2q_h", [128, HB * D], BF16)
    w3q_h = nc.dram_tensor("w3q_h", [128, 16 * HF], BF16)

    P = Plan()
    NW = 2 * HB + 16 * ((HB + 15) // 16)

    with contextlib.ExitStack() as G:
        sems = {}
        for nm in ["tsem", "vsem", "ssem", "spre", "spreg", "sx0", "sx1", "sx2",
                   "sy0", "sy1", "scc", "psem", "swg0", "swg1", "swg2", "swg3", "sqz0", "sqz1", "sqz2", "sqz3",
                   "sqw0", "sqw1", "sq0", "sq1", "sq2", "sqb0", "sqb1", "sqb2", "sw30", "sw31", "sxf0", "sxf1",
                   "sg0", "sg1", "sxb0", "sxb1", "sout0", "sout1"]:
            s = Sem(nm)
            s.h = G.enter_context(nc.semaphore(nm))
            sems[nm] = s
        ts, vs, ss = sems["tsem"], sems["vsem"], sems["ssem"]
        sx = [sems["sx0"], sems["sx1"], sems["sx2"]]
        swg = [sems[f"swg{i}"] for i in range(4)]
        sqz = [sems[f"sqz{i}"] for i in range(4)]
        psm = sems["psem"]
        sq = [sems["sq0"], sems["sq1"], sems["sq2"]]
        sqb = [sems["sqb0"], sems["sqb1"], sems["sqb2"]]
        sw3 = [sems["sw30"], sems["sw31"]]
        sxf = [sems["sxf0"], sems["sxf1"]]
        sg = [sems["sg0"], sems["sg1"]]
        sxb = [sems["sxb0"], sems["sxb1"]]
        spre, spreg = sems["spre"], sems["spreg"]
        scc = sems["scc"]
        sy = [sems["sy0"], sems["sy1"]]
        sqw = [sems["sqw0"], sems["sqw1"]]
        sout = [sems["sout0"], sems["sout1"]]

        ps01 = G.enter_context(nc.psum_tensor("ps01", [128, 1024], F32))
        ps23 = G.enter_context(nc.psum_tensor("ps23", [128, 1024], F32))
        ps45 = G.enter_context(nc.psum_tensor("ps45", [128, 1024], F32))
        ps6 = G.enter_context(nc.psum_tensor("ps6", [128, 512], F32))
        ps7 = G.enter_context(nc.psum_tensor("ps7", [128, 512], F32))
        ps4r = ps45[0:1, 0:512]
        ps5r = ps45[0:1, 512:1024]

        sb = lambda n, shp, dt: G.enter_context(nc.sbuf_tensor(n, shp, dt))
        wq_eff = sb("wq_eff", [128, KD * 128], BF16)
        wk_sb = sb("wk_sb", [128, KD * 128], BF16)
        wv_sb = sb("wv_sb", [128, KD * 128], BF16)
        atm_sb = sb("atm_sb", [128, 128], BF16)
        btm_sb = sb("btm_sb", [128, 128], BF16)
        idb_sb = sb("idb_sb", [128, 128], BF16)
        idf_sb = sb("idf_sb", [16, 16], F32)
        oneb_sb = sb("oneb_sb", [128, 1], BF16)
        onef_sb = sb("onef_sb", [128, 1], F32)
        onerow_sb = sb("onerow_sb", [1, 128], F32)
        halfrow_sb = sb("halfrow_sb", [1, 128], F32)
        msel_sb = sb("msel_sb", [128, 4], F32)
        cv_sb = sb("cv_sb", [128, 4], F32)
        bq_col = sb("bq_col", [128, 1], F32)
        bk_col = sb("bk_col", [128, 1], F32)
        bv_col = sb("bv_col", [128, 1], F32)
        bo_sb = sb("bo_sb", [128, 16], F32)
        csq = sb("csq", [1, 128], BF16)
        csk = sb("csk", [1, 128], BF16)
        csv = sb("csv", [1, 128], BF16)
        mrow = sb("mrow", [1, 512], F32)
        srow = sb("srow", [1, 512], F32)
        m2row = sb("m2row", [1, 512], F32)
        varrow = sb("varrow", [1, 512], F32)
        lnrow = m2row
        rrow = sb("rrow", [1, 512], F32)
        rmrow = sb("rmrow", [1, 512], BF16)
        recrow = sb("recrow", [1, 512], F32)
        gred = sb("gred", [128, NW], F32)
        gcol = sb("gcol", [128, 3], F32)
        ge_row = sb("ge_row", [1, 1], F32)
        bpc = [sb(f"bp{m}", [128, 1], F32) for m in range(3)]
        bnc = [sb(f"bn{m}", [128, 1], F32) for m in range(3)]

        with contextlib.ExitStack() as PR:
            wq_st = PR.enter_context(nc.sbuf_tensor("wq_st", [128, KD * 512], F32))
            bq4_sb = PR.enter_context(nc.sbuf_tensor("bq4_sb", [4, 128], F32))
            bk1_sb = PR.enter_context(nc.sbuf_tensor("bk1_sb", [1, 128], F32))
            bv1_sb = PR.enter_context(nc.sbuf_tensor("bv1_sb", [1, 128], F32))
            bo16_sb = PR.enter_context(nc.sbuf_tensor("bo16_sb", [16, 128], F32))
            tqa = PR.enter_context(nc.sbuf_tensor("tqa", [128, 128], F32))
            tqb = PR.enter_context(nc.sbuf_tensor("tqb", [128, 128], F32))
            PRE_N = 0
            for dst, src in [(wq_st.ap().rearrange("p (k i) -> p k i", k=KD), wq_s.ap().rearrange("(k p) i -> p k i", p=128)),
                             (atm_sb[:, :], atm[:, :]), (btm_sb[:, :], btm[:, :]),
                             (idb_sb[:, :], idb[:, :]), (idf_sb[:, :], idf[:, :]),
                             (oneb_sb[:, :], oneb[:, :]), (onef_sb[:, :], onef[:, :]),
                             (onerow_sb[:, :], onerow[:, :]), (halfrow_sb[:, :], halfrow[:, :]),
                             (msel_sb[:, :], msel[:, :]),
                             (cv_sb[:, :], cvals[:, :]),
                             (bq4_sb[:, :], bq4[:, :]), (bk1_sb[:, :], bk1[:, :]),
                             (bv1_sb[:, :], bv1[:, :]), (bo16_sb[:, :], bo16[:, :])]:
                PRE_N = P.dma("sync", dst, src, spre)
            PREG_N = P.dma("gpsimd", wk_sb.ap().rearrange("p (k i) -> p k i", k=KD),
                           wk_s.ap().rearrange("(k p) i -> p k i", p=128), spreg)
            PREG_N = P.dma("gpsimd", wv_sb.ap().rearrange("p (k i) -> p k i", k=KD),
                           wv_s.ap().rearrange("(k p) i -> p k i", p=128), spreg)

            P.wait("vector", spre, PRE_N)
            V_WQE = 0
            for k in range(KD):
                c = k * 512
                if k > 0:
                    P.wait("vector", vs, V_WQE)
                P.vop("tt", out=tqa[:, :], in0=wq_st[:, c:c + 128],
                      in1=wq_st[:, c + 128:c + 256], op=ALU.add)
                P.vop("tt", out=tqb[:, :], in0=wq_st[:, c + 256:c + 384],
                      in1=wq_st[:, c + 384:c + 512], op=ALU.add, sem=vs)
                P.wait("vector", vs, vs.v)
                V_WQE = P.vop("tt", out=wq_eff[:, k * 128:(k + 1) * 128],
                              in0=tqa[:, :], in1=tqb[:, :], op=ALU.add, sem=vs)

            P.wait("tensor", vs, V_WQE)
            P.wait("tensor", spre, PRE_N)
            P.wait("tensor", spreg, PREG_N)
            T_CSQ = 0
            for k in range(KD):
                T_CSQ = P.mm(ps45[0:1, 0:128], oneb_sb[:, :],
                             wq_eff[:, k * 128:(k + 1) * 128],
                             start=(k == 0), stop=(k == KD - 1),
                             sem=ts if k == KD - 1 else None)
            for k in range(KD):
                P.mm(ps6[0:1, 0:128], oneb_sb[:, :], wk_sb[:, k * 128:(k + 1) * 128],
                     start=(k == 0), stop=(k == KD - 1))
            T_CSKV = 0
            for k in range(KD):
                T_CSKV = P.mm(ps7[0:1, 0:128], oneb_sb[:, :],
                              wv_sb[:, k * 128:(k + 1) * 128],
                              start=(k == 0), stop=(k == KD - 1),
                              sem=ts if k == KD - 1 else None)
            P.wait("scalar", ts, T_CSQ)
            P.act(csq[:, :], ps45[0:1, 0:128], AF.Copy, scale=-1.0, sem=ss)
            P.wait("scalar", ts, T_CSKV)
            P.act(csk[:, :], ps6[0:1, 0:128], AF.Copy, scale=-1.0, sem=ss)
            SS_CS = P.act(csv[:, :], ps7[0:1, 0:128], AF.Copy, scale=-1.0, sem=ss)
            P.wait("tensor", ss, SS_CS)
            P.mm(ps45[:, 0:1], bq4_sb[:, :], onef_sb[0:4, 0:1], True, True)
            P.mm(ps45[:, 512:513], bk1_sb[:, :], onef_sb[0:1, 0:1], True, True)
            P.mm(ps6[:, 0:1], bv1_sb[:, :], onef_sb[0:1, 0:1], True, True)
            T_BIAS = P.tr(ps7[:, 0:16], bo16_sb[:, :], idf_sb[:, :], sem=ts)
            P.wait("scalar", ts, T_BIAS)
            P.act(bq_col[:, :], ps45[:, 0:1], AF.Copy, sem=ss)
            P.act(bk_col[:, :], ps45[:, 512:513], AF.Copy, sem=ss)
            P.act(bv_col[:, :], ps6[:, 0:1], AF.Copy, sem=ss)
            SS_PRO = P.act(bo_sb[:, :], ps7[:, 0:16], AF.Copy, sem=ss)

        stk_p1c = contextlib.ExitStack()
        qT = stk_p1c.enter_context(nc.sbuf_tensor("qT", [128, T], BF16))
        kT = stk_p1c.enter_context(nc.sbuf_tensor("kT", [128, T], BF16))
        vS = stk_p1c.enter_context(nc.sbuf_tensor("vS", [128, T], BF16))

        stk_p1a = contextlib.ExitStack()
        xb = [stk_p1a.enter_context(nc.sbuf_tensor(f"xb{p}", [128, KD * 512], BF16))
              for p in range(3)]
        xq1 = [stk_p1a.enter_context(nc.sbuf_tensor(f"xq1{p}", [128, 512], BF16))
               for p in range(2)]
        m1s = [stk_p1a.enter_context(nc.sbuf_tensor(f"m1s{p}", [128, 512], F32))
               for p in range(3)]
        m2s = [stk_p1a.enter_context(nc.sbuf_tensor(f"m2s{p}", [128, 512], F32))
               for p in range(3)]
        tq_sb = stk_p1a.enter_context(nc.sbuf_tensor("tq_sb", [128, 512], F32))
        tk_sb = stk_p1a.enter_context(nc.sbuf_tensor("tk_sb", [128, 512], F32))
        tv_sb = stk_p1a.enter_context(nc.sbuf_tensor("tv_sb", [128, 512], F32))
        rbc_sb = stk_p1a.enter_context(nc.sbuf_tensor("rbc_sb", [128, 512], F32))
        t1_sb = rbc_sb
        t2_sb = tq_sb
        uq_sb = stk_p1a.enter_context(nc.sbuf_tensor("uq_sb", [128, 512], BF16))
        uk_sb = stk_p1a.enter_context(nc.sbuf_tensor("uk_sb", [128, 512], BF16))
        vr_sb = stk_p1a.enter_context(nc.sbuf_tensor("vr_sb", [128, 512], BF16))

        T_XB = [0] * NBLK
        T_MS = [0] * NBLK
        T_SK = {}
        T_SKL = [0] * NBLK
        T_RANK = [0] * NBLK
        SS_SROW = [0] * NBLK
        SS_RBC = [0] * NBLK
        SS_VSB = [0] * NBLK
        V_SQ = [0] * NBLK
        V_QEV = [0] * NBLK
        V_KEV = [0] * NBLK
        V_TV = [0] * NBLK
        V_QT2 = [0] * NBLK
        V_KT2 = [0] * NBLK
        V_KROT = [0] * NBLK

        for i in range(NBLK):
            p = i % 3
            c0 = i * 512
            if i >= 3:
                P.wait("gpsimd", ts, T_XB[i - 3])
                P.wait("gpsimd", vs, V_KROT[i - 3])
            else:
                P.wait("gpsimd", ss, SS_PRO)
            P.dma("gpsimd", xb[p].ap().rearrange("p (k t) -> p k t", k=KD),
                  xT[:, c0:c0 + 512].rearrange("(k p) t -> p k t", p=128), sx[p])
            P.dma("gpsimd", m1s[p][:, :], m1t[:, c0:c0 + 512], sx[p])
            SX_N = P.dma("gpsimd", m2s[p][:, :], m2t[:, c0:c0 + 512], sx[p])

            assert SX_N == 48 * (i // 3 + 1)
            if i == 0:
                P.wait("tensor", sx[p], SX_N)
                P.wait("tensor", ss, SS_PRO)
                for k in range(KD):
                    P.mm(ps4r, oneb_sb[:, :], xb[p][:, k * 512:(k + 1) * 512],
                         start=(k == 0), stop=(k == KD - 1))
            P.wait("vector", sx[p], SX_N)
            for k in range(KD):
                if k >= 2:
                    P.wait("vector", ts, T_SK[(i, k - 2)])
                elif i >= 1:
                    P.wait("vector", ts, T_SKL[i - 1])
                vsq = P.vop("tt", out=xq1[k & 1][:, :],
                            in0=xb[p][:, k * 512:(k + 1) * 512],
                            in1=xb[p][:, k * 512:(k + 1) * 512],
                            op=ALU.mult, sem=vs)
                P.wait("tensor", vs, vsq)
                T_MS[i] = P.mm(ps5r, oneb_sb[:, :], xq1[k & 1][:, :],
                               start=(k == 0), stop=(k == KD - 1), sem=ts)
                T_SK[(i, k)] = T_MS[i]
            T_SKL[i] = T_MS[i]

            P.wait("scalar", ts, T_MS[i])
            P.act(mrow[:, :], ps4r, AF.Copy, scale=1.0 / D, sem=ss)
            SS_SROW[i] = P.act(srow[:, :], ps5r, AF.Copy, scale=1.0 / D, sem=ss)
            P.wait("vector", ss, SS_SROW[i])
            P.vop("tt", out=m2row[:, :], in0=mrow[:, :], in1=mrow[:, :],
                  op=ALU.mult, sem=vs)
            P.wait("vector", vs, vs.v)
            V_VAR = P.vop("tt", out=varrow[:, :], in0=srow[:, :], in1=m2row[:, :],
                          op=ALU.subtract, sem=vs)
            P.wait("scalar", vs, V_VAR)
            P.act(lnrow[:, :], varrow[:, :], AF.Ln, bias=cv_sb[0:1, 1:2], sem=ss)
            P.wait("scalar", ss, ss.v)
            SS_R = P.act(rrow[:, :], lnrow[:, :], AF.Exp, scale=-0.5, sem=ss)
            P.wait("vector", ss, SS_R)
            V_RM = P.vop("tt", out=rmrow[:, :], in0=rrow[:, :], in1=mrow[:, :],
                         op=ALU.mult, sem=vs)

            if i >= 1:
                P.wait("tensor", vs, V_QEV[i - 1])
            for k in range(KD):
                P.mm(ps01[:, 0:512], wq_eff[:, k * 128:(k + 1) * 128],
                     xb[p][:, k * 512:(k + 1) * 512], start=(k == 0), stop=False)
            if i >= 1:
                P.wait("tensor", vs, V_KEV[i - 1])
            for k in range(KD):
                P.mm(ps01[:, 512:1024], wk_sb[:, k * 128:(k + 1) * 128],
                     xb[p][:, k * 512:(k + 1) * 512], start=(k == 0), stop=False)
            if i >= 1:
                P.wait("tensor", vs, V_TV[i - 1])
            for k in range(KD):
                T_XB[i] = P.mm(ps23[:, 0:512], wv_sb[:, k * 128:(k + 1) * 128],
                               xb[p][:, k * 512:(k + 1) * 512],
                               start=(k == 0), stop=False,
                               sem=ts if k == KD - 1 else None)
            if i >= 1:
                P.wait("tensor", ss, SS_RBC[i - 1])
            P.wait("tensor", ss, SS_R)
            P.mm(ps7[:, :], onerow_sb[:, :], rrow[:, :], True, True)
            P.wait("tensor", vs, V_RM)
            P.mm(ps01[:, 0:512], csq[:, :], rmrow[:, :], start=False, stop=True)
            P.mm(ps01[:, 512:1024], csk[:, :], rmrow[:, :], start=False, stop=True)
            T_RANK[i] = P.mm(ps23[:, 0:512], csv[:, :], rmrow[:, :],
                             start=False, stop=True, sem=ts)

            if i + 1 < NBLK:
                p2 = (i + 1) % 3
                P.wait("tensor", sx[p2], 48 * ((i + 1) // 3 + 1))
                P.wait("tensor", ss, SS_SROW[i])
                for k in range(KD):
                    P.mm(ps4r, oneb_sb[:, :], xb[p2][:, k * 512:(k + 1) * 512],
                         start=(k == 0), stop=(k == KD - 1))

            P.wait("scalar", ts, T_RANK[i])
            SS_RBC[i] = P.act(rbc_sb[:, :], ps7[:, :], AF.Copy, sem=ss)

            P.wait("vector", ts, T_RANK[i])
            P.wait("vector", ss, SS_RBC[i])
            V_QEV[i] = P.vop("tt", out=tq_sb[:, :], in0=ps01[:, 0:512],
                             in1=rbc_sb[:, :], op=ALU.mult, sem=vs)
            V_KEV[i] = P.vop("tt", out=tk_sb[:, :], in0=ps01[:, 512:1024],
                             in1=rbc_sb[:, :], op=ALU.mult, sem=vs)
            V_TV[i] = P.vop("tt", out=tv_sb[:, :], in0=ps23[:, 0:512],
                            in1=rbc_sb[:, :], op=ALU.mult, sem=vs)
            P.wait("scalar", vs, V_QEV[i])
            SS_UQ = P.act(uq_sb[:, :], tq_sb[:, :], AF.Identity, bias=bq_col[:, 0:1], sem=ss)
            P.wait("scalar", vs, V_KEV[i])
            SS_UK = P.act(uk_sb[:, :], tk_sb[:, :], AF.Identity, bias=bk_col[:, 0:1], sem=ss)
            P.wait("scalar", vs, V_TV[i])
            SS_VR = P.act(vr_sb[:, :], tv_sb[:, :], AF.Identity, bias=bv_col[:, 0:1], sem=ss)

            P.wait("tensor", ss, SS_UQ)
            if i >= 1:
                P.wait("tensor", vs, V_KT2[i - 1])
                P.wait("tensor", ss, SS_VSB[i - 1])
            P.mm(ps23[:, 512:1024], atm_sb[:, :], uq_sb[:, :], True, True)
            T_QAB = P.mm(ps6[:, :], btm_sb[:, :], uq_sb[:, :], True, True, sem=ts)
            P.wait("vector", ts, T_QAB)
            P.wait("vector", vs, V_TV[i])
            P.vop("tt", out=t1_sb[:, :], in0=ps23[:, 512:1024], in1=m1s[p][:, :],
                  op=ALU.mult, sem=vs)
            V_QT2[i] = P.vop("tt", out=t2_sb[:, :], in0=ps6[:, :], in1=m2s[p][:, :],
                             op=ALU.mult, sem=vs)
            P.wait("vector", vs, V_QT2[i])
            V_QROT = P.vop("tt", out=qT[:, c0:c0 + 512], in0=t1_sb[:, :],
                           in1=t2_sb[:, :], op=ALU.add, sem=vs)

            P.wait("tensor", ss, SS_UK)
            P.wait("tensor", vs, V_QT2[i])
            P.mm(ps23[:, 512:1024], atm_sb[:, :], uk_sb[:, :], True, True)
            T_KAB = P.mm(ps6[:, :], btm_sb[:, :], uk_sb[:, :], True, True, sem=ts)
            P.wait("vector", ts, T_KAB)
            P.wait("vector", vs, V_QROT)
            P.vop("tt", out=t1_sb[:, :], in0=ps23[:, 512:1024], in1=m1s[p][:, :],
                  op=ALU.mult, sem=vs)
            V_KT2[i] = P.vop("tt", out=t2_sb[:, :], in0=ps6[:, :], in1=m2s[p][:, :],
                             op=ALU.mult, sem=vs)
            P.wait("vector", vs, V_KT2[i])
            V_KROT[i] = P.vop("tt", out=kT[:, c0:c0 + 512], in0=t1_sb[:, :],
                              in1=t2_sb[:, :], op=ALU.add, sem=vs)

            P.wait("tensor", ss, SS_VR)
            P.wait("tensor", vs, V_KT2[i])
            ps6b = ps6.ap().bitcast(BF16)
            T_VTR = 0
            for j in range(4):
                T_VTR = P.tr(ps6b[:, j * 128:(j + 1) * 128],
                             vr_sb[:, j * 128:(j + 1) * 128], idb_sb[:, :],
                             sem=ts if j == 3 else None)
            P.wait("scalar", ts, T_VTR)
            for j in range(4):
                SS_VSB[i] = P.act(vS[:, (4 * i + j) * 128:(4 * i + j + 1) * 128],
                                  ps6b[:, j * 128:(j + 1) * 128], AF.Copy,
                                  sem=ss if j == 3 else None)

        stk_p1a.close()

        stk_p1b = contextlib.ExitStack()
        attT = [stk_p1b.enter_context(nc.sbuf_tensor(f"attT{p}", [128, 1024], BF16))
                for p in range(6)]
        attTm = stk_p1b.enter_context(nc.sbuf_tensor("attTm", [128, 1024], BF16))
        ybuf = [stk_p1b.enter_context(nc.sbuf_tensor(f"ybuf{p}", [128, 512], BF16))
                for p in range(2)]
        bcc_sb = stk_p1b.enter_context(nc.sbuf_tensor("bcc_sb", [128, 512], F32))
        mskd_sb = stk_p1b.enter_context(nc.sbuf_tensor("mskd_sb", [128, 2048], BF16))
        wstr = [stk_p1b.enter_context(nc.sbuf_tensor(f"wstr{p}", [128, 2048], F32))
                for p in range(4)]
        sgA = [stk_p1b.enter_context(nc.sbuf_tensor(f"sgA{p}", [128, 2048], BF16))
               for p in range(2)]
        sgB = [stk_p1b.enter_context(nc.sbuf_tensor(f"sgB{p}", [128, 2048], BF16))
               for p in range(2)]
        qbk = [stk_p1b.enter_context(nc.sbuf_tensor(f"qbk{p}", [128, 2048], BF16))
               for p in range(2)]

        P_BAR_T = ts.v
        P_BAR_V = vs.v
        P_BAR_S = ss.v
        P.wait("sync", ts, P_BAR_T)
        P.wait("sync", vs, P_BAR_V)
        P.wait("sync", ss, P_BAR_S)
        PRE_MSK = P.dma("sync", mskd_sb[:, :], mskd[:, :], spre)
        P.wait("vector", spre, PRE_MSK)

        units = []
        for hb in range(HB):
            units.append((0, w1T[:, hb * 128:(hb + 1) * 128]
                          .rearrange("(k p) h -> p k h", p=128),
                          w1q_h[:, hb * D:(hb + 1) * D], D))
        for hb in range(HB):
            units.append((1, w2T[:, hb * 128:(hb + 1) * 128]
                          .rearrange("(k p) h -> p k h", p=128),
                          w2q_h[:, hb * D:(hb + 1) * D], D))
        for dt in range(16):
            w3v = w3T[:, dt * 128:(dt + 1) * 128].rearrange("(k p) d_ -> p k d_", p=128)
            for k0 in range(0, HB, 16):
                k1 = min(k0 + 16, HB)
                units.append((2, w3v[:, k0:k1, :],
                              w3q_h[:, dt * HF + k0 * 128:dt * HF + k1 * 128],
                              (k1 - k0) * 128))
        assert len(units) == NW

        G_GRED = [0] * NW
        g_state = {"next": 0}

        def emit_gamma(n):
            for _ in range(n):
                j = g_state["next"]
                if j >= NW:
                    return
                g_state["next"] += 1
                pj = j % 4
                if j >= 4:
                    P.wait("sync", vs, G_GRED[j - 4])
                m, gsrc, _, gw = units[j]
                swn = P.dma("sync", wstr[pj][:, 0:gw]
                            .rearrange("p (k h) -> p k h", h=128), gsrc, swg[pj])
                P.wait("vector", swg[pj], swn)
                G_GRED[j] = P.vop("red", out=gred[:, j:j + 1], in_=wstr[pj][:, 0:gw],
                                  absv=True, sem=vs)

        V_SG = [0] * NW
        V_QB = [0] * NW
        q_state = {"next": 0, "V_GF": 0, "SS_GDONE": 0, "pend": None}

        def emit_qz(n):
            for _ in range(n):
                u = q_state["next"]
                if u >= NW:
                    if q_state["pend"] is not None:
                        q_state["pend"]()
                        q_state["pend"] = None
                    return
                q_state["next"] += 1
                pu = u % 4
                ps2 = u & 1
                m, src, dst, wdt = units[u]
                if u < 4:
                    P.wait("sync", vs, G_GRED[NW - 1])
                else:
                    P.wait("sync", vs, V_SG[u - 4])
                sqzn = P.dma("sync", wstr[pu][:, 0:wdt]
                             .rearrange("p (k h) -> p k h", h=128), src, sqz[pu])
                P.wait("vector", sqz[pu], sqzn)
                if u == 0:
                    P.wait("vector", ss, q_state["SS_GDONE"])
                if u >= 2:
                    P.wait("vector", vs, V_QB[u - 2])
                P.vop("ts2", out=sgA[ps2][:, 0:wdt], in0=wstr[pu][:, 0:wdt],
                      s1=bpc[m][:, 0:1], s2=0.0, op0=ALU.add, op1=ALU.is_ge,
                      sem=vs)
                V_SG[u] = P.vop("ts2", out=sgB[ps2][:, 0:wdt], in0=wstr[pu][:, 0:wdt],
                                s1=bnc[m][:, 0:1], s2=0.0, op0=ALU.add,
                                op1=ALU.is_ge, sem=vs)
                P.wait("vector", vs, V_SG[u])
                if u >= 2:
                    P.wait("vector", sqw[ps2], 16 * (u // 2))
                V_QB[u] = P.vop("stt", out=qbk[ps2][:, 0:wdt], in0=sgA[ps2][:, 0:wdt],
                                scalar=-1.0, in1=sgB[ps2][:, 0:wdt],
                                op0=ALU.add, op1=ALU.add, sem=vs)
                P.wait("sync", vs, V_QB[u])
                P.dma("sync", dst, qbk[ps2][:, 0:wdt], sqw[ps2])

        wides = [ps01, ps23, ps45]
        T_AV = {}
        T_SC = {}
        SS_EXP = {}
        V_MSK = {}
        V_Y = [0] * NBLK
        V_REC = [0] * NBLK
        SS_BCC = [0] * NBLK
        V_MSK_LAST = 0
        ps7_free = {"ss": P_BAR_S}
        FINB = max(1, min(NBLK - 2, (NBLK * 45) // 100))
        npre = sum(2 * (b + 1) for b in range(FINB))
        GPRATE = max(1, npre // (NW + 1))
        QHALF = (NW + 2 * (NBLK - 1 - FINB) - 1) // (2 * (NBLK - 1 - FINB))

        pairs = []
        for b in range(NBLK):
            npair = 2 * (b + 1)
            for pp in range(npair):
                pairs.append((b, pp, npair, b * 512))
        NP = len(pairs)

        def emit_scores(gp):
            b, pp, npair, q0 = pairs[gp]
            W = wides[gp % 3]
            if gp < 3:
                P.wait("tensor", vs, P_BAR_V)
                P.wait("tensor", ss, P_BAR_S)
            else:
                P.wait("tensor", ss, SS_EXP[gp - 3])
            st = 2 * pp
            P.mm(W[:, 0:512], kT[:, st * 128:(st + 1) * 128], qT[:, q0:q0 + 512],
                 True, True)
            T_SC[gp] = P.mm(W[:, 512:1024], kT[:, (st + 1) * 128:(st + 2) * 128],
                            qT[:, q0:q0 + 512], True, True, sem=ts)

        for gp in range(min(3, NP)):
            emit_scores(gp)

        for gp in range(NP):
            b, pp, npair, q0 = pairs[gp]
            W = wides[gp % 3]
            At = attT[gp % 6]
            masked = pp >= npair - 2
            st = 2 * pp
            P.wait("scalar", ts, T_SC[gp])
            if gp >= 6:
                P.wait("scalar", ts, T_AV[gp - 6])
            if masked and pp == npair - 1:
                P.wait("scalar", vs, V_MSK_LAST)
            tgt = attTm if masked else At
            SS_EXP[gp] = P.act(tgt[:, :], W[:, :], AF.Exp,
                               bias=cv_sb[:, 0:1], scale=SCALE, sem=ss)
            if masked:
                slot0 = st - 4 * b
                P.wait("vector", ss, SS_EXP[gp])
                P.vop("tt", out=At[:, 0:512], in0=attTm[:, 0:512],
                      in1=mskd_sb[:, slot0 * 512:(slot0 + 1) * 512],
                      op=ALU.mult, sem=vs)
                V_MSK_LAST = P.vop("tt", out=At[:, 512:1024],
                                   in0=attTm[:, 512:1024],
                                   in1=mskd_sb[:, (slot0 + 1) * 512:(slot0 + 2) * 512],
                                   op=ALU.mult, sem=vs)
            if masked:
                P.wait("tensor", vs, V_MSK_LAST)
            else:
                P.wait("tensor", ss, SS_EXP[gp])
            if pp == 0:
                if b >= 1:
                    P.wait("tensor", vs, V_Y[b - 1])
                P.wait("tensor", ss, ps7_free["ss"])
            P.mm(ps6[:, :], vS[:, st * 128:(st + 1) * 128], At[:, 0:512],
                 start=(pp == 0), stop=False)
            P.mm(ps6[:, :], vS[:, (st + 1) * 128:(st + 2) * 128], At[:, 512:1024],
                 start=False, stop=(pp == npair - 1))
            P.mm(ps7[0:1, :], oneb_sb[:, :], At[:, 0:512],
                 start=(pp == 0), stop=False)
            T_AV[gp] = P.mm(ps7[0:1, :], oneb_sb[:, :], At[:, 512:1024],
                            start=False, stop=(pp == npair - 1), sem=ts)
            if gp + 3 < NP:
                emit_scores(gp + 3)
            if b > FINB and (pp == 1 or pp == npair // 2):
                emit_qz(QHALF)
            elif b < FINB and gp % GPRATE == 0:
                emit_gamma(1)
            if pp != npair - 1:
                continue
            P.wait("vector", ts, T_AV[gp])
            V_REC[b] = P.vop("recip", out=recrow[:, :], in_=ps7[0:1, :], sem=vs)
            P.wait("tensor", vs, V_REC[b])
            T_BC = P.mm(ps7[:, :], onerow_sb[:, :], recrow[:, :], True, True, sem=ts)
            P.wait("scalar", ts, T_BC)
            SS_BCC[b] = P.act(bcc_sb[:, :], ps7[:, :], AF.Copy, sem=ss)
            ps7_free["ss"] = SS_BCC[b]
            P.wait("vector", ss, SS_BCC[b])
            if b >= 2:
                P.wait("vector", sy[b & 1], 16 * (b // 2))
            V_Y[b] = P.vop("tt", out=ybuf[b & 1][:, :], in0=ps6[:, :],
                           in1=bcc_sb[:, :], op=ALU.mult, sem=vs)
            P.wait("gpsimd", vs, V_Y[b])
            P.dma("gpsimd", ag_in[:, q0:q0 + 512], ybuf[b & 1][:, :], sy[b & 1])

            if b == FINB:
                emit_gamma(NW)
                P.wait("vector", vs, G_GRED[NW - 1])
                P.vop("red", out=gcol[:, 0:1], in_=gred[:, 0:HB])
                P.vop("red", out=gcol[:, 1:2], in_=gred[:, HB:2 * HB])
                V_GF = P.vop("red", out=gcol[:, 2:3], in_=gred[:, 2 * HB:NW],
                             sem=vs)
                q_state["V_GF"] = V_GF
                SS_GDONE = 0
                for m in range(3):
                    P.wait("tensor", vs, V_GF)
                    P.wait("tensor", ss, ps7_free["ss"] if m == 0 else SS_GDONE)
                    T_G1 = P.mm(ps7[0:1, 0:1], gcol[:, m:m + 1], onef_sb[:, :],
                                True, True, sem=ts)
                    P.wait("scalar", ts, T_G1)
                    SS_GE = P.act(ge_row[:, :], ps7[0:1, 0:1], AF.Identity,
                                  scale=1.0 / float(HF * D), bias=cv_sb[0:1, 2:3],
                                  sem=ss)
                    P.wait("tensor", ss, SS_GE)
                    T_G2 = P.mm(ps7[:, 0:1], onerow_sb[:, :], ge_row[:, :],
                                True, True, sem=ts)
                    P.wait("scalar", ts, T_G2)
                    P.act(bpc[m][:, :], ps7[:, 0:1], AF.Copy, scale=0.5, sem=ss)
                    SS_GDONE = P.act(bnc[m][:, :], ps7[:, 0:1], AF.Copy,
                                     scale=-0.5, sem=ss)
                q_state["SS_GDONE"] = SS_GDONE
                ps7_free["ss"] = SS_GDONE
        emit_qz(NW)

        P.wait("gpsimd", sy[0], sy[0].v)
        P.wait("gpsimd", sy[1], sy[1].v)
        CC_N = P.coll("AllGather", ALU.bypass, [[0, 1, 2, 3], [4, 5, 6, 7]],
                      [ag_in.ap().opt()], [ag_out.ap().opt()], scc)

        TS_QZ = ts.v
        VS_QZ = vs.v
        SS_QZ = ss.v
        stk_p1b.close()
        stk_p1c.close()

        stk_p2 = contextlib.ExitStack()
        p2 = lambda n, shp, dt: stk_p2.enter_context(nc.sbuf_tensor(n, shp, dt))
        xbo = p2("xbo", [128, KD * 512], BF16)
        cmb = [p2(f"cmb{c}", [128, 512], BF16) for c in range(4)]
        yTg = [p2(f"yTg{q}", [128, 4 * 512], BF16) for q in range(2)]
        gld = p2("gld", [128, 3 * 512], BF16)
        xq2 = [p2(f"xq2{q}", [128, 512], BF16) for q in range(2)]
        wo_sb = p2("wo_sb", [128, 4 * D], BF16)
        cs1 = p2("cs1", [1, HF], BF16)
        cs2 = p2("cs2", [1, HF], BF16)
        hid = p2("hid", [128, HB * 512], BF16)
        w1qb = [p2(f"w1qb{q}", [128, D], BF16) for q in range(3)]
        w2qb = [p2(f"w2qb{q}", [128, D], BF16) for q in range(3)]
        w3qb = [p2(f"w3qb{q}", [128, HF], BF16) for q in range(2)]
        xf32 = [p2(f"xf32{q}", [128, 512], F32) for q in range(2)]
        x1t = p2("x1t", [128, 512], F32)
        x2t = p2("x2t", [128, 512], F32)
        sig_sb = p2("sig_sb", [128, 512], BF16)
        rbc2 = p2("rbc2", [128, 512], F32)
        osb = [p2(f"osb{q}", [128, 512], F32) for q in range(2)]


        T_GRP = [0] * NGRP
        V_OUT = {}
        SS_RBC2 = [0] * NGRP
        SS_SIG_LAST = 0
        T_X12L = [0] * NGRP
        V_CMBL = [0] * NGRP
        T_S2KL = [0] * NGRP
        SS_SR2L = [0] * NGRP
        V_HIDL = [0] * NGRP

        for g in range(NGRP):
            pg = g & 1
            gc = g * 512
            if g >= 1:
                P.wait("gpsimd", ts, T_X12L[g - 1])
                P.wait("gpsimd", vs, V_CMBL[g - 1])
            SXB_N = P.dma("gpsimd", xbo.ap().rearrange("p (k t) -> p k t", k=KD),
                  xTo[:, gc:gc + 512].rearrange("(k p) t -> p k t", p=128), sxb[pg])
            if g == 0:
                P.wait("gpsimd", scc, CC_N)
            P.wait("tensor", sxb[pg], SXB_N)
            if g == 0:
                P.wait("tensor", ss, SS_EXP[NP - 1])
            else:
                P.wait("tensor", ss, SS_SR2L[g - 1])
            for k in range(KD):
                P.mm(ps4r, oneb_sb[:, :], xbo[:, k * 512:(k + 1) * 512],
                     start=(k == 0), stop=(k == KD - 1))
            P.wait("vector", sxb[pg], SXB_N)
            T_S2K = [0] * KD
            T_MS2 = 0
            for k in range(KD):
                if k >= 2:
                    P.wait("vector", ts, T_S2K[k - 2])
                elif g >= 1:
                    P.wait("vector", ts, T_S2KL[g - 1])
                vsq = P.vop("tt", out=xq2[k & 1][:, :],
                            in0=xbo[:, k * 512:(k + 1) * 512],
                            in1=xbo[:, k * 512:(k + 1) * 512],
                            op=ALU.mult, sem=vs)
                P.wait("tensor", vs, vsq)
                T_MS2 = P.mm(ps5r, oneb_sb[:, :], xq2[k & 1][:, :],
                             start=(k == 0), stop=(k == KD - 1), sem=ts)
                T_S2K[k] = T_MS2
            T_S2KL[g] = T_S2K[KD - 1]
            P.wait("scalar", ts, T_MS2)
            P.act(mrow[:, :], ps4r, AF.Copy, scale=1.0 / D, sem=ss)
            SS_SR2 = P.act(srow[:, :], ps5r, AF.Copy, scale=1.0 / D, sem=ss)
            SS_SR2L[g] = SS_SR2
            P.wait("vector", ss, SS_SR2)
            P.vop("tt", out=m2row[:, :], in0=mrow[:, :], in1=mrow[:, :],
                  op=ALU.mult, sem=vs)
            P.wait("vector", vs, vs.v)
            V_VAR2 = P.vop("tt", out=varrow[:, :], in0=srow[:, :], in1=m2row[:, :],
                           op=ALU.subtract, sem=vs)
            P.wait("scalar", vs, V_VAR2)
            P.act(lnrow[:, :], varrow[:, :], AF.Ln, bias=cv_sb[0:1, 1:2], sem=ss)
            P.wait("scalar", ss, ss.v)
            SS_R2 = P.act(rrow[:, :], lnrow[:, :], AF.Exp, scale=-0.5, sem=ss)
            P.wait("vector", ss, SS_R2)
            V_RM2 = P.vop("tt", out=rmrow[:, :], in0=rrow[:, :], in1=mrow[:, :],
                          op=ALU.mult, sem=vs)
            P.wait("tensor", ss, SS_R2)
            if g >= 1:
                P.wait("tensor", ss, SS_RBC2[g - 1])
                P.wait("tensor", vs, V_OUT[(g - 1, 15)])
            T_BC2 = P.mm(ps7[:, :], onerow_sb[:, :], rrow[:, :], True, True, sem=ts)
            P.wait("scalar", ts, T_BC2)
            SS_RBC2[g] = P.act(rbc2[:, :], ps7[:, :], AF.Copy, sem=ss)

            V_CMB = 0
            SG_N = 0
            for rt in range(4):
                if rt >= 1:
                    P.wait("gpsimd", vs, V_CMB)
                for c in range(4):
                    SG_N = P.dma("gpsimd", cmb[c][:, :],
                                 ag_out[rt * 128:(rt + 1) * 128,
                                        c * NTO + gc:c * NTO + gc + 512], sg[pg])
                P.wait("vector", sg[pg], SG_N)
                if rt == 0 and g == 0:
                    P.wait("vector", vs, VS_QZ)
                if rt == 0 and g >= 2:
                    P.wait("vector", ts, T_GRP[g - 2])
                if rt >= 1:
                    P.wait("vector", vs, V_CMB)
                P.vop("ts", out=gld[:, 0:512], in0=cmb[0][:, :],
                      s1=msel_sb[:, 0:1], op0=ALU.mult, sem=vs)
                P.wait("vector", vs, vs.v)
                P.vop("stt", out=gld[:, 512:1024], in0=cmb[1][:, :],
                      scalar=msel_sb[:, 1:2], in1=gld[:, 0:512],
                      op0=ALU.mult, op1=ALU.add, sem=vs)
                P.wait("vector", vs, vs.v)
                P.vop("stt", out=gld[:, 1024:1536], in0=cmb[2][:, :],
                      scalar=msel_sb[:, 2:3], in1=gld[:, 512:1024],
                      op0=ALU.mult, op1=ALU.add, sem=vs)
                P.wait("vector", vs, vs.v)
                V_CMB = P.vop("stt", out=yTg[pg][:, rt * 512:(rt + 1) * 512],
                              in0=cmb[3][:, :], scalar=msel_sb[:, 3:4],
                              in1=gld[:, 1024:1536], op0=ALU.mult, op1=ALU.add,
                              sem=vs)
            V_CMBL[g] = V_CMB
            if g == 0:
                P.wait("gpsimd", ts, TS_QZ)
                P.wait("gpsimd", vs, VS_QZ)
                P.wait("gpsimd", ss, SS_QZ)
                P.wait("gpsimd", sqw[0], sqw[0].v)
                P.wait("gpsimd", sqw[1], sqw[1].v)
                PREG2 = P.dma("gpsimd", wo_sb.ap().rearrange("p (k n) -> p k n", k=4),
                              wo_d.ap().rearrange("(k p) n -> p k n", p=128), spreg)

            V_HID = [0] * HB
            T_X12 = [0] * HB
            SS_CS2 = [0] * HB
            for hb in range(HB):
                ph = hb % 3
                psA = ps01 if (hb & 1) == 0 else ps23
                if hb >= 3:
                    P.wait("sync", ts, T_X12[hb - 3])
                elif g >= 1:
                    P.wait("sync", ts, T_GRP[g - 1])
                else:
                    P.wait("sync", sqw[0], sqw[0].v)
                    P.wait("sync", sqw[1], sqw[1].v)
                    P.wait("sync", ts, TS_QZ)
                    P.wait("sync", vs, VS_QZ)
                sqn = P.dma("sync", w1qb[ph][:, :], w1q_h[:, hb * D:(hb + 1) * D],
                            sq[ph])
                sqbn = P.dma("sync", w2qb[ph][:, :], w2q_h[:, hb * D:(hb + 1) * D],
                             sqb[ph])
                P.wait("tensor", sq[ph], sqn)
                P.wait("tensor", sqb[ph], sqbn)
                if hb >= 2:
                    P.wait("tensor", vs, V_HID[hb - 2])
                elif g >= 1:
                    P.wait("tensor", vs, V_HIDL[g - 1])
                if hb == 0:
                    P.wait("tensor", vs, V_RM2)
                    P.wait("tensor", vs, V_CMB)
                if g == 0:
                    if hb >= 1:
                        P.wait("tensor", ss, SS_CS2[hb - 1])
                    for k in range(KD):
                        P.mm(ps45[0:1, 0:128], oneb_sb[:, :],
                             w1qb[ph][:, k * 128:(k + 1) * 128],
                             start=(k == 0), stop=(k == KD - 1))
                    for k in range(KD):
                        T_CSC = P.mm(ps45[0:1, 512:640], oneb_sb[:, :],
                                     w2qb[ph][:, k * 128:(k + 1) * 128],
                                     start=(k == 0), stop=(k == KD - 1),
                                     sem=ts if k == KD - 1 else None)
                    P.wait("scalar", ts, T_CSC)
                    P.act(cs1[:, hb * 128:(hb + 1) * 128], ps45[0:1, 0:128],
                          AF.Copy, scale=-1.0, sem=ss)
                    SS_CS2[hb] = P.act(cs2[:, hb * 128:(hb + 1) * 128],
                                       ps45[0:1, 512:640], AF.Copy, scale=-1.0,
                                       sem=ss)
                for k in range(KD):
                    P.mm(psA[:, 0:512], w1qb[ph][:, k * 128:(k + 1) * 128],
                         xbo[:, k * 512:(k + 1) * 512], start=(k == 0), stop=False)
                if g == 0:
                    P.wait("tensor", ss, SS_CS2[hb])
                P.mm(psA[:, 0:512], cs1[:, hb * 128:(hb + 1) * 128], rmrow[:, :],
                     start=False, stop=True)
                for k in range(KD):
                    P.mm(psA[:, 512:1024], w2qb[ph][:, k * 128:(k + 1) * 128],
                         xbo[:, k * 512:(k + 1) * 512], start=(k == 0), stop=False)
                T_X12[hb] = P.mm(psA[:, 512:1024], cs2[:, hb * 128:(hb + 1) * 128],
                                 rmrow[:, :], start=False, stop=True, sem=ts)
                P.wait("vector", ts, T_X12[hb])
                if g == 0 and hb == 0:
                    P.wait("vector", ss, SS_RBC2[g])
                P.wait("vector", ss, SS_SIG_LAST)
                if hb >= 1:
                    P.wait("vector", vs, V_HID[hb - 1])
                P.vop("tt", out=x2t[:, :], in0=psA[:, 512:1024], in1=rbc2[:, :],
                      op=ALU.mult, sem=vs)
                V_X1 = P.vop("tt", out=x1t[:, :], in0=psA[:, 0:512], in1=rbc2[:, :],
                             op=ALU.mult, sem=vs)
                P.wait("scalar", vs, V_X1)
                SS_SIG_LAST = P.act(sig_sb[:, :], x1t[:, :], AF.Sigmoid, sem=ss)
                P.wait("vector", ss, SS_SIG_LAST)
                V_HID[hb] = P.vop("tt", out=hid[:, hb * 512:(hb + 1) * 512],
                                  in0=x2t[:, :], in1=sig_sb[:, :],
                                  op=ALU.mult, sem=vs)
            T_X12L[g] = T_X12[HB - 1]
            V_HIDL[g] = V_HID[HB - 1]

            T_OUTL = [0] * 16
            pend_store = [None]
            for dt in range(16):
                pw = dt & 1
                psO = ps6 if pw == 0 else ps7
                if dt >= 2:
                    P.wait("sync", ts, T_OUTL[dt - 2])
                elif g >= 1:
                    P.wait("sync", ts, T_GRP[g - 1])
                SW3_N = P.dma("sync", w3qb[pw][:, :], w3q_h[:, dt * HF:(dt + 1) * HF],
                              sw3[pw])
                if pend_store[0] is not None:
                    pend_store[0]()
                    pend_store[0] = None
                if dt >= 2:
                    P.wait("scalar", vs, V_OUT[(g, dt - 2)])
                elif g >= 1:
                    P.wait("scalar", vs, V_OUT[(g - 1, 14 + dt)])
                else:
                    P.wait("scalar", vs, VS_QZ)
                SXF_N = P.dma("scalar", xf32[pw][:, :],
                              xTo[dt * 128:(dt + 1) * 128, gc:gc + 512], sxf[pw])
                P.wait("tensor", sw3[pw], SW3_N)
                if dt == 0:
                    if g == 0:
                        P.wait("tensor", spreg, PREG2)
                    P.wait("tensor", vs, V_HID[HB - 1])
                if dt == 1:
                    P.wait("tensor", ss, SS_RBC2[g])
                if dt >= 2:
                    P.wait("tensor", vs, V_OUT[(g, dt - 2)])
                elif g >= 1:
                    P.wait("tensor", vs, V_OUT[(g - 1, 14 + dt)])
                for k in range(4):
                    P.mm(psO[:, :], wo_sb[:, k * D + dt * 128:k * D + (dt + 1) * 128],
                         yTg[pg][:, k * 512:(k + 1) * 512], start=(k == 0), stop=False)
                T_OUT = 0
                for k in range(HB):
                    T_OUT = P.mm(psO[:, :], w3qb[pw][:, k * 128:(k + 1) * 128],
                                 hid[:, k * 512:(k + 1) * 512], start=False,
                                 stop=(k == HB - 1), sem=ts if k == HB - 1 else None)
                T_OUTL[dt] = T_OUT
                nglob = g * 16 + dt
                P.wait("vector", ts, T_OUT)
                P.wait("vector", sxf[pw], SXF_N)
                if nglob >= 2:
                    P.wait("vector", sout[pw], 16 * (nglob // 2))
                V_OUT[(g, dt)] = P.vop("stt", out=osb[pw][:, :], in0=psO[:, :],
                                       scalar=bo_sb[:, dt:dt + 1], in1=xf32[pw][:, :],
                                       op0=ALU.add, op1=ALU.add, sem=vs)
                def _store(g=g, dt=dt, pw=pw, gc=gc):
                    P.wait("sync", vs, V_OUT[(g, dt)])
                    P.dma("sync", outT[dt * 128:(dt + 1) * 128, gc:gc + 512],
                          osb[pw][:, :], sout[pw])
                if dt == 15:
                    _store()
                else:
                    pend_store[0] = _store
            T_GRP[g] = T_OUTL[15]

        stk_p2.close()

        P.wait("sync", sout[0], sout[0].v)
        P.wait("sync", sout[1], sout[1].v)

        with nc.Block() as block:
            @block.sync
            def _(e):
                for f in P.q["sync"]:
                    f(e)

            @block.gpsimd
            def _(e):
                for f in P.q["gpsimd"]:
                    f(e)

            @block.tensor
            def _(e):
                for f in P.q["tensor"]:
                    f(e)

            @block.vector
            def _(e):
                for f in P.q["vector"]:
                    f(e)

            @block.scalar
            def _(e):
                for f in P.q["scalar"]:
                    f(e)

    return nc


def _constants(T):
    j = np.arange(64, dtype=np.float64)
    inv = 10000.0 ** (-2.0 * j / 128.0)
    t = np.arange(T, dtype=np.float64)
    emb = np.outer(t, inv)
    cos = np.cos(emb).T
    sin = np.sin(emb).T
    m1 = np.concatenate([cos, sin], axis=0).astype(np.float32)
    m2 = np.concatenate([-sin, cos], axis=0).astype(np.float32)
    A = np.zeros((128, 128), np.float32)
    Bm = np.zeros((128, 128), np.float32)
    for i in range(64):
        A[i, 2 * i] = 1.0
        A[64 + i, 2 * i] = 1.0
        Bm[i, 2 * i + 1] = 1.0
        Bm[64 + i, 2 * i + 1] = 1.0
    mskd = np.zeros((128, 2048), np.float32)
    sl = np.arange(128)
    ql = np.arange(128)
    for i in range(4):
        for qt in range(4):
            blockm = ((i * 128 + sl)[:, None] <= (qt * 128 + ql)[None, :])
            mskd[:, i * 512 + qt * 128:i * 512 + (qt + 1) * 128] = blockm
    return {
        "m1t": m1, "m2t": m2,
        "atm": np.ascontiguousarray(A.T).astype(BFNP),
        "btm": np.ascontiguousarray(Bm.T).astype(BFNP),
        "idb": np.eye(128, dtype=np.float32).astype(BFNP),
        "idf": np.eye(16, dtype=np.float32),
        "oneb": np.ones((128, 1), np.float32).astype(BFNP),
        "onef": np.ones((128, 1), np.float32),
        "onerow": np.ones((1, 128), np.float32),
        "halfrow": np.full((1, 128), 0.5, np.float32),
        "mskd": mskd.astype(BFNP),
        "cvals": np.repeat(np.array([[ESHIFT, LNEPS, QEPS, 0.0]], np.float32), 128, 0),
    }


def _stage_core(c, x, Wq, bq, Wk, bk, Wv, bv, Wo, bo, w1T, w2T, w3T, consts, T):
    b, h = c // 4, c % 4
    NTO = T // 4
    d = dict(consts)
    d["xT"] = np.ascontiguousarray(x[b].T)
    d["xTo"] = np.ascontiguousarray(x[b, h * NTO:(h + 1) * NTO].T)
    d["wq_s"] = np.ascontiguousarray(Wq[:, h * 512:(h + 1) * 512])
    d["wk_s"] = np.ascontiguousarray(Wk[:, h * 128:(h + 1) * 128])
    d["wv_s"] = np.ascontiguousarray(Wv[:, h * 128:(h + 1) * 128])
    d["bq4"] = np.ascontiguousarray(bq[h * 512:(h + 1) * 512].reshape(4, 128))
    d["bk1"] = np.ascontiguousarray(bk[h * 128:(h + 1) * 128].reshape(1, 128))
    d["bv1"] = np.ascontiguousarray(bv[h * 128:(h + 1) * 128].reshape(1, 128))
    d["wo"] = np.ascontiguousarray(Wo)
    d["bo16"] = np.ascontiguousarray(bo.reshape(16, 128))
    d["w1T"] = w1T
    d["w2T"] = w2T
    d["w3T"] = w3T
    ms = np.zeros((128, 4), np.float32)
    ms[:, h] = 1.0
    d["msel"] = ms
    return d


def stage_all(x, Wq, bq, Wk, bk, Wv, bv, Wo, bo, w1, w2, w3):
    T = x.shape[1]
    w1T = np.ascontiguousarray(np.asarray(w1, dtype=np.float32).T)
    w2T = np.ascontiguousarray(np.asarray(w2, dtype=np.float32).T)
    w3T = np.ascontiguousarray(np.asarray(w3, dtype=np.float32).T)
    consts = _constants(T)
    return [_stage_core(c, np.asarray(x, dtype=np.float32),
                        np.asarray(Wq, dtype=np.float32), np.asarray(bq, dtype=np.float32),
                        np.asarray(Wk, dtype=np.float32), np.asarray(bk, dtype=np.float32),
                        np.asarray(Wv, dtype=np.float32), np.asarray(bv, dtype=np.float32),
                        np.asarray(Wo, dtype=np.float32), np.asarray(bo, dtype=np.float32),
                        w1T, w2T, w3T, consts, T)
            for c in range(NCORE)]


def kernel(L0_attn_x, L0_attn_Wq, L0_attn_bq, L0_attn_Wk, L0_attn_bk,
           L0_attn_Wv, L0_attn_bv, L0_attn_Wo, L0_attn_bo,
           L0_ffn_w1, L0_ffn_w2, L0_ffn_w3):
    T = L0_attn_x.shape[1]
    HF = L0_ffn_w1.shape[0]
    in_maps = stage_all(L0_attn_x, L0_attn_Wq, L0_attn_bq, L0_attn_Wk, L0_attn_bk,
                        L0_attn_Wv, L0_attn_bv, L0_attn_Wo, L0_attn_bo,
                        L0_ffn_w1, L0_ffn_w2, L0_ffn_w3)
    nc = build_nc(T, HF)
    res = run_bass_kernel_spmd(nc, in_maps, core_ids=list(range(NCORE)))
    out = np.empty((2, T, D), np.float32)
    NTO = T // 4
    for c in range(NCORE):
        b, h = c // 4, c % 4
        out[b, h * NTO:(h + 1) * NTO, :] = np.asarray(res.results[c]["outT"]).T
    return out


# version 31
# speedup vs baseline: 1.0095x; 1.0095x over previous
import contextlib
import numpy as np
import ml_dtypes

import concourse.bass as bass
import concourse.mybir as mybir
from concourse.bass_utils import run_bass_kernel_spmd

F32 = mybir.dt.float32
BF16 = mybir.dt.bfloat16
AF = mybir.ActivationFunctionType
ALU = mybir.AluOpType
AX = mybir.AxisListType
BFNP = ml_dtypes.bfloat16

D = 2048
KD = D // 128
NCORE = 8
SCALE = float(np.float64(1.0) / np.sqrt(np.float64(128.0)))
ESHIFT = -20.0
LNEPS = 1e-5
QEPS = 1e-8


class Sem:
    def __init__(self, name):
        self.name = name
        self.v = 0
        self.h = None

    def inc(self, n=1):
        self.v += n
        return self.v


class Plan:
    def __init__(self):
        self.q = {"sync": [], "gpsimd": [], "tensor": [], "vector": [], "scalar": []}

    def wait(self, eng, sem, val):
        if val and val > 0:
            self.q[eng].append(lambda e, s=sem, v=val: e.wait_ge(s.h, v))

    def mm(self, out, lhsT, rhs, start, stop, sem=None):
        v = sem.inc(1) if sem is not None else None

        def f(e, out=out, lhsT=lhsT, rhs=rhs, start=start, stop=stop, sem=sem):
            r = e.matmul(out, lhsT, rhs, start=start, stop=stop)
            if sem is not None:
                r.then_inc(sem.h, 1)
        self.q["tensor"].append(f)
        return v

    def tr(self, out, in_, ident, sem=None):
        v = sem.inc(1) if sem is not None else None

        def f(e, out=out, in_=in_, ident=ident, sem=sem):
            r = e.transpose(out, in_, ident)
            if sem is not None:
                r.then_inc(sem.h, 1)
        self.q["tensor"].append(f)
        return v

    def act(self, out, in_, func, bias=None, scale=None, sem=None):
        v = sem.inc(1) if sem is not None else None

        def f(e, out=out, in_=in_, func=func, bias=bias, scale=scale, sem=sem):
            kw = {}
            if bias is not None:
                kw["bias"] = bias
            if scale is not None:
                kw["scale"] = scale
            r = e.activation(out, in_, func, **kw)
            if sem is not None:
                r.then_inc(sem.h, 1)
        self.q["scalar"].append(f)
        return v

    def vop(self, kind, sem=None, **kw):
        v = sem.inc(1) if sem is not None else None

        def f(e, kind=kind, kw=kw, sem=sem):
            if kind == "tt":
                r = e.tensor_tensor(kw["out"], kw["in0"], kw["in1"], kw["op"])
            elif kind == "ts":
                r = e.tensor_scalar(kw["out"], kw["in0"], kw["s1"], None,
                                    kw["op0"])
            elif kind == "ts2":
                r = e.tensor_scalar(kw["out"], kw["in0"], kw["s1"], kw["s2"],
                                    kw["op0"], kw["op1"])
            elif kind == "stt":
                r = e.scalar_tensor_tensor(kw["out"], kw["in0"], kw["scalar"],
                                           kw["in1"], kw["op0"], kw["op1"])
            elif kind == "red":
                r = e.reduce_sum(kw["out"], kw["in_"], axis=AX.X,
                                 apply_absolute_value=kw.get("absv", False))
            elif kind == "recip":
                r = e.reciprocal(kw["out"], kw["in_"])
            else:
                raise ValueError(kind)
            if sem is not None:
                r.then_inc(sem.h, 1)
        self.q["vector"].append(f)
        return v

    def pop_red(self, out, in_, absv, sem=None):
        v = sem.inc(1) if sem is not None else None

        def f(e, out=out, in_=in_, absv=absv, sem=sem):
            r = e.reduce_sum(out, in_, axis=AX.XYZWC, apply_absolute_value=absv)
            if sem is not None:
                r.then_inc(sem.h, 1)
        self.q["gpsimd"].append(f)
        return v

    def dma(self, eng, out, in_, sem):
        v = sem.inc(16)

        def f(e, out=out, in_=in_, sem=sem):
            e.dma_start(out=out, in_=in_).then_inc(sem.h, 16)
        self.q[eng].append(f)
        return v

    def coll(self, kind, op, groups, ins, outs, sem):
        v = sem.inc(1)

        def f(e, kind=kind, op=op, groups=groups, ins=ins, outs=outs, sem=sem):
            e.collective_compute(kind, op, replica_groups=groups, ins=ins,
                                 outs=outs).then_inc(sem.h, 1)
        self.q["gpsimd"].append(f)
        return v


def build_nc(T, HF):
    NBLK = T // 512
    NTO = T // 4
    NGRP = NTO // 512
    HB = HF // 128
    WMX = max(D, HF)
    assert T % 2048 == 0 and HF % 128 == 0

    nc = bass.Bass("TRN2", target_bir_lowering=False, debug=False)

    xT = nc.dram_tensor("xT", [D, T], F32, kind="ExternalInput")
    xTo = nc.dram_tensor("xTo", [D, NTO], F32, kind="ExternalInput")
    wq_s = nc.dram_tensor("wq_s", [D, 512], F32, kind="ExternalInput")
    wk_s = nc.dram_tensor("wk_s", [D, 128], F32, kind="ExternalInput")
    wv_s = nc.dram_tensor("wv_s", [D, 128], F32, kind="ExternalInput")
    bq4 = nc.dram_tensor("bq4", [4, 128], F32, kind="ExternalInput")
    bk1 = nc.dram_tensor("bk1", [1, 128], F32, kind="ExternalInput")
    bv1 = nc.dram_tensor("bv1", [1, 128], F32, kind="ExternalInput")
    wo_d = nc.dram_tensor("wo", [512, D], F32, kind="ExternalInput")
    bo16 = nc.dram_tensor("bo16", [16, 128], F32, kind="ExternalInput")
    w1T = nc.dram_tensor("w1T", [D, HF], F32, kind="ExternalInput")
    w2T = nc.dram_tensor("w2T", [D, HF], F32, kind="ExternalInput")
    w3T = nc.dram_tensor("w3T", [HF, D], F32, kind="ExternalInput")
    m1t = nc.dram_tensor("m1t", [128, T], F32, kind="ExternalInput")
    m2t = nc.dram_tensor("m2t", [128, T], F32, kind="ExternalInput")
    atm = nc.dram_tensor("atm", [128, 128], BF16, kind="ExternalInput")
    btm = nc.dram_tensor("btm", [128, 128], BF16, kind="ExternalInput")
    idb = nc.dram_tensor("idb", [128, 128], BF16, kind="ExternalInput")
    idf = nc.dram_tensor("idf", [16, 16], F32, kind="ExternalInput")
    oneb = nc.dram_tensor("oneb", [128, 1], BF16, kind="ExternalInput")
    onef = nc.dram_tensor("onef", [128, 1], F32, kind="ExternalInput")
    onerow = nc.dram_tensor("onerow", [1, 128], F32, kind="ExternalInput")
    halfrow = nc.dram_tensor("halfrow", [1, 128], F32, kind="ExternalInput")
    mskd = nc.dram_tensor("mskd", [128, 2048], BF16, kind="ExternalInput")
    msel = nc.dram_tensor("msel", [128, 4], F32, kind="ExternalInput")
    cvals = nc.dram_tensor("cvals", [128, 4], F32, kind="ExternalInput")

    outT = nc.dram_tensor("outT", [D, NTO], F32, kind="ExternalOutput")

    ag_in = nc.dram_tensor("ag_in", [128, T], BF16)
    ag_out = nc.dram_tensor("ag_out", [512, T], BF16)
    w1q_h = nc.dram_tensor("w1q_h", [128, HB * D], BF16)
    w2q_h = nc.dram_tensor("w2q_h", [128, HB * D], BF16)
    w3q_h = nc.dram_tensor("w3q_h", [128, 16 * HF], BF16)

    P = Plan()
    NW = 2 * HB + 16 * ((HB + 15) // 16)

    with contextlib.ExitStack() as G:
        sems = {}
        for nm in ["tsem", "vsem", "ssem", "spre", "spreg", "sx0", "sx1", "sx2",
                   "sy0", "sy1", "scc", "psem", "swg0", "swg1", "swg2", "swg3", "sqz0", "sqz1", "sqz2", "sqz3",
                   "sqw0", "sqw1", "sq0", "sq1", "sq2", "sqb0", "sqb1", "sqb2", "sw30", "sw31", "sxf0", "sxf1",
                   "sg0", "sg1", "sxb0", "sxb1", "sout0", "sout1"]:
            s = Sem(nm)
            s.h = G.enter_context(nc.semaphore(nm))
            sems[nm] = s
        ts, vs, ss = sems["tsem"], sems["vsem"], sems["ssem"]
        sx = [sems["sx0"], sems["sx1"], sems["sx2"]]
        swg = [sems[f"swg{i}"] for i in range(4)]
        sqz = [sems[f"sqz{i}"] for i in range(4)]
        psm = sems["psem"]
        sq = [sems["sq0"], sems["sq1"], sems["sq2"]]
        sqb = [sems["sqb0"], sems["sqb1"], sems["sqb2"]]
        sw3 = [sems["sw30"], sems["sw31"]]
        sxf = [sems["sxf0"], sems["sxf1"]]
        sg = [sems["sg0"], sems["sg1"]]
        sxb = [sems["sxb0"], sems["sxb1"]]
        spre, spreg = sems["spre"], sems["spreg"]
        scc = sems["scc"]
        sy = [sems["sy0"], sems["sy1"]]
        sqw = [sems["sqw0"], sems["sqw1"]]
        sout = [sems["sout0"], sems["sout1"]]

        ps01 = G.enter_context(nc.psum_tensor("ps01", [128, 1024], F32))
        ps23 = G.enter_context(nc.psum_tensor("ps23", [128, 1024], F32))
        ps45 = G.enter_context(nc.psum_tensor("ps45", [128, 1024], F32))
        ps6 = G.enter_context(nc.psum_tensor("ps6", [128, 512], F32))
        ps7 = G.enter_context(nc.psum_tensor("ps7", [128, 512], F32))
        ps4r = ps45[0:1, 0:512]
        ps5r = ps45[0:1, 512:1024]

        sb = lambda n, shp, dt: G.enter_context(nc.sbuf_tensor(n, shp, dt))
        wq_eff = sb("wq_eff", [128, KD * 128], BF16)
        wk_sb = sb("wk_sb", [128, KD * 128], BF16)
        wv_sb = sb("wv_sb", [128, KD * 128], BF16)
        atm_sb = sb("atm_sb", [128, 128], BF16)
        btm_sb = sb("btm_sb", [128, 128], BF16)
        idb_sb = sb("idb_sb", [128, 128], BF16)
        idf_sb = sb("idf_sb", [16, 16], F32)
        oneb_sb = sb("oneb_sb", [128, 1], BF16)
        onef_sb = sb("onef_sb", [128, 1], F32)
        onerow_sb = sb("onerow_sb", [1, 128], F32)
        halfrow_sb = sb("halfrow_sb", [1, 128], F32)
        msel_sb = sb("msel_sb", [128, 4], F32)
        cv_sb = sb("cv_sb", [128, 4], F32)
        bq_col = sb("bq_col", [128, 1], F32)
        bk_col = sb("bk_col", [128, 1], F32)
        bv_col = sb("bv_col", [128, 1], F32)
        bo_sb = sb("bo_sb", [128, 16], F32)
        csq = sb("csq", [1, 128], BF16)
        csk = sb("csk", [1, 128], BF16)
        csv = sb("csv", [1, 128], BF16)
        mrow = sb("mrow", [1, 512], F32)
        srow = sb("srow", [1, 512], F32)
        m2row = sb("m2row", [1, 512], F32)
        varrow = sb("varrow", [1, 512], F32)
        lnrow = m2row
        rrow = sb("rrow", [1, 512], F32)
        rmrow = sb("rmrow", [1, 512], BF16)
        recrow = sb("recrow", [1, 512], F32)
        gred = sb("gred", [128, NW], F32)
        gcol = sb("gcol", [128, 3], F32)
        ge_row = sb("ge_row", [1, 1], F32)
        bpc = [sb(f"bp{m}", [128, 1], F32) for m in range(3)]
        bnc = [sb(f"bn{m}", [128, 1], F32) for m in range(3)]

        with contextlib.ExitStack() as PR:
            wq_st = PR.enter_context(nc.sbuf_tensor("wq_st", [128, KD * 512], F32))
            bq4_sb = PR.enter_context(nc.sbuf_tensor("bq4_sb", [4, 128], F32))
            bk1_sb = PR.enter_context(nc.sbuf_tensor("bk1_sb", [1, 128], F32))
            bv1_sb = PR.enter_context(nc.sbuf_tensor("bv1_sb", [1, 128], F32))
            bo16_sb = PR.enter_context(nc.sbuf_tensor("bo16_sb", [16, 128], F32))
            tqa = PR.enter_context(nc.sbuf_tensor("tqa", [128, 128], F32))
            tqb = PR.enter_context(nc.sbuf_tensor("tqb", [128, 128], F32))
            PRE_N = 0
            for dst, src in [(wq_st.ap().rearrange("p (k i) -> p k i", k=KD), wq_s.ap().rearrange("(k p) i -> p k i", p=128)),
                             (atm_sb[:, :], atm[:, :]), (btm_sb[:, :], btm[:, :]),
                             (idb_sb[:, :], idb[:, :]), (idf_sb[:, :], idf[:, :]),
                             (oneb_sb[:, :], oneb[:, :]), (onef_sb[:, :], onef[:, :]),
                             (onerow_sb[:, :], onerow[:, :]), (halfrow_sb[:, :], halfrow[:, :]),
                             (msel_sb[:, :], msel[:, :]),
                             (cv_sb[:, :], cvals[:, :]),
                             (bq4_sb[:, :], bq4[:, :]), (bk1_sb[:, :], bk1[:, :]),
                             (bv1_sb[:, :], bv1[:, :]), (bo16_sb[:, :], bo16[:, :])]:
                PRE_N = P.dma("sync", dst, src, spre)
            PREG_N = P.dma("gpsimd", wk_sb.ap().rearrange("p (k i) -> p k i", k=KD),
                           wk_s.ap().rearrange("(k p) i -> p k i", p=128), spreg)
            PREG_N = P.dma("gpsimd", wv_sb.ap().rearrange("p (k i) -> p k i", k=KD),
                           wv_s.ap().rearrange("(k p) i -> p k i", p=128), spreg)

            P.wait("vector", spre, PRE_N)
            V_WQE = 0
            for k in range(KD):
                c = k * 512
                if k > 0:
                    P.wait("vector", vs, V_WQE)
                P.vop("tt", out=tqa[:, :], in0=wq_st[:, c:c + 128],
                      in1=wq_st[:, c + 128:c + 256], op=ALU.add)
                P.vop("tt", out=tqb[:, :], in0=wq_st[:, c + 256:c + 384],
                      in1=wq_st[:, c + 384:c + 512], op=ALU.add, sem=vs)
                P.wait("vector", vs, vs.v)
                V_WQE = P.vop("tt", out=wq_eff[:, k * 128:(k + 1) * 128],
                              in0=tqa[:, :], in1=tqb[:, :], op=ALU.add, sem=vs)

            P.wait("tensor", vs, V_WQE)
            P.wait("tensor", spre, PRE_N)
            P.wait("tensor", spreg, PREG_N)
            T_CSQ = 0
            for k in range(KD):
                T_CSQ = P.mm(ps45[0:1, 0:128], oneb_sb[:, :],
                             wq_eff[:, k * 128:(k + 1) * 128],
                             start=(k == 0), stop=(k == KD - 1),
                             sem=ts if k == KD - 1 else None)
            for k in range(KD):
                P.mm(ps6[0:1, 0:128], oneb_sb[:, :], wk_sb[:, k * 128:(k + 1) * 128],
                     start=(k == 0), stop=(k == KD - 1))
            T_CSKV = 0
            for k in range(KD):
                T_CSKV = P.mm(ps7[0:1, 0:128], oneb_sb[:, :],
                              wv_sb[:, k * 128:(k + 1) * 128],
                              start=(k == 0), stop=(k == KD - 1),
                              sem=ts if k == KD - 1 else None)
            P.wait("scalar", ts, T_CSQ)
            P.act(csq[:, :], ps45[0:1, 0:128], AF.Copy, scale=-1.0, sem=ss)
            P.wait("scalar", ts, T_CSKV)
            P.act(csk[:, :], ps6[0:1, 0:128], AF.Copy, scale=-1.0, sem=ss)
            SS_CS = P.act(csv[:, :], ps7[0:1, 0:128], AF.Copy, scale=-1.0, sem=ss)
            P.wait("tensor", ss, SS_CS)
            P.mm(ps45[:, 0:1], bq4_sb[:, :], onef_sb[0:4, 0:1], True, True)
            P.mm(ps45[:, 512:513], bk1_sb[:, :], onef_sb[0:1, 0:1], True, True)
            P.mm(ps6[:, 0:1], bv1_sb[:, :], onef_sb[0:1, 0:1], True, True)
            T_BIAS = P.tr(ps7[:, 0:16], bo16_sb[:, :], idf_sb[:, :], sem=ts)
            P.wait("scalar", ts, T_BIAS)
            P.act(bq_col[:, :], ps45[:, 0:1], AF.Copy, sem=ss)
            P.act(bk_col[:, :], ps45[:, 512:513], AF.Copy, sem=ss)
            P.act(bv_col[:, :], ps6[:, 0:1], AF.Copy, sem=ss)
            SS_PRO = P.act(bo_sb[:, :], ps7[:, 0:16], AF.Copy, sem=ss)

        stk_p1c = contextlib.ExitStack()
        qT = stk_p1c.enter_context(nc.sbuf_tensor("qT", [128, T], BF16))
        kT = stk_p1c.enter_context(nc.sbuf_tensor("kT", [128, T], BF16))
        vS = stk_p1c.enter_context(nc.sbuf_tensor("vS", [128, T], BF16))

        stk_p1a = contextlib.ExitStack()
        xb = [stk_p1a.enter_context(nc.sbuf_tensor(f"xb{p}", [128, KD * 512], BF16))
              for p in range(3)]
        xsqh = stk_p1a.enter_context(nc.sbuf_tensor("xsqh", [128, 8 * 512], BF16))
        m1s = [stk_p1a.enter_context(nc.sbuf_tensor(f"m1s{p}", [128, 512], F32))
               for p in range(3)]
        m2s = [stk_p1a.enter_context(nc.sbuf_tensor(f"m2s{p}", [128, 512], F32))
               for p in range(3)]
        tq_sb = stk_p1a.enter_context(nc.sbuf_tensor("tq_sb", [128, 512], F32))
        tk_sb = stk_p1a.enter_context(nc.sbuf_tensor("tk_sb", [128, 512], F32))
        tv_sb = stk_p1a.enter_context(nc.sbuf_tensor("tv_sb", [128, 512], F32))
        rbc_sb = stk_p1a.enter_context(nc.sbuf_tensor("rbc_sb", [128, 512], F32))
        t1_sb = rbc_sb
        t2_sb = tq_sb
        uq_sb = stk_p1a.enter_context(nc.sbuf_tensor("uq_sb", [128, 512], BF16))
        uk_sb = stk_p1a.enter_context(nc.sbuf_tensor("uk_sb", [128, 512], BF16))
        vr_sb = stk_p1a.enter_context(nc.sbuf_tensor("vr_sb", [128, 512], BF16))

        T_XB = [0] * NBLK
        T_MS = [0] * NBLK
        T_SK = {}
        T_SKL = [0] * NBLK
        T_RANK = [0] * NBLK
        SS_SROW = [0] * NBLK
        SS_RBC = [0] * NBLK
        SS_VSB = [0] * NBLK
        V_SQ = [0] * NBLK
        V_QEV = [0] * NBLK
        V_KEV = [0] * NBLK
        V_TV = [0] * NBLK
        V_QT2 = [0] * NBLK
        V_KT2 = [0] * NBLK
        V_KROT = [0] * NBLK

        for i in range(NBLK):
            p = i % 3
            c0 = i * 512
            if i >= 3:
                P.wait("gpsimd", ts, T_XB[i - 3])
                P.wait("gpsimd", vs, V_KROT[i - 3])
            else:
                P.wait("gpsimd", ss, SS_PRO)
            P.dma("gpsimd", xb[p].ap().rearrange("p (k t) -> p k t", k=KD),
                  xT[:, c0:c0 + 512].rearrange("(k p) t -> p k t", p=128), sx[p])
            P.dma("gpsimd", m1s[p][:, :], m1t[:, c0:c0 + 512], sx[p])
            SX_N = P.dma("gpsimd", m2s[p][:, :], m2t[:, c0:c0 + 512], sx[p])

            assert SX_N == 48 * (i // 3 + 1)
            if i == 0:
                P.wait("tensor", sx[p], SX_N)
                P.wait("tensor", ss, SS_PRO)
                for k in range(KD):
                    P.mm(ps4r, oneb_sb[:, :], xb[p][:, k * 512:(k + 1) * 512],
                         start=(k == 0), stop=(k == KD - 1))
            P.wait("tensor", sx[p], SX_N)
            if i >= 1:
                P.wait("tensor", vs, V_QEV[i - 1])
            for k in range(KD):
                P.mm(ps01[:, 0:512], wq_eff[:, k * 128:(k + 1) * 128],
                     xb[p][:, k * 512:(k + 1) * 512], start=(k == 0), stop=False)

            P.wait("vector", sx[p], SX_N)
            if i >= 1:
                P.wait("vector", ts, T_MS[i - 1])
            V_FA = 0
            for k in range(8):
                V_FA = P.vop("tt", out=xsqh[:, k * 512:(k + 1) * 512],
                             in0=xb[p][:, k * 512:(k + 1) * 512],
                             in1=xb[p][:, k * 512:(k + 1) * 512],
                             op=ALU.mult, sem=vs if k == 7 else None)
            P.wait("tensor", vs, V_FA)
            T_SA = 0
            for k in range(8):
                T_SA = P.mm(ps5r, oneb_sb[:, :], xsqh[:, k * 512:(k + 1) * 512],
                            start=(k == 0), stop=False,
                            sem=ts if k == 7 else None)

            if i >= 1:
                P.wait("tensor", vs, V_KEV[i - 1])
            for k in range(KD):
                P.mm(ps01[:, 512:1024], wk_sb[:, k * 128:(k + 1) * 128],
                     xb[p][:, k * 512:(k + 1) * 512], start=(k == 0), stop=False)

            P.wait("vector", ts, T_SA)
            V_FB = 0
            for k in range(8, KD):
                V_FB = P.vop("tt", out=xsqh[:, (k - 8) * 512:(k - 7) * 512],
                             in0=xb[p][:, k * 512:(k + 1) * 512],
                             in1=xb[p][:, k * 512:(k + 1) * 512],
                             op=ALU.mult, sem=vs if k == KD - 1 else None)
            P.wait("tensor", vs, V_FB)
            for k in range(8, KD):
                T_MS[i] = P.mm(ps5r, oneb_sb[:, :],
                               xsqh[:, (k - 8) * 512:(k - 7) * 512],
                               start=False, stop=(k == KD - 1),
                               sem=ts if k == KD - 1 else None)

            P.wait("scalar", ts, T_MS[i])
            P.act(mrow[:, :], ps4r, AF.Copy, scale=1.0 / D, sem=ss)
            SS_SROW[i] = P.act(srow[:, :], ps5r, AF.Copy, scale=1.0 / D, sem=ss)
            P.wait("vector", ss, SS_SROW[i])
            P.vop("tt", out=m2row[:, :], in0=mrow[:, :], in1=mrow[:, :],
                  op=ALU.mult, sem=vs)
            P.wait("vector", vs, vs.v)
            V_VAR = P.vop("tt", out=varrow[:, :], in0=srow[:, :], in1=m2row[:, :],
                          op=ALU.subtract, sem=vs)
            P.wait("scalar", vs, V_VAR)
            P.act(lnrow[:, :], varrow[:, :], AF.Ln, bias=cv_sb[0:1, 1:2], sem=ss)
            P.wait("scalar", ss, ss.v)
            SS_R = P.act(rrow[:, :], lnrow[:, :], AF.Exp, scale=-0.5, sem=ss)
            P.wait("vector", ss, SS_R)
            V_RM = P.vop("tt", out=rmrow[:, :], in0=rrow[:, :], in1=mrow[:, :],
                         op=ALU.mult, sem=vs)

            if i >= 1:
                P.wait("tensor", vs, V_TV[i - 1])
            for k in range(KD):
                T_XB[i] = P.mm(ps23[:, 0:512], wv_sb[:, k * 128:(k + 1) * 128],
                               xb[p][:, k * 512:(k + 1) * 512],
                               start=(k == 0), stop=False,
                               sem=ts if k == KD - 1 else None)
            if i >= 1:
                P.wait("tensor", ss, SS_RBC[i - 1])
            P.wait("tensor", ss, SS_R)
            P.mm(ps7[:, :], onerow_sb[:, :], rrow[:, :], True, True)
            P.wait("tensor", vs, V_RM)
            P.mm(ps01[:, 0:512], csq[:, :], rmrow[:, :], start=False, stop=True)
            P.mm(ps01[:, 512:1024], csk[:, :], rmrow[:, :], start=False, stop=True)
            T_RANK[i] = P.mm(ps23[:, 0:512], csv[:, :], rmrow[:, :],
                             start=False, stop=True, sem=ts)

            if i + 1 < NBLK:
                p2 = (i + 1) % 3
                P.wait("tensor", sx[p2], 48 * ((i + 1) // 3 + 1))
                P.wait("tensor", ss, SS_SROW[i])
                for k in range(KD):
                    P.mm(ps4r, oneb_sb[:, :], xb[p2][:, k * 512:(k + 1) * 512],
                         start=(k == 0), stop=(k == KD - 1))

            P.wait("scalar", ts, T_RANK[i])
            SS_RBC[i] = P.act(rbc_sb[:, :], ps7[:, :], AF.Copy, sem=ss)

            P.wait("vector", ts, T_RANK[i])
            P.wait("vector", ss, SS_RBC[i])
            V_QEV[i] = P.vop("tt", out=tq_sb[:, :], in0=ps01[:, 0:512],
                             in1=rbc_sb[:, :], op=ALU.mult, sem=vs)
            V_KEV[i] = P.vop("tt", out=tk_sb[:, :], in0=ps01[:, 512:1024],
                             in1=rbc_sb[:, :], op=ALU.mult, sem=vs)
            V_TV[i] = P.vop("tt", out=tv_sb[:, :], in0=ps23[:, 0:512],
                            in1=rbc_sb[:, :], op=ALU.mult, sem=vs)
            P.wait("scalar", vs, V_QEV[i])
            SS_UQ = P.act(uq_sb[:, :], tq_sb[:, :], AF.Identity, bias=bq_col[:, 0:1], sem=ss)
            P.wait("scalar", vs, V_KEV[i])
            SS_UK = P.act(uk_sb[:, :], tk_sb[:, :], AF.Identity, bias=bk_col[:, 0:1], sem=ss)
            P.wait("scalar", vs, V_TV[i])
            SS_VR = P.act(vr_sb[:, :], tv_sb[:, :], AF.Identity, bias=bv_col[:, 0:1], sem=ss)

            P.wait("tensor", ss, SS_UQ)
            if i >= 1:
                P.wait("tensor", vs, V_KT2[i - 1])
                P.wait("tensor", ss, SS_VSB[i - 1])
            P.mm(ps23[:, 512:1024], atm_sb[:, :], uq_sb[:, :], True, True)
            T_QAB = P.mm(ps6[:, :], btm_sb[:, :], uq_sb[:, :], True, True, sem=ts)
            P.wait("vector", ts, T_QAB)
            P.wait("vector", vs, V_TV[i])
            P.vop("tt", out=t1_sb[:, :], in0=ps23[:, 512:1024], in1=m1s[p][:, :],
                  op=ALU.mult, sem=vs)
            V_QT2[i] = P.vop("tt", out=t2_sb[:, :], in0=ps6[:, :], in1=m2s[p][:, :],
                             op=ALU.mult, sem=vs)
            P.wait("vector", vs, V_QT2[i])
            V_QROT = P.vop("tt", out=qT[:, c0:c0 + 512], in0=t1_sb[:, :],
                           in1=t2_sb[:, :], op=ALU.add, sem=vs)

            P.wait("tensor", ss, SS_UK)
            P.wait("tensor", vs, V_QT2[i])
            P.mm(ps23[:, 512:1024], atm_sb[:, :], uk_sb[:, :], True, True)
            T_KAB = P.mm(ps6[:, :], btm_sb[:, :], uk_sb[:, :], True, True, sem=ts)
            P.wait("vector", ts, T_KAB)
            P.wait("vector", vs, V_QROT)
            P.vop("tt", out=t1_sb[:, :], in0=ps23[:, 512:1024], in1=m1s[p][:, :],
                  op=ALU.mult, sem=vs)
            V_KT2[i] = P.vop("tt", out=t2_sb[:, :], in0=ps6[:, :], in1=m2s[p][:, :],
                             op=ALU.mult, sem=vs)
            P.wait("vector", vs, V_KT2[i])
            V_KROT[i] = P.vop("tt", out=kT[:, c0:c0 + 512], in0=t1_sb[:, :],
                              in1=t2_sb[:, :], op=ALU.add, sem=vs)

            P.wait("tensor", ss, SS_VR)
            P.wait("tensor", vs, V_KT2[i])
            ps6b = ps6.ap().bitcast(BF16)
            T_VTR = 0
            for j in range(4):
                T_VTR = P.tr(ps6b[:, j * 128:(j + 1) * 128],
                             vr_sb[:, j * 128:(j + 1) * 128], idb_sb[:, :],
                             sem=ts if j == 3 else None)
            P.wait("scalar", ts, T_VTR)
            for j in range(4):
                SS_VSB[i] = P.act(vS[:, (4 * i + j) * 128:(4 * i + j + 1) * 128],
                                  ps6b[:, j * 128:(j + 1) * 128], AF.Copy,
                                  sem=ss if j == 3 else None)

        stk_p1a.close()

        stk_p1b = contextlib.ExitStack()
        attT = [stk_p1b.enter_context(nc.sbuf_tensor(f"attT{p}", [128, 1024], BF16))
                for p in range(6)]
        attTm = stk_p1b.enter_context(nc.sbuf_tensor("attTm", [128, 1024], BF16))
        ybuf = [stk_p1b.enter_context(nc.sbuf_tensor(f"ybuf{p}", [128, 512], BF16))
                for p in range(2)]
        bcc_sb = stk_p1b.enter_context(nc.sbuf_tensor("bcc_sb", [128, 512], F32))
        mskd_sb = stk_p1b.enter_context(nc.sbuf_tensor("mskd_sb", [128, 2048], BF16))
        wstr = [stk_p1b.enter_context(nc.sbuf_tensor(f"wstr{p}", [128, 2048], F32))
                for p in range(4)]
        sgA = [stk_p1b.enter_context(nc.sbuf_tensor(f"sgA{p}", [128, 2048], BF16))
               for p in range(2)]
        sgB = [stk_p1b.enter_context(nc.sbuf_tensor(f"sgB{p}", [128, 2048], BF16))
               for p in range(2)]
        qbk = [stk_p1b.enter_context(nc.sbuf_tensor(f"qbk{p}", [128, 2048], BF16))
               for p in range(2)]

        P_BAR_T = ts.v
        P_BAR_V = vs.v
        P_BAR_S = ss.v
        P.wait("sync", ts, P_BAR_T)
        P.wait("sync", vs, P_BAR_V)
        P.wait("sync", ss, P_BAR_S)
        PRE_MSK = P.dma("sync", mskd_sb[:, :], mskd[:, :], spre)
        P.wait("vector", spre, PRE_MSK)

        units = []
        for hb in range(HB):
            units.append((0, w1T[:, hb * 128:(hb + 1) * 128]
                          .rearrange("(k p) h -> p k h", p=128),
                          w1q_h[:, hb * D:(hb + 1) * D], D))
        for hb in range(HB):
            units.append((1, w2T[:, hb * 128:(hb + 1) * 128]
                          .rearrange("(k p) h -> p k h", p=128),
                          w2q_h[:, hb * D:(hb + 1) * D], D))
        for dt in range(16):
            w3v = w3T[:, dt * 128:(dt + 1) * 128].rearrange("(k p) d_ -> p k d_", p=128)
            for k0 in range(0, HB, 16):
                k1 = min(k0 + 16, HB)
                units.append((2, w3v[:, k0:k1, :],
                              w3q_h[:, dt * HF + k0 * 128:dt * HF + k1 * 128],
                              (k1 - k0) * 128))
        assert len(units) == NW

        G_GRED = [0] * NW
        g_state = {"next": 0}

        def emit_gamma(n):
            for _ in range(n):
                j = g_state["next"]
                if j >= NW:
                    return
                g_state["next"] += 1
                pj = j % 4
                if j >= 4:
                    P.wait("sync", vs, G_GRED[j - 4])
                m, gsrc, _, gw = units[j]
                swn = P.dma("sync", wstr[pj][:, 0:gw]
                            .rearrange("p (k h) -> p k h", h=128), gsrc, swg[pj])
                P.wait("vector", swg[pj], swn)
                G_GRED[j] = P.vop("red", out=gred[:, j:j + 1], in_=wstr[pj][:, 0:gw],
                                  absv=True, sem=vs)

        V_SG = [0] * NW
        V_QB = [0] * NW
        q_state = {"next": 0, "V_GF": 0, "SS_GDONE": 0, "pend": None}

        def emit_qz(n):
            for _ in range(n):
                u = q_state["next"]
                if u >= NW:
                    if q_state["pend"] is not None:
                        q_state["pend"]()
                        q_state["pend"] = None
                    return
                q_state["next"] += 1
                pu = u % 4
                ps2 = u & 1
                m, src, dst, wdt = units[u]
                if u < 4:
                    P.wait("sync", vs, G_GRED[NW - 1])
                else:
                    P.wait("sync", vs, V_SG[u - 4])
                sqzn = P.dma("sync", wstr[pu][:, 0:wdt]
                             .rearrange("p (k h) -> p k h", h=128), src, sqz[pu])
                P.wait("vector", sqz[pu], sqzn)
                if u == 0:
                    P.wait("vector", ss, q_state["SS_GDONE"])
                if u >= 2:
                    P.wait("vector", vs, V_QB[u - 2])
                P.vop("ts2", out=sgA[ps2][:, 0:wdt], in0=wstr[pu][:, 0:wdt],
                      s1=bpc[m][:, 0:1], s2=0.0, op0=ALU.add, op1=ALU.is_ge,
                      sem=vs)
                V_SG[u] = P.vop("ts2", out=sgB[ps2][:, 0:wdt], in0=wstr[pu][:, 0:wdt],
                                s1=bnc[m][:, 0:1], s2=0.0, op0=ALU.add,
                                op1=ALU.is_ge, sem=vs)
                P.wait("vector", vs, V_SG[u])
                if u >= 2:
                    P.wait("vector", sqw[ps2], 16 * (u // 2))
                V_QB[u] = P.vop("stt", out=qbk[ps2][:, 0:wdt], in0=sgA[ps2][:, 0:wdt],
                                scalar=-1.0, in1=sgB[ps2][:, 0:wdt],
                                op0=ALU.add, op1=ALU.add, sem=vs)
                P.wait("sync", vs, V_QB[u])
                P.dma("sync", dst, qbk[ps2][:, 0:wdt], sqw[ps2])

        wides = [ps01, ps23, ps45]
        T_AV = {}
        T_SC = {}
        SS_EXP = {}
        V_MSK = {}
        V_Y = [0] * NBLK
        V_REC = [0] * NBLK
        SS_BCC = [0] * NBLK
        V_MSK_LAST = 0
        ps7_free = {"ss": P_BAR_S}
        FINB = max(1, min(NBLK - 2, (NBLK * 45) // 100))
        npre = sum(2 * (b + 1) for b in range(FINB))
        GPRATE = max(1, npre // (NW + 1))
        QHALF = (NW + 2 * (NBLK - 1 - FINB) - 1) // (2 * (NBLK - 1 - FINB))

        pairs = []
        for b in range(NBLK):
            npair = 2 * (b + 1)
            for pp in range(npair):
                pairs.append((b, pp, npair, b * 512))
        NP = len(pairs)

        def emit_scores(gp):
            b, pp, npair, q0 = pairs[gp]
            W = wides[gp % 3]
            if gp < 3:
                P.wait("tensor", vs, P_BAR_V)
                P.wait("tensor", ss, P_BAR_S)
            else:
                P.wait("tensor", ss, SS_EXP[gp - 3])
            st = 2 * pp
            P.mm(W[:, 0:512], kT[:, st * 128:(st + 1) * 128], qT[:, q0:q0 + 512],
                 True, True)
            T_SC[gp] = P.mm(W[:, 512:1024], kT[:, (st + 1) * 128:(st + 2) * 128],
                            qT[:, q0:q0 + 512], True, True, sem=ts)

        for gp in range(min(3, NP)):
            emit_scores(gp)

        for gp in range(NP):
            b, pp, npair, q0 = pairs[gp]
            W = wides[gp % 3]
            At = attT[gp % 6]
            masked = pp >= npair - 2
            st = 2 * pp
            P.wait("scalar", ts, T_SC[gp])
            if gp >= 6:
                P.wait("scalar", ts, T_AV[gp - 6])
            if masked and pp == npair - 1:
                P.wait("scalar", vs, V_MSK_LAST)
            tgt = attTm if masked else At
            SS_EXP[gp] = P.act(tgt[:, :], W[:, :], AF.Exp,
                               bias=cv_sb[:, 0:1], scale=SCALE, sem=ss)
            if masked:
                slot0 = st - 4 * b
                P.wait("vector", ss, SS_EXP[gp])
                P.vop("tt", out=At[:, 0:512], in0=attTm[:, 0:512],
                      in1=mskd_sb[:, slot0 * 512:(slot0 + 1) * 512],
                      op=ALU.mult, sem=vs)
                V_MSK_LAST = P.vop("tt", out=At[:, 512:1024],
                                   in0=attTm[:, 512:1024],
                                   in1=mskd_sb[:, (slot0 + 1) * 512:(slot0 + 2) * 512],
                                   op=ALU.mult, sem=vs)
            if masked:
                P.wait("tensor", vs, V_MSK_LAST)
            else:
                P.wait("tensor", ss, SS_EXP[gp])
            if pp == 0:
                if b >= 1:
                    P.wait("tensor", vs, V_Y[b - 1])
                P.wait("tensor", ss, ps7_free["ss"])
            P.mm(ps6[:, :], vS[:, st * 128:(st + 1) * 128], At[:, 0:512],
                 start=(pp == 0), stop=False)
            P.mm(ps6[:, :], vS[:, (st + 1) * 128:(st + 2) * 128], At[:, 512:1024],
                 start=False, stop=(pp == npair - 1))
            P.mm(ps7[0:1, :], oneb_sb[:, :], At[:, 0:512],
                 start=(pp == 0), stop=False)
            T_AV[gp] = P.mm(ps7[0:1, :], oneb_sb[:, :], At[:, 512:1024],
                            start=False, stop=(pp == npair - 1), sem=ts)
            if gp + 3 < NP:
                emit_scores(gp + 3)
            if b > FINB and (pp == 1 or pp == npair // 2):
                emit_qz(QHALF)
            elif b < FINB and gp % GPRATE == 0:
                emit_gamma(1)
            if pp != npair - 1:
                continue
            P.wait("vector", ts, T_AV[gp])
            V_REC[b] = P.vop("recip", out=recrow[:, :], in_=ps7[0:1, :], sem=vs)
            P.wait("tensor", vs, V_REC[b])
            T_BC = P.mm(ps7[:, :], onerow_sb[:, :], recrow[:, :], True, True, sem=ts)
            P.wait("scalar", ts, T_BC)
            SS_BCC[b] = P.act(bcc_sb[:, :], ps7[:, :], AF.Copy, sem=ss)
            ps7_free["ss"] = SS_BCC[b]
            P.wait("vector", ss, SS_BCC[b])
            if b >= 2:
                P.wait("vector", sy[b & 1], 16 * (b // 2))
            V_Y[b] = P.vop("tt", out=ybuf[b & 1][:, :], in0=ps6[:, :],
                           in1=bcc_sb[:, :], op=ALU.mult, sem=vs)
            P.wait("gpsimd", vs, V_Y[b])
            P.dma("gpsimd", ag_in[:, q0:q0 + 512], ybuf[b & 1][:, :], sy[b & 1])

            if b == FINB:
                emit_gamma(NW)
                P.wait("vector", vs, G_GRED[NW - 1])
                P.vop("red", out=gcol[:, 0:1], in_=gred[:, 0:HB])
                P.vop("red", out=gcol[:, 1:2], in_=gred[:, HB:2 * HB])
                V_GF = P.vop("red", out=gcol[:, 2:3], in_=gred[:, 2 * HB:NW],
                             sem=vs)
                q_state["V_GF"] = V_GF
                SS_GDONE = 0
                for m in range(3):
                    P.wait("tensor", vs, V_GF)
                    P.wait("tensor", ss, ps7_free["ss"] if m == 0 else SS_GDONE)
                    T_G1 = P.mm(ps7[0:1, 0:1], gcol[:, m:m + 1], onef_sb[:, :],
                                True, True, sem=ts)
                    P.wait("scalar", ts, T_G1)
                    SS_GE = P.act(ge_row[:, :], ps7[0:1, 0:1], AF.Identity,
                                  scale=1.0 / float(HF * D), bias=cv_sb[0:1, 2:3],
                                  sem=ss)
                    P.wait("tensor", ss, SS_GE)
                    T_G2 = P.mm(ps7[:, 0:1], onerow_sb[:, :], ge_row[:, :],
                                True, True, sem=ts)
                    P.wait("scalar", ts, T_G2)
                    P.act(bpc[m][:, :], ps7[:, 0:1], AF.Copy, scale=0.5, sem=ss)
                    SS_GDONE = P.act(bnc[m][:, :], ps7[:, 0:1], AF.Copy,
                                     scale=-0.5, sem=ss)
                q_state["SS_GDONE"] = SS_GDONE
                ps7_free["ss"] = SS_GDONE
        emit_qz(NW)

        P.wait("gpsimd", sy[0], sy[0].v)
        P.wait("gpsimd", sy[1], sy[1].v)
        CC_N = P.coll("AllGather", ALU.bypass, [[0, 1, 2, 3], [4, 5, 6, 7]],
                      [ag_in.ap().opt()], [ag_out.ap().opt()], scc)

        TS_QZ = ts.v
        VS_QZ = vs.v
        SS_QZ = ss.v
        stk_p1b.close()
        stk_p1c.close()

        stk_p2 = contextlib.ExitStack()
        p2 = lambda n, shp, dt: stk_p2.enter_context(nc.sbuf_tensor(n, shp, dt))
        xbo = p2("xbo", [128, KD * 512], BF16)
        cmb = [p2(f"cmb{c}", [128, 512], BF16) for c in range(4)]
        yTg = [p2(f"yTg{q}", [128, 4 * 512], BF16) for q in range(2)]
        gld = p2("gld", [128, 3 * 512], BF16)
        xq2 = [p2(f"xq2{q}", [128, 512], BF16) for q in range(2)]
        wo_sb = p2("wo_sb", [128, 4 * D], BF16)
        cs1 = p2("cs1", [1, HF], BF16)
        cs2 = p2("cs2", [1, HF], BF16)
        hid = p2("hid", [128, HB * 512], BF16)
        w1qb = [p2(f"w1qb{q}", [128, D], BF16) for q in range(3)]
        w2qb = [p2(f"w2qb{q}", [128, D], BF16) for q in range(3)]
        w3qb = [p2(f"w3qb{q}", [128, HF], BF16) for q in range(2)]
        xf32 = [p2(f"xf32{q}", [128, 512], F32) for q in range(2)]
        x1t = p2("x1t", [128, 512], F32)
        x2t = p2("x2t", [128, 512], F32)
        sig_sb = p2("sig_sb", [128, 512], BF16)
        rbc2 = p2("rbc2", [128, 512], F32)
        osb = [p2(f"osb{q}", [128, 512], F32) for q in range(2)]


        T_GRP = [0] * NGRP
        V_OUT = {}
        SS_RBC2 = [0] * NGRP
        SS_SIG_LAST = 0
        T_X12L = [0] * NGRP
        V_CMBL = [0] * NGRP
        T_S2KL = [0] * NGRP
        SS_SR2L = [0] * NGRP
        V_HIDL = [0] * NGRP

        for g in range(NGRP):
            pg = g & 1
            gc = g * 512
            if g >= 1:
                P.wait("gpsimd", ts, T_X12L[g - 1])
                P.wait("gpsimd", vs, V_CMBL[g - 1])
            SXB_N = P.dma("gpsimd", xbo.ap().rearrange("p (k t) -> p k t", k=KD),
                  xTo[:, gc:gc + 512].rearrange("(k p) t -> p k t", p=128), sxb[pg])
            if g == 0:
                P.wait("gpsimd", scc, CC_N)
            P.wait("tensor", sxb[pg], SXB_N)
            if g == 0:
                P.wait("tensor", ss, SS_EXP[NP - 1])
            else:
                P.wait("tensor", ss, SS_SR2L[g - 1])
            for k in range(KD):
                P.mm(ps4r, oneb_sb[:, :], xbo[:, k * 512:(k + 1) * 512],
                     start=(k == 0), stop=(k == KD - 1))
            P.wait("vector", sxb[pg], SXB_N)
            T_S2K = [0] * KD
            T_MS2 = 0
            for k in range(KD):
                if k >= 2:
                    P.wait("vector", ts, T_S2K[k - 2])
                elif g >= 1:
                    P.wait("vector", ts, T_S2KL[g - 1])
                vsq = P.vop("tt", out=xq2[k & 1][:, :],
                            in0=xbo[:, k * 512:(k + 1) * 512],
                            in1=xbo[:, k * 512:(k + 1) * 512],
                            op=ALU.mult, sem=vs)
                P.wait("tensor", vs, vsq)
                T_MS2 = P.mm(ps5r, oneb_sb[:, :], xq2[k & 1][:, :],
                             start=(k == 0), stop=(k == KD - 1), sem=ts)
                T_S2K[k] = T_MS2
            T_S2KL[g] = T_S2K[KD - 1]
            P.wait("scalar", ts, T_MS2)
            P.act(mrow[:, :], ps4r, AF.Copy, scale=1.0 / D, sem=ss)
            SS_SR2 = P.act(srow[:, :], ps5r, AF.Copy, scale=1.0 / D, sem=ss)
            SS_SR2L[g] = SS_SR2
            P.wait("vector", ss, SS_SR2)
            P.vop("tt", out=m2row[:, :], in0=mrow[:, :], in1=mrow[:, :],
                  op=ALU.mult, sem=vs)
            P.wait("vector", vs, vs.v)
            V_VAR2 = P.vop("tt", out=varrow[:, :], in0=srow[:, :], in1=m2row[:, :],
                           op=ALU.subtract, sem=vs)
            P.wait("scalar", vs, V_VAR2)
            P.act(lnrow[:, :], varrow[:, :], AF.Ln, bias=cv_sb[0:1, 1:2], sem=ss)
            P.wait("scalar", ss, ss.v)
            SS_R2 = P.act(rrow[:, :], lnrow[:, :], AF.Exp, scale=-0.5, sem=ss)
            P.wait("vector", ss, SS_R2)
            V_RM2 = P.vop("tt", out=rmrow[:, :], in0=rrow[:, :], in1=mrow[:, :],
                          op=ALU.mult, sem=vs)
            P.wait("tensor", ss, SS_R2)
            if g >= 1:
                P.wait("tensor", ss, SS_RBC2[g - 1])
                P.wait("tensor", vs, V_OUT[(g - 1, 15)])
            T_BC2 = P.mm(ps7[:, :], onerow_sb[:, :], rrow[:, :], True, True, sem=ts)
            P.wait("scalar", ts, T_BC2)
            SS_RBC2[g] = P.act(rbc2[:, :], ps7[:, :], AF.Copy, sem=ss)

            V_CMB = 0
            SG_N = 0
            for rt in range(4):
                if rt >= 1:
                    P.wait("gpsimd", vs, V_CMB)
                for c in range(4):
                    SG_N = P.dma("gpsimd", cmb[c][:, :],
                                 ag_out[rt * 128:(rt + 1) * 128,
                                        c * NTO + gc:c * NTO + gc + 512], sg[pg])
                P.wait("vector", sg[pg], SG_N)
                if rt == 0 and g == 0:
                    P.wait("vector", vs, VS_QZ)
                if rt == 0 and g >= 2:
                    P.wait("vector", ts, T_GRP[g - 2])
                if rt >= 1:
                    P.wait("vector", vs, V_CMB)
                P.vop("ts", out=gld[:, 0:512], in0=cmb[0][:, :],
                      s1=msel_sb[:, 0:1], op0=ALU.mult, sem=vs)
                P.wait("vector", vs, vs.v)
                P.vop("stt", out=gld[:, 512:1024], in0=cmb[1][:, :],
                      scalar=msel_sb[:, 1:2], in1=gld[:, 0:512],
                      op0=ALU.mult, op1=ALU.add, sem=vs)
                P.wait("vector", vs, vs.v)
                P.vop("stt", out=gld[:, 1024:1536], in0=cmb[2][:, :],
                      scalar=msel_sb[:, 2:3], in1=gld[:, 512:1024],
                      op0=ALU.mult, op1=ALU.add, sem=vs)
                P.wait("vector", vs, vs.v)
                V_CMB = P.vop("stt", out=yTg[pg][:, rt * 512:(rt + 1) * 512],
                              in0=cmb[3][:, :], scalar=msel_sb[:, 3:4],
                              in1=gld[:, 1024:1536], op0=ALU.mult, op1=ALU.add,
                              sem=vs)
            V_CMBL[g] = V_CMB
            if g == 0:
                P.wait("gpsimd", ts, TS_QZ)
                P.wait("gpsimd", vs, VS_QZ)
                P.wait("gpsimd", ss, SS_QZ)
                P.wait("gpsimd", sqw[0], sqw[0].v)
                P.wait("gpsimd", sqw[1], sqw[1].v)
                PREG2 = P.dma("gpsimd", wo_sb.ap().rearrange("p (k n) -> p k n", k=4),
                              wo_d.ap().rearrange("(k p) n -> p k n", p=128), spreg)

            V_HID = [0] * HB
            T_X12 = [0] * HB
            SS_CS2 = [0] * HB
            for hb in range(HB):
                ph = hb % 3
                psA = ps01 if (hb & 1) == 0 else ps23
                if hb >= 3:
                    P.wait("sync", ts, T_X12[hb - 3])
                elif g >= 1:
                    P.wait("sync", ts, T_GRP[g - 1])
                else:
                    P.wait("sync", sqw[0], sqw[0].v)
                    P.wait("sync", sqw[1], sqw[1].v)
                    P.wait("sync", ts, TS_QZ)
                    P.wait("sync", vs, VS_QZ)
                sqn = P.dma("sync", w1qb[ph][:, :], w1q_h[:, hb * D:(hb + 1) * D],
                            sq[ph])
                sqbn = P.dma("sync", w2qb[ph][:, :], w2q_h[:, hb * D:(hb + 1) * D],
                             sqb[ph])
                P.wait("tensor", sq[ph], sqn)
                P.wait("tensor", sqb[ph], sqbn)
                if hb >= 2:
                    P.wait("tensor", vs, V_HID[hb - 2])
                elif g >= 1:
                    P.wait("tensor", vs, V_HIDL[g - 1])
                if hb == 0:
                    P.wait("tensor", vs, V_RM2)
                    P.wait("tensor", vs, V_CMB)
                if g == 0:
                    if hb >= 1:
                        P.wait("tensor", ss, SS_CS2[hb - 1])
                    for k in range(KD):
                        P.mm(ps45[0:1, 0:128], oneb_sb[:, :],
                             w1qb[ph][:, k * 128:(k + 1) * 128],
                             start=(k == 0), stop=(k == KD - 1))
                    for k in range(KD):
                        T_CSC = P.mm(ps45[0:1, 512:640], oneb_sb[:, :],
                                     w2qb[ph][:, k * 128:(k + 1) * 128],
                                     start=(k == 0), stop=(k == KD - 1),
                                     sem=ts if k == KD - 1 else None)
                    P.wait("scalar", ts, T_CSC)
                    P.act(cs1[:, hb * 128:(hb + 1) * 128], ps45[0:1, 0:128],
                          AF.Copy, scale=-1.0, sem=ss)
                    SS_CS2[hb] = P.act(cs2[:, hb * 128:(hb + 1) * 128],
                                       ps45[0:1, 512:640], AF.Copy, scale=-1.0,
                                       sem=ss)
                for k in range(KD):
                    P.mm(psA[:, 0:512], w1qb[ph][:, k * 128:(k + 1) * 128],
                         xbo[:, k * 512:(k + 1) * 512], start=(k == 0), stop=False)
                if g == 0:
                    P.wait("tensor", ss, SS_CS2[hb])
                P.mm(psA[:, 0:512], cs1[:, hb * 128:(hb + 1) * 128], rmrow[:, :],
                     start=False, stop=True)
                for k in range(KD):
                    P.mm(psA[:, 512:1024], w2qb[ph][:, k * 128:(k + 1) * 128],
                         xbo[:, k * 512:(k + 1) * 512], start=(k == 0), stop=False)
                T_X12[hb] = P.mm(psA[:, 512:1024], cs2[:, hb * 128:(hb + 1) * 128],
                                 rmrow[:, :], start=False, stop=True, sem=ts)
                P.wait("vector", ts, T_X12[hb])
                if g == 0 and hb == 0:
                    P.wait("vector", ss, SS_RBC2[g])
                P.wait("vector", ss, SS_SIG_LAST)
                if hb >= 1:
                    P.wait("vector", vs, V_HID[hb - 1])
                P.vop("tt", out=x2t[:, :], in0=psA[:, 512:1024], in1=rbc2[:, :],
                      op=ALU.mult, sem=vs)
                V_X1 = P.vop("tt", out=x1t[:, :], in0=psA[:, 0:512], in1=rbc2[:, :],
                             op=ALU.mult, sem=vs)
                P.wait("scalar", vs, V_X1)
                SS_SIG_LAST = P.act(sig_sb[:, :], x1t[:, :], AF.Sigmoid, sem=ss)
                P.wait("vector", ss, SS_SIG_LAST)
                V_HID[hb] = P.vop("tt", out=hid[:, hb * 512:(hb + 1) * 512],
                                  in0=x2t[:, :], in1=sig_sb[:, :],
                                  op=ALU.mult, sem=vs)
            T_X12L[g] = T_X12[HB - 1]
            V_HIDL[g] = V_HID[HB - 1]

            T_OUTL = [0] * 16
            pend_store = [None]
            for dt in range(16):
                pw = dt & 1
                psO = ps6 if pw == 0 else ps7
                if dt >= 2:
                    P.wait("sync", ts, T_OUTL[dt - 2])
                elif g >= 1:
                    P.wait("sync", ts, T_GRP[g - 1])
                SW3_N = P.dma("sync", w3qb[pw][:, :], w3q_h[:, dt * HF:(dt + 1) * HF],
                              sw3[pw])
                if pend_store[0] is not None:
                    pend_store[0]()
                    pend_store[0] = None
                if dt >= 2:
                    P.wait("scalar", vs, V_OUT[(g, dt - 2)])
                elif g >= 1:
                    P.wait("scalar", vs, V_OUT[(g - 1, 14 + dt)])
                else:
                    P.wait("scalar", vs, VS_QZ)
                SXF_N = P.dma("scalar", xf32[pw][:, :],
                              xTo[dt * 128:(dt + 1) * 128, gc:gc + 512], sxf[pw])
                P.wait("tensor", sw3[pw], SW3_N)
                if dt == 0:
                    if g == 0:
                        P.wait("tensor", spreg, PREG2)
                    P.wait("tensor", vs, V_HID[HB - 1])
                if dt == 1:
                    P.wait("tensor", ss, SS_RBC2[g])
                if dt >= 2:
                    P.wait("tensor", vs, V_OUT[(g, dt - 2)])
                elif g >= 1:
                    P.wait("tensor", vs, V_OUT[(g - 1, 14 + dt)])
                for k in range(4):
                    P.mm(psO[:, :], wo_sb[:, k * D + dt * 128:k * D + (dt + 1) * 128],
                         yTg[pg][:, k * 512:(k + 1) * 512], start=(k == 0), stop=False)
                T_OUT = 0
                for k in range(HB):
                    T_OUT = P.mm(psO[:, :], w3qb[pw][:, k * 128:(k + 1) * 128],
                                 hid[:, k * 512:(k + 1) * 512], start=False,
                                 stop=(k == HB - 1), sem=ts if k == HB - 1 else None)
                T_OUTL[dt] = T_OUT
                nglob = g * 16 + dt
                P.wait("vector", ts, T_OUT)
                P.wait("vector", sxf[pw], SXF_N)
                if nglob >= 2:
                    P.wait("vector", sout[pw], 16 * (nglob // 2))
                V_OUT[(g, dt)] = P.vop("stt", out=osb[pw][:, :], in0=psO[:, :],
                                       scalar=bo_sb[:, dt:dt + 1], in1=xf32[pw][:, :],
                                       op0=ALU.add, op1=ALU.add, sem=vs)
                def _store(g=g, dt=dt, pw=pw, gc=gc):
                    P.wait("sync", vs, V_OUT[(g, dt)])
                    P.dma("sync", outT[dt * 128:(dt + 1) * 128, gc:gc + 512],
                          osb[pw][:, :], sout[pw])
                if dt == 15:
                    _store()
                else:
                    pend_store[0] = _store
            T_GRP[g] = T_OUTL[15]

        stk_p2.close()

        P.wait("sync", sout[0], sout[0].v)
        P.wait("sync", sout[1], sout[1].v)

        with nc.Block() as block:
            @block.sync
            def _(e):
                for f in P.q["sync"]:
                    f(e)

            @block.gpsimd
            def _(e):
                for f in P.q["gpsimd"]:
                    f(e)

            @block.tensor
            def _(e):
                for f in P.q["tensor"]:
                    f(e)

            @block.vector
            def _(e):
                for f in P.q["vector"]:
                    f(e)

            @block.scalar
            def _(e):
                for f in P.q["scalar"]:
                    f(e)

    return nc


def _constants(T):
    j = np.arange(64, dtype=np.float64)
    inv = 10000.0 ** (-2.0 * j / 128.0)
    t = np.arange(T, dtype=np.float64)
    emb = np.outer(t, inv)
    cos = np.cos(emb).T
    sin = np.sin(emb).T
    m1 = np.concatenate([cos, sin], axis=0).astype(np.float32)
    m2 = np.concatenate([-sin, cos], axis=0).astype(np.float32)
    A = np.zeros((128, 128), np.float32)
    Bm = np.zeros((128, 128), np.float32)
    for i in range(64):
        A[i, 2 * i] = 1.0
        A[64 + i, 2 * i] = 1.0
        Bm[i, 2 * i + 1] = 1.0
        Bm[64 + i, 2 * i + 1] = 1.0
    mskd = np.zeros((128, 2048), np.float32)
    sl = np.arange(128)
    ql = np.arange(128)
    for i in range(4):
        for qt in range(4):
            blockm = ((i * 128 + sl)[:, None] <= (qt * 128 + ql)[None, :])
            mskd[:, i * 512 + qt * 128:i * 512 + (qt + 1) * 128] = blockm
    return {
        "m1t": m1, "m2t": m2,
        "atm": np.ascontiguousarray(A.T).astype(BFNP),
        "btm": np.ascontiguousarray(Bm.T).astype(BFNP),
        "idb": np.eye(128, dtype=np.float32).astype(BFNP),
        "idf": np.eye(16, dtype=np.float32),
        "oneb": np.ones((128, 1), np.float32).astype(BFNP),
        "onef": np.ones((128, 1), np.float32),
        "onerow": np.ones((1, 128), np.float32),
        "halfrow": np.full((1, 128), 0.5, np.float32),
        "mskd": mskd.astype(BFNP),
        "cvals": np.repeat(np.array([[ESHIFT, LNEPS, QEPS, 0.0]], np.float32), 128, 0),
    }


def _stage_core(c, x, Wq, bq, Wk, bk, Wv, bv, Wo, bo, w1T, w2T, w3T, consts, T):
    b, h = c // 4, c % 4
    NTO = T // 4
    d = dict(consts)
    d["xT"] = np.ascontiguousarray(x[b].T)
    d["xTo"] = np.ascontiguousarray(x[b, h * NTO:(h + 1) * NTO].T)
    d["wq_s"] = np.ascontiguousarray(Wq[:, h * 512:(h + 1) * 512])
    d["wk_s"] = np.ascontiguousarray(Wk[:, h * 128:(h + 1) * 128])
    d["wv_s"] = np.ascontiguousarray(Wv[:, h * 128:(h + 1) * 128])
    d["bq4"] = np.ascontiguousarray(bq[h * 512:(h + 1) * 512].reshape(4, 128))
    d["bk1"] = np.ascontiguousarray(bk[h * 128:(h + 1) * 128].reshape(1, 128))
    d["bv1"] = np.ascontiguousarray(bv[h * 128:(h + 1) * 128].reshape(1, 128))
    d["wo"] = np.ascontiguousarray(Wo)
    d["bo16"] = np.ascontiguousarray(bo.reshape(16, 128))
    d["w1T"] = w1T
    d["w2T"] = w2T
    d["w3T"] = w3T
    ms = np.zeros((128, 4), np.float32)
    ms[:, h] = 1.0
    d["msel"] = ms
    return d


def stage_all(x, Wq, bq, Wk, bk, Wv, bv, Wo, bo, w1, w2, w3):
    T = x.shape[1]
    w1T = np.ascontiguousarray(np.asarray(w1, dtype=np.float32).T)
    w2T = np.ascontiguousarray(np.asarray(w2, dtype=np.float32).T)
    w3T = np.ascontiguousarray(np.asarray(w3, dtype=np.float32).T)
    consts = _constants(T)
    return [_stage_core(c, np.asarray(x, dtype=np.float32),
                        np.asarray(Wq, dtype=np.float32), np.asarray(bq, dtype=np.float32),
                        np.asarray(Wk, dtype=np.float32), np.asarray(bk, dtype=np.float32),
                        np.asarray(Wv, dtype=np.float32), np.asarray(bv, dtype=np.float32),
                        np.asarray(Wo, dtype=np.float32), np.asarray(bo, dtype=np.float32),
                        w1T, w2T, w3T, consts, T)
            for c in range(NCORE)]


def kernel(L0_attn_x, L0_attn_Wq, L0_attn_bq, L0_attn_Wk, L0_attn_bk,
           L0_attn_Wv, L0_attn_bv, L0_attn_Wo, L0_attn_bo,
           L0_ffn_w1, L0_ffn_w2, L0_ffn_w3):
    T = L0_attn_x.shape[1]
    HF = L0_ffn_w1.shape[0]
    in_maps = stage_all(L0_attn_x, L0_attn_Wq, L0_attn_bq, L0_attn_Wk, L0_attn_bk,
                        L0_attn_Wv, L0_attn_bv, L0_attn_Wo, L0_attn_bo,
                        L0_ffn_w1, L0_ffn_w2, L0_ffn_w3)
    nc = build_nc(T, HF)
    res = run_bass_kernel_spmd(nc, in_maps, core_ids=list(range(NCORE)))
    out = np.empty((2, T, D), np.float32)
    NTO = T // 4
    for c in range(NCORE):
        b, h = c // 4, c % 4
        out[b, h * NTO:(h + 1) * NTO, :] = np.asarray(res.results[c]["outT"]).T
    return out


# version 32
# speedup vs baseline: 1.0206x; 1.0110x over previous
import contextlib
import numpy as np
import ml_dtypes

import concourse.bass as bass
import concourse.mybir as mybir
from concourse.bass_utils import run_bass_kernel_spmd

F32 = mybir.dt.float32
BF16 = mybir.dt.bfloat16
AF = mybir.ActivationFunctionType
ALU = mybir.AluOpType
AX = mybir.AxisListType
BFNP = ml_dtypes.bfloat16

D = 2048
KD = D // 128
NCORE = 8
SCALE = float(np.float64(1.0) / np.sqrt(np.float64(128.0)))
ESHIFT = -20.0
LNEPS = 1e-5
QEPS = 1e-8


class Sem:
    def __init__(self, name):
        self.name = name
        self.v = 0
        self.h = None

    def inc(self, n=1):
        self.v += n
        return self.v


class Plan:
    def __init__(self):
        self.q = {"sync": [], "gpsimd": [], "tensor": [], "vector": [], "scalar": []}

    def wait(self, eng, sem, val):
        if val and val > 0:
            self.q[eng].append(lambda e, s=sem, v=val: e.wait_ge(s.h, v))

    def mm(self, out, lhsT, rhs, start, stop, sem=None):
        v = sem.inc(1) if sem is not None else None

        def f(e, out=out, lhsT=lhsT, rhs=rhs, start=start, stop=stop, sem=sem):
            r = e.matmul(out, lhsT, rhs, start=start, stop=stop)
            if sem is not None:
                r.then_inc(sem.h, 1)
        self.q["tensor"].append(f)
        return v

    def tr(self, out, in_, ident, sem=None):
        v = sem.inc(1) if sem is not None else None

        def f(e, out=out, in_=in_, ident=ident, sem=sem):
            r = e.transpose(out, in_, ident)
            if sem is not None:
                r.then_inc(sem.h, 1)
        self.q["tensor"].append(f)
        return v

    def act(self, out, in_, func, bias=None, scale=None, sem=None):
        v = sem.inc(1) if sem is not None else None

        def f(e, out=out, in_=in_, func=func, bias=bias, scale=scale, sem=sem):
            kw = {}
            if bias is not None:
                kw["bias"] = bias
            if scale is not None:
                kw["scale"] = scale
            r = e.activation(out, in_, func, **kw)
            if sem is not None:
                r.then_inc(sem.h, 1)
        self.q["scalar"].append(f)
        return v

    def vop(self, kind, sem=None, **kw):
        v = sem.inc(1) if sem is not None else None

        def f(e, kind=kind, kw=kw, sem=sem):
            if kind == "tt":
                r = e.tensor_tensor(kw["out"], kw["in0"], kw["in1"], kw["op"])
            elif kind == "ts":
                r = e.tensor_scalar(kw["out"], kw["in0"], kw["s1"], None,
                                    kw["op0"])
            elif kind == "ts2":
                r = e.tensor_scalar(kw["out"], kw["in0"], kw["s1"], kw["s2"],
                                    kw["op0"], kw["op1"])
            elif kind == "stt":
                r = e.scalar_tensor_tensor(kw["out"], kw["in0"], kw["scalar"],
                                           kw["in1"], kw["op0"], kw["op1"])
            elif kind == "red":
                r = e.reduce_sum(kw["out"], kw["in_"], axis=AX.X,
                                 apply_absolute_value=kw.get("absv", False))
            elif kind == "recip":
                r = e.reciprocal(kw["out"], kw["in_"])
            else:
                raise ValueError(kind)
            if sem is not None:
                r.then_inc(sem.h, 1)
        self.q["vector"].append(f)
        return v

    def pop_red(self, out, in_, absv, sem=None):
        v = sem.inc(1) if sem is not None else None

        def f(e, out=out, in_=in_, absv=absv, sem=sem):
            r = e.reduce_sum(out, in_, axis=AX.XYZWC, apply_absolute_value=absv)
            if sem is not None:
                r.then_inc(sem.h, 1)
        self.q["gpsimd"].append(f)
        return v

    def dma(self, eng, out, in_, sem):
        v = sem.inc(16)

        def f(e, out=out, in_=in_, sem=sem):
            e.dma_start(out=out, in_=in_).then_inc(sem.h, 16)
        self.q[eng].append(f)
        return v

    def coll(self, kind, op, groups, ins, outs, sem):
        v = sem.inc(1)

        def f(e, kind=kind, op=op, groups=groups, ins=ins, outs=outs, sem=sem):
            e.collective_compute(kind, op, replica_groups=groups, ins=ins,
                                 outs=outs).then_inc(sem.h, 1)
        self.q["gpsimd"].append(f)
        return v


def build_nc(T, HF):
    NBLK = T // 512
    NTO = T // 4
    NGRP = NTO // 512
    HB = HF // 128
    WMX = max(D, HF)
    assert T % 2048 == 0 and HF % 128 == 0

    nc = bass.Bass("TRN2", target_bir_lowering=False, debug=False)

    xT = nc.dram_tensor("xT", [D, T], F32, kind="ExternalInput")
    xTo = nc.dram_tensor("xTo", [D, NTO], F32, kind="ExternalInput")
    wq_s = nc.dram_tensor("wq_s", [D, 512], F32, kind="ExternalInput")
    wk_s = nc.dram_tensor("wk_s", [D, 128], F32, kind="ExternalInput")
    wv_s = nc.dram_tensor("wv_s", [D, 128], F32, kind="ExternalInput")
    bq4 = nc.dram_tensor("bq4", [4, 128], F32, kind="ExternalInput")
    bk1 = nc.dram_tensor("bk1", [1, 128], F32, kind="ExternalInput")
    bv1 = nc.dram_tensor("bv1", [1, 128], F32, kind="ExternalInput")
    wo_d = nc.dram_tensor("wo", [512, D], F32, kind="ExternalInput")
    bo16 = nc.dram_tensor("bo16", [16, 128], F32, kind="ExternalInput")
    w1T = nc.dram_tensor("w1T", [D, HF], F32, kind="ExternalInput")
    w2T = nc.dram_tensor("w2T", [D, HF], F32, kind="ExternalInput")
    w3T = nc.dram_tensor("w3T", [HF, D], F32, kind="ExternalInput")
    m1t = nc.dram_tensor("m1t", [128, T], F32, kind="ExternalInput")
    m2t = nc.dram_tensor("m2t", [128, T], F32, kind="ExternalInput")
    atm = nc.dram_tensor("atm", [128, 128], BF16, kind="ExternalInput")
    btm = nc.dram_tensor("btm", [128, 128], BF16, kind="ExternalInput")
    idb = nc.dram_tensor("idb", [128, 128], BF16, kind="ExternalInput")
    idf = nc.dram_tensor("idf", [16, 16], F32, kind="ExternalInput")
    oneb = nc.dram_tensor("oneb", [128, 1], BF16, kind="ExternalInput")
    onef = nc.dram_tensor("onef", [128, 1], F32, kind="ExternalInput")
    onerow = nc.dram_tensor("onerow", [1, 128], F32, kind="ExternalInput")
    halfrow = nc.dram_tensor("halfrow", [1, 128], F32, kind="ExternalInput")
    mskd = nc.dram_tensor("mskd", [128, 2048], BF16, kind="ExternalInput")
    msel = nc.dram_tensor("msel", [128, 4], F32, kind="ExternalInput")
    cvals = nc.dram_tensor("cvals", [128, 4], F32, kind="ExternalInput")

    outT = nc.dram_tensor("outT", [D, NTO], F32, kind="ExternalOutput")

    ag_in = nc.dram_tensor("ag_in", [128, T], BF16)
    ag_out = nc.dram_tensor("ag_out", [512, T], BF16)
    w1q_h = nc.dram_tensor("w1q_h", [128, HB * D], BF16)
    w2q_h = nc.dram_tensor("w2q_h", [128, HB * D], BF16)
    w3q_h = nc.dram_tensor("w3q_h", [128, 16 * HF], BF16)

    P = Plan()
    NW = 2 * HB + 16 * ((HB + 15) // 16)

    with contextlib.ExitStack() as G:
        sems = {}
        for nm in ["tsem", "vsem", "ssem", "spre", "spreg", "sx0", "sx1", "sx2",
                   "sy0", "sy1", "scc", "psem", "swg0", "swg1", "swg2", "swg3", "sqz0", "sqz1", "sqz2", "sqz3",
                   "sqw0", "sqw1", "sq0", "sq1", "sq2", "sqb0", "sqb1", "sqb2", "sw30", "sw31", "sxf0", "sxf1",
                   "sg0", "sg1", "sxb0", "sxb1", "sout0", "sout1"]:
            s = Sem(nm)
            s.h = G.enter_context(nc.semaphore(nm))
            sems[nm] = s
        ts, vs, ss = sems["tsem"], sems["vsem"], sems["ssem"]
        sx = [sems["sx0"], sems["sx1"], sems["sx2"]]
        swg = [sems[f"swg{i}"] for i in range(4)]
        sqz = [sems[f"sqz{i}"] for i in range(4)]
        psm = sems["psem"]
        sq = [sems["sq0"], sems["sq1"], sems["sq2"]]
        sqb = [sems["sqb0"], sems["sqb1"], sems["sqb2"]]
        sw3 = [sems["sw30"], sems["sw31"]]
        sxf = [sems["sxf0"], sems["sxf1"]]
        sg = [sems["sg0"], sems["sg1"]]
        sxb = [sems["sxb0"], sems["sxb1"]]
        spre, spreg = sems["spre"], sems["spreg"]
        scc = sems["scc"]
        sy = [sems["sy0"], sems["sy1"]]
        sqw = [sems["sqw0"], sems["sqw1"]]
        sout = [sems["sout0"], sems["sout1"]]

        ps01 = G.enter_context(nc.psum_tensor("ps01", [128, 1024], F32))
        ps23 = G.enter_context(nc.psum_tensor("ps23", [128, 1024], F32))
        ps45 = G.enter_context(nc.psum_tensor("ps45", [128, 1024], F32))
        ps6 = G.enter_context(nc.psum_tensor("ps6", [128, 512], F32))
        ps7 = G.enter_context(nc.psum_tensor("ps7", [128, 512], F32))
        ps4r = ps45[0:1, 0:512]
        ps5r = ps45[0:1, 512:1024]

        sb = lambda n, shp, dt: G.enter_context(nc.sbuf_tensor(n, shp, dt))
        wq_eff = sb("wq_eff", [128, KD * 128], BF16)
        wk_sb = sb("wk_sb", [128, KD * 128], BF16)
        wv_sb = sb("wv_sb", [128, KD * 128], BF16)
        atm_sb = sb("atm_sb", [128, 128], BF16)
        btm_sb = sb("btm_sb", [128, 128], BF16)
        idb_sb = sb("idb_sb", [128, 128], BF16)
        idf_sb = sb("idf_sb", [16, 16], F32)
        oneb_sb = sb("oneb_sb", [128, 1], BF16)
        onef_sb = sb("onef_sb", [128, 1], F32)
        onerow_sb = sb("onerow_sb", [1, 128], F32)
        halfrow_sb = sb("halfrow_sb", [1, 128], F32)
        msel_sb = sb("msel_sb", [128, 4], F32)
        cv_sb = sb("cv_sb", [128, 4], F32)
        bq_col = sb("bq_col", [128, 1], F32)
        bk_col = sb("bk_col", [128, 1], F32)
        bv_col = sb("bv_col", [128, 1], F32)
        bo_sb = sb("bo_sb", [128, 16], F32)
        csq = sb("csq", [1, 128], BF16)
        csk = sb("csk", [1, 128], BF16)
        csv = sb("csv", [1, 128], BF16)
        mrow = sb("mrow", [1, 512], F32)
        srow = sb("srow", [1, 512], F32)
        m2row = sb("m2row", [1, 512], F32)
        varrow = sb("varrow", [1, 512], F32)
        lnrow = m2row
        rrow = sb("rrow", [1, 512], F32)
        rmrow = sb("rmrow", [1, 512], BF16)
        recrow = sb("recrow", [1, 512], F32)
        gred = sb("gred", [128, NW], F32)
        gcol = sb("gcol", [128, 3], F32)
        ge_row = sb("ge_row", [1, 1], F32)
        bpc = [sb(f"bp{m}", [128, 1], F32) for m in range(3)]
        bnc = [sb(f"bn{m}", [128, 1], F32) for m in range(3)]

        with contextlib.ExitStack() as PR:
            wq_st = PR.enter_context(nc.sbuf_tensor("wq_st", [128, KD * 512], F32))
            bq4_sb = PR.enter_context(nc.sbuf_tensor("bq4_sb", [4, 128], F32))
            bk1_sb = PR.enter_context(nc.sbuf_tensor("bk1_sb", [1, 128], F32))
            bv1_sb = PR.enter_context(nc.sbuf_tensor("bv1_sb", [1, 128], F32))
            bo16_sb = PR.enter_context(nc.sbuf_tensor("bo16_sb", [16, 128], F32))
            tqa = PR.enter_context(nc.sbuf_tensor("tqa", [128, 128], F32))
            tqb = PR.enter_context(nc.sbuf_tensor("tqb", [128, 128], F32))
            PRE_N = 0
            for dst, src in [(wq_st.ap().rearrange("p (k i) -> p k i", k=KD), wq_s.ap().rearrange("(k p) i -> p k i", p=128)),
                             (atm_sb[:, :], atm[:, :]), (btm_sb[:, :], btm[:, :]),
                             (idb_sb[:, :], idb[:, :]), (idf_sb[:, :], idf[:, :]),
                             (oneb_sb[:, :], oneb[:, :]), (onef_sb[:, :], onef[:, :]),
                             (onerow_sb[:, :], onerow[:, :]), (halfrow_sb[:, :], halfrow[:, :]),
                             (msel_sb[:, :], msel[:, :]),
                             (cv_sb[:, :], cvals[:, :]),
                             (bq4_sb[:, :], bq4[:, :]), (bk1_sb[:, :], bk1[:, :]),
                             (bv1_sb[:, :], bv1[:, :]), (bo16_sb[:, :], bo16[:, :])]:
                PRE_N = P.dma("sync", dst, src, spre)
            PREG_N = P.dma("gpsimd", wk_sb.ap().rearrange("p (k i) -> p k i", k=KD),
                           wk_s.ap().rearrange("(k p) i -> p k i", p=128), spreg)
            PREG_N = P.dma("gpsimd", wv_sb.ap().rearrange("p (k i) -> p k i", k=KD),
                           wv_s.ap().rearrange("(k p) i -> p k i", p=128), spreg)

            P.wait("vector", spre, PRE_N)
            V_WQE = 0
            for k in range(KD):
                c = k * 512
                if k > 0:
                    P.wait("vector", vs, V_WQE)
                P.vop("tt", out=tqa[:, :], in0=wq_st[:, c:c + 128],
                      in1=wq_st[:, c + 128:c + 256], op=ALU.add)
                P.vop("tt", out=tqb[:, :], in0=wq_st[:, c + 256:c + 384],
                      in1=wq_st[:, c + 384:c + 512], op=ALU.add, sem=vs)
                P.wait("vector", vs, vs.v)
                V_WQE = P.vop("tt", out=wq_eff[:, k * 128:(k + 1) * 128],
                              in0=tqa[:, :], in1=tqb[:, :], op=ALU.add, sem=vs)

            P.wait("tensor", vs, V_WQE)
            P.wait("tensor", spre, PRE_N)
            P.wait("tensor", spreg, PREG_N)
            T_CSQ = 0
            for k in range(KD):
                T_CSQ = P.mm(ps45[0:1, 0:128], oneb_sb[:, :],
                             wq_eff[:, k * 128:(k + 1) * 128],
                             start=(k == 0), stop=(k == KD - 1),
                             sem=ts if k == KD - 1 else None)
            for k in range(KD):
                P.mm(ps6[0:1, 0:128], oneb_sb[:, :], wk_sb[:, k * 128:(k + 1) * 128],
                     start=(k == 0), stop=(k == KD - 1))
            T_CSKV = 0
            for k in range(KD):
                T_CSKV = P.mm(ps7[0:1, 0:128], oneb_sb[:, :],
                              wv_sb[:, k * 128:(k + 1) * 128],
                              start=(k == 0), stop=(k == KD - 1),
                              sem=ts if k == KD - 1 else None)
            P.wait("scalar", ts, T_CSQ)
            P.act(csq[:, :], ps45[0:1, 0:128], AF.Copy, scale=-1.0, sem=ss)
            P.wait("scalar", ts, T_CSKV)
            P.act(csk[:, :], ps6[0:1, 0:128], AF.Copy, scale=-1.0, sem=ss)
            SS_CS = P.act(csv[:, :], ps7[0:1, 0:128], AF.Copy, scale=-1.0, sem=ss)
            P.wait("tensor", ss, SS_CS)
            P.mm(ps45[:, 0:1], bq4_sb[:, :], onef_sb[0:4, 0:1], True, True)
            P.mm(ps45[:, 512:513], bk1_sb[:, :], onef_sb[0:1, 0:1], True, True)
            P.mm(ps6[:, 0:1], bv1_sb[:, :], onef_sb[0:1, 0:1], True, True)
            T_BIAS = P.tr(ps7[:, 0:16], bo16_sb[:, :], idf_sb[:, :], sem=ts)
            P.wait("scalar", ts, T_BIAS)
            P.act(bq_col[:, :], ps45[:, 0:1], AF.Copy, sem=ss)
            P.act(bk_col[:, :], ps45[:, 512:513], AF.Copy, sem=ss)
            P.act(bv_col[:, :], ps6[:, 0:1], AF.Copy, sem=ss)
            SS_PRO = P.act(bo_sb[:, :], ps7[:, 0:16], AF.Copy, sem=ss)

        stk_p1c = contextlib.ExitStack()
        qT = stk_p1c.enter_context(nc.sbuf_tensor("qT", [128, T], BF16))
        kT = stk_p1c.enter_context(nc.sbuf_tensor("kT", [128, T], BF16))
        vS = stk_p1c.enter_context(nc.sbuf_tensor("vS", [128, T], BF16))

        stk_p1a = contextlib.ExitStack()
        xb = [stk_p1a.enter_context(nc.sbuf_tensor(f"xb{p}", [128, KD * 512], BF16))
              for p in range(3)]
        xsqh = stk_p1a.enter_context(nc.sbuf_tensor("xsqh", [128, 8 * 512], BF16))
        m1s = [stk_p1a.enter_context(nc.sbuf_tensor(f"m1s{p}", [128, 512], F32))
               for p in range(3)]
        m2s = [stk_p1a.enter_context(nc.sbuf_tensor(f"m2s{p}", [128, 512], F32))
               for p in range(3)]
        tq_sb = stk_p1a.enter_context(nc.sbuf_tensor("tq_sb", [128, 512], F32))
        tk_sb = stk_p1a.enter_context(nc.sbuf_tensor("tk_sb", [128, 512], F32))
        tv_sb = stk_p1a.enter_context(nc.sbuf_tensor("tv_sb", [128, 512], F32))
        rbc_sb = stk_p1a.enter_context(nc.sbuf_tensor("rbc_sb", [128, 512], F32))
        t1_sb = rbc_sb
        t2_sb = tq_sb
        uq_sb = stk_p1a.enter_context(nc.sbuf_tensor("uq_sb", [128, 512], BF16))
        uk_sb = stk_p1a.enter_context(nc.sbuf_tensor("uk_sb", [128, 512], BF16))
        vr_sb = stk_p1a.enter_context(nc.sbuf_tensor("vr_sb", [128, 512], BF16))

        T_XB = [0] * NBLK
        T_MS = [0] * NBLK
        T_SK = {}
        T_SKL = [0] * NBLK
        T_RANK = [0] * NBLK
        SS_SROW = [0] * NBLK
        SS_RBC = [0] * NBLK
        SS_VSB = [0] * NBLK
        V_SQ = [0] * NBLK
        V_QEV = [0] * NBLK
        V_KEV = [0] * NBLK
        V_TV = [0] * NBLK
        V_QT2 = [0] * NBLK
        V_KT2 = [0] * NBLK
        V_KROT = [0] * NBLK

        for i in range(NBLK):
            p = i % 3
            c0 = i * 512
            if i >= 3:
                P.wait("gpsimd", ts, T_XB[i - 3])
                P.wait("gpsimd", vs, V_KROT[i - 3])
            else:
                P.wait("gpsimd", ss, SS_PRO)
            P.dma("gpsimd", xb[p].ap().rearrange("p (k t) -> p k t", k=KD),
                  xT[:, c0:c0 + 512].rearrange("(k p) t -> p k t", p=128), sx[p])
            P.dma("gpsimd", m1s[p][:, :], m1t[:, c0:c0 + 512], sx[p])
            SX_N = P.dma("gpsimd", m2s[p][:, :], m2t[:, c0:c0 + 512], sx[p])

            assert SX_N == 48 * (i // 3 + 1)
            if i == 0:
                P.wait("tensor", sx[p], SX_N)
                P.wait("tensor", ss, SS_PRO)
                for k in range(KD):
                    P.mm(ps4r, oneb_sb[:, :], xb[p][:, k * 512:(k + 1) * 512],
                         start=(k == 0), stop=(k == KD - 1))
            P.wait("tensor", sx[p], SX_N)
            if i >= 1:
                P.wait("tensor", vs, V_QEV[i - 1])
            for k in range(KD):
                P.mm(ps01[:, 0:512], wq_eff[:, k * 128:(k + 1) * 128],
                     xb[p][:, k * 512:(k + 1) * 512], start=(k == 0), stop=False)

            P.wait("vector", sx[p], SX_N)
            if i >= 1:
                P.wait("vector", ts, T_MS[i - 1])
            V_FA = 0
            for k in range(8):
                V_FA = P.vop("tt", out=xsqh[:, k * 512:(k + 1) * 512],
                             in0=xb[p][:, k * 512:(k + 1) * 512],
                             in1=xb[p][:, k * 512:(k + 1) * 512],
                             op=ALU.mult, sem=vs if k == 7 else None)
            P.wait("tensor", vs, V_FA)
            T_SA = 0
            for k in range(8):
                T_SA = P.mm(ps5r, oneb_sb[:, :], xsqh[:, k * 512:(k + 1) * 512],
                            start=(k == 0), stop=False,
                            sem=ts if k == 7 else None)

            if i >= 1:
                P.wait("tensor", vs, V_KEV[i - 1])
            for k in range(KD):
                P.mm(ps01[:, 512:1024], wk_sb[:, k * 128:(k + 1) * 128],
                     xb[p][:, k * 512:(k + 1) * 512], start=(k == 0), stop=False)

            P.wait("vector", ts, T_SA)
            V_FB = 0
            for k in range(8, KD):
                V_FB = P.vop("tt", out=xsqh[:, (k - 8) * 512:(k - 7) * 512],
                             in0=xb[p][:, k * 512:(k + 1) * 512],
                             in1=xb[p][:, k * 512:(k + 1) * 512],
                             op=ALU.mult, sem=vs if k == KD - 1 else None)
            P.wait("tensor", vs, V_FB)
            for k in range(8, KD):
                T_MS[i] = P.mm(ps5r, oneb_sb[:, :],
                               xsqh[:, (k - 8) * 512:(k - 7) * 512],
                               start=False, stop=(k == KD - 1),
                               sem=ts if k == KD - 1 else None)

            P.wait("scalar", ts, T_MS[i])
            P.act(mrow[:, :], ps4r, AF.Copy, scale=1.0 / D, sem=ss)
            SS_SROW[i] = P.act(srow[:, :], ps5r, AF.Copy, scale=1.0 / D, sem=ss)
            P.wait("vector", ss, SS_SROW[i])
            P.vop("tt", out=m2row[:, :], in0=mrow[:, :], in1=mrow[:, :],
                  op=ALU.mult, sem=vs)
            P.wait("vector", vs, vs.v)
            V_VAR = P.vop("tt", out=varrow[:, :], in0=srow[:, :], in1=m2row[:, :],
                          op=ALU.subtract, sem=vs)
            P.wait("scalar", vs, V_VAR)
            P.act(lnrow[:, :], varrow[:, :], AF.Ln, bias=cv_sb[0:1, 1:2], sem=ss)
            P.wait("scalar", ss, ss.v)
            SS_R = P.act(rrow[:, :], lnrow[:, :], AF.Exp, scale=-0.5, sem=ss)
            P.wait("vector", ss, SS_R)
            V_RM = P.vop("tt", out=rmrow[:, :], in0=rrow[:, :], in1=mrow[:, :],
                         op=ALU.mult, sem=vs)

            if i >= 1:
                P.wait("tensor", vs, V_TV[i - 1])
            for k in range(KD):
                T_XB[i] = P.mm(ps23[:, 0:512], wv_sb[:, k * 128:(k + 1) * 128],
                               xb[p][:, k * 512:(k + 1) * 512],
                               start=(k == 0), stop=False,
                               sem=ts if k == KD - 1 else None)
            if i >= 1:
                P.wait("tensor", ss, SS_RBC[i - 1])
            P.wait("tensor", ss, SS_R)
            P.mm(ps7[:, :], onerow_sb[:, :], rrow[:, :], True, True)
            P.wait("tensor", vs, V_RM)
            P.mm(ps01[:, 0:512], csq[:, :], rmrow[:, :], start=False, stop=True)
            P.mm(ps01[:, 512:1024], csk[:, :], rmrow[:, :], start=False, stop=True)
            T_RANK[i] = P.mm(ps23[:, 0:512], csv[:, :], rmrow[:, :],
                             start=False, stop=True, sem=ts)

            if i + 1 < NBLK:
                p2 = (i + 1) % 3
                P.wait("tensor", sx[p2], 48 * ((i + 1) // 3 + 1))
                P.wait("tensor", ss, SS_SROW[i])
                for k in range(KD):
                    P.mm(ps4r, oneb_sb[:, :], xb[p2][:, k * 512:(k + 1) * 512],
                         start=(k == 0), stop=(k == KD - 1))

            P.wait("scalar", ts, T_RANK[i])
            SS_RBC[i] = P.act(rbc_sb[:, :], ps7[:, :], AF.Copy, sem=ss)

            P.wait("vector", ts, T_RANK[i])
            P.wait("vector", ss, SS_RBC[i])
            V_QEV[i] = P.vop("tt", out=tq_sb[:, :], in0=ps01[:, 0:512],
                             in1=rbc_sb[:, :], op=ALU.mult, sem=vs)
            V_KEV[i] = P.vop("tt", out=tk_sb[:, :], in0=ps01[:, 512:1024],
                             in1=rbc_sb[:, :], op=ALU.mult, sem=vs)
            V_TV[i] = P.vop("tt", out=tv_sb[:, :], in0=ps23[:, 0:512],
                            in1=rbc_sb[:, :], op=ALU.mult, sem=vs)
            P.wait("scalar", vs, V_QEV[i])
            SS_UQ = P.act(uq_sb[:, :], tq_sb[:, :], AF.Identity, bias=bq_col[:, 0:1], sem=ss)
            P.wait("scalar", vs, V_KEV[i])
            SS_UK = P.act(uk_sb[:, :], tk_sb[:, :], AF.Identity, bias=bk_col[:, 0:1], sem=ss)
            P.wait("scalar", vs, V_TV[i])
            SS_VR = P.act(vr_sb[:, :], tv_sb[:, :], AF.Identity, bias=bv_col[:, 0:1], sem=ss)

            P.wait("tensor", ss, SS_UQ)
            if i >= 1:
                P.wait("tensor", vs, V_KT2[i - 1])
                P.wait("tensor", ss, SS_VSB[i - 1])
            P.mm(ps23[:, 512:1024], atm_sb[:, :], uq_sb[:, :], True, True)
            T_QAB = P.mm(ps6[:, :], btm_sb[:, :], uq_sb[:, :], True, True, sem=ts)
            P.wait("vector", ts, T_QAB)
            P.wait("vector", vs, V_TV[i])
            P.vop("tt", out=t1_sb[:, :], in0=ps23[:, 512:1024], in1=m1s[p][:, :],
                  op=ALU.mult, sem=vs)
            V_QT2[i] = P.vop("tt", out=t2_sb[:, :], in0=ps6[:, :], in1=m2s[p][:, :],
                             op=ALU.mult, sem=vs)
            P.wait("vector", vs, V_QT2[i])
            V_QROT = P.vop("tt", out=qT[:, c0:c0 + 512], in0=t1_sb[:, :],
                           in1=t2_sb[:, :], op=ALU.add, sem=vs)

            P.wait("tensor", ss, SS_UK)
            P.wait("tensor", vs, V_QT2[i])
            P.mm(ps23[:, 512:1024], atm_sb[:, :], uk_sb[:, :], True, True)
            T_KAB = P.mm(ps6[:, :], btm_sb[:, :], uk_sb[:, :], True, True, sem=ts)
            P.wait("vector", ts, T_KAB)
            P.wait("vector", vs, V_QROT)
            P.vop("tt", out=t1_sb[:, :], in0=ps23[:, 512:1024], in1=m1s[p][:, :],
                  op=ALU.mult, sem=vs)
            V_KT2[i] = P.vop("tt", out=t2_sb[:, :], in0=ps6[:, :], in1=m2s[p][:, :],
                             op=ALU.mult, sem=vs)
            P.wait("vector", vs, V_KT2[i])
            V_KROT[i] = P.vop("tt", out=kT[:, c0:c0 + 512], in0=t1_sb[:, :],
                              in1=t2_sb[:, :], op=ALU.add, sem=vs)

            P.wait("tensor", ss, SS_VR)
            P.wait("tensor", vs, V_KT2[i])
            ps6b = ps6.ap().bitcast(BF16)
            T_VTR = 0
            for j in range(4):
                T_VTR = P.tr(ps6b[:, j * 128:(j + 1) * 128],
                             vr_sb[:, j * 128:(j + 1) * 128], idb_sb[:, :],
                             sem=ts if j == 3 else None)
            P.wait("scalar", ts, T_VTR)
            for j in range(4):
                SS_VSB[i] = P.act(vS[:, (4 * i + j) * 128:(4 * i + j + 1) * 128],
                                  ps6b[:, j * 128:(j + 1) * 128], AF.Copy,
                                  sem=ss if j == 3 else None)

        stk_p1a.close()

        stk_p1b = contextlib.ExitStack()
        attT = [stk_p1b.enter_context(nc.sbuf_tensor(f"attT{p}", [128, 1024], BF16))
                for p in range(6)]
        attTm = stk_p1b.enter_context(nc.sbuf_tensor("attTm", [128, 1024], BF16))
        ybuf = [stk_p1b.enter_context(nc.sbuf_tensor(f"ybuf{p}", [128, 512], BF16))
                for p in range(2)]
        bcc_sb = stk_p1b.enter_context(nc.sbuf_tensor("bcc_sb", [128, 512], F32))
        mskd_sb = stk_p1b.enter_context(nc.sbuf_tensor("mskd_sb", [128, 2048], BF16))
        wstr = [stk_p1b.enter_context(nc.sbuf_tensor(f"wstr{p}", [128, 2048], F32))
                for p in range(4)]
        sgA = [stk_p1b.enter_context(nc.sbuf_tensor(f"sgA{p}", [128, 2048], BF16))
               for p in range(2)]
        sgB = [stk_p1b.enter_context(nc.sbuf_tensor(f"sgB{p}", [128, 2048], BF16))
               for p in range(2)]
        qbk = [stk_p1b.enter_context(nc.sbuf_tensor(f"qbk{p}", [128, 2048], BF16))
               for p in range(2)]

        P_BAR_T = ts.v
        P_BAR_V = vs.v
        P_BAR_S = ss.v
        P.wait("sync", ts, P_BAR_T)
        P.wait("sync", vs, P_BAR_V)
        P.wait("sync", ss, P_BAR_S)
        PRE_MSK = P.dma("sync", mskd_sb[:, :], mskd[:, :], spre)
        P.wait("vector", spre, PRE_MSK)

        units = []
        for hb in range(HB):
            units.append((0, w1T[:, hb * 128:(hb + 1) * 128]
                          .rearrange("(k p) h -> p k h", p=128),
                          w1q_h[:, hb * D:(hb + 1) * D], D))
        for hb in range(HB):
            units.append((1, w2T[:, hb * 128:(hb + 1) * 128]
                          .rearrange("(k p) h -> p k h", p=128),
                          w2q_h[:, hb * D:(hb + 1) * D], D))
        for dt in range(16):
            w3v = w3T[:, dt * 128:(dt + 1) * 128].rearrange("(k p) d_ -> p k d_", p=128)
            for k0 in range(0, HB, 16):
                k1 = min(k0 + 16, HB)
                units.append((2, w3v[:, k0:k1, :],
                              w3q_h[:, dt * HF + k0 * 128:dt * HF + k1 * 128],
                              (k1 - k0) * 128))
        assert len(units) == NW

        G_GRED = [0] * NW
        GSWN = [0] * NW
        g_state = {"iss": 0, "drn": 0}

        def gamma_issue_one(j):
            pj = j % 4
            if j >= 4:
                P.wait("sync", vs, G_GRED[j - 4])
            m, gsrc, _, gw = units[j]
            GSWN[j] = P.dma("sync", wstr[pj][:, 0:gw]
                            .rearrange("p (k h) -> p k h", h=128), gsrc, swg[pj])

        def gamma_drain_one(j):
            if g_state["drn"] != j:
                return
            g_state["drn"] = j + 1
            pj = j % 4
            m, gsrc, _, gw = units[j]
            P.wait("vector", swg[pj], GSWN[j])
            G_GRED[j] = P.vop("red", out=gred[:, j:j + 1], in_=wstr[pj][:, 0:gw],
                              absv=True, sem=vs)

        def emit_gamma(n, final=False):
            for _ in range(n):
                j = g_state["iss"]
                if j >= NW:
                    break
                gamma_issue_one(j)
                g_state["iss"] = j + 1
                if j >= 2:
                    gamma_drain_one(j - 2)
            if final:
                for k in range(g_state["drn"], NW):
                    gamma_drain_one(k)

        V_SG = [0] * NW
        V_QB = [0] * NW
        QSQN = [0] * NW
        q_state = {"iss": 0, "drn": 0, "stored": 0, "V_GF": 0, "SS_GDONE": 0}

        def qz_issue_one(u):
            pu = u % 4
            m, src, dst, wdt = units[u]
            if u < 4:
                P.wait("sync", vs, G_GRED[NW - 1])
            else:
                P.wait("sync", vs, V_SG[u - 4])
            QSQN[u] = P.dma("sync", wstr[pu][:, 0:wdt]
                            .rearrange("p (k h) -> p k h", h=128), src, sqz[pu])

        def qz_drain_one(u):
            if q_state["drn"] != u:
                return
            q_state["drn"] = u + 1
            pu = u % 4
            ps2 = u & 1
            m, src, dst, wdt = units[u]
            P.wait("vector", sqz[pu], QSQN[u])
            if u == 0:
                P.wait("vector", ss, q_state["SS_GDONE"])
            if u >= 2:
                P.wait("vector", vs, V_QB[u - 2])
            P.vop("ts2", out=sgA[ps2][:, 0:wdt], in0=wstr[pu][:, 0:wdt],
                  s1=bpc[m][:, 0:1], s2=0.0, op0=ALU.add, op1=ALU.is_ge,
                  sem=vs)
            V_SG[u] = P.vop("ts2", out=sgB[ps2][:, 0:wdt], in0=wstr[pu][:, 0:wdt],
                            s1=bnc[m][:, 0:1], s2=0.0, op0=ALU.add,
                            op1=ALU.is_ge, sem=vs)
            P.wait("vector", vs, V_SG[u])
            if u >= 2:
                P.wait("vector", sqw[ps2], 16 * (u // 2))
            V_QB[u] = P.vop("stt", out=qbk[ps2][:, 0:wdt], in0=sgA[ps2][:, 0:wdt],
                            scalar=-1.0, in1=sgB[ps2][:, 0:wdt],
                            op0=ALU.add, op1=ALU.add, sem=vs)

        def qz_store_one(u):
            if q_state["stored"] != u:
                return
            q_state["stored"] = u + 1
            m, src, dst, wdt = units[u]
            P.wait("sync", vs, V_QB[u])
            P.dma("sync", dst, qbk[u & 1][:, 0:wdt], sqw[u & 1])

        def emit_qz(n, final=False):
            for _ in range(n):
                u = q_state["iss"]
                if u >= NW:
                    break
                qz_issue_one(u)
                q_state["iss"] = u + 1
                if u >= 2:
                    qz_drain_one(u - 2)
                    qz_store_one(u - 2)
            if final:
                for k in range(q_state["drn"], NW):
                    qz_drain_one(k)
                for k in range(q_state["stored"], NW):
                    qz_store_one(k)

        wides = [ps01, ps23, ps45]
        T_AV = {}
        T_SC = {}
        SS_EXP = {}
        V_MSK = {}
        V_Y = [0] * NBLK
        V_REC = [0] * NBLK
        SS_BCC = [0] * NBLK
        V_MSK_LAST = 0
        ps7_free = {"ss": P_BAR_S}
        FINB = max(1, min(NBLK - 2, (NBLK * 45) // 100))
        npre = sum(2 * (b + 1) for b in range(FINB))
        GPRATE = max(1, npre // (NW + 1))
        QHALF = (NW + 2 * (NBLK - 1 - FINB) - 1) // (2 * (NBLK - 1 - FINB))

        pairs = []
        for b in range(NBLK):
            npair = 2 * (b + 1)
            for pp in range(npair):
                pairs.append((b, pp, npair, b * 512))
        NP = len(pairs)

        def emit_scores(gp):
            b, pp, npair, q0 = pairs[gp]
            W = wides[gp % 3]
            if gp < 3:
                P.wait("tensor", vs, P_BAR_V)
                P.wait("tensor", ss, P_BAR_S)
            else:
                P.wait("tensor", ss, SS_EXP[gp - 3])
            st = 2 * pp
            P.mm(W[:, 0:512], kT[:, st * 128:(st + 1) * 128], qT[:, q0:q0 + 512],
                 True, True)
            T_SC[gp] = P.mm(W[:, 512:1024], kT[:, (st + 1) * 128:(st + 2) * 128],
                            qT[:, q0:q0 + 512], True, True, sem=ts)

        for gp in range(min(3, NP)):
            emit_scores(gp)

        for gp in range(NP):
            b, pp, npair, q0 = pairs[gp]
            W = wides[gp % 3]
            At = attT[gp % 6]
            masked = pp >= npair - 2
            st = 2 * pp
            P.wait("scalar", ts, T_SC[gp])
            if gp >= 6:
                P.wait("scalar", ts, T_AV[gp - 6])
            if masked and pp == npair - 1:
                P.wait("scalar", vs, V_MSK_LAST)
            tgt = attTm if masked else At
            SS_EXP[gp] = P.act(tgt[:, :], W[:, :], AF.Exp,
                               bias=cv_sb[:, 0:1], scale=SCALE, sem=ss)
            if masked:
                slot0 = st - 4 * b
                P.wait("vector", ss, SS_EXP[gp])
                P.vop("tt", out=At[:, 0:512], in0=attTm[:, 0:512],
                      in1=mskd_sb[:, slot0 * 512:(slot0 + 1) * 512],
                      op=ALU.mult, sem=vs)
                V_MSK_LAST = P.vop("tt", out=At[:, 512:1024],
                                   in0=attTm[:, 512:1024],
                                   in1=mskd_sb[:, (slot0 + 1) * 512:(slot0 + 2) * 512],
                                   op=ALU.mult, sem=vs)
            if masked:
                P.wait("tensor", vs, V_MSK_LAST)
            else:
                P.wait("tensor", ss, SS_EXP[gp])
            if pp == 0:
                if b >= 1:
                    P.wait("tensor", vs, V_Y[b - 1])
                P.wait("tensor", ss, ps7_free["ss"])
            P.mm(ps6[:, :], vS[:, st * 128:(st + 1) * 128], At[:, 0:512],
                 start=(pp == 0), stop=False)
            P.mm(ps6[:, :], vS[:, (st + 1) * 128:(st + 2) * 128], At[:, 512:1024],
                 start=False, stop=(pp == npair - 1))
            P.mm(ps7[0:1, :], oneb_sb[:, :], At[:, 0:512],
                 start=(pp == 0), stop=False)
            T_AV[gp] = P.mm(ps7[0:1, :], oneb_sb[:, :], At[:, 512:1024],
                            start=False, stop=(pp == npair - 1), sem=ts)
            if gp + 3 < NP:
                emit_scores(gp + 3)
            if b > FINB and (pp == 1 or pp == npair // 2):
                emit_qz(QHALF)
            elif b < FINB and gp % GPRATE == 0:
                emit_gamma(1)
            if pp != npair - 1:
                continue
            P.wait("vector", ts, T_AV[gp])
            V_REC[b] = P.vop("recip", out=recrow[:, :], in_=ps7[0:1, :], sem=vs)
            P.wait("tensor", vs, V_REC[b])
            T_BC = P.mm(ps7[:, :], onerow_sb[:, :], recrow[:, :], True, True, sem=ts)
            P.wait("scalar", ts, T_BC)
            SS_BCC[b] = P.act(bcc_sb[:, :], ps7[:, :], AF.Copy, sem=ss)
            ps7_free["ss"] = SS_BCC[b]
            P.wait("vector", ss, SS_BCC[b])
            if b >= 2:
                P.wait("vector", sy[b & 1], 16 * (b // 2))
            V_Y[b] = P.vop("tt", out=ybuf[b & 1][:, :], in0=ps6[:, :],
                           in1=bcc_sb[:, :], op=ALU.mult, sem=vs)
            P.wait("gpsimd", vs, V_Y[b])
            P.dma("gpsimd", ag_in[:, q0:q0 + 512], ybuf[b & 1][:, :], sy[b & 1])

            if b == FINB:
                emit_gamma(NW, final=True)
                P.wait("vector", vs, G_GRED[NW - 1])
                P.vop("red", out=gcol[:, 0:1], in_=gred[:, 0:HB])
                P.vop("red", out=gcol[:, 1:2], in_=gred[:, HB:2 * HB])
                V_GF = P.vop("red", out=gcol[:, 2:3], in_=gred[:, 2 * HB:NW],
                             sem=vs)
                q_state["V_GF"] = V_GF
                SS_GDONE = 0
                for m in range(3):
                    P.wait("tensor", vs, V_GF)
                    P.wait("tensor", ss, ps7_free["ss"] if m == 0 else SS_GDONE)
                    T_G1 = P.mm(ps7[0:1, 0:1], gcol[:, m:m + 1], onef_sb[:, :],
                                True, True, sem=ts)
                    P.wait("scalar", ts, T_G1)
                    SS_GE = P.act(ge_row[:, :], ps7[0:1, 0:1], AF.Identity,
                                  scale=1.0 / float(HF * D), bias=cv_sb[0:1, 2:3],
                                  sem=ss)
                    P.wait("tensor", ss, SS_GE)
                    T_G2 = P.mm(ps7[:, 0:1], onerow_sb[:, :], ge_row[:, :],
                                True, True, sem=ts)
                    P.wait("scalar", ts, T_G2)
                    P.act(bpc[m][:, :], ps7[:, 0:1], AF.Copy, scale=0.5, sem=ss)
                    SS_GDONE = P.act(bnc[m][:, :], ps7[:, 0:1], AF.Copy,
                                     scale=-0.5, sem=ss)
                q_state["SS_GDONE"] = SS_GDONE
                ps7_free["ss"] = SS_GDONE
        emit_qz(NW, final=True)

        P.wait("gpsimd", sy[0], sy[0].v)
        P.wait("gpsimd", sy[1], sy[1].v)
        CC_N = P.coll("AllGather", ALU.bypass, [[0, 1, 2, 3], [4, 5, 6, 7]],
                      [ag_in.ap().opt()], [ag_out.ap().opt()], scc)

        TS_QZ = ts.v
        VS_QZ = vs.v
        SS_QZ = ss.v
        stk_p1b.close()
        stk_p1c.close()

        stk_p2 = contextlib.ExitStack()
        p2 = lambda n, shp, dt: stk_p2.enter_context(nc.sbuf_tensor(n, shp, dt))
        xbo = p2("xbo", [128, KD * 512], BF16)
        cmb = [p2(f"cmb{c}", [128, 512], BF16) for c in range(4)]
        yTg = [p2(f"yTg{q}", [128, 4 * 512], BF16) for q in range(2)]
        gld = p2("gld", [128, 3 * 512], BF16)
        xq2 = [p2(f"xq2{q}", [128, 512], BF16) for q in range(2)]
        wo_sb = p2("wo_sb", [128, 4 * D], BF16)
        cs1 = p2("cs1", [1, HF], BF16)
        cs2 = p2("cs2", [1, HF], BF16)
        hid = p2("hid", [128, HB * 512], BF16)
        w1qb = [p2(f"w1qb{q}", [128, D], BF16) for q in range(3)]
        w2qb = [p2(f"w2qb{q}", [128, D], BF16) for q in range(3)]
        w3qb = [p2(f"w3qb{q}", [128, HF], BF16) for q in range(2)]
        xf32 = [p2(f"xf32{q}", [128, 512], F32) for q in range(2)]
        x1t = p2("x1t", [128, 512], F32)
        x2t = p2("x2t", [128, 512], F32)
        sig_sb = p2("sig_sb", [128, 512], BF16)
        rbc2 = p2("rbc2", [128, 512], F32)
        osb = [p2(f"osb{q}", [128, 512], F32) for q in range(2)]


        T_GRP = [0] * NGRP
        V_OUT = {}
        SS_RBC2 = [0] * NGRP
        SS_SIG_LAST = 0
        T_X12L = [0] * NGRP
        V_CMBL = [0] * NGRP
        T_S2KL = [0] * NGRP
        SS_SR2L = [0] * NGRP
        V_HIDL = [0] * NGRP

        for g in range(NGRP):
            pg = g & 1
            gc = g * 512
            if g >= 1:
                P.wait("gpsimd", ts, T_X12L[g - 1])
                P.wait("gpsimd", vs, V_CMBL[g - 1])
            SXB_N = P.dma("gpsimd", xbo.ap().rearrange("p (k t) -> p k t", k=KD),
                  xTo[:, gc:gc + 512].rearrange("(k p) t -> p k t", p=128), sxb[pg])
            if g == 0:
                P.wait("gpsimd", scc, CC_N)
            P.wait("tensor", sxb[pg], SXB_N)
            if g == 0:
                P.wait("tensor", ss, SS_EXP[NP - 1])
            else:
                P.wait("tensor", ss, SS_SR2L[g - 1])
            for k in range(KD):
                P.mm(ps4r, oneb_sb[:, :], xbo[:, k * 512:(k + 1) * 512],
                     start=(k == 0), stop=(k == KD - 1))
            P.wait("vector", sxb[pg], SXB_N)
            T_S2K = [0] * KD
            T_MS2 = 0
            for k in range(KD):
                if k >= 2:
                    P.wait("vector", ts, T_S2K[k - 2])
                elif g >= 1:
                    P.wait("vector", ts, T_S2KL[g - 1])
                vsq = P.vop("tt", out=xq2[k & 1][:, :],
                            in0=xbo[:, k * 512:(k + 1) * 512],
                            in1=xbo[:, k * 512:(k + 1) * 512],
                            op=ALU.mult, sem=vs)
                P.wait("tensor", vs, vsq)
                T_MS2 = P.mm(ps5r, oneb_sb[:, :], xq2[k & 1][:, :],
                             start=(k == 0), stop=(k == KD - 1), sem=ts)
                T_S2K[k] = T_MS2
            T_S2KL[g] = T_S2K[KD - 1]
            P.wait("scalar", ts, T_MS2)
            P.act(mrow[:, :], ps4r, AF.Copy, scale=1.0 / D, sem=ss)
            SS_SR2 = P.act(srow[:, :], ps5r, AF.Copy, scale=1.0 / D, sem=ss)
            SS_SR2L[g] = SS_SR2
            P.wait("vector", ss, SS_SR2)
            P.vop("tt", out=m2row[:, :], in0=mrow[:, :], in1=mrow[:, :],
                  op=ALU.mult, sem=vs)
            P.wait("vector", vs, vs.v)
            V_VAR2 = P.vop("tt", out=varrow[:, :], in0=srow[:, :], in1=m2row[:, :],
                           op=ALU.subtract, sem=vs)
            P.wait("scalar", vs, V_VAR2)
            P.act(lnrow[:, :], varrow[:, :], AF.Ln, bias=cv_sb[0:1, 1:2], sem=ss)
            P.wait("scalar", ss, ss.v)
            SS_R2 = P.act(rrow[:, :], lnrow[:, :], AF.Exp, scale=-0.5, sem=ss)
            P.wait("vector", ss, SS_R2)
            V_RM2 = P.vop("tt", out=rmrow[:, :], in0=rrow[:, :], in1=mrow[:, :],
                          op=ALU.mult, sem=vs)
            P.wait("tensor", ss, SS_R2)
            if g >= 1:
                P.wait("tensor", ss, SS_RBC2[g - 1])
                P.wait("tensor", vs, V_OUT[(g - 1, 15)])
            T_BC2 = P.mm(ps7[:, :], onerow_sb[:, :], rrow[:, :], True, True, sem=ts)
            P.wait("scalar", ts, T_BC2)
            SS_RBC2[g] = P.act(rbc2[:, :], ps7[:, :], AF.Copy, sem=ss)

            V_CMB = 0
            SG_N = 0
            for rt in range(4):
                if rt >= 1:
                    P.wait("gpsimd", vs, V_CMB)
                for c in range(4):
                    SG_N = P.dma("gpsimd", cmb[c][:, :],
                                 ag_out[rt * 128:(rt + 1) * 128,
                                        c * NTO + gc:c * NTO + gc + 512], sg[pg])
                P.wait("vector", sg[pg], SG_N)
                if rt == 0 and g == 0:
                    P.wait("vector", vs, VS_QZ)
                if rt == 0 and g >= 2:
                    P.wait("vector", ts, T_GRP[g - 2])
                if rt >= 1:
                    P.wait("vector", vs, V_CMB)
                P.vop("ts", out=gld[:, 0:512], in0=cmb[0][:, :],
                      s1=msel_sb[:, 0:1], op0=ALU.mult, sem=vs)
                P.wait("vector", vs, vs.v)
                P.vop("stt", out=gld[:, 512:1024], in0=cmb[1][:, :],
                      scalar=msel_sb[:, 1:2], in1=gld[:, 0:512],
                      op0=ALU.mult, op1=ALU.add, sem=vs)
                P.wait("vector", vs, vs.v)
                P.vop("stt", out=gld[:, 1024:1536], in0=cmb[2][:, :],
                      scalar=msel_sb[:, 2:3], in1=gld[:, 512:1024],
                      op0=ALU.mult, op1=ALU.add, sem=vs)
                P.wait("vector", vs, vs.v)
                V_CMB = P.vop("stt", out=yTg[pg][:, rt * 512:(rt + 1) * 512],
                              in0=cmb[3][:, :], scalar=msel_sb[:, 3:4],
                              in1=gld[:, 1024:1536], op0=ALU.mult, op1=ALU.add,
                              sem=vs)
            V_CMBL[g] = V_CMB
            if g == 0:
                P.wait("gpsimd", ts, TS_QZ)
                P.wait("gpsimd", vs, VS_QZ)
                P.wait("gpsimd", ss, SS_QZ)
                P.wait("gpsimd", sqw[0], sqw[0].v)
                P.wait("gpsimd", sqw[1], sqw[1].v)
                PREG2 = P.dma("gpsimd", wo_sb.ap().rearrange("p (k n) -> p k n", k=4),
                              wo_d.ap().rearrange("(k p) n -> p k n", p=128), spreg)

            V_HID = [0] * HB
            T_X12 = [0] * HB
            SS_CS2 = [0] * HB
            for hb in range(HB):
                ph = hb % 3
                psA = ps01 if (hb & 1) == 0 else ps23
                if hb >= 3:
                    P.wait("sync", ts, T_X12[hb - 3])
                elif g >= 1:
                    P.wait("sync", ts, T_GRP[g - 1])
                else:
                    P.wait("sync", sqw[0], sqw[0].v)
                    P.wait("sync", sqw[1], sqw[1].v)
                    P.wait("sync", ts, TS_QZ)
                    P.wait("sync", vs, VS_QZ)
                sqn = P.dma("sync", w1qb[ph][:, :], w1q_h[:, hb * D:(hb + 1) * D],
                            sq[ph])
                sqbn = P.dma("sync", w2qb[ph][:, :], w2q_h[:, hb * D:(hb + 1) * D],
                             sqb[ph])
                P.wait("tensor", sq[ph], sqn)
                P.wait("tensor", sqb[ph], sqbn)
                if hb >= 2:
                    P.wait("tensor", vs, V_HID[hb - 2])
                elif g >= 1:
                    P.wait("tensor", vs, V_HIDL[g - 1])
                if hb == 0:
                    P.wait("tensor", vs, V_RM2)
                    P.wait("tensor", vs, V_CMB)
                if g == 0:
                    if hb >= 1:
                        P.wait("tensor", ss, SS_CS2[hb - 1])
                    for k in range(KD):
                        P.mm(ps45[0:1, 0:128], oneb_sb[:, :],
                             w1qb[ph][:, k * 128:(k + 1) * 128],
                             start=(k == 0), stop=(k == KD - 1))
                    for k in range(KD):
                        T_CSC = P.mm(ps45[0:1, 512:640], oneb_sb[:, :],
                                     w2qb[ph][:, k * 128:(k + 1) * 128],
                                     start=(k == 0), stop=(k == KD - 1),
                                     sem=ts if k == KD - 1 else None)
                    P.wait("scalar", ts, T_CSC)
                    P.act(cs1[:, hb * 128:(hb + 1) * 128], ps45[0:1, 0:128],
                          AF.Copy, scale=-1.0, sem=ss)
                    SS_CS2[hb] = P.act(cs2[:, hb * 128:(hb + 1) * 128],
                                       ps45[0:1, 512:640], AF.Copy, scale=-1.0,
                                       sem=ss)
                for k in range(KD):
                    P.mm(psA[:, 0:512], w1qb[ph][:, k * 128:(k + 1) * 128],
                         xbo[:, k * 512:(k + 1) * 512], start=(k == 0), stop=False)
                if g == 0:
                    P.wait("tensor", ss, SS_CS2[hb])
                P.mm(psA[:, 0:512], cs1[:, hb * 128:(hb + 1) * 128], rmrow[:, :],
                     start=False, stop=True)
                for k in range(KD):
                    P.mm(psA[:, 512:1024], w2qb[ph][:, k * 128:(k + 1) * 128],
                         xbo[:, k * 512:(k + 1) * 512], start=(k == 0), stop=False)
                T_X12[hb] = P.mm(psA[:, 512:1024], cs2[:, hb * 128:(hb + 1) * 128],
                                 rmrow[:, :], start=False, stop=True, sem=ts)
                P.wait("vector", ts, T_X12[hb])
                if g == 0 and hb == 0:
                    P.wait("vector", ss, SS_RBC2[g])
                P.wait("vector", ss, SS_SIG_LAST)
                if hb >= 1:
                    P.wait("vector", vs, V_HID[hb - 1])
                P.vop("tt", out=x2t[:, :], in0=psA[:, 512:1024], in1=rbc2[:, :],
                      op=ALU.mult, sem=vs)
                V_X1 = P.vop("tt", out=x1t[:, :], in0=psA[:, 0:512], in1=rbc2[:, :],
                             op=ALU.mult, sem=vs)
                P.wait("scalar", vs, V_X1)
                SS_SIG_LAST = P.act(sig_sb[:, :], x1t[:, :], AF.Sigmoid, sem=ss)
                P.wait("vector", ss, SS_SIG_LAST)
                V_HID[hb] = P.vop("tt", out=hid[:, hb * 512:(hb + 1) * 512],
                                  in0=x2t[:, :], in1=sig_sb[:, :],
                                  op=ALU.mult, sem=vs)
            T_X12L[g] = T_X12[HB - 1]
            V_HIDL[g] = V_HID[HB - 1]

            T_OUTL = [0] * 16
            pend_store = [None]
            for dt in range(16):
                pw = dt & 1
                psO = ps6 if pw == 0 else ps7
                if dt >= 2:
                    P.wait("sync", ts, T_OUTL[dt - 2])
                elif g >= 1:
                    P.wait("sync", ts, T_GRP[g - 1])
                SW3_N = P.dma("sync", w3qb[pw][:, :], w3q_h[:, dt * HF:(dt + 1) * HF],
                              sw3[pw])
                if pend_store[0] is not None:
                    pend_store[0]()
                    pend_store[0] = None
                if dt >= 2:
                    P.wait("scalar", vs, V_OUT[(g, dt - 2)])
                elif g >= 1:
                    P.wait("scalar", vs, V_OUT[(g - 1, 14 + dt)])
                else:
                    P.wait("scalar", vs, VS_QZ)
                SXF_N = P.dma("scalar", xf32[pw][:, :],
                              xTo[dt * 128:(dt + 1) * 128, gc:gc + 512], sxf[pw])
                P.wait("tensor", sw3[pw], SW3_N)
                if dt == 0:
                    if g == 0:
                        P.wait("tensor", spreg, PREG2)
                    P.wait("tensor", vs, V_HID[HB - 1])
                if dt == 1:
                    P.wait("tensor", ss, SS_RBC2[g])
                if dt >= 2:
                    P.wait("tensor", vs, V_OUT[(g, dt - 2)])
                elif g >= 1:
                    P.wait("tensor", vs, V_OUT[(g - 1, 14 + dt)])
                for k in range(4):
                    P.mm(psO[:, :], wo_sb[:, k * D + dt * 128:k * D + (dt + 1) * 128],
                         yTg[pg][:, k * 512:(k + 1) * 512], start=(k == 0), stop=False)
                T_OUT = 0
                for k in range(HB):
                    T_OUT = P.mm(psO[:, :], w3qb[pw][:, k * 128:(k + 1) * 128],
                                 hid[:, k * 512:(k + 1) * 512], start=False,
                                 stop=(k == HB - 1), sem=ts if k == HB - 1 else None)
                T_OUTL[dt] = T_OUT
                nglob = g * 16 + dt
                P.wait("vector", ts, T_OUT)
                P.wait("vector", sxf[pw], SXF_N)
                if nglob >= 2:
                    P.wait("vector", sout[pw], 16 * (nglob // 2))
                V_OUT[(g, dt)] = P.vop("stt", out=osb[pw][:, :], in0=psO[:, :],
                                       scalar=bo_sb[:, dt:dt + 1], in1=xf32[pw][:, :],
                                       op0=ALU.add, op1=ALU.add, sem=vs)
                def _store(g=g, dt=dt, pw=pw, gc=gc):
                    P.wait("sync", vs, V_OUT[(g, dt)])
                    P.dma("sync", outT[dt * 128:(dt + 1) * 128, gc:gc + 512],
                          osb[pw][:, :], sout[pw])
                if dt == 15:
                    _store()
                else:
                    pend_store[0] = _store
            T_GRP[g] = T_OUTL[15]

        stk_p2.close()

        P.wait("sync", sout[0], sout[0].v)
        P.wait("sync", sout[1], sout[1].v)

        with nc.Block() as block:
            @block.sync
            def _(e):
                for f in P.q["sync"]:
                    f(e)

            @block.gpsimd
            def _(e):
                for f in P.q["gpsimd"]:
                    f(e)

            @block.tensor
            def _(e):
                for f in P.q["tensor"]:
                    f(e)

            @block.vector
            def _(e):
                for f in P.q["vector"]:
                    f(e)

            @block.scalar
            def _(e):
                for f in P.q["scalar"]:
                    f(e)

    return nc


def _constants(T):
    j = np.arange(64, dtype=np.float64)
    inv = 10000.0 ** (-2.0 * j / 128.0)
    t = np.arange(T, dtype=np.float64)
    emb = np.outer(t, inv)
    cos = np.cos(emb).T
    sin = np.sin(emb).T
    m1 = np.concatenate([cos, sin], axis=0).astype(np.float32)
    m2 = np.concatenate([-sin, cos], axis=0).astype(np.float32)
    A = np.zeros((128, 128), np.float32)
    Bm = np.zeros((128, 128), np.float32)
    for i in range(64):
        A[i, 2 * i] = 1.0
        A[64 + i, 2 * i] = 1.0
        Bm[i, 2 * i + 1] = 1.0
        Bm[64 + i, 2 * i + 1] = 1.0
    mskd = np.zeros((128, 2048), np.float32)
    sl = np.arange(128)
    ql = np.arange(128)
    for i in range(4):
        for qt in range(4):
            blockm = ((i * 128 + sl)[:, None] <= (qt * 128 + ql)[None, :])
            mskd[:, i * 512 + qt * 128:i * 512 + (qt + 1) * 128] = blockm
    return {
        "m1t": m1, "m2t": m2,
        "atm": np.ascontiguousarray(A.T).astype(BFNP),
        "btm": np.ascontiguousarray(Bm.T).astype(BFNP),
        "idb": np.eye(128, dtype=np.float32).astype(BFNP),
        "idf": np.eye(16, dtype=np.float32),
        "oneb": np.ones((128, 1), np.float32).astype(BFNP),
        "onef": np.ones((128, 1), np.float32),
        "onerow": np.ones((1, 128), np.float32),
        "halfrow": np.full((1, 128), 0.5, np.float32),
        "mskd": mskd.astype(BFNP),
        "cvals": np.repeat(np.array([[ESHIFT, LNEPS, QEPS, 0.0]], np.float32), 128, 0),
    }


def _stage_core(c, x, Wq, bq, Wk, bk, Wv, bv, Wo, bo, w1T, w2T, w3T, consts, T):
    b, h = c // 4, c % 4
    NTO = T // 4
    d = dict(consts)
    d["xT"] = np.ascontiguousarray(x[b].T)
    d["xTo"] = np.ascontiguousarray(x[b, h * NTO:(h + 1) * NTO].T)
    d["wq_s"] = np.ascontiguousarray(Wq[:, h * 512:(h + 1) * 512])
    d["wk_s"] = np.ascontiguousarray(Wk[:, h * 128:(h + 1) * 128])
    d["wv_s"] = np.ascontiguousarray(Wv[:, h * 128:(h + 1) * 128])
    d["bq4"] = np.ascontiguousarray(bq[h * 512:(h + 1) * 512].reshape(4, 128))
    d["bk1"] = np.ascontiguousarray(bk[h * 128:(h + 1) * 128].reshape(1, 128))
    d["bv1"] = np.ascontiguousarray(bv[h * 128:(h + 1) * 128].reshape(1, 128))
    d["wo"] = np.ascontiguousarray(Wo)
    d["bo16"] = np.ascontiguousarray(bo.reshape(16, 128))
    d["w1T"] = w1T
    d["w2T"] = w2T
    d["w3T"] = w3T
    ms = np.zeros((128, 4), np.float32)
    ms[:, h] = 1.0
    d["msel"] = ms
    return d


def stage_all(x, Wq, bq, Wk, bk, Wv, bv, Wo, bo, w1, w2, w3):
    T = x.shape[1]
    w1T = np.ascontiguousarray(np.asarray(w1, dtype=np.float32).T)
    w2T = np.ascontiguousarray(np.asarray(w2, dtype=np.float32).T)
    w3T = np.ascontiguousarray(np.asarray(w3, dtype=np.float32).T)
    consts = _constants(T)
    return [_stage_core(c, np.asarray(x, dtype=np.float32),
                        np.asarray(Wq, dtype=np.float32), np.asarray(bq, dtype=np.float32),
                        np.asarray(Wk, dtype=np.float32), np.asarray(bk, dtype=np.float32),
                        np.asarray(Wv, dtype=np.float32), np.asarray(bv, dtype=np.float32),
                        np.asarray(Wo, dtype=np.float32), np.asarray(bo, dtype=np.float32),
                        w1T, w2T, w3T, consts, T)
            for c in range(NCORE)]


def kernel(L0_attn_x, L0_attn_Wq, L0_attn_bq, L0_attn_Wk, L0_attn_bk,
           L0_attn_Wv, L0_attn_bv, L0_attn_Wo, L0_attn_bo,
           L0_ffn_w1, L0_ffn_w2, L0_ffn_w3):
    T = L0_attn_x.shape[1]
    HF = L0_ffn_w1.shape[0]
    in_maps = stage_all(L0_attn_x, L0_attn_Wq, L0_attn_bq, L0_attn_Wk, L0_attn_bk,
                        L0_attn_Wv, L0_attn_bv, L0_attn_Wo, L0_attn_bo,
                        L0_ffn_w1, L0_ffn_w2, L0_ffn_w3)
    nc = build_nc(T, HF)
    res = run_bass_kernel_spmd(nc, in_maps, core_ids=list(range(NCORE)))
    out = np.empty((2, T, D), np.float32)
    NTO = T // 4
    for c in range(NCORE):
        b, h = c // 4, c % 4
        out[b, h * NTO:(h + 1) * NTO, :] = np.asarray(res.results[c]["outT"]).T
    return out


# version 33
# speedup vs baseline: 1.0361x; 1.0152x over previous
import contextlib
import numpy as np
import ml_dtypes

import concourse.bass as bass
import concourse.mybir as mybir
from concourse.bass_utils import run_bass_kernel_spmd

F32 = mybir.dt.float32
BF16 = mybir.dt.bfloat16
AF = mybir.ActivationFunctionType
ALU = mybir.AluOpType
AX = mybir.AxisListType
BFNP = ml_dtypes.bfloat16

D = 2048
KD = D // 128
NCORE = 8
SCALE = float(np.float64(1.0) / np.sqrt(np.float64(128.0)))
ESHIFT = -20.0
LNEPS = 1e-5
QEPS = 1e-8


class Sem:
    def __init__(self, name):
        self.name = name
        self.v = 0
        self.h = None

    def inc(self, n=1):
        self.v += n
        return self.v


class Plan:
    def __init__(self):
        self.q = {"sync": [], "gpsimd": [], "tensor": [], "vector": [], "scalar": []}

    def wait(self, eng, sem, val):
        if val and val > 0:
            self.q[eng].append(lambda e, s=sem, v=val: e.wait_ge(s.h, v))

    def mm(self, out, lhsT, rhs, start, stop, sem=None):
        v = sem.inc(1) if sem is not None else None

        def f(e, out=out, lhsT=lhsT, rhs=rhs, start=start, stop=stop, sem=sem):
            r = e.matmul(out, lhsT, rhs, start=start, stop=stop)
            if sem is not None:
                r.then_inc(sem.h, 1)
        self.q["tensor"].append(f)
        return v

    def tr(self, out, in_, ident, sem=None):
        v = sem.inc(1) if sem is not None else None

        def f(e, out=out, in_=in_, ident=ident, sem=sem):
            r = e.transpose(out, in_, ident)
            if sem is not None:
                r.then_inc(sem.h, 1)
        self.q["tensor"].append(f)
        return v

    def act(self, out, in_, func, bias=None, scale=None, sem=None):
        v = sem.inc(1) if sem is not None else None

        def f(e, out=out, in_=in_, func=func, bias=bias, scale=scale, sem=sem):
            kw = {}
            if bias is not None:
                kw["bias"] = bias
            if scale is not None:
                kw["scale"] = scale
            r = e.activation(out, in_, func, **kw)
            if sem is not None:
                r.then_inc(sem.h, 1)
        self.q["scalar"].append(f)
        return v

    def vop(self, kind, sem=None, **kw):
        v = sem.inc(1) if sem is not None else None

        def f(e, kind=kind, kw=kw, sem=sem):
            if kind == "tt":
                r = e.tensor_tensor(kw["out"], kw["in0"], kw["in1"], kw["op"])
            elif kind == "ts":
                r = e.tensor_scalar(kw["out"], kw["in0"], kw["s1"], None,
                                    kw["op0"])
            elif kind == "ts2":
                r = e.tensor_scalar(kw["out"], kw["in0"], kw["s1"], kw["s2"],
                                    kw["op0"], kw["op1"])
            elif kind == "stt":
                r = e.scalar_tensor_tensor(kw["out"], kw["in0"], kw["scalar"],
                                           kw["in1"], kw["op0"], kw["op1"])
            elif kind == "red":
                r = e.reduce_sum(kw["out"], kw["in_"], axis=AX.X,
                                 apply_absolute_value=kw.get("absv", False))
            elif kind == "recip":
                r = e.reciprocal(kw["out"], kw["in_"])
            else:
                raise ValueError(kind)
            if sem is not None:
                r.then_inc(sem.h, 1)
        self.q["vector"].append(f)
        return v

    def pop_red(self, out, in_, absv, sem=None):
        v = sem.inc(1) if sem is not None else None

        def f(e, out=out, in_=in_, absv=absv, sem=sem):
            r = e.reduce_sum(out, in_, axis=AX.XYZWC, apply_absolute_value=absv)
            if sem is not None:
                r.then_inc(sem.h, 1)
        self.q["gpsimd"].append(f)
        return v

    def dma(self, eng, out, in_, sem):
        v = sem.inc(16)

        def f(e, out=out, in_=in_, sem=sem):
            e.dma_start(out=out, in_=in_).then_inc(sem.h, 16)
        self.q[eng].append(f)
        return v

    def coll(self, kind, op, groups, ins, outs, sem):
        v = sem.inc(1)

        def f(e, kind=kind, op=op, groups=groups, ins=ins, outs=outs, sem=sem):
            e.collective_compute(kind, op, replica_groups=groups, ins=ins,
                                 outs=outs).then_inc(sem.h, 1)
        self.q["gpsimd"].append(f)
        return v


def build_nc(T, HF):
    NBLK = T // 512
    NTO = T // 4
    NGRP = NTO // 512
    HB = HF // 128
    WMX = max(D, HF)
    assert T % 2048 == 0 and HF % 128 == 0

    nc = bass.Bass("TRN2", target_bir_lowering=False, debug=False)

    xT = nc.dram_tensor("xT", [D, T], F32, kind="ExternalInput")
    xTo = nc.dram_tensor("xTo", [D, NTO], F32, kind="ExternalInput")
    wq_s = nc.dram_tensor("wq_s", [D, 512], F32, kind="ExternalInput")
    wk_s = nc.dram_tensor("wk_s", [D, 128], F32, kind="ExternalInput")
    wv_s = nc.dram_tensor("wv_s", [D, 128], F32, kind="ExternalInput")
    bq4 = nc.dram_tensor("bq4", [4, 128], F32, kind="ExternalInput")
    bk1 = nc.dram_tensor("bk1", [1, 128], F32, kind="ExternalInput")
    bv1 = nc.dram_tensor("bv1", [1, 128], F32, kind="ExternalInput")
    wo_d = nc.dram_tensor("wo", [512, D], F32, kind="ExternalInput")
    bo16 = nc.dram_tensor("bo16", [16, 128], F32, kind="ExternalInput")
    w1T = nc.dram_tensor("w1T", [D, HF], F32, kind="ExternalInput")
    w2T = nc.dram_tensor("w2T", [D, HF], F32, kind="ExternalInput")
    w3T = nc.dram_tensor("w3T", [HF, D], F32, kind="ExternalInput")
    m1t = nc.dram_tensor("m1t", [128, T], F32, kind="ExternalInput")
    m2t = nc.dram_tensor("m2t", [128, T], F32, kind="ExternalInput")
    atm = nc.dram_tensor("atm", [128, 128], BF16, kind="ExternalInput")
    btm = nc.dram_tensor("btm", [128, 128], BF16, kind="ExternalInput")
    idb = nc.dram_tensor("idb", [128, 128], BF16, kind="ExternalInput")
    idf = nc.dram_tensor("idf", [16, 16], F32, kind="ExternalInput")
    oneb = nc.dram_tensor("oneb", [128, 1], BF16, kind="ExternalInput")
    onef = nc.dram_tensor("onef", [128, 1], F32, kind="ExternalInput")
    onerow = nc.dram_tensor("onerow", [1, 128], F32, kind="ExternalInput")
    halfrow = nc.dram_tensor("halfrow", [1, 128], F32, kind="ExternalInput")
    mskd = nc.dram_tensor("mskd", [128, 2048], BF16, kind="ExternalInput")
    msel = nc.dram_tensor("msel", [128, 4], F32, kind="ExternalInput")
    cvals = nc.dram_tensor("cvals", [128, 4], F32, kind="ExternalInput")

    outT = nc.dram_tensor("outT", [D, NTO], F32, kind="ExternalOutput")

    ag_in = nc.dram_tensor("ag_in", [128, T], BF16)
    ag_out = nc.dram_tensor("ag_out", [512, T], BF16)
    w1q_h = nc.dram_tensor("w1q_h", [128, HB * D], BF16)
    w2q_h = nc.dram_tensor("w2q_h", [128, HB * D], BF16)
    w3q_h = nc.dram_tensor("w3q_h", [128, 16 * HF], BF16)

    P = Plan()
    NW = 2 * HB + 16 * ((HB + 15) // 16)

    with contextlib.ExitStack() as G:
        sems = {}
        for nm in ["tsem", "vsem", "ssem", "spre", "spreg", "sx0", "sx1", "sx2",
                   "sy0", "sy1", "scc", "psem", "swg0", "swg1", "swg2", "swg3", "sqz0", "sqz1", "sqz2", "sqz3",
                   "sqw0", "sqw1", "sq0", "sq1", "sq2", "sqb0", "sqb1", "sqb2", "sw30", "sw31", "sxf0", "sxf1",
                   "sg0", "sg1", "sxb0", "sxb1", "sout0", "sout1"]:
            s = Sem(nm)
            s.h = G.enter_context(nc.semaphore(nm))
            sems[nm] = s
        ts, vs, ss = sems["tsem"], sems["vsem"], sems["ssem"]
        sx = [sems["sx0"], sems["sx1"], sems["sx2"]]
        swg = [sems[f"swg{i}"] for i in range(4)]
        sqz = [sems[f"sqz{i}"] for i in range(4)]
        psm = sems["psem"]
        sq = [sems["sq0"], sems["sq1"], sems["sq2"]]
        sqb = [sems["sqb0"], sems["sqb1"], sems["sqb2"]]
        sw3 = [sems["sw30"], sems["sw31"]]
        sxf = [sems["sxf0"], sems["sxf1"]]
        sg = [sems["sg0"], sems["sg1"]]
        sxb = [sems["sxb0"], sems["sxb1"]]
        spre, spreg = sems["spre"], sems["spreg"]
        scc = sems["scc"]
        sy = [sems["sy0"], sems["sy1"]]
        sqw = [sems["sqw0"], sems["sqw1"]]
        sout = [sems["sout0"], sems["sout1"]]

        ps01 = G.enter_context(nc.psum_tensor("ps01", [128, 1024], F32))
        ps23 = G.enter_context(nc.psum_tensor("ps23", [128, 1024], F32))
        ps45 = G.enter_context(nc.psum_tensor("ps45", [128, 1024], F32))
        ps6 = G.enter_context(nc.psum_tensor("ps6", [128, 512], F32))
        ps7 = G.enter_context(nc.psum_tensor("ps7", [128, 512], F32))
        ps4r = ps45[0:1, 0:512]
        ps5r = ps45[0:1, 512:1024]

        sb = lambda n, shp, dt: G.enter_context(nc.sbuf_tensor(n, shp, dt))
        wq_eff = sb("wq_eff", [128, KD * 128], BF16)
        wk_sb = sb("wk_sb", [128, KD * 128], BF16)
        wv_sb = sb("wv_sb", [128, KD * 128], BF16)
        atm_sb = sb("atm_sb", [128, 128], BF16)
        btm_sb = sb("btm_sb", [128, 128], BF16)
        idb_sb = sb("idb_sb", [128, 128], BF16)
        idf_sb = sb("idf_sb", [16, 16], F32)
        oneb_sb = sb("oneb_sb", [128, 1], BF16)
        onef_sb = sb("onef_sb", [128, 1], F32)
        onerow_sb = sb("onerow_sb", [1, 128], F32)
        halfrow_sb = sb("halfrow_sb", [1, 128], F32)
        msel_sb = sb("msel_sb", [128, 4], F32)
        cv_sb = sb("cv_sb", [128, 4], F32)
        bq_col = sb("bq_col", [128, 1], F32)
        bk_col = sb("bk_col", [128, 1], F32)
        bv_col = sb("bv_col", [128, 1], F32)
        bo_sb = sb("bo_sb", [128, 16], F32)
        csq = sb("csq", [1, 128], BF16)
        csk = sb("csk", [1, 128], BF16)
        csv = sb("csv", [1, 128], BF16)
        mrow = sb("mrow", [1, 512], F32)
        srow = sb("srow", [1, 512], F32)
        m2row = sb("m2row", [1, 512], F32)
        varrow = sb("varrow", [1, 512], F32)
        lnrow = m2row
        rrow = sb("rrow", [1, 512], F32)
        rmrow = sb("rmrow", [1, 512], BF16)
        recrow = sb("recrow", [1, 512], F32)
        gred = sb("gred", [128, NW], F32)
        gcol = sb("gcol", [128, 3], F32)
        ge_row = sb("ge_row", [1, 1], F32)
        bpc = [sb(f"bp{m}", [128, 1], F32) for m in range(3)]
        bnc = [sb(f"bn{m}", [128, 1], F32) for m in range(3)]

        with contextlib.ExitStack() as PR:
            wq_st = PR.enter_context(nc.sbuf_tensor("wq_st", [128, KD * 512], F32))
            bq4_sb = PR.enter_context(nc.sbuf_tensor("bq4_sb", [4, 128], F32))
            bk1_sb = PR.enter_context(nc.sbuf_tensor("bk1_sb", [1, 128], F32))
            bv1_sb = PR.enter_context(nc.sbuf_tensor("bv1_sb", [1, 128], F32))
            bo16_sb = PR.enter_context(nc.sbuf_tensor("bo16_sb", [16, 128], F32))
            tqa = PR.enter_context(nc.sbuf_tensor("tqa", [128, 128], F32))
            tqb = PR.enter_context(nc.sbuf_tensor("tqb", [128, 128], F32))
            PRE_N = 0
            for dst, src in [(wq_st.ap().rearrange("p (k i) -> p k i", k=KD), wq_s.ap().rearrange("(k p) i -> p k i", p=128)),
                             (atm_sb[:, :], atm[:, :]), (btm_sb[:, :], btm[:, :]),
                             (idb_sb[:, :], idb[:, :]), (idf_sb[:, :], idf[:, :]),
                             (oneb_sb[:, :], oneb[:, :]), (onef_sb[:, :], onef[:, :]),
                             (onerow_sb[:, :], onerow[:, :]), (halfrow_sb[:, :], halfrow[:, :]),
                             (msel_sb[:, :], msel[:, :]),
                             (cv_sb[:, :], cvals[:, :]),
                             (bq4_sb[:, :], bq4[:, :]), (bk1_sb[:, :], bk1[:, :]),
                             (bv1_sb[:, :], bv1[:, :]), (bo16_sb[:, :], bo16[:, :])]:
                PRE_N = P.dma("sync", dst, src, spre)
            PREG_N = P.dma("gpsimd", wk_sb.ap().rearrange("p (k i) -> p k i", k=KD),
                           wk_s.ap().rearrange("(k p) i -> p k i", p=128), spreg)
            PREG_N = P.dma("gpsimd", wv_sb.ap().rearrange("p (k i) -> p k i", k=KD),
                           wv_s.ap().rearrange("(k p) i -> p k i", p=128), spreg)

            P.wait("vector", spre, PRE_N)
            V_WQE = 0
            for k in range(KD):
                c = k * 512
                if k > 0:
                    P.wait("vector", vs, V_WQE)
                P.vop("tt", out=tqa[:, :], in0=wq_st[:, c:c + 128],
                      in1=wq_st[:, c + 128:c + 256], op=ALU.add)
                P.vop("tt", out=tqb[:, :], in0=wq_st[:, c + 256:c + 384],
                      in1=wq_st[:, c + 384:c + 512], op=ALU.add, sem=vs)
                P.wait("vector", vs, vs.v)
                V_WQE = P.vop("tt", out=wq_eff[:, k * 128:(k + 1) * 128],
                              in0=tqa[:, :], in1=tqb[:, :], op=ALU.add, sem=vs)

            P.wait("tensor", vs, V_WQE)
            P.wait("tensor", spre, PRE_N)
            P.wait("tensor", spreg, PREG_N)
            T_CSQ = 0
            for k in range(KD):
                T_CSQ = P.mm(ps45[0:1, 0:128], oneb_sb[:, :],
                             wq_eff[:, k * 128:(k + 1) * 128],
                             start=(k == 0), stop=(k == KD - 1),
                             sem=ts if k == KD - 1 else None)
            for k in range(KD):
                P.mm(ps6[0:1, 0:128], oneb_sb[:, :], wk_sb[:, k * 128:(k + 1) * 128],
                     start=(k == 0), stop=(k == KD - 1))
            T_CSKV = 0
            for k in range(KD):
                T_CSKV = P.mm(ps7[0:1, 0:128], oneb_sb[:, :],
                              wv_sb[:, k * 128:(k + 1) * 128],
                              start=(k == 0), stop=(k == KD - 1),
                              sem=ts if k == KD - 1 else None)
            P.wait("scalar", ts, T_CSQ)
            P.act(csq[:, :], ps45[0:1, 0:128], AF.Copy, scale=-1.0, sem=ss)
            P.wait("scalar", ts, T_CSKV)
            P.act(csk[:, :], ps6[0:1, 0:128], AF.Copy, scale=-1.0, sem=ss)
            SS_CS = P.act(csv[:, :], ps7[0:1, 0:128], AF.Copy, scale=-1.0, sem=ss)
            P.wait("tensor", ss, SS_CS)
            P.mm(ps45[:, 0:1], bq4_sb[:, :], onef_sb[0:4, 0:1], True, True)
            P.mm(ps45[:, 512:513], bk1_sb[:, :], onef_sb[0:1, 0:1], True, True)
            P.mm(ps6[:, 0:1], bv1_sb[:, :], onef_sb[0:1, 0:1], True, True)
            T_BIAS = P.tr(ps7[:, 0:16], bo16_sb[:, :], idf_sb[:, :], sem=ts)
            P.wait("scalar", ts, T_BIAS)
            P.act(bq_col[:, :], ps45[:, 0:1], AF.Copy, sem=ss)
            P.act(bk_col[:, :], ps45[:, 512:513], AF.Copy, sem=ss)
            P.act(bv_col[:, :], ps6[:, 0:1], AF.Copy, sem=ss)
            SS_PRO = P.act(bo_sb[:, :], ps7[:, 0:16], AF.Copy, sem=ss)

        stk_p1c = contextlib.ExitStack()
        qT = stk_p1c.enter_context(nc.sbuf_tensor("qT", [128, T], BF16))
        kT = stk_p1c.enter_context(nc.sbuf_tensor("kT", [128, T], BF16))
        vS = stk_p1c.enter_context(nc.sbuf_tensor("vS", [128, T], BF16))

        stk_p1a = contextlib.ExitStack()
        xb = [stk_p1a.enter_context(nc.sbuf_tensor(f"xb{p}", [128, KD * 512], BF16))
              for p in range(3)]
        xsqh = stk_p1a.enter_context(nc.sbuf_tensor("xsqh", [128, 8 * 512], BF16))
        m1s = [stk_p1a.enter_context(nc.sbuf_tensor(f"m1s{p}", [128, 512], F32))
               for p in range(3)]
        m2s = [stk_p1a.enter_context(nc.sbuf_tensor(f"m2s{p}", [128, 512], F32))
               for p in range(3)]
        tq_sb = stk_p1a.enter_context(nc.sbuf_tensor("tq_sb", [128, 512], F32))
        tk_sb = stk_p1a.enter_context(nc.sbuf_tensor("tk_sb", [128, 512], F32))
        tv_sb = stk_p1a.enter_context(nc.sbuf_tensor("tv_sb", [128, 512], F32))
        rbc_sb = stk_p1a.enter_context(nc.sbuf_tensor("rbc_sb", [128, 512], F32))
        t1_sb = rbc_sb
        t2_sb = tq_sb
        uq_sb = stk_p1a.enter_context(nc.sbuf_tensor("uq_sb", [128, 512], BF16))
        uk_sb = stk_p1a.enter_context(nc.sbuf_tensor("uk_sb", [128, 512], BF16))
        vr_sb = stk_p1a.enter_context(nc.sbuf_tensor("vr_sb", [128, 512], BF16))

        T_XB = [0] * NBLK
        T_MS = [0] * NBLK
        T_SK = {}
        T_SKL = [0] * NBLK
        T_RANK = [0] * NBLK
        SS_SROW = [0] * NBLK
        SS_RBC = [0] * NBLK
        SS_VSB = [0] * NBLK
        V_SQ = [0] * NBLK
        V_QEV = [0] * NBLK
        V_KEV = [0] * NBLK
        V_TV = [0] * NBLK
        V_QT2 = [0] * NBLK
        V_KT2 = [0] * NBLK
        V_KROT = [0] * NBLK

        for i in range(NBLK):
            p = i % 3
            c0 = i * 512
            if i >= 3:
                P.wait("gpsimd", ts, T_XB[i - 3])
                P.wait("gpsimd", vs, V_KROT[i - 3])
            else:
                P.wait("gpsimd", ss, SS_PRO)
            P.dma("gpsimd", xb[p].ap().rearrange("p (k t) -> p k t", k=KD),
                  xT[:, c0:c0 + 512].rearrange("(k p) t -> p k t", p=128), sx[p])
            P.dma("gpsimd", m1s[p][:, :], m1t[:, c0:c0 + 512], sx[p])
            SX_N = P.dma("gpsimd", m2s[p][:, :], m2t[:, c0:c0 + 512], sx[p])

            assert SX_N == 48 * (i // 3 + 1)
            if i == 0:
                P.wait("tensor", sx[p], SX_N)
                P.wait("tensor", ss, SS_PRO)
                for k in range(KD):
                    P.mm(ps4r, oneb_sb[:, :], xb[p][:, k * 512:(k + 1) * 512],
                         start=(k == 0), stop=(k == KD - 1))
            P.wait("tensor", sx[p], SX_N)
            if i >= 1:
                P.wait("tensor", vs, V_QEV[i - 1])
            for k in range(KD):
                P.mm(ps01[:, 0:512], wq_eff[:, k * 128:(k + 1) * 128],
                     xb[p][:, k * 512:(k + 1) * 512], start=(k == 0), stop=False)

            P.wait("vector", sx[p], SX_N)
            if i >= 1:
                P.wait("vector", ts, T_MS[i - 1])
            V_FA = 0
            for k in range(8):
                V_FA = P.vop("tt", out=xsqh[:, k * 512:(k + 1) * 512],
                             in0=xb[p][:, k * 512:(k + 1) * 512],
                             in1=xb[p][:, k * 512:(k + 1) * 512],
                             op=ALU.mult, sem=vs if k == 7 else None)
            P.wait("tensor", vs, V_FA)
            T_SA = 0
            for k in range(8):
                T_SA = P.mm(ps5r, oneb_sb[:, :], xsqh[:, k * 512:(k + 1) * 512],
                            start=(k == 0), stop=False,
                            sem=ts if k == 7 else None)

            if i >= 1:
                P.wait("tensor", vs, V_KEV[i - 1])
            for k in range(KD):
                P.mm(ps01[:, 512:1024], wk_sb[:, k * 128:(k + 1) * 128],
                     xb[p][:, k * 512:(k + 1) * 512], start=(k == 0), stop=False)

            P.wait("vector", ts, T_SA)
            V_FB = 0
            for k in range(8, KD):
                V_FB = P.vop("tt", out=xsqh[:, (k - 8) * 512:(k - 7) * 512],
                             in0=xb[p][:, k * 512:(k + 1) * 512],
                             in1=xb[p][:, k * 512:(k + 1) * 512],
                             op=ALU.mult, sem=vs if k == KD - 1 else None)
            P.wait("tensor", vs, V_FB)
            for k in range(8, KD):
                T_MS[i] = P.mm(ps5r, oneb_sb[:, :],
                               xsqh[:, (k - 8) * 512:(k - 7) * 512],
                               start=False, stop=(k == KD - 1),
                               sem=ts if k == KD - 1 else None)

            P.wait("scalar", ts, T_MS[i])
            P.act(mrow[:, :], ps4r, AF.Copy, scale=1.0 / D, sem=ss)
            SS_SROW[i] = P.act(srow[:, :], ps5r, AF.Copy, scale=1.0 / D, sem=ss)
            P.wait("vector", ss, SS_SROW[i])
            P.vop("tt", out=m2row[:, :], in0=mrow[:, :], in1=mrow[:, :],
                  op=ALU.mult, sem=vs)
            P.wait("vector", vs, vs.v)
            V_VAR = P.vop("tt", out=varrow[:, :], in0=srow[:, :], in1=m2row[:, :],
                          op=ALU.subtract, sem=vs)
            P.wait("scalar", vs, V_VAR)
            P.act(lnrow[:, :], varrow[:, :], AF.Ln, bias=cv_sb[0:1, 1:2], sem=ss)
            P.wait("scalar", ss, ss.v)
            SS_R = P.act(rrow[:, :], lnrow[:, :], AF.Exp, scale=-0.5, sem=ss)
            P.wait("vector", ss, SS_R)
            V_RM = P.vop("tt", out=rmrow[:, :], in0=rrow[:, :], in1=mrow[:, :],
                         op=ALU.mult, sem=vs)

            if i >= 1:
                P.wait("tensor", vs, V_TV[i - 1])
            for k in range(KD):
                T_XB[i] = P.mm(ps23[:, 0:512], wv_sb[:, k * 128:(k + 1) * 128],
                               xb[p][:, k * 512:(k + 1) * 512],
                               start=(k == 0), stop=False,
                               sem=ts if k == KD - 1 else None)
            if i >= 1:
                P.wait("tensor", ss, SS_RBC[i - 1])
            P.wait("tensor", ss, SS_R)
            P.mm(ps7[:, :], onerow_sb[:, :], rrow[:, :], True, True)
            P.wait("tensor", vs, V_RM)
            P.mm(ps01[:, 0:512], csq[:, :], rmrow[:, :], start=False, stop=True)
            P.mm(ps01[:, 512:1024], csk[:, :], rmrow[:, :], start=False, stop=True)
            T_RANK[i] = P.mm(ps23[:, 0:512], csv[:, :], rmrow[:, :],
                             start=False, stop=True, sem=ts)

            if i + 1 < NBLK:
                p2 = (i + 1) % 3
                P.wait("tensor", sx[p2], 48 * ((i + 1) // 3 + 1))
                P.wait("tensor", ss, SS_SROW[i])
                for k in range(KD):
                    P.mm(ps4r, oneb_sb[:, :], xb[p2][:, k * 512:(k + 1) * 512],
                         start=(k == 0), stop=(k == KD - 1))

            P.wait("scalar", ts, T_RANK[i])
            SS_RBC[i] = P.act(rbc_sb[:, :], ps7[:, :], AF.Copy, sem=ss)

            P.wait("vector", ts, T_RANK[i])
            P.wait("vector", ss, SS_RBC[i])
            V_QEV[i] = P.vop("tt", out=tq_sb[:, :], in0=ps01[:, 0:512],
                             in1=rbc_sb[:, :], op=ALU.mult, sem=vs)
            V_KEV[i] = P.vop("tt", out=tk_sb[:, :], in0=ps01[:, 512:1024],
                             in1=rbc_sb[:, :], op=ALU.mult, sem=vs)
            V_TV[i] = P.vop("tt", out=tv_sb[:, :], in0=ps23[:, 0:512],
                            in1=rbc_sb[:, :], op=ALU.mult, sem=vs)
            P.wait("scalar", vs, V_QEV[i])
            SS_UQ = P.act(uq_sb[:, :], tq_sb[:, :], AF.Identity, bias=bq_col[:, 0:1], sem=ss)
            P.wait("scalar", vs, V_KEV[i])
            SS_UK = P.act(uk_sb[:, :], tk_sb[:, :], AF.Identity, bias=bk_col[:, 0:1], sem=ss)
            P.wait("scalar", vs, V_TV[i])
            SS_VR = P.act(vr_sb[:, :], tv_sb[:, :], AF.Identity, bias=bv_col[:, 0:1], sem=ss)

            P.wait("tensor", ss, SS_UQ)
            if i >= 1:
                P.wait("tensor", vs, V_KT2[i - 1])
                P.wait("tensor", ss, SS_VSB[i - 1])
            P.mm(ps23[:, 512:1024], atm_sb[:, :], uq_sb[:, :], True, True)
            T_QAB = P.mm(ps6[:, :], btm_sb[:, :], uq_sb[:, :], True, True, sem=ts)
            P.wait("vector", ts, T_QAB)
            P.wait("vector", vs, V_TV[i])
            P.vop("tt", out=t1_sb[:, :], in0=ps23[:, 512:1024], in1=m1s[p][:, :],
                  op=ALU.mult, sem=vs)
            V_QT2[i] = P.vop("tt", out=t2_sb[:, :], in0=ps6[:, :], in1=m2s[p][:, :],
                             op=ALU.mult, sem=vs)
            P.wait("vector", vs, V_QT2[i])
            V_QROT = P.vop("tt", out=qT[:, c0:c0 + 512], in0=t1_sb[:, :],
                           in1=t2_sb[:, :], op=ALU.add, sem=vs)

            P.wait("tensor", ss, SS_UK)
            P.wait("tensor", vs, V_QT2[i])
            P.mm(ps23[:, 512:1024], atm_sb[:, :], uk_sb[:, :], True, True)
            T_KAB = P.mm(ps6[:, :], btm_sb[:, :], uk_sb[:, :], True, True, sem=ts)
            P.wait("vector", ts, T_KAB)
            P.wait("vector", vs, V_QROT)
            P.vop("tt", out=t1_sb[:, :], in0=ps23[:, 512:1024], in1=m1s[p][:, :],
                  op=ALU.mult, sem=vs)
            V_KT2[i] = P.vop("tt", out=t2_sb[:, :], in0=ps6[:, :], in1=m2s[p][:, :],
                             op=ALU.mult, sem=vs)
            P.wait("vector", vs, V_KT2[i])
            V_KROT[i] = P.vop("tt", out=kT[:, c0:c0 + 512], in0=t1_sb[:, :],
                              in1=t2_sb[:, :], op=ALU.add, sem=vs)

            P.wait("tensor", ss, SS_VR)
            P.wait("tensor", vs, V_KT2[i])
            ps6b = ps6.ap().bitcast(BF16)
            T_VTR = 0
            for j in range(4):
                T_VTR = P.tr(ps6b[:, j * 128:(j + 1) * 128],
                             vr_sb[:, j * 128:(j + 1) * 128], idb_sb[:, :],
                             sem=ts if j == 3 else None)
            P.wait("scalar", ts, T_VTR)
            for j in range(4):
                SS_VSB[i] = P.act(vS[:, (4 * i + j) * 128:(4 * i + j + 1) * 128],
                                  ps6b[:, j * 128:(j + 1) * 128], AF.Copy,
                                  sem=ss if j == 3 else None)

        stk_p1a.close()

        stk_p1b = contextlib.ExitStack()
        attT = [stk_p1b.enter_context(nc.sbuf_tensor(f"attT{p}", [128, 1024], BF16))
                for p in range(6)]
        attTm = stk_p1b.enter_context(nc.sbuf_tensor("attTm", [128, 1024], BF16))
        ybuf = [stk_p1b.enter_context(nc.sbuf_tensor(f"ybuf{p}", [128, 512], BF16))
                for p in range(2)]
        bcc_sb = stk_p1b.enter_context(nc.sbuf_tensor("bcc_sb", [128, 512], F32))
        mskd_sb = stk_p1b.enter_context(nc.sbuf_tensor("mskd_sb", [128, 2048], BF16))
        wstr = [stk_p1b.enter_context(nc.sbuf_tensor(f"wstr{p}", [128, 2048], F32))
                for p in range(4)]
        sgA = [stk_p1b.enter_context(nc.sbuf_tensor(f"sgA{p}", [128, 2048], BF16))
               for p in range(2)]
        sgB = [stk_p1b.enter_context(nc.sbuf_tensor(f"sgB{p}", [128, 2048], BF16))
               for p in range(2)]
        qbk = [stk_p1b.enter_context(nc.sbuf_tensor(f"qbk{p}", [128, 2048], BF16))
               for p in range(2)]

        P_BAR_T = ts.v
        P_BAR_V = vs.v
        P_BAR_S = ss.v
        P.wait("sync", ts, P_BAR_T)
        P.wait("sync", vs, P_BAR_V)
        P.wait("sync", ss, P_BAR_S)
        PRE_MSK = P.dma("sync", mskd_sb[:, :], mskd[:, :], spre)
        P.wait("vector", spre, PRE_MSK)

        units = []
        for hb in range(HB):
            units.append((0, w1T[:, hb * 128:(hb + 1) * 128]
                          .rearrange("(k p) h -> p k h", p=128),
                          w1q_h[:, hb * D:(hb + 1) * D], D))
        for hb in range(HB):
            units.append((1, w2T[:, hb * 128:(hb + 1) * 128]
                          .rearrange("(k p) h -> p k h", p=128),
                          w2q_h[:, hb * D:(hb + 1) * D], D))
        for dt in range(16):
            w3v = w3T[:, dt * 128:(dt + 1) * 128].rearrange("(k p) d_ -> p k d_", p=128)
            for k0 in range(0, HB, 16):
                k1 = min(k0 + 16, HB)
                units.append((2, w3v[:, k0:k1, :],
                              w3q_h[:, dt * HF + k0 * 128:dt * HF + k1 * 128],
                              (k1 - k0) * 128))
        assert len(units) == NW

        G_GRED = [0] * NW
        GSWN = [0] * NW
        g_state = {"iss": 0, "drn": 0}

        def gamma_issue_one(j):
            pj = j % 4
            if j >= 4:
                P.wait("sync", vs, G_GRED[j - 4])
            m, gsrc, _, gw = units[j]
            GSWN[j] = P.dma("sync", wstr[pj][:, 0:gw]
                            .rearrange("p (k h) -> p k h", h=128), gsrc, swg[pj])

        def gamma_drain_one(j):
            if g_state["drn"] != j:
                return
            g_state["drn"] = j + 1
            pj = j % 4
            m, gsrc, _, gw = units[j]
            P.wait("vector", swg[pj], GSWN[j])
            G_GRED[j] = P.vop("red", out=gred[:, j:j + 1], in_=wstr[pj][:, 0:gw],
                              absv=True, sem=vs)

        def emit_gamma(n, final=False):
            for _ in range(n):
                j = g_state["iss"]
                if j >= NW:
                    break
                gamma_issue_one(j)
                g_state["iss"] = j + 1
                if j >= 3:
                    gamma_drain_one(j - 3)
            if final:
                for k in range(g_state["drn"], NW):
                    gamma_drain_one(k)

        V_SG = [0] * NW
        V_QB = [0] * NW
        QSQN = [0] * NW
        q_state = {"iss": 0, "drn": 0, "stored": 0, "V_GF": 0, "SS_GDONE": 0}

        def qz_issue_one(u):
            pu = u % 4
            m, src, dst, wdt = units[u]
            if u < 4:
                P.wait("sync", vs, G_GRED[NW - 1])
            else:
                P.wait("sync", vs, V_SG[u - 4])
            QSQN[u] = P.dma("sync", wstr[pu][:, 0:wdt]
                            .rearrange("p (k h) -> p k h", h=128), src, sqz[pu])

        def qz_drain_one(u):
            if q_state["drn"] != u:
                return
            q_state["drn"] = u + 1
            pu = u % 4
            ps2 = u & 1
            m, src, dst, wdt = units[u]
            P.wait("vector", sqz[pu], QSQN[u])
            if u == 0:
                P.wait("vector", ss, q_state["SS_GDONE"])
            if u >= 2:
                P.wait("vector", vs, V_QB[u - 2])
            P.vop("ts2", out=sgA[ps2][:, 0:wdt], in0=wstr[pu][:, 0:wdt],
                  s1=bpc[m][:, 0:1], s2=0.0, op0=ALU.add, op1=ALU.is_ge,
                  sem=vs)
            V_SG[u] = P.vop("ts2", out=sgB[ps2][:, 0:wdt], in0=wstr[pu][:, 0:wdt],
                            s1=bnc[m][:, 0:1], s2=0.0, op0=ALU.add,
                            op1=ALU.is_ge, sem=vs)
            P.wait("vector", vs, V_SG[u])
            if u >= 2:
                P.wait("vector", sqw[ps2], 16 * (u // 2))
            V_QB[u] = P.vop("stt", out=qbk[ps2][:, 0:wdt], in0=sgA[ps2][:, 0:wdt],
                            scalar=-1.0, in1=sgB[ps2][:, 0:wdt],
                            op0=ALU.add, op1=ALU.add, sem=vs)

        def qz_store_one(u):
            if q_state["stored"] != u:
                return
            q_state["stored"] = u + 1
            m, src, dst, wdt = units[u]
            P.wait("sync", vs, V_QB[u])
            P.dma("sync", dst, qbk[u & 1][:, 0:wdt], sqw[u & 1])

        def emit_qz(n, final=False):
            for _ in range(n):
                u = q_state["iss"]
                if u >= NW:
                    break
                qz_issue_one(u)
                q_state["iss"] = u + 1
                if u >= 3:
                    qz_drain_one(u - 3)
                    qz_store_one(u - 3)
            if final:
                for k in range(q_state["drn"], NW):
                    qz_drain_one(k)
                for k in range(q_state["stored"], NW):
                    qz_store_one(k)

        wides = [ps01, ps23, ps45]
        T_AV = {}
        T_SC = {}
        SS_EXP = {}
        V_MSK = {}
        V_Y = [0] * NBLK
        V_REC = [0] * NBLK
        SS_BCC = [0] * NBLK
        V_MSK_LAST = 0
        ps7_free = {"ss": P_BAR_S}
        FINB = max(1, min(NBLK - 2, (NBLK * 45) // 100))
        npre = sum(2 * (b + 1) for b in range(FINB))
        GPRATE = max(1, npre // (NW + 1))
        QHALF = (NW + 2 * (NBLK - 1 - FINB) - 1) // (2 * (NBLK - 1 - FINB))

        pairs = []
        for b in range(NBLK):
            npair = 2 * (b + 1)
            for pp in range(npair):
                pairs.append((b, pp, npair, b * 512))
        NP = len(pairs)

        def emit_scores(gp):
            b, pp, npair, q0 = pairs[gp]
            W = wides[gp % 3]
            if gp < 3:
                P.wait("tensor", vs, P_BAR_V)
                P.wait("tensor", ss, P_BAR_S)
            else:
                P.wait("tensor", ss, SS_EXP[gp - 3])
            st = 2 * pp
            P.mm(W[:, 0:512], kT[:, st * 128:(st + 1) * 128], qT[:, q0:q0 + 512],
                 True, True)
            T_SC[gp] = P.mm(W[:, 512:1024], kT[:, (st + 1) * 128:(st + 2) * 128],
                            qT[:, q0:q0 + 512], True, True, sem=ts)

        for gp in range(min(3, NP)):
            emit_scores(gp)

        for gp in range(NP):
            b, pp, npair, q0 = pairs[gp]
            W = wides[gp % 3]
            At = attT[gp % 6]
            masked = pp >= npair - 2
            st = 2 * pp
            P.wait("scalar", ts, T_SC[gp])
            if gp >= 6:
                P.wait("scalar", ts, T_AV[gp - 6])
            if masked and pp == npair - 1:
                P.wait("scalar", vs, V_MSK_LAST)
            tgt = attTm if masked else At
            SS_EXP[gp] = P.act(tgt[:, :], W[:, :], AF.Exp,
                               bias=cv_sb[:, 0:1], scale=SCALE, sem=ss)
            if masked:
                slot0 = st - 4 * b
                P.wait("vector", ss, SS_EXP[gp])
                P.vop("tt", out=At[:, 0:512], in0=attTm[:, 0:512],
                      in1=mskd_sb[:, slot0 * 512:(slot0 + 1) * 512],
                      op=ALU.mult, sem=vs)
                V_MSK_LAST = P.vop("tt", out=At[:, 512:1024],
                                   in0=attTm[:, 512:1024],
                                   in1=mskd_sb[:, (slot0 + 1) * 512:(slot0 + 2) * 512],
                                   op=ALU.mult, sem=vs)
            if masked:
                P.wait("tensor", vs, V_MSK_LAST)
            else:
                P.wait("tensor", ss, SS_EXP[gp])
            if pp == 0:
                if b >= 1:
                    P.wait("tensor", vs, V_Y[b - 1])
                P.wait("tensor", ss, ps7_free["ss"])
            P.mm(ps6[:, :], vS[:, st * 128:(st + 1) * 128], At[:, 0:512],
                 start=(pp == 0), stop=False)
            P.mm(ps6[:, :], vS[:, (st + 1) * 128:(st + 2) * 128], At[:, 512:1024],
                 start=False, stop=(pp == npair - 1))
            P.mm(ps7[0:1, :], oneb_sb[:, :], At[:, 0:512],
                 start=(pp == 0), stop=False)
            T_AV[gp] = P.mm(ps7[0:1, :], oneb_sb[:, :], At[:, 512:1024],
                            start=False, stop=(pp == npair - 1), sem=ts)
            if gp + 3 < NP:
                emit_scores(gp + 3)
            if b > FINB and (pp == 1 or pp == npair // 2):
                emit_qz(QHALF)
            elif b < FINB and gp % GPRATE == 0:
                emit_gamma(1)
            if pp != npair - 1:
                continue
            P.wait("vector", ts, T_AV[gp])
            V_REC[b] = P.vop("recip", out=recrow[:, :], in_=ps7[0:1, :], sem=vs)
            P.wait("tensor", vs, V_REC[b])
            T_BC = P.mm(ps7[:, :], onerow_sb[:, :], recrow[:, :], True, True, sem=ts)
            P.wait("scalar", ts, T_BC)
            SS_BCC[b] = P.act(bcc_sb[:, :], ps7[:, :], AF.Copy, sem=ss)
            ps7_free["ss"] = SS_BCC[b]
            P.wait("vector", ss, SS_BCC[b])
            if b >= 2:
                P.wait("vector", sy[b & 1], 16 * (b // 2))
            V_Y[b] = P.vop("tt", out=ybuf[b & 1][:, :], in0=ps6[:, :],
                           in1=bcc_sb[:, :], op=ALU.mult, sem=vs)
            P.wait("gpsimd", vs, V_Y[b])
            P.dma("gpsimd", ag_in[:, q0:q0 + 512], ybuf[b & 1][:, :], sy[b & 1])

            if b == FINB:
                emit_gamma(NW, final=True)
                P.wait("vector", vs, G_GRED[NW - 1])
                P.vop("red", out=gcol[:, 0:1], in_=gred[:, 0:HB])
                P.vop("red", out=gcol[:, 1:2], in_=gred[:, HB:2 * HB])
                V_GF = P.vop("red", out=gcol[:, 2:3], in_=gred[:, 2 * HB:NW],
                             sem=vs)
                q_state["V_GF"] = V_GF
                SS_GDONE = 0
                for m in range(3):
                    P.wait("tensor", vs, V_GF)
                    P.wait("tensor", ss, ps7_free["ss"] if m == 0 else SS_GDONE)
                    T_G1 = P.mm(ps7[0:1, 0:1], gcol[:, m:m + 1], onef_sb[:, :],
                                True, True, sem=ts)
                    P.wait("scalar", ts, T_G1)
                    SS_GE = P.act(ge_row[:, :], ps7[0:1, 0:1], AF.Identity,
                                  scale=1.0 / float(HF * D), bias=cv_sb[0:1, 2:3],
                                  sem=ss)
                    P.wait("tensor", ss, SS_GE)
                    T_G2 = P.mm(ps7[:, 0:1], onerow_sb[:, :], ge_row[:, :],
                                True, True, sem=ts)
                    P.wait("scalar", ts, T_G2)
                    P.act(bpc[m][:, :], ps7[:, 0:1], AF.Copy, scale=0.5, sem=ss)
                    SS_GDONE = P.act(bnc[m][:, :], ps7[:, 0:1], AF.Copy,
                                     scale=-0.5, sem=ss)
                q_state["SS_GDONE"] = SS_GDONE
                ps7_free["ss"] = SS_GDONE
        emit_qz(NW, final=True)

        P.wait("gpsimd", sy[0], sy[0].v)
        P.wait("gpsimd", sy[1], sy[1].v)
        CC_N = P.coll("AllGather", ALU.bypass, [[0, 1, 2, 3], [4, 5, 6, 7]],
                      [ag_in.ap().opt()], [ag_out.ap().opt()], scc)

        TS_QZ = ts.v
        VS_QZ = vs.v
        SS_QZ = ss.v
        stk_p1b.close()
        stk_p1c.close()

        stk_p2 = contextlib.ExitStack()
        p2 = lambda n, shp, dt: stk_p2.enter_context(nc.sbuf_tensor(n, shp, dt))
        xbo = p2("xbo", [128, KD * 512], BF16)
        cmb = [p2(f"cmb{c}", [128, 512], BF16) for c in range(4)]
        yTg = [p2(f"yTg{q}", [128, 4 * 512], BF16) for q in range(2)]
        gld = p2("gld", [128, 3 * 512], BF16)
        xq2 = [p2(f"xq2{q}", [128, 512], BF16) for q in range(2)]
        wo_sb = p2("wo_sb", [128, 4 * D], BF16)
        cs1 = p2("cs1", [1, HF], BF16)
        cs2 = p2("cs2", [1, HF], BF16)
        hid = p2("hid", [128, HB * 512], BF16)
        w1qb = [p2(f"w1qb{q}", [128, D], BF16) for q in range(3)]
        w2qb = [p2(f"w2qb{q}", [128, D], BF16) for q in range(3)]
        w3qb = [p2(f"w3qb{q}", [128, HF], BF16) for q in range(2)]
        xf32 = [p2(f"xf32{q}", [128, 512], F32) for q in range(2)]
        x1t = p2("x1t", [128, 512], F32)
        x2t = p2("x2t", [128, 512], F32)
        sig_sb = p2("sig_sb", [128, 512], BF16)
        rbc2 = p2("rbc2", [128, 512], F32)
        osb = [p2(f"osb{q}", [128, 512], F32) for q in range(2)]


        T_GRP = [0] * NGRP
        V_OUT = {}
        SS_RBC2 = [0] * NGRP
        SS_SIG_LAST = 0
        T_X12L = [0] * NGRP
        V_CMBL = [0] * NGRP
        T_S2KL = [0] * NGRP
        SS_SR2L = [0] * NGRP
        V_HIDL = [0] * NGRP

        for g in range(NGRP):
            pg = g & 1
            gc = g * 512
            if g >= 1:
                P.wait("gpsimd", ts, T_X12L[g - 1])
                P.wait("gpsimd", vs, V_CMBL[g - 1])
            SXB_N = P.dma("gpsimd", xbo.ap().rearrange("p (k t) -> p k t", k=KD),
                  xTo[:, gc:gc + 512].rearrange("(k p) t -> p k t", p=128), sxb[pg])
            if g == 0:
                P.wait("gpsimd", scc, CC_N)
            P.wait("tensor", sxb[pg], SXB_N)
            if g == 0:
                P.wait("tensor", ss, SS_EXP[NP - 1])
            else:
                P.wait("tensor", ss, SS_SR2L[g - 1])
            for k in range(KD):
                P.mm(ps4r, oneb_sb[:, :], xbo[:, k * 512:(k + 1) * 512],
                     start=(k == 0), stop=(k == KD - 1))
            P.wait("vector", sxb[pg], SXB_N)
            T_S2K = [0] * KD
            T_MS2 = 0
            for k in range(KD):
                if k >= 2:
                    P.wait("vector", ts, T_S2K[k - 2])
                elif g >= 1:
                    P.wait("vector", ts, T_S2KL[g - 1])
                vsq = P.vop("tt", out=xq2[k & 1][:, :],
                            in0=xbo[:, k * 512:(k + 1) * 512],
                            in1=xbo[:, k * 512:(k + 1) * 512],
                            op=ALU.mult, sem=vs)
                P.wait("tensor", vs, vsq)
                T_MS2 = P.mm(ps5r, oneb_sb[:, :], xq2[k & 1][:, :],
                             start=(k == 0), stop=(k == KD - 1), sem=ts)
                T_S2K[k] = T_MS2
            T_S2KL[g] = T_S2K[KD - 1]
            P.wait("scalar", ts, T_MS2)
            P.act(mrow[:, :], ps4r, AF.Copy, scale=1.0 / D, sem=ss)
            SS_SR2 = P.act(srow[:, :], ps5r, AF.Copy, scale=1.0 / D, sem=ss)
            SS_SR2L[g] = SS_SR2
            P.wait("vector", ss, SS_SR2)
            P.vop("tt", out=m2row[:, :], in0=mrow[:, :], in1=mrow[:, :],
                  op=ALU.mult, sem=vs)
            P.wait("vector", vs, vs.v)
            V_VAR2 = P.vop("tt", out=varrow[:, :], in0=srow[:, :], in1=m2row[:, :],
                           op=ALU.subtract, sem=vs)
            P.wait("scalar", vs, V_VAR2)
            P.act(lnrow[:, :], varrow[:, :], AF.Ln, bias=cv_sb[0:1, 1:2], sem=ss)
            P.wait("scalar", ss, ss.v)
            SS_R2 = P.act(rrow[:, :], lnrow[:, :], AF.Exp, scale=-0.5, sem=ss)
            P.wait("vector", ss, SS_R2)
            V_RM2 = P.vop("tt", out=rmrow[:, :], in0=rrow[:, :], in1=mrow[:, :],
                          op=ALU.mult, sem=vs)
            P.wait("tensor", ss, SS_R2)
            if g >= 1:
                P.wait("tensor", ss, SS_RBC2[g - 1])
                P.wait("tensor", vs, V_OUT[(g - 1, 15)])
            T_BC2 = P.mm(ps7[:, :], onerow_sb[:, :], rrow[:, :], True, True, sem=ts)
            P.wait("scalar", ts, T_BC2)
            SS_RBC2[g] = P.act(rbc2[:, :], ps7[:, :], AF.Copy, sem=ss)

            V_CMB = 0
            SG_N = 0
            for rt in range(4):
                if rt >= 1:
                    P.wait("gpsimd", vs, V_CMB)
                for c in range(4):
                    SG_N = P.dma("gpsimd", cmb[c][:, :],
                                 ag_out[rt * 128:(rt + 1) * 128,
                                        c * NTO + gc:c * NTO + gc + 512], sg[pg])
                P.wait("vector", sg[pg], SG_N)
                if rt == 0 and g == 0:
                    P.wait("vector", vs, VS_QZ)
                if rt == 0 and g >= 2:
                    P.wait("vector", ts, T_GRP[g - 2])
                if rt >= 1:
                    P.wait("vector", vs, V_CMB)
                P.vop("ts", out=gld[:, 0:512], in0=cmb[0][:, :],
                      s1=msel_sb[:, 0:1], op0=ALU.mult, sem=vs)
                P.wait("vector", vs, vs.v)
                P.vop("stt", out=gld[:, 512:1024], in0=cmb[1][:, :],
                      scalar=msel_sb[:, 1:2], in1=gld[:, 0:512],
                      op0=ALU.mult, op1=ALU.add, sem=vs)
                P.wait("vector", vs, vs.v)
                P.vop("stt", out=gld[:, 1024:1536], in0=cmb[2][:, :],
                      scalar=msel_sb[:, 2:3], in1=gld[:, 512:1024],
                      op0=ALU.mult, op1=ALU.add, sem=vs)
                P.wait("vector", vs, vs.v)
                V_CMB = P.vop("stt", out=yTg[pg][:, rt * 512:(rt + 1) * 512],
                              in0=cmb[3][:, :], scalar=msel_sb[:, 3:4],
                              in1=gld[:, 1024:1536], op0=ALU.mult, op1=ALU.add,
                              sem=vs)
            V_CMBL[g] = V_CMB
            if g == 0:
                P.wait("gpsimd", ts, TS_QZ)
                P.wait("gpsimd", vs, VS_QZ)
                P.wait("gpsimd", ss, SS_QZ)
                P.wait("gpsimd", sqw[0], sqw[0].v)
                P.wait("gpsimd", sqw[1], sqw[1].v)
                PREG2 = P.dma("gpsimd", wo_sb.ap().rearrange("p (k n) -> p k n", k=4),
                              wo_d.ap().rearrange("(k p) n -> p k n", p=128), spreg)

            V_HID = [0] * HB
            T_X12 = [0] * HB
            SS_CS2 = [0] * HB
            for hb in range(HB):
                ph = hb % 3
                psA = ps01 if (hb & 1) == 0 else ps23
                if hb >= 3:
                    P.wait("sync", ts, T_X12[hb - 3])
                elif g >= 1:
                    P.wait("sync", ts, T_GRP[g - 1])
                else:
                    P.wait("sync", sqw[0], sqw[0].v)
                    P.wait("sync", sqw[1], sqw[1].v)
                    P.wait("sync", ts, TS_QZ)
                    P.wait("sync", vs, VS_QZ)
                sqn = P.dma("sync", w1qb[ph][:, :], w1q_h[:, hb * D:(hb + 1) * D],
                            sq[ph])
                sqbn = P.dma("sync", w2qb[ph][:, :], w2q_h[:, hb * D:(hb + 1) * D],
                             sqb[ph])
                P.wait("tensor", sq[ph], sqn)
                P.wait("tensor", sqb[ph], sqbn)
                if hb >= 2:
                    P.wait("tensor", vs, V_HID[hb - 2])
                elif g >= 1:
                    P.wait("tensor", vs, V_HIDL[g - 1])
                if hb == 0:
                    P.wait("tensor", vs, V_RM2)
                    P.wait("tensor", vs, V_CMB)
                if g == 0:
                    if hb >= 1:
                        P.wait("tensor", ss, SS_CS2[hb - 1])
                    for k in range(KD):
                        P.mm(ps45[0:1, 0:128], oneb_sb[:, :],
                             w1qb[ph][:, k * 128:(k + 1) * 128],
                             start=(k == 0), stop=(k == KD - 1))
                    for k in range(KD):
                        T_CSC = P.mm(ps45[0:1, 512:640], oneb_sb[:, :],
                                     w2qb[ph][:, k * 128:(k + 1) * 128],
                                     start=(k == 0), stop=(k == KD - 1),
                                     sem=ts if k == KD - 1 else None)
                    P.wait("scalar", ts, T_CSC)
                    P.act(cs1[:, hb * 128:(hb + 1) * 128], ps45[0:1, 0:128],
                          AF.Copy, scale=-1.0, sem=ss)
                    SS_CS2[hb] = P.act(cs2[:, hb * 128:(hb + 1) * 128],
                                       ps45[0:1, 512:640], AF.Copy, scale=-1.0,
                                       sem=ss)
                for k in range(KD):
                    P.mm(psA[:, 0:512], w1qb[ph][:, k * 128:(k + 1) * 128],
                         xbo[:, k * 512:(k + 1) * 512], start=(k == 0), stop=False)
                if g == 0:
                    P.wait("tensor", ss, SS_CS2[hb])
                P.mm(psA[:, 0:512], cs1[:, hb * 128:(hb + 1) * 128], rmrow[:, :],
                     start=False, stop=True)
                for k in range(KD):
                    P.mm(psA[:, 512:1024], w2qb[ph][:, k * 128:(k + 1) * 128],
                         xbo[:, k * 512:(k + 1) * 512], start=(k == 0), stop=False)
                T_X12[hb] = P.mm(psA[:, 512:1024], cs2[:, hb * 128:(hb + 1) * 128],
                                 rmrow[:, :], start=False, stop=True, sem=ts)
                P.wait("vector", ts, T_X12[hb])
                if g == 0 and hb == 0:
                    P.wait("vector", ss, SS_RBC2[g])
                P.wait("vector", ss, SS_SIG_LAST)
                if hb >= 1:
                    P.wait("vector", vs, V_HID[hb - 1])
                P.vop("tt", out=x2t[:, :], in0=psA[:, 512:1024], in1=rbc2[:, :],
                      op=ALU.mult, sem=vs)
                V_X1 = P.vop("tt", out=x1t[:, :], in0=psA[:, 0:512], in1=rbc2[:, :],
                             op=ALU.mult, sem=vs)
                P.wait("scalar", vs, V_X1)
                SS_SIG_LAST = P.act(sig_sb[:, :], x1t[:, :], AF.Sigmoid, sem=ss)
                P.wait("vector", ss, SS_SIG_LAST)
                V_HID[hb] = P.vop("tt", out=hid[:, hb * 512:(hb + 1) * 512],
                                  in0=x2t[:, :], in1=sig_sb[:, :],
                                  op=ALU.mult, sem=vs)
            T_X12L[g] = T_X12[HB - 1]
            V_HIDL[g] = V_HID[HB - 1]

            T_OUTL = [0] * 16
            pend_store = [None]
            for dt in range(16):
                pw = dt & 1
                psO = ps6 if pw == 0 else ps7
                if dt >= 2:
                    P.wait("sync", ts, T_OUTL[dt - 2])
                elif g >= 1:
                    P.wait("sync", ts, T_GRP[g - 1])
                SW3_N = P.dma("sync", w3qb[pw][:, :], w3q_h[:, dt * HF:(dt + 1) * HF],
                              sw3[pw])
                if pend_store[0] is not None:
                    pend_store[0]()
                    pend_store[0] = None
                if dt >= 2:
                    P.wait("scalar", vs, V_OUT[(g, dt - 2)])
                elif g >= 1:
                    P.wait("scalar", vs, V_OUT[(g - 1, 14 + dt)])
                else:
                    P.wait("scalar", vs, VS_QZ)
                SXF_N = P.dma("scalar", xf32[pw][:, :],
                              xTo[dt * 128:(dt + 1) * 128, gc:gc + 512], sxf[pw])
                P.wait("tensor", sw3[pw], SW3_N)
                if dt == 0:
                    if g == 0:
                        P.wait("tensor", spreg, PREG2)
                    P.wait("tensor", vs, V_HID[HB - 1])
                if dt == 1:
                    P.wait("tensor", ss, SS_RBC2[g])
                if dt >= 2:
                    P.wait("tensor", vs, V_OUT[(g, dt - 2)])
                elif g >= 1:
                    P.wait("tensor", vs, V_OUT[(g - 1, 14 + dt)])
                for k in range(4):
                    P.mm(psO[:, :], wo_sb[:, k * D + dt * 128:k * D + (dt + 1) * 128],
                         yTg[pg][:, k * 512:(k + 1) * 512], start=(k == 0), stop=False)
                T_OUT = 0
                for k in range(HB):
                    T_OUT = P.mm(psO[:, :], w3qb[pw][:, k * 128:(k + 1) * 128],
                                 hid[:, k * 512:(k + 1) * 512], start=False,
                                 stop=(k == HB - 1), sem=ts if k == HB - 1 else None)
                T_OUTL[dt] = T_OUT
                nglob = g * 16 + dt
                P.wait("vector", ts, T_OUT)
                P.wait("vector", sxf[pw], SXF_N)
                if nglob >= 2:
                    P.wait("vector", sout[pw], 16 * (nglob // 2))
                V_OUT[(g, dt)] = P.vop("stt", out=osb[pw][:, :], in0=psO[:, :],
                                       scalar=bo_sb[:, dt:dt + 1], in1=xf32[pw][:, :],
                                       op0=ALU.add, op1=ALU.add, sem=vs)
                def _store(g=g, dt=dt, pw=pw, gc=gc):
                    P.wait("sync", vs, V_OUT[(g, dt)])
                    P.dma("sync", outT[dt * 128:(dt + 1) * 128, gc:gc + 512],
                          osb[pw][:, :], sout[pw])
                if dt == 15:
                    _store()
                else:
                    pend_store[0] = _store
            T_GRP[g] = T_OUTL[15]

        stk_p2.close()

        P.wait("sync", sout[0], sout[0].v)
        P.wait("sync", sout[1], sout[1].v)

        with nc.Block() as block:
            @block.sync
            def _(e):
                for f in P.q["sync"]:
                    f(e)

            @block.gpsimd
            def _(e):
                for f in P.q["gpsimd"]:
                    f(e)

            @block.tensor
            def _(e):
                for f in P.q["tensor"]:
                    f(e)

            @block.vector
            def _(e):
                for f in P.q["vector"]:
                    f(e)

            @block.scalar
            def _(e):
                for f in P.q["scalar"]:
                    f(e)

    return nc


def _constants(T):
    j = np.arange(64, dtype=np.float64)
    inv = 10000.0 ** (-2.0 * j / 128.0)
    t = np.arange(T, dtype=np.float64)
    emb = np.outer(t, inv)
    cos = np.cos(emb).T
    sin = np.sin(emb).T
    m1 = np.concatenate([cos, sin], axis=0).astype(np.float32)
    m2 = np.concatenate([-sin, cos], axis=0).astype(np.float32)
    A = np.zeros((128, 128), np.float32)
    Bm = np.zeros((128, 128), np.float32)
    for i in range(64):
        A[i, 2 * i] = 1.0
        A[64 + i, 2 * i] = 1.0
        Bm[i, 2 * i + 1] = 1.0
        Bm[64 + i, 2 * i + 1] = 1.0
    mskd = np.zeros((128, 2048), np.float32)
    sl = np.arange(128)
    ql = np.arange(128)
    for i in range(4):
        for qt in range(4):
            blockm = ((i * 128 + sl)[:, None] <= (qt * 128 + ql)[None, :])
            mskd[:, i * 512 + qt * 128:i * 512 + (qt + 1) * 128] = blockm
    return {
        "m1t": m1, "m2t": m2,
        "atm": np.ascontiguousarray(A.T).astype(BFNP),
        "btm": np.ascontiguousarray(Bm.T).astype(BFNP),
        "idb": np.eye(128, dtype=np.float32).astype(BFNP),
        "idf": np.eye(16, dtype=np.float32),
        "oneb": np.ones((128, 1), np.float32).astype(BFNP),
        "onef": np.ones((128, 1), np.float32),
        "onerow": np.ones((1, 128), np.float32),
        "halfrow": np.full((1, 128), 0.5, np.float32),
        "mskd": mskd.astype(BFNP),
        "cvals": np.repeat(np.array([[ESHIFT, LNEPS, QEPS, 0.0]], np.float32), 128, 0),
    }


def _stage_core(c, x, Wq, bq, Wk, bk, Wv, bv, Wo, bo, w1T, w2T, w3T, consts, T):
    b, h = c // 4, c % 4
    NTO = T // 4
    d = dict(consts)
    d["xT"] = np.ascontiguousarray(x[b].T)
    d["xTo"] = np.ascontiguousarray(x[b, h * NTO:(h + 1) * NTO].T)
    d["wq_s"] = np.ascontiguousarray(Wq[:, h * 512:(h + 1) * 512])
    d["wk_s"] = np.ascontiguousarray(Wk[:, h * 128:(h + 1) * 128])
    d["wv_s"] = np.ascontiguousarray(Wv[:, h * 128:(h + 1) * 128])
    d["bq4"] = np.ascontiguousarray(bq[h * 512:(h + 1) * 512].reshape(4, 128))
    d["bk1"] = np.ascontiguousarray(bk[h * 128:(h + 1) * 128].reshape(1, 128))
    d["bv1"] = np.ascontiguousarray(bv[h * 128:(h + 1) * 128].reshape(1, 128))
    d["wo"] = np.ascontiguousarray(Wo)
    d["bo16"] = np.ascontiguousarray(bo.reshape(16, 128))
    d["w1T"] = w1T
    d["w2T"] = w2T
    d["w3T"] = w3T
    ms = np.zeros((128, 4), np.float32)
    ms[:, h] = 1.0
    d["msel"] = ms
    return d


def stage_all(x, Wq, bq, Wk, bk, Wv, bv, Wo, bo, w1, w2, w3):
    T = x.shape[1]
    w1T = np.ascontiguousarray(np.asarray(w1, dtype=np.float32).T)
    w2T = np.ascontiguousarray(np.asarray(w2, dtype=np.float32).T)
    w3T = np.ascontiguousarray(np.asarray(w3, dtype=np.float32).T)
    consts = _constants(T)
    return [_stage_core(c, np.asarray(x, dtype=np.float32),
                        np.asarray(Wq, dtype=np.float32), np.asarray(bq, dtype=np.float32),
                        np.asarray(Wk, dtype=np.float32), np.asarray(bk, dtype=np.float32),
                        np.asarray(Wv, dtype=np.float32), np.asarray(bv, dtype=np.float32),
                        np.asarray(Wo, dtype=np.float32), np.asarray(bo, dtype=np.float32),
                        w1T, w2T, w3T, consts, T)
            for c in range(NCORE)]


def kernel(L0_attn_x, L0_attn_Wq, L0_attn_bq, L0_attn_Wk, L0_attn_bk,
           L0_attn_Wv, L0_attn_bv, L0_attn_Wo, L0_attn_bo,
           L0_ffn_w1, L0_ffn_w2, L0_ffn_w3):
    T = L0_attn_x.shape[1]
    HF = L0_ffn_w1.shape[0]
    in_maps = stage_all(L0_attn_x, L0_attn_Wq, L0_attn_bq, L0_attn_Wk, L0_attn_bk,
                        L0_attn_Wv, L0_attn_bv, L0_attn_Wo, L0_attn_bo,
                        L0_ffn_w1, L0_ffn_w2, L0_ffn_w3)
    nc = build_nc(T, HF)
    res = run_bass_kernel_spmd(nc, in_maps, core_ids=list(range(NCORE)))
    out = np.empty((2, T, D), np.float32)
    NTO = T // 4
    for c in range(NCORE):
        b, h = c // 4, c % 4
        out[b, h * NTO:(h + 1) * NTO, :] = np.asarray(res.results[c]["outT"]).T
    return out


# version 34
# speedup vs baseline: 1.0437x; 1.0074x over previous
import contextlib
import numpy as np
import ml_dtypes

import concourse.bass as bass
import concourse.mybir as mybir
from concourse.bass_utils import run_bass_kernel_spmd

F32 = mybir.dt.float32
BF16 = mybir.dt.bfloat16
AF = mybir.ActivationFunctionType
ALU = mybir.AluOpType
AX = mybir.AxisListType
BFNP = ml_dtypes.bfloat16

D = 2048
KD = D // 128
NCORE = 8
SCALE = float(np.float64(1.0) / np.sqrt(np.float64(128.0)))
ESHIFT = -20.0
LNEPS = 1e-5
QEPS = 1e-8


class Sem:
    def __init__(self, name):
        self.name = name
        self.v = 0
        self.h = None

    def inc(self, n=1):
        self.v += n
        return self.v


class Plan:
    def __init__(self):
        self.q = {"sync": [], "gpsimd": [], "tensor": [], "vector": [], "scalar": []}

    def wait(self, eng, sem, val):
        if val and val > 0:
            self.q[eng].append(lambda e, s=sem, v=val: e.wait_ge(s.h, v))

    def mm(self, out, lhsT, rhs, start, stop, sem=None):
        v = sem.inc(1) if sem is not None else None

        def f(e, out=out, lhsT=lhsT, rhs=rhs, start=start, stop=stop, sem=sem):
            r = e.matmul(out, lhsT, rhs, start=start, stop=stop)
            if sem is not None:
                r.then_inc(sem.h, 1)
        self.q["tensor"].append(f)
        return v

    def tr(self, out, in_, ident, sem=None):
        v = sem.inc(1) if sem is not None else None

        def f(e, out=out, in_=in_, ident=ident, sem=sem):
            r = e.transpose(out, in_, ident)
            if sem is not None:
                r.then_inc(sem.h, 1)
        self.q["tensor"].append(f)
        return v

    def act(self, out, in_, func, bias=None, scale=None, sem=None):
        v = sem.inc(1) if sem is not None else None

        def f(e, out=out, in_=in_, func=func, bias=bias, scale=scale, sem=sem):
            kw = {}
            if bias is not None:
                kw["bias"] = bias
            if scale is not None:
                kw["scale"] = scale
            r = e.activation(out, in_, func, **kw)
            if sem is not None:
                r.then_inc(sem.h, 1)
        self.q["scalar"].append(f)
        return v

    def vop(self, kind, sem=None, **kw):
        v = sem.inc(1) if sem is not None else None

        def f(e, kind=kind, kw=kw, sem=sem):
            if kind == "tt":
                r = e.tensor_tensor(kw["out"], kw["in0"], kw["in1"], kw["op"])
            elif kind == "ts":
                r = e.tensor_scalar(kw["out"], kw["in0"], kw["s1"], None,
                                    kw["op0"])
            elif kind == "ts2":
                r = e.tensor_scalar(kw["out"], kw["in0"], kw["s1"], kw["s2"],
                                    kw["op0"], kw["op1"])
            elif kind == "stt":
                r = e.scalar_tensor_tensor(kw["out"], kw["in0"], kw["scalar"],
                                           kw["in1"], kw["op0"], kw["op1"])
            elif kind == "red":
                r = e.reduce_sum(kw["out"], kw["in_"], axis=AX.X,
                                 apply_absolute_value=kw.get("absv", False))
            elif kind == "recip":
                r = e.reciprocal(kw["out"], kw["in_"])
            else:
                raise ValueError(kind)
            if sem is not None:
                r.then_inc(sem.h, 1)
        self.q["vector"].append(f)
        return v

    def pop_red(self, out, in_, absv, sem=None):
        v = sem.inc(1) if sem is not None else None

        def f(e, out=out, in_=in_, absv=absv, sem=sem):
            r = e.reduce_sum(out, in_, axis=AX.XYZWC, apply_absolute_value=absv)
            if sem is not None:
                r.then_inc(sem.h, 1)
        self.q["gpsimd"].append(f)
        return v

    def dma(self, eng, out, in_, sem):
        v = sem.inc(16)

        def f(e, out=out, in_=in_, sem=sem):
            e.dma_start(out=out, in_=in_).then_inc(sem.h, 16)
        self.q[eng].append(f)
        return v

    def coll(self, kind, op, groups, ins, outs, sem):
        v = sem.inc(1)

        def f(e, kind=kind, op=op, groups=groups, ins=ins, outs=outs, sem=sem):
            e.collective_compute(kind, op, replica_groups=groups, ins=ins,
                                 outs=outs).then_inc(sem.h, 1)
        self.q["gpsimd"].append(f)
        return v


def build_nc(T, HF):
    NBLK = T // 512
    NTO = T // 4
    NGRP = NTO // 512
    HB = HF // 128
    WMX = max(D, HF)
    assert T % 2048 == 0 and HF % 128 == 0

    nc = bass.Bass("TRN2", target_bir_lowering=False, debug=False)

    xT = nc.dram_tensor("xT", [D, T], F32, kind="ExternalInput")
    xTo = nc.dram_tensor("xTo", [D, NTO], F32, kind="ExternalInput")
    wq_s = nc.dram_tensor("wq_s", [D, 512], F32, kind="ExternalInput")
    wk_s = nc.dram_tensor("wk_s", [D, 128], F32, kind="ExternalInput")
    wv_s = nc.dram_tensor("wv_s", [D, 128], F32, kind="ExternalInput")
    bq4 = nc.dram_tensor("bq4", [4, 128], F32, kind="ExternalInput")
    bk1 = nc.dram_tensor("bk1", [1, 128], F32, kind="ExternalInput")
    bv1 = nc.dram_tensor("bv1", [1, 128], F32, kind="ExternalInput")
    wo_d = nc.dram_tensor("wo", [512, D], F32, kind="ExternalInput")
    bo16 = nc.dram_tensor("bo16", [16, 128], F32, kind="ExternalInput")
    w1T = nc.dram_tensor("w1T", [D, HF], F32, kind="ExternalInput")
    w2T = nc.dram_tensor("w2T", [D, HF], F32, kind="ExternalInput")
    w3T = nc.dram_tensor("w3T", [HF, D], F32, kind="ExternalInput")
    m1t = nc.dram_tensor("m1t", [128, T], F32, kind="ExternalInput")
    m2t = nc.dram_tensor("m2t", [128, T], F32, kind="ExternalInput")
    atm = nc.dram_tensor("atm", [128, 128], BF16, kind="ExternalInput")
    btm = nc.dram_tensor("btm", [128, 128], BF16, kind="ExternalInput")
    idb = nc.dram_tensor("idb", [128, 128], BF16, kind="ExternalInput")
    idf = nc.dram_tensor("idf", [16, 16], F32, kind="ExternalInput")
    oneb = nc.dram_tensor("oneb", [128, 1], BF16, kind="ExternalInput")
    onef = nc.dram_tensor("onef", [128, 1], F32, kind="ExternalInput")
    onerow = nc.dram_tensor("onerow", [1, 128], F32, kind="ExternalInput")
    halfrow = nc.dram_tensor("halfrow", [1, 128], F32, kind="ExternalInput")
    mskd = nc.dram_tensor("mskd", [128, 2048], BF16, kind="ExternalInput")
    msel = nc.dram_tensor("msel", [128, 4], F32, kind="ExternalInput")
    cvals = nc.dram_tensor("cvals", [128, 4], F32, kind="ExternalInput")

    outT = nc.dram_tensor("outT", [D, NTO], F32, kind="ExternalOutput")

    ag_in = nc.dram_tensor("ag_in", [128, T], BF16)
    ag_out = nc.dram_tensor("ag_out", [512, T], BF16)
    w1q_h = nc.dram_tensor("w1q_h", [128, HB * D], BF16)
    w2q_h = nc.dram_tensor("w2q_h", [128, HB * D], BF16)
    w3q_h = nc.dram_tensor("w3q_h", [128, 16 * HF], BF16)

    P = Plan()
    NW = 2 * HB + 16 * ((HB + 15) // 16)

    with contextlib.ExitStack() as G:
        sems = {}
        for nm in ["tsem", "vsem", "ssem", "spre", "spreg", "sx0", "sx1", "sx2",
                   "sy0", "sy1", "scc", "psem", "swg0", "swg1", "swg2", "swg3", "sqz0", "sqz1", "sqz2", "sqz3",
                   "sqw0", "sqw1", "sq0", "sq1", "sq2", "sqb0", "sqb1", "sqb2", "sw30", "sw31", "sxf0", "sxf1",
                   "sg0", "sg1", "sxb0", "sxb1", "sout0", "sout1"]:
            s = Sem(nm)
            s.h = G.enter_context(nc.semaphore(nm))
            sems[nm] = s
        ts, vs, ss = sems["tsem"], sems["vsem"], sems["ssem"]
        sx = [sems["sx0"], sems["sx1"], sems["sx2"]]
        swg = [sems[f"swg{i}"] for i in range(4)]
        sqz = [sems[f"sqz{i}"] for i in range(4)]
        psm = sems["psem"]
        sq = [sems["sq0"], sems["sq1"], sems["sq2"]]
        sqb = [sems["sqb0"], sems["sqb1"], sems["sqb2"]]
        sw3 = [sems["sw30"], sems["sw31"]]
        sxf = [sems["sxf0"], sems["sxf1"]]
        sg = [sems["sg0"], sems["sg1"]]
        sxb = [sems["sxb0"], sems["sxb1"]]
        spre, spreg = sems["spre"], sems["spreg"]
        scc = sems["scc"]
        sy = [sems["sy0"], sems["sy1"]]
        sqw = [sems["sqw0"], sems["sqw1"]]
        sout = [sems["sout0"], sems["sout1"]]

        ps01 = G.enter_context(nc.psum_tensor("ps01", [128, 1024], F32))
        ps23 = G.enter_context(nc.psum_tensor("ps23", [128, 1024], F32))
        ps45 = G.enter_context(nc.psum_tensor("ps45", [128, 1024], F32))
        ps6 = G.enter_context(nc.psum_tensor("ps6", [128, 512], F32))
        ps7 = G.enter_context(nc.psum_tensor("ps7", [128, 512], F32))
        ps4r = ps45[0:1, 0:512]
        ps5r = ps45[0:1, 512:1024]

        sb = lambda n, shp, dt: G.enter_context(nc.sbuf_tensor(n, shp, dt))
        wq_eff = sb("wq_eff", [128, KD * 128], BF16)
        wk_sb = sb("wk_sb", [128, KD * 128], BF16)
        wv_sb = sb("wv_sb", [128, KD * 128], BF16)
        atm_sb = sb("atm_sb", [128, 128], BF16)
        btm_sb = sb("btm_sb", [128, 128], BF16)
        idb_sb = sb("idb_sb", [128, 128], BF16)
        idf_sb = sb("idf_sb", [16, 16], F32)
        oneb_sb = sb("oneb_sb", [128, 1], BF16)
        onef_sb = sb("onef_sb", [128, 1], F32)
        onerow_sb = sb("onerow_sb", [1, 128], F32)
        halfrow_sb = sb("halfrow_sb", [1, 128], F32)
        msel_sb = sb("msel_sb", [128, 4], F32)
        cv_sb = sb("cv_sb", [128, 4], F32)
        bq_col = sb("bq_col", [128, 1], F32)
        bk_col = sb("bk_col", [128, 1], F32)
        bv_col = sb("bv_col", [128, 1], F32)
        bo_sb = sb("bo_sb", [128, 16], F32)
        csq = sb("csq", [1, 128], BF16)
        csk = sb("csk", [1, 128], BF16)
        csv = sb("csv", [1, 128], BF16)
        mrow = sb("mrow", [1, 512], F32)
        srow = sb("srow", [1, 512], F32)
        m2row = sb("m2row", [1, 512], F32)
        varrow = sb("varrow", [1, 512], F32)
        lnrow = m2row
        rrow = sb("rrow", [1, 512], F32)
        rmrow = sb("rmrow", [1, 512], BF16)
        recrow = sb("recrow", [1, 512], F32)
        gred = sb("gred", [128, NW], F32)
        gcol = sb("gcol", [128, 3], F32)
        ge_row = sb("ge_row", [1, 1], F32)
        bpc = [sb(f"bp{m}", [128, 1], F32) for m in range(3)]
        bnc = [sb(f"bn{m}", [128, 1], F32) for m in range(3)]

        with contextlib.ExitStack() as PR:
            wq_st = PR.enter_context(nc.sbuf_tensor("wq_st", [128, KD * 512], F32))
            bq4_sb = PR.enter_context(nc.sbuf_tensor("bq4_sb", [4, 128], F32))
            bk1_sb = PR.enter_context(nc.sbuf_tensor("bk1_sb", [1, 128], F32))
            bv1_sb = PR.enter_context(nc.sbuf_tensor("bv1_sb", [1, 128], F32))
            bo16_sb = PR.enter_context(nc.sbuf_tensor("bo16_sb", [16, 128], F32))
            tqa = PR.enter_context(nc.sbuf_tensor("tqa", [128, 128], F32))
            tqb = PR.enter_context(nc.sbuf_tensor("tqb", [128, 128], F32))
            PRE_N = 0
            for dst, src in [(wq_st.ap().rearrange("p (k i) -> p k i", k=KD), wq_s.ap().rearrange("(k p) i -> p k i", p=128)),
                             (atm_sb[:, :], atm[:, :]), (btm_sb[:, :], btm[:, :]),
                             (idb_sb[:, :], idb[:, :]), (idf_sb[:, :], idf[:, :]),
                             (oneb_sb[:, :], oneb[:, :]), (onef_sb[:, :], onef[:, :]),
                             (onerow_sb[:, :], onerow[:, :]), (halfrow_sb[:, :], halfrow[:, :]),
                             (msel_sb[:, :], msel[:, :]),
                             (cv_sb[:, :], cvals[:, :]),
                             (bq4_sb[:, :], bq4[:, :]), (bk1_sb[:, :], bk1[:, :]),
                             (bv1_sb[:, :], bv1[:, :]), (bo16_sb[:, :], bo16[:, :])]:
                PRE_N = P.dma("sync", dst, src, spre)
            PREG_N = P.dma("gpsimd", wk_sb.ap().rearrange("p (k i) -> p k i", k=KD),
                           wk_s.ap().rearrange("(k p) i -> p k i", p=128), spreg)
            PREG_N = P.dma("gpsimd", wv_sb.ap().rearrange("p (k i) -> p k i", k=KD),
                           wv_s.ap().rearrange("(k p) i -> p k i", p=128), spreg)

            P.wait("vector", spre, PRE_N)
            V_WQE = 0
            for k in range(KD):
                c = k * 512
                if k > 0:
                    P.wait("vector", vs, V_WQE)
                P.vop("tt", out=tqa[:, :], in0=wq_st[:, c:c + 128],
                      in1=wq_st[:, c + 128:c + 256], op=ALU.add)
                P.vop("tt", out=tqb[:, :], in0=wq_st[:, c + 256:c + 384],
                      in1=wq_st[:, c + 384:c + 512], op=ALU.add, sem=vs)
                P.wait("vector", vs, vs.v)
                V_WQE = P.vop("tt", out=wq_eff[:, k * 128:(k + 1) * 128],
                              in0=tqa[:, :], in1=tqb[:, :], op=ALU.add, sem=vs)

            P.wait("tensor", vs, V_WQE)
            P.wait("tensor", spre, PRE_N)
            P.wait("tensor", spreg, PREG_N)
            T_CSQ = 0
            for k in range(KD):
                T_CSQ = P.mm(ps45[0:1, 0:128], oneb_sb[:, :],
                             wq_eff[:, k * 128:(k + 1) * 128],
                             start=(k == 0), stop=(k == KD - 1),
                             sem=ts if k == KD - 1 else None)
            for k in range(KD):
                P.mm(ps6[0:1, 0:128], oneb_sb[:, :], wk_sb[:, k * 128:(k + 1) * 128],
                     start=(k == 0), stop=(k == KD - 1))
            T_CSKV = 0
            for k in range(KD):
                T_CSKV = P.mm(ps7[0:1, 0:128], oneb_sb[:, :],
                              wv_sb[:, k * 128:(k + 1) * 128],
                              start=(k == 0), stop=(k == KD - 1),
                              sem=ts if k == KD - 1 else None)
            P.wait("scalar", ts, T_CSQ)
            P.act(csq[:, :], ps45[0:1, 0:128], AF.Copy, scale=-1.0, sem=ss)
            P.wait("scalar", ts, T_CSKV)
            P.act(csk[:, :], ps6[0:1, 0:128], AF.Copy, scale=-1.0, sem=ss)
            SS_CS = P.act(csv[:, :], ps7[0:1, 0:128], AF.Copy, scale=-1.0, sem=ss)
            P.wait("tensor", ss, SS_CS)
            P.mm(ps45[:, 0:1], bq4_sb[:, :], onef_sb[0:4, 0:1], True, True)
            P.mm(ps45[:, 512:513], bk1_sb[:, :], onef_sb[0:1, 0:1], True, True)
            P.mm(ps6[:, 0:1], bv1_sb[:, :], onef_sb[0:1, 0:1], True, True)
            T_BIAS = P.tr(ps7[:, 0:16], bo16_sb[:, :], idf_sb[:, :], sem=ts)
            P.wait("scalar", ts, T_BIAS)
            P.act(bq_col[:, :], ps45[:, 0:1], AF.Copy, sem=ss)
            P.act(bk_col[:, :], ps45[:, 512:513], AF.Copy, sem=ss)
            P.act(bv_col[:, :], ps6[:, 0:1], AF.Copy, sem=ss)
            SS_PRO = P.act(bo_sb[:, :], ps7[:, 0:16], AF.Copy, sem=ss)

        stk_p1c = contextlib.ExitStack()
        qT = stk_p1c.enter_context(nc.sbuf_tensor("qT", [128, T], BF16))
        kT = stk_p1c.enter_context(nc.sbuf_tensor("kT", [128, T], BF16))
        vS = stk_p1c.enter_context(nc.sbuf_tensor("vS", [128, T], BF16))

        stk_p1a = contextlib.ExitStack()
        xb = [stk_p1a.enter_context(nc.sbuf_tensor(f"xb{p}", [128, KD * 512], BF16))
              for p in range(3)]
        xsqh = stk_p1a.enter_context(nc.sbuf_tensor("xsqh", [128, 8 * 512], BF16))
        m1s = [stk_p1a.enter_context(nc.sbuf_tensor(f"m1s{p}", [128, 512], F32))
               for p in range(3)]
        m2s = [stk_p1a.enter_context(nc.sbuf_tensor(f"m2s{p}", [128, 512], F32))
               for p in range(3)]
        tq_sb = stk_p1a.enter_context(nc.sbuf_tensor("tq_sb", [128, 512], F32))
        tk_sb = stk_p1a.enter_context(nc.sbuf_tensor("tk_sb", [128, 512], F32))
        tv_sb = stk_p1a.enter_context(nc.sbuf_tensor("tv_sb", [128, 512], F32))
        rbc_sb = stk_p1a.enter_context(nc.sbuf_tensor("rbc_sb", [128, 512], F32))
        t1_sb = rbc_sb
        t2_sb = tq_sb
        uq_sb = stk_p1a.enter_context(nc.sbuf_tensor("uq_sb", [128, 512], BF16))
        uk_sb = stk_p1a.enter_context(nc.sbuf_tensor("uk_sb", [128, 512], BF16))
        vr_sb = stk_p1a.enter_context(nc.sbuf_tensor("vr_sb", [128, 512], BF16))

        T_XB = [0] * NBLK
        T_MS = [0] * NBLK
        T_SK = {}
        T_SKL = [0] * NBLK
        T_RANK = [0] * NBLK
        SS_SROW = [0] * NBLK
        SS_RBC = [0] * NBLK
        SS_VSB = [0] * NBLK
        V_SQ = [0] * NBLK
        V_QEV = [0] * NBLK
        V_KEV = [0] * NBLK
        V_TV = [0] * NBLK
        V_QT2 = [0] * NBLK
        V_KT2 = [0] * NBLK
        V_KROT = [0] * NBLK

        for i in range(NBLK):
            p = i % 3
            c0 = i * 512
            if i >= 3:
                P.wait("gpsimd", ts, T_XB[i - 3])
                P.wait("gpsimd", vs, V_KROT[i - 3])
            else:
                P.wait("gpsimd", ss, SS_PRO)
            P.dma("gpsimd", xb[p].ap().rearrange("p (k t) -> p k t", k=KD),
                  xT[:, c0:c0 + 512].rearrange("(k p) t -> p k t", p=128), sx[p])
            P.dma("gpsimd", m1s[p][:, :], m1t[:, c0:c0 + 512], sx[p])
            SX_N = P.dma("gpsimd", m2s[p][:, :], m2t[:, c0:c0 + 512], sx[p])

            assert SX_N == 48 * (i // 3 + 1)
            if i == 0:
                P.wait("tensor", sx[p], SX_N)
                P.wait("tensor", ss, SS_PRO)
                for k in range(KD):
                    P.mm(ps4r, oneb_sb[:, :], xb[p][:, k * 512:(k + 1) * 512],
                         start=(k == 0), stop=(k == KD - 1))
            P.wait("tensor", sx[p], SX_N)
            if i >= 1:
                P.wait("tensor", vs, V_QEV[i - 1])
            for k in range(KD):
                P.mm(ps01[:, 0:512], wq_eff[:, k * 128:(k + 1) * 128],
                     xb[p][:, k * 512:(k + 1) * 512], start=(k == 0), stop=False)

            P.wait("vector", sx[p], SX_N)
            if i >= 1:
                P.wait("vector", ts, T_MS[i - 1])
            V_FA = 0
            for k in range(8):
                V_FA = P.vop("tt", out=xsqh[:, k * 512:(k + 1) * 512],
                             in0=xb[p][:, k * 512:(k + 1) * 512],
                             in1=xb[p][:, k * 512:(k + 1) * 512],
                             op=ALU.mult, sem=vs if k == 7 else None)
            P.wait("tensor", vs, V_FA)
            T_SA = 0
            for k in range(8):
                T_SA = P.mm(ps5r, oneb_sb[:, :], xsqh[:, k * 512:(k + 1) * 512],
                            start=(k == 0), stop=False,
                            sem=ts if k == 7 else None)

            if i >= 1:
                P.wait("tensor", vs, V_KEV[i - 1])
            for k in range(KD):
                P.mm(ps01[:, 512:1024], wk_sb[:, k * 128:(k + 1) * 128],
                     xb[p][:, k * 512:(k + 1) * 512], start=(k == 0), stop=False)

            P.wait("vector", ts, T_SA)
            V_FB = 0
            for k in range(8, KD):
                V_FB = P.vop("tt", out=xsqh[:, (k - 8) * 512:(k - 7) * 512],
                             in0=xb[p][:, k * 512:(k + 1) * 512],
                             in1=xb[p][:, k * 512:(k + 1) * 512],
                             op=ALU.mult, sem=vs if k == KD - 1 else None)
            P.wait("tensor", vs, V_FB)
            for k in range(8, KD):
                T_MS[i] = P.mm(ps5r, oneb_sb[:, :],
                               xsqh[:, (k - 8) * 512:(k - 7) * 512],
                               start=False, stop=(k == KD - 1),
                               sem=ts if k == KD - 1 else None)

            P.wait("scalar", ts, T_MS[i])
            P.act(mrow[:, :], ps4r, AF.Copy, scale=1.0 / D, sem=ss)
            SS_SROW[i] = P.act(srow[:, :], ps5r, AF.Copy, scale=1.0 / D, sem=ss)
            P.wait("vector", ss, SS_SROW[i])
            P.vop("tt", out=m2row[:, :], in0=mrow[:, :], in1=mrow[:, :],
                  op=ALU.mult, sem=vs)
            P.wait("vector", vs, vs.v)
            V_VAR = P.vop("tt", out=varrow[:, :], in0=srow[:, :], in1=m2row[:, :],
                          op=ALU.subtract, sem=vs)
            P.wait("scalar", vs, V_VAR)
            P.act(lnrow[:, :], varrow[:, :], AF.Ln, bias=cv_sb[0:1, 1:2], sem=ss)
            P.wait("scalar", ss, ss.v)
            SS_R = P.act(rrow[:, :], lnrow[:, :], AF.Exp, scale=-0.5, sem=ss)
            P.wait("vector", ss, SS_R)
            V_RM = P.vop("tt", out=rmrow[:, :], in0=rrow[:, :], in1=mrow[:, :],
                         op=ALU.mult, sem=vs)

            if i >= 1:
                P.wait("tensor", vs, V_TV[i - 1])
            for k in range(KD):
                T_XB[i] = P.mm(ps23[:, 0:512], wv_sb[:, k * 128:(k + 1) * 128],
                               xb[p][:, k * 512:(k + 1) * 512],
                               start=(k == 0), stop=False,
                               sem=ts if k == KD - 1 else None)
            if i >= 1:
                P.wait("tensor", ss, SS_RBC[i - 1])
            P.wait("tensor", ss, SS_R)
            P.mm(ps7[:, :], onerow_sb[:, :], rrow[:, :], True, True)
            P.wait("tensor", vs, V_RM)
            P.mm(ps01[:, 0:512], csq[:, :], rmrow[:, :], start=False, stop=True)
            P.mm(ps01[:, 512:1024], csk[:, :], rmrow[:, :], start=False, stop=True)
            T_RANK[i] = P.mm(ps23[:, 0:512], csv[:, :], rmrow[:, :],
                             start=False, stop=True, sem=ts)

            if i + 1 < NBLK:
                p2 = (i + 1) % 3
                P.wait("tensor", sx[p2], 48 * ((i + 1) // 3 + 1))
                P.wait("tensor", ss, SS_SROW[i])
                for k in range(KD):
                    P.mm(ps4r, oneb_sb[:, :], xb[p2][:, k * 512:(k + 1) * 512],
                         start=(k == 0), stop=(k == KD - 1))

            P.wait("scalar", ts, T_RANK[i])
            SS_RBC[i] = P.act(rbc_sb[:, :], ps7[:, :], AF.Copy, sem=ss)

            P.wait("vector", ts, T_RANK[i])
            P.wait("vector", ss, SS_RBC[i])
            V_QEV[i] = P.vop("tt", out=tq_sb[:, :], in0=ps01[:, 0:512],
                             in1=rbc_sb[:, :], op=ALU.mult, sem=vs)
            V_KEV[i] = P.vop("tt", out=tk_sb[:, :], in0=ps01[:, 512:1024],
                             in1=rbc_sb[:, :], op=ALU.mult, sem=vs)
            V_TV[i] = P.vop("tt", out=tv_sb[:, :], in0=ps23[:, 0:512],
                            in1=rbc_sb[:, :], op=ALU.mult, sem=vs)
            P.wait("scalar", vs, V_QEV[i])
            SS_UQ = P.act(uq_sb[:, :], tq_sb[:, :], AF.Identity, bias=bq_col[:, 0:1], sem=ss)
            P.wait("scalar", vs, V_KEV[i])
            SS_UK = P.act(uk_sb[:, :], tk_sb[:, :], AF.Identity, bias=bk_col[:, 0:1], sem=ss)
            P.wait("scalar", vs, V_TV[i])
            SS_VR = P.act(vr_sb[:, :], tv_sb[:, :], AF.Identity, bias=bv_col[:, 0:1], sem=ss)

            P.wait("tensor", ss, SS_UQ)
            if i >= 1:
                P.wait("tensor", vs, V_KT2[i - 1])
                P.wait("tensor", ss, SS_VSB[i - 1])
            P.mm(ps23[:, 512:1024], atm_sb[:, :], uq_sb[:, :], True, True)
            T_QAB = P.mm(ps6[:, :], btm_sb[:, :], uq_sb[:, :], True, True, sem=ts)
            P.wait("vector", ts, T_QAB)
            P.wait("vector", vs, V_TV[i])
            P.vop("tt", out=t1_sb[:, :], in0=ps23[:, 512:1024], in1=m1s[p][:, :],
                  op=ALU.mult, sem=vs)
            V_QT2[i] = P.vop("tt", out=t2_sb[:, :], in0=ps6[:, :], in1=m2s[p][:, :],
                             op=ALU.mult, sem=vs)
            P.wait("vector", vs, V_QT2[i])
            V_QROT = P.vop("tt", out=qT[:, c0:c0 + 512], in0=t1_sb[:, :],
                           in1=t2_sb[:, :], op=ALU.add, sem=vs)

            P.wait("tensor", ss, SS_UK)
            P.wait("tensor", vs, V_QT2[i])
            P.mm(ps23[:, 512:1024], atm_sb[:, :], uk_sb[:, :], True, True)
            T_KAB = P.mm(ps6[:, :], btm_sb[:, :], uk_sb[:, :], True, True, sem=ts)
            P.wait("vector", ts, T_KAB)
            P.wait("vector", vs, V_QROT)
            P.vop("tt", out=t1_sb[:, :], in0=ps23[:, 512:1024], in1=m1s[p][:, :],
                  op=ALU.mult, sem=vs)
            V_KT2[i] = P.vop("tt", out=t2_sb[:, :], in0=ps6[:, :], in1=m2s[p][:, :],
                             op=ALU.mult, sem=vs)
            P.wait("vector", vs, V_KT2[i])
            V_KROT[i] = P.vop("tt", out=kT[:, c0:c0 + 512], in0=t1_sb[:, :],
                              in1=t2_sb[:, :], op=ALU.add, sem=vs)

            P.wait("tensor", ss, SS_VR)
            P.wait("tensor", vs, V_KT2[i])
            ps6b = ps6.ap().bitcast(BF16)
            T_VTR = 0
            for j in range(4):
                T_VTR = P.tr(ps6b[:, j * 128:(j + 1) * 128],
                             vr_sb[:, j * 128:(j + 1) * 128], idb_sb[:, :],
                             sem=ts if j == 3 else None)
            P.wait("scalar", ts, T_VTR)
            for j in range(4):
                SS_VSB[i] = P.act(vS[:, (4 * i + j) * 128:(4 * i + j + 1) * 128],
                                  ps6b[:, j * 128:(j + 1) * 128], AF.Copy,
                                  sem=ss if j == 3 else None)

        stk_p1a.close()

        stk_p1b = contextlib.ExitStack()
        attT = [stk_p1b.enter_context(nc.sbuf_tensor(f"attT{p}", [128, 1024], BF16))
                for p in range(6)]
        attTm = stk_p1b.enter_context(nc.sbuf_tensor("attTm", [128, 1024], BF16))
        ybuf = [stk_p1b.enter_context(nc.sbuf_tensor(f"ybuf{p}", [128, 512], BF16))
                for p in range(2)]
        bcc_sb = stk_p1b.enter_context(nc.sbuf_tensor("bcc_sb", [128, 512], F32))
        mskd_sb = stk_p1b.enter_context(nc.sbuf_tensor("mskd_sb", [128, 2048], BF16))
        wstr = [stk_p1b.enter_context(nc.sbuf_tensor(f"wstr{p}", [128, 2048], F32))
                for p in range(4)]
        sgA = [stk_p1b.enter_context(nc.sbuf_tensor(f"sgA{p}", [128, 2048], BF16))
               for p in range(2)]
        sgB = [stk_p1b.enter_context(nc.sbuf_tensor(f"sgB{p}", [128, 2048], BF16))
               for p in range(2)]
        qbk = [stk_p1b.enter_context(nc.sbuf_tensor(f"qbk{p}", [128, 2048], BF16))
               for p in range(2)]

        P_BAR_T = ts.v
        P_BAR_V = vs.v
        P_BAR_S = ss.v
        P.wait("sync", ts, P_BAR_T)
        P.wait("sync", vs, P_BAR_V)
        P.wait("sync", ss, P_BAR_S)
        PRE_MSK = P.dma("sync", mskd_sb[:, :], mskd[:, :], spre)
        P.wait("vector", spre, PRE_MSK)

        units = []
        for hb in range(HB):
            units.append((0, w1T[:, hb * 128:(hb + 1) * 128]
                          .rearrange("(k p) h -> p k h", p=128),
                          w1q_h[:, hb * D:(hb + 1) * D], D))
        for hb in range(HB):
            units.append((1, w2T[:, hb * 128:(hb + 1) * 128]
                          .rearrange("(k p) h -> p k h", p=128),
                          w2q_h[:, hb * D:(hb + 1) * D], D))
        for dt in range(16):
            w3v = w3T[:, dt * 128:(dt + 1) * 128].rearrange("(k p) d_ -> p k d_", p=128)
            for k0 in range(0, HB, 16):
                k1 = min(k0 + 16, HB)
                units.append((2, w3v[:, k0:k1, :],
                              w3q_h[:, dt * HF + k0 * 128:dt * HF + k1 * 128],
                              (k1 - k0) * 128))
        assert len(units) == NW

        G_GRED = [0] * NW
        GSWN = [0] * NW
        g_state = {"iss": 0, "drn": 0}

        def gamma_issue_one(j):
            pj = j % 4
            if j >= 4:
                P.wait("sync", vs, G_GRED[j - 4])
            m, gsrc, _, gw = units[j]
            GSWN[j] = P.dma("sync", wstr[pj][:, 0:gw]
                            .rearrange("p (k h) -> p k h", h=128), gsrc, swg[pj])

        def gamma_drain_one(j):
            if g_state["drn"] != j:
                return
            g_state["drn"] = j + 1
            pj = j % 4
            m, gsrc, _, gw = units[j]
            P.wait("vector", swg[pj], GSWN[j])
            G_GRED[j] = P.vop("red", out=gred[:, j:j + 1], in_=wstr[pj][:, 0:gw],
                              absv=True, sem=vs)

        def emit_gamma(n, final=False):
            for _ in range(n):
                j = g_state["iss"]
                if j >= NW:
                    break
                gamma_issue_one(j)
                g_state["iss"] = j + 1
                if j >= 3:
                    gamma_drain_one(j - 3)
            if final:
                for k in range(g_state["drn"], NW):
                    gamma_drain_one(k)

        V_SG = [0] * NW
        V_QB = [0] * NW
        QSQN = [0] * NW
        q_state = {"iss": 0, "drn": 0, "stored": 0, "V_GF": 0, "SS_GDONE": 0}

        def qz_issue_one(u):
            pu = u % 4
            m, src, dst, wdt = units[u]
            if u < 4:
                P.wait("sync", vs, G_GRED[NW - 1])
            else:
                P.wait("sync", vs, V_SG[u - 4])
            QSQN[u] = P.dma("sync", wstr[pu][:, 0:wdt]
                            .rearrange("p (k h) -> p k h", h=128), src, sqz[pu])

        def qz_drain_one(u):
            if q_state["drn"] != u:
                return
            q_state["drn"] = u + 1
            pu = u % 4
            ps2 = u & 1
            m, src, dst, wdt = units[u]
            P.wait("vector", sqz[pu], QSQN[u])
            if u == 0:
                P.wait("vector", ss, q_state["SS_GDONE"])
            if u >= 2:
                P.wait("vector", vs, V_QB[u - 2])
            P.vop("ts2", out=sgA[ps2][:, 0:wdt], in0=wstr[pu][:, 0:wdt],
                  s1=bpc[m][:, 0:1], s2=0.0, op0=ALU.add, op1=ALU.is_ge,
                  sem=vs)
            V_SG[u] = P.vop("ts2", out=sgB[ps2][:, 0:wdt], in0=wstr[pu][:, 0:wdt],
                            s1=bnc[m][:, 0:1], s2=0.0, op0=ALU.add,
                            op1=ALU.is_ge, sem=vs)
            P.wait("vector", vs, V_SG[u])
            if u >= 2:
                P.wait("vector", sqw[ps2], 16 * (u // 2))
            V_QB[u] = P.vop("stt", out=qbk[ps2][:, 0:wdt], in0=sgA[ps2][:, 0:wdt],
                            scalar=-1.0, in1=sgB[ps2][:, 0:wdt],
                            op0=ALU.add, op1=ALU.add, sem=vs)

        def qz_store_one(u):
            if q_state["stored"] != u:
                return
            q_state["stored"] = u + 1
            m, src, dst, wdt = units[u]
            P.wait("sync", vs, V_QB[u])
            P.dma("sync", dst, qbk[u & 1][:, 0:wdt], sqw[u & 1])

        def emit_qz(n, final=False):
            for _ in range(n):
                u = q_state["iss"]
                if u >= NW:
                    break
                qz_issue_one(u)
                q_state["iss"] = u + 1
                if u >= 3:
                    qz_drain_one(u - 3)
                    qz_store_one(u - 3)
            if final:
                for k in range(q_state["drn"], NW):
                    qz_drain_one(k)
                for k in range(q_state["stored"], NW):
                    qz_store_one(k)

        wides = [ps01, ps23, ps45]
        T_AV = {}
        T_SC = {}
        SS_EXP = {}
        V_MSK = {}
        V_Y = [0] * NBLK
        V_REC = [0] * NBLK
        SS_BCC = [0] * NBLK
        V_MSK_LAST = 0
        ps7_free = {"ss": P_BAR_S}
        FINB = max(1, min(NBLK - 2, (NBLK * 45) // 100))
        npre = sum(2 * (b + 1) for b in range(FINB))
        GPRATE = max(1, npre // (NW + 1))
        QHALF = (NW + 3 * (NBLK - 1 - FINB) - 1) // (3 * (NBLK - 1 - FINB))

        pairs = []
        for b in range(NBLK):
            npair = 2 * (b + 1)
            for pp in range(npair):
                pairs.append((b, pp, npair, b * 512))
        NP = len(pairs)

        def emit_scores(gp):
            b, pp, npair, q0 = pairs[gp]
            W = wides[gp % 3]
            if gp < 3:
                P.wait("tensor", vs, P_BAR_V)
                P.wait("tensor", ss, P_BAR_S)
            else:
                P.wait("tensor", ss, SS_EXP[gp - 3])
            st = 2 * pp
            P.mm(W[:, 0:512], kT[:, st * 128:(st + 1) * 128], qT[:, q0:q0 + 512],
                 True, True)
            T_SC[gp] = P.mm(W[:, 512:1024], kT[:, (st + 1) * 128:(st + 2) * 128],
                            qT[:, q0:q0 + 512], True, True, sem=ts)

        for gp in range(min(3, NP)):
            emit_scores(gp)

        for gp in range(NP):
            b, pp, npair, q0 = pairs[gp]
            W = wides[gp % 3]
            At = attT[gp % 6]
            masked = pp >= npair - 2
            st = 2 * pp
            P.wait("scalar", ts, T_SC[gp])
            if gp >= 6:
                P.wait("scalar", ts, T_AV[gp - 6])
            if masked and pp == npair - 1:
                P.wait("scalar", vs, V_MSK_LAST)
            tgt = attTm if masked else At
            SS_EXP[gp] = P.act(tgt[:, :], W[:, :], AF.Exp,
                               bias=cv_sb[:, 0:1], scale=SCALE, sem=ss)
            if masked:
                slot0 = st - 4 * b
                P.wait("vector", ss, SS_EXP[gp])
                P.vop("tt", out=At[:, 0:512], in0=attTm[:, 0:512],
                      in1=mskd_sb[:, slot0 * 512:(slot0 + 1) * 512],
                      op=ALU.mult, sem=vs)
                V_MSK_LAST = P.vop("tt", out=At[:, 512:1024],
                                   in0=attTm[:, 512:1024],
                                   in1=mskd_sb[:, (slot0 + 1) * 512:(slot0 + 2) * 512],
                                   op=ALU.mult, sem=vs)
            if masked:
                P.wait("tensor", vs, V_MSK_LAST)
            else:
                P.wait("tensor", ss, SS_EXP[gp])
            if pp == 0:
                if b >= 1:
                    P.wait("tensor", vs, V_Y[b - 1])
                P.wait("tensor", ss, ps7_free["ss"])
            P.mm(ps6[:, :], vS[:, st * 128:(st + 1) * 128], At[:, 0:512],
                 start=(pp == 0), stop=False)
            P.mm(ps6[:, :], vS[:, (st + 1) * 128:(st + 2) * 128], At[:, 512:1024],
                 start=False, stop=(pp == npair - 1))
            P.mm(ps7[0:1, :], oneb_sb[:, :], At[:, 0:512],
                 start=(pp == 0), stop=False)
            T_AV[gp] = P.mm(ps7[0:1, :], oneb_sb[:, :], At[:, 512:1024],
                            start=False, stop=(pp == npair - 1), sem=ts)
            if gp + 3 < NP:
                emit_scores(gp + 3)
            if b > FINB and (pp == 1 or pp == npair // 3 or pp == (2 * npair) // 3):
                emit_qz(QHALF)
            elif b < FINB and gp % GPRATE == 0:
                emit_gamma(1)
            if pp != npair - 1:
                continue
            P.wait("vector", ts, T_AV[gp])
            V_REC[b] = P.vop("recip", out=recrow[:, :], in_=ps7[0:1, :], sem=vs)
            P.wait("tensor", vs, V_REC[b])
            T_BC = P.mm(ps7[:, :], onerow_sb[:, :], recrow[:, :], True, True, sem=ts)
            P.wait("scalar", ts, T_BC)
            SS_BCC[b] = P.act(bcc_sb[:, :], ps7[:, :], AF.Copy, sem=ss)
            ps7_free["ss"] = SS_BCC[b]
            P.wait("vector", ss, SS_BCC[b])
            if b >= 2:
                P.wait("vector", sy[b & 1], 16 * (b // 2))
            V_Y[b] = P.vop("tt", out=ybuf[b & 1][:, :], in0=ps6[:, :],
                           in1=bcc_sb[:, :], op=ALU.mult, sem=vs)
            P.wait("gpsimd", vs, V_Y[b])
            P.dma("gpsimd", ag_in[:, q0:q0 + 512], ybuf[b & 1][:, :], sy[b & 1])

            if b == FINB:
                emit_gamma(NW, final=True)
                P.wait("vector", vs, G_GRED[NW - 1])
                P.vop("red", out=gcol[:, 0:1], in_=gred[:, 0:HB])
                P.vop("red", out=gcol[:, 1:2], in_=gred[:, HB:2 * HB])
                V_GF = P.vop("red", out=gcol[:, 2:3], in_=gred[:, 2 * HB:NW],
                             sem=vs)
                q_state["V_GF"] = V_GF
                SS_GDONE = 0
                for m in range(3):
                    P.wait("tensor", vs, V_GF)
                    P.wait("tensor", ss, ps7_free["ss"] if m == 0 else SS_GDONE)
                    T_G1 = P.mm(ps7[0:1, 0:1], gcol[:, m:m + 1], onef_sb[:, :],
                                True, True, sem=ts)
                    P.wait("scalar", ts, T_G1)
                    SS_GE = P.act(ge_row[:, :], ps7[0:1, 0:1], AF.Identity,
                                  scale=1.0 / float(HF * D), bias=cv_sb[0:1, 2:3],
                                  sem=ss)
                    P.wait("tensor", ss, SS_GE)
                    T_G2 = P.mm(ps7[:, 0:1], onerow_sb[:, :], ge_row[:, :],
                                True, True, sem=ts)
                    P.wait("scalar", ts, T_G2)
                    P.act(bpc[m][:, :], ps7[:, 0:1], AF.Copy, scale=0.5, sem=ss)
                    SS_GDONE = P.act(bnc[m][:, :], ps7[:, 0:1], AF.Copy,
                                     scale=-0.5, sem=ss)
                q_state["SS_GDONE"] = SS_GDONE
                ps7_free["ss"] = SS_GDONE
        emit_qz(NW, final=True)

        P.wait("gpsimd", sy[0], sy[0].v)
        P.wait("gpsimd", sy[1], sy[1].v)
        CC_N = P.coll("AllGather", ALU.bypass, [[0, 1, 2, 3], [4, 5, 6, 7]],
                      [ag_in.ap().opt()], [ag_out.ap().opt()], scc)

        TS_QZ = ts.v
        VS_QZ = vs.v
        SS_QZ = ss.v
        stk_p1b.close()
        stk_p1c.close()

        stk_p2 = contextlib.ExitStack()
        p2 = lambda n, shp, dt: stk_p2.enter_context(nc.sbuf_tensor(n, shp, dt))
        xbo = p2("xbo", [128, KD * 512], BF16)
        cmb = [p2(f"cmb{c}", [128, 512], BF16) for c in range(4)]
        yTg = [p2(f"yTg{q}", [128, 4 * 512], BF16) for q in range(2)]
        gld = p2("gld", [128, 3 * 512], BF16)
        xq2 = [p2(f"xq2{q}", [128, 512], BF16) for q in range(2)]
        wo_sb = p2("wo_sb", [128, 4 * D], BF16)
        cs1 = p2("cs1", [1, HF], BF16)
        cs2 = p2("cs2", [1, HF], BF16)
        hid = p2("hid", [128, HB * 512], BF16)
        w1qb = [p2(f"w1qb{q}", [128, D], BF16) for q in range(3)]
        w2qb = [p2(f"w2qb{q}", [128, D], BF16) for q in range(3)]
        w3qb = [p2(f"w3qb{q}", [128, HF], BF16) for q in range(2)]
        xf32 = [p2(f"xf32{q}", [128, 512], F32) for q in range(2)]
        x1t = p2("x1t", [128, 512], F32)
        x2t = p2("x2t", [128, 512], F32)
        sig_sb = p2("sig_sb", [128, 512], BF16)
        rbc2 = p2("rbc2", [128, 512], F32)
        osb = [p2(f"osb{q}", [128, 512], F32) for q in range(2)]


        T_GRP = [0] * NGRP
        V_OUT = {}
        SS_RBC2 = [0] * NGRP
        SS_SIG_LAST = 0
        T_X12L = [0] * NGRP
        V_CMBL = [0] * NGRP
        T_S2KL = [0] * NGRP
        SS_SR2L = [0] * NGRP
        V_HIDL = [0] * NGRP

        for g in range(NGRP):
            pg = g & 1
            gc = g * 512
            if g >= 1:
                P.wait("gpsimd", ts, T_X12L[g - 1])
                P.wait("gpsimd", vs, V_CMBL[g - 1])
            SXB_N = P.dma("gpsimd", xbo.ap().rearrange("p (k t) -> p k t", k=KD),
                  xTo[:, gc:gc + 512].rearrange("(k p) t -> p k t", p=128), sxb[pg])
            if g == 0:
                P.wait("gpsimd", scc, CC_N)
            P.wait("tensor", sxb[pg], SXB_N)
            if g == 0:
                P.wait("tensor", ss, SS_EXP[NP - 1])
            else:
                P.wait("tensor", ss, SS_SR2L[g - 1])
            for k in range(KD):
                P.mm(ps4r, oneb_sb[:, :], xbo[:, k * 512:(k + 1) * 512],
                     start=(k == 0), stop=(k == KD - 1))
            P.wait("vector", sxb[pg], SXB_N)
            T_S2K = [0] * KD
            T_MS2 = 0
            for k in range(KD):
                if k >= 2:
                    P.wait("vector", ts, T_S2K[k - 2])
                elif g >= 1:
                    P.wait("vector", ts, T_S2KL[g - 1])
                vsq = P.vop("tt", out=xq2[k & 1][:, :],
                            in0=xbo[:, k * 512:(k + 1) * 512],
                            in1=xbo[:, k * 512:(k + 1) * 512],
                            op=ALU.mult, sem=vs)
                P.wait("tensor", vs, vsq)
                T_MS2 = P.mm(ps5r, oneb_sb[:, :], xq2[k & 1][:, :],
                             start=(k == 0), stop=(k == KD - 1), sem=ts)
                T_S2K[k] = T_MS2
            T_S2KL[g] = T_S2K[KD - 1]
            P.wait("scalar", ts, T_MS2)
            P.act(mrow[:, :], ps4r, AF.Copy, scale=1.0 / D, sem=ss)
            SS_SR2 = P.act(srow[:, :], ps5r, AF.Copy, scale=1.0 / D, sem=ss)
            SS_SR2L[g] = SS_SR2
            P.wait("vector", ss, SS_SR2)
            P.vop("tt", out=m2row[:, :], in0=mrow[:, :], in1=mrow[:, :],
                  op=ALU.mult, sem=vs)
            P.wait("vector", vs, vs.v)
            V_VAR2 = P.vop("tt", out=varrow[:, :], in0=srow[:, :], in1=m2row[:, :],
                           op=ALU.subtract, sem=vs)
            P.wait("scalar", vs, V_VAR2)
            P.act(lnrow[:, :], varrow[:, :], AF.Ln, bias=cv_sb[0:1, 1:2], sem=ss)
            P.wait("scalar", ss, ss.v)
            SS_R2 = P.act(rrow[:, :], lnrow[:, :], AF.Exp, scale=-0.5, sem=ss)
            P.wait("vector", ss, SS_R2)
            V_RM2 = P.vop("tt", out=rmrow[:, :], in0=rrow[:, :], in1=mrow[:, :],
                          op=ALU.mult, sem=vs)
            P.wait("tensor", ss, SS_R2)
            if g >= 1:
                P.wait("tensor", ss, SS_RBC2[g - 1])
                P.wait("tensor", vs, V_OUT[(g - 1, 15)])
            T_BC2 = P.mm(ps7[:, :], onerow_sb[:, :], rrow[:, :], True, True, sem=ts)
            P.wait("scalar", ts, T_BC2)
            SS_RBC2[g] = P.act(rbc2[:, :], ps7[:, :], AF.Copy, sem=ss)

            V_CMB = 0
            SG_N = 0
            for rt in range(4):
                if rt >= 1:
                    P.wait("gpsimd", vs, V_CMB)
                for c in range(4):
                    SG_N = P.dma("gpsimd", cmb[c][:, :],
                                 ag_out[rt * 128:(rt + 1) * 128,
                                        c * NTO + gc:c * NTO + gc + 512], sg[pg])
                P.wait("vector", sg[pg], SG_N)
                if rt == 0 and g == 0:
                    P.wait("vector", vs, VS_QZ)
                if rt == 0 and g >= 2:
                    P.wait("vector", ts, T_GRP[g - 2])
                if rt >= 1:
                    P.wait("vector", vs, V_CMB)
                P.vop("ts", out=gld[:, 0:512], in0=cmb[0][:, :],
                      s1=msel_sb[:, 0:1], op0=ALU.mult, sem=vs)
                P.wait("vector", vs, vs.v)
                P.vop("stt", out=gld[:, 512:1024], in0=cmb[1][:, :],
                      scalar=msel_sb[:, 1:2], in1=gld[:, 0:512],
                      op0=ALU.mult, op1=ALU.add, sem=vs)
                P.wait("vector", vs, vs.v)
                P.vop("stt", out=gld[:, 1024:1536], in0=cmb[2][:, :],
                      scalar=msel_sb[:, 2:3], in1=gld[:, 512:1024],
                      op0=ALU.mult, op1=ALU.add, sem=vs)
                P.wait("vector", vs, vs.v)
                V_CMB = P.vop("stt", out=yTg[pg][:, rt * 512:(rt + 1) * 512],
                              in0=cmb[3][:, :], scalar=msel_sb[:, 3:4],
                              in1=gld[:, 1024:1536], op0=ALU.mult, op1=ALU.add,
                              sem=vs)
            V_CMBL[g] = V_CMB
            if g == 0:
                P.wait("gpsimd", ts, TS_QZ)
                P.wait("gpsimd", vs, VS_QZ)
                P.wait("gpsimd", ss, SS_QZ)
                P.wait("gpsimd", sqw[0], sqw[0].v)
                P.wait("gpsimd", sqw[1], sqw[1].v)
                PREG2 = P.dma("gpsimd", wo_sb.ap().rearrange("p (k n) -> p k n", k=4),
                              wo_d.ap().rearrange("(k p) n -> p k n", p=128), spreg)

            V_HID = [0] * HB
            T_X12 = [0] * HB
            SS_CS2 = [0] * HB
            for hb in range(HB):
                ph = hb % 3
                psA = ps01 if (hb & 1) == 0 else ps23
                if hb >= 3:
                    P.wait("sync", ts, T_X12[hb - 3])
                elif g >= 1:
                    P.wait("sync", ts, T_GRP[g - 1])
                else:
                    P.wait("sync", sqw[0], sqw[0].v)
                    P.wait("sync", sqw[1], sqw[1].v)
                    P.wait("sync", ts, TS_QZ)
                    P.wait("sync", vs, VS_QZ)
                sqn = P.dma("sync", w1qb[ph][:, :], w1q_h[:, hb * D:(hb + 1) * D],
                            sq[ph])
                sqbn = P.dma("sync", w2qb[ph][:, :], w2q_h[:, hb * D:(hb + 1) * D],
                             sqb[ph])
                P.wait("tensor", sq[ph], sqn)
                P.wait("tensor", sqb[ph], sqbn)
                if hb >= 2:
                    P.wait("tensor", vs, V_HID[hb - 2])
                elif g >= 1:
                    P.wait("tensor", vs, V_HIDL[g - 1])
                if hb == 0:
                    P.wait("tensor", vs, V_RM2)
                    P.wait("tensor", vs, V_CMB)
                if g == 0:
                    if hb >= 1:
                        P.wait("tensor", ss, SS_CS2[hb - 1])
                    for k in range(KD):
                        P.mm(ps45[0:1, 0:128], oneb_sb[:, :],
                             w1qb[ph][:, k * 128:(k + 1) * 128],
                             start=(k == 0), stop=(k == KD - 1))
                    for k in range(KD):
                        T_CSC = P.mm(ps45[0:1, 512:640], oneb_sb[:, :],
                                     w2qb[ph][:, k * 128:(k + 1) * 128],
                                     start=(k == 0), stop=(k == KD - 1),
                                     sem=ts if k == KD - 1 else None)
                    P.wait("scalar", ts, T_CSC)
                    P.act(cs1[:, hb * 128:(hb + 1) * 128], ps45[0:1, 0:128],
                          AF.Copy, scale=-1.0, sem=ss)
                    SS_CS2[hb] = P.act(cs2[:, hb * 128:(hb + 1) * 128],
                                       ps45[0:1, 512:640], AF.Copy, scale=-1.0,
                                       sem=ss)
                for k in range(KD):
                    P.mm(psA[:, 0:512], w1qb[ph][:, k * 128:(k + 1) * 128],
                         xbo[:, k * 512:(k + 1) * 512], start=(k == 0), stop=False)
                if g == 0:
                    P.wait("tensor", ss, SS_CS2[hb])
                P.mm(psA[:, 0:512], cs1[:, hb * 128:(hb + 1) * 128], rmrow[:, :],
                     start=False, stop=True)
                for k in range(KD):
                    P.mm(psA[:, 512:1024], w2qb[ph][:, k * 128:(k + 1) * 128],
                         xbo[:, k * 512:(k + 1) * 512], start=(k == 0), stop=False)
                T_X12[hb] = P.mm(psA[:, 512:1024], cs2[:, hb * 128:(hb + 1) * 128],
                                 rmrow[:, :], start=False, stop=True, sem=ts)
                P.wait("vector", ts, T_X12[hb])
                if g == 0 and hb == 0:
                    P.wait("vector", ss, SS_RBC2[g])
                P.wait("vector", ss, SS_SIG_LAST)
                if hb >= 1:
                    P.wait("vector", vs, V_HID[hb - 1])
                P.vop("tt", out=x2t[:, :], in0=psA[:, 512:1024], in1=rbc2[:, :],
                      op=ALU.mult, sem=vs)
                V_X1 = P.vop("tt", out=x1t[:, :], in0=psA[:, 0:512], in1=rbc2[:, :],
                             op=ALU.mult, sem=vs)
                P.wait("scalar", vs, V_X1)
                SS_SIG_LAST = P.act(sig_sb[:, :], x1t[:, :], AF.Sigmoid, sem=ss)
                P.wait("vector", ss, SS_SIG_LAST)
                V_HID[hb] = P.vop("tt", out=hid[:, hb * 512:(hb + 1) * 512],
                                  in0=x2t[:, :], in1=sig_sb[:, :],
                                  op=ALU.mult, sem=vs)
            T_X12L[g] = T_X12[HB - 1]
            V_HIDL[g] = V_HID[HB - 1]

            T_OUTL = [0] * 16
            pend_store = [None]
            for dt in range(16):
                pw = dt & 1
                psO = ps6 if pw == 0 else ps7
                if dt >= 2:
                    P.wait("sync", ts, T_OUTL[dt - 2])
                elif g >= 1:
                    P.wait("sync", ts, T_GRP[g - 1])
                SW3_N = P.dma("sync", w3qb[pw][:, :], w3q_h[:, dt * HF:(dt + 1) * HF],
                              sw3[pw])
                if pend_store[0] is not None:
                    pend_store[0]()
                    pend_store[0] = None
                if dt >= 2:
                    P.wait("scalar", vs, V_OUT[(g, dt - 2)])
                elif g >= 1:
                    P.wait("scalar", vs, V_OUT[(g - 1, 14 + dt)])
                else:
                    P.wait("scalar", vs, VS_QZ)
                SXF_N = P.dma("scalar", xf32[pw][:, :],
                              xTo[dt * 128:(dt + 1) * 128, gc:gc + 512], sxf[pw])
                P.wait("tensor", sw3[pw], SW3_N)
                if dt == 0:
                    if g == 0:
                        P.wait("tensor", spreg, PREG2)
                    P.wait("tensor", vs, V_HID[HB - 1])
                if dt == 1:
                    P.wait("tensor", ss, SS_RBC2[g])
                if dt >= 2:
                    P.wait("tensor", vs, V_OUT[(g, dt - 2)])
                elif g >= 1:
                    P.wait("tensor", vs, V_OUT[(g - 1, 14 + dt)])
                for k in range(4):
                    P.mm(psO[:, :], wo_sb[:, k * D + dt * 128:k * D + (dt + 1) * 128],
                         yTg[pg][:, k * 512:(k + 1) * 512], start=(k == 0), stop=False)
                T_OUT = 0
                for k in range(HB):
                    T_OUT = P.mm(psO[:, :], w3qb[pw][:, k * 128:(k + 1) * 128],
                                 hid[:, k * 512:(k + 1) * 512], start=False,
                                 stop=(k == HB - 1), sem=ts if k == HB - 1 else None)
                T_OUTL[dt] = T_OUT
                nglob = g * 16 + dt
                P.wait("vector", ts, T_OUT)
                P.wait("vector", sxf[pw], SXF_N)
                if nglob >= 2:
                    P.wait("vector", sout[pw], 16 * (nglob // 2))
                V_OUT[(g, dt)] = P.vop("stt", out=osb[pw][:, :], in0=psO[:, :],
                                       scalar=bo_sb[:, dt:dt + 1], in1=xf32[pw][:, :],
                                       op0=ALU.add, op1=ALU.add, sem=vs)
                def _store(g=g, dt=dt, pw=pw, gc=gc):
                    P.wait("sync", vs, V_OUT[(g, dt)])
                    P.dma("sync", outT[dt * 128:(dt + 1) * 128, gc:gc + 512],
                          osb[pw][:, :], sout[pw])
                if dt == 15:
                    _store()
                else:
                    pend_store[0] = _store
            T_GRP[g] = T_OUTL[15]

        stk_p2.close()

        P.wait("sync", sout[0], sout[0].v)
        P.wait("sync", sout[1], sout[1].v)

        with nc.Block() as block:
            @block.sync
            def _(e):
                for f in P.q["sync"]:
                    f(e)

            @block.gpsimd
            def _(e):
                for f in P.q["gpsimd"]:
                    f(e)

            @block.tensor
            def _(e):
                for f in P.q["tensor"]:
                    f(e)

            @block.vector
            def _(e):
                for f in P.q["vector"]:
                    f(e)

            @block.scalar
            def _(e):
                for f in P.q["scalar"]:
                    f(e)

    return nc


def _constants(T):
    j = np.arange(64, dtype=np.float64)
    inv = 10000.0 ** (-2.0 * j / 128.0)
    t = np.arange(T, dtype=np.float64)
    emb = np.outer(t, inv)
    cos = np.cos(emb).T
    sin = np.sin(emb).T
    m1 = np.concatenate([cos, sin], axis=0).astype(np.float32)
    m2 = np.concatenate([-sin, cos], axis=0).astype(np.float32)
    A = np.zeros((128, 128), np.float32)
    Bm = np.zeros((128, 128), np.float32)
    for i in range(64):
        A[i, 2 * i] = 1.0
        A[64 + i, 2 * i] = 1.0
        Bm[i, 2 * i + 1] = 1.0
        Bm[64 + i, 2 * i + 1] = 1.0
    mskd = np.zeros((128, 2048), np.float32)
    sl = np.arange(128)
    ql = np.arange(128)
    for i in range(4):
        for qt in range(4):
            blockm = ((i * 128 + sl)[:, None] <= (qt * 128 + ql)[None, :])
            mskd[:, i * 512 + qt * 128:i * 512 + (qt + 1) * 128] = blockm
    return {
        "m1t": m1, "m2t": m2,
        "atm": np.ascontiguousarray(A.T).astype(BFNP),
        "btm": np.ascontiguousarray(Bm.T).astype(BFNP),
        "idb": np.eye(128, dtype=np.float32).astype(BFNP),
        "idf": np.eye(16, dtype=np.float32),
        "oneb": np.ones((128, 1), np.float32).astype(BFNP),
        "onef": np.ones((128, 1), np.float32),
        "onerow": np.ones((1, 128), np.float32),
        "halfrow": np.full((1, 128), 0.5, np.float32),
        "mskd": mskd.astype(BFNP),
        "cvals": np.repeat(np.array([[ESHIFT, LNEPS, QEPS, 0.0]], np.float32), 128, 0),
    }


def _stage_core(c, x, Wq, bq, Wk, bk, Wv, bv, Wo, bo, w1T, w2T, w3T, consts, T):
    b, h = c // 4, c % 4
    NTO = T // 4
    d = dict(consts)
    d["xT"] = np.ascontiguousarray(x[b].T)
    d["xTo"] = np.ascontiguousarray(x[b, h * NTO:(h + 1) * NTO].T)
    d["wq_s"] = np.ascontiguousarray(Wq[:, h * 512:(h + 1) * 512])
    d["wk_s"] = np.ascontiguousarray(Wk[:, h * 128:(h + 1) * 128])
    d["wv_s"] = np.ascontiguousarray(Wv[:, h * 128:(h + 1) * 128])
    d["bq4"] = np.ascontiguousarray(bq[h * 512:(h + 1) * 512].reshape(4, 128))
    d["bk1"] = np.ascontiguousarray(bk[h * 128:(h + 1) * 128].reshape(1, 128))
    d["bv1"] = np.ascontiguousarray(bv[h * 128:(h + 1) * 128].reshape(1, 128))
    d["wo"] = np.ascontiguousarray(Wo)
    d["bo16"] = np.ascontiguousarray(bo.reshape(16, 128))
    d["w1T"] = w1T
    d["w2T"] = w2T
    d["w3T"] = w3T
    ms = np.zeros((128, 4), np.float32)
    ms[:, h] = 1.0
    d["msel"] = ms
    return d


def stage_all(x, Wq, bq, Wk, bk, Wv, bv, Wo, bo, w1, w2, w3):
    T = x.shape[1]
    w1T = np.ascontiguousarray(np.asarray(w1, dtype=np.float32).T)
    w2T = np.ascontiguousarray(np.asarray(w2, dtype=np.float32).T)
    w3T = np.ascontiguousarray(np.asarray(w3, dtype=np.float32).T)
    consts = _constants(T)
    return [_stage_core(c, np.asarray(x, dtype=np.float32),
                        np.asarray(Wq, dtype=np.float32), np.asarray(bq, dtype=np.float32),
                        np.asarray(Wk, dtype=np.float32), np.asarray(bk, dtype=np.float32),
                        np.asarray(Wv, dtype=np.float32), np.asarray(bv, dtype=np.float32),
                        np.asarray(Wo, dtype=np.float32), np.asarray(bo, dtype=np.float32),
                        w1T, w2T, w3T, consts, T)
            for c in range(NCORE)]


def kernel(L0_attn_x, L0_attn_Wq, L0_attn_bq, L0_attn_Wk, L0_attn_bk,
           L0_attn_Wv, L0_attn_bv, L0_attn_Wo, L0_attn_bo,
           L0_ffn_w1, L0_ffn_w2, L0_ffn_w3):
    T = L0_attn_x.shape[1]
    HF = L0_ffn_w1.shape[0]
    in_maps = stage_all(L0_attn_x, L0_attn_Wq, L0_attn_bq, L0_attn_Wk, L0_attn_bk,
                        L0_attn_Wv, L0_attn_bv, L0_attn_Wo, L0_attn_bo,
                        L0_ffn_w1, L0_ffn_w2, L0_ffn_w3)
    nc = build_nc(T, HF)
    res = run_bass_kernel_spmd(nc, in_maps, core_ids=list(range(NCORE)))
    out = np.empty((2, T, D), np.float32)
    NTO = T // 4
    for c in range(NCORE):
        b, h = c // 4, c % 4
        out[b, h * NTO:(h + 1) * NTO, :] = np.asarray(res.results[c]["outT"]).T
    return out


# version 35
# speedup vs baseline: 1.0474x; 1.0035x over previous
import contextlib
import numpy as np
import ml_dtypes

import concourse.bass as bass
import concourse.mybir as mybir
from concourse.bass_utils import run_bass_kernel_spmd

F32 = mybir.dt.float32
BF16 = mybir.dt.bfloat16
AF = mybir.ActivationFunctionType
ALU = mybir.AluOpType
AX = mybir.AxisListType
BFNP = ml_dtypes.bfloat16

D = 2048
KD = D // 128
NCORE = 8
SCALE = float(np.float64(1.0) / np.sqrt(np.float64(128.0)))
ESHIFT = -20.0
LNEPS = 1e-5
QEPS = 1e-8


class Sem:
    def __init__(self, name):
        self.name = name
        self.v = 0
        self.h = None

    def inc(self, n=1):
        self.v += n
        return self.v


class Plan:
    def __init__(self):
        self.q = {"sync": [], "gpsimd": [], "tensor": [], "vector": [], "scalar": []}

    def wait(self, eng, sem, val):
        if val and val > 0:
            self.q[eng].append(lambda e, s=sem, v=val: e.wait_ge(s.h, v))

    def mm(self, out, lhsT, rhs, start, stop, sem=None):
        v = sem.inc(1) if sem is not None else None

        def f(e, out=out, lhsT=lhsT, rhs=rhs, start=start, stop=stop, sem=sem):
            r = e.matmul(out, lhsT, rhs, start=start, stop=stop)
            if sem is not None:
                r.then_inc(sem.h, 1)
        self.q["tensor"].append(f)
        return v

    def tr(self, out, in_, ident, sem=None):
        v = sem.inc(1) if sem is not None else None

        def f(e, out=out, in_=in_, ident=ident, sem=sem):
            r = e.transpose(out, in_, ident)
            if sem is not None:
                r.then_inc(sem.h, 1)
        self.q["tensor"].append(f)
        return v

    def act(self, out, in_, func, bias=None, scale=None, sem=None):
        v = sem.inc(1) if sem is not None else None

        def f(e, out=out, in_=in_, func=func, bias=bias, scale=scale, sem=sem):
            kw = {}
            if bias is not None:
                kw["bias"] = bias
            if scale is not None:
                kw["scale"] = scale
            r = e.activation(out, in_, func, **kw)
            if sem is not None:
                r.then_inc(sem.h, 1)
        self.q["scalar"].append(f)
        return v

    def vop(self, kind, sem=None, **kw):
        v = sem.inc(1) if sem is not None else None

        def f(e, kind=kind, kw=kw, sem=sem):
            if kind == "tt":
                r = e.tensor_tensor(kw["out"], kw["in0"], kw["in1"], kw["op"])
            elif kind == "ts":
                r = e.tensor_scalar(kw["out"], kw["in0"], kw["s1"], None,
                                    kw["op0"])
            elif kind == "ts2":
                r = e.tensor_scalar(kw["out"], kw["in0"], kw["s1"], kw["s2"],
                                    kw["op0"], kw["op1"])
            elif kind == "stt":
                r = e.scalar_tensor_tensor(kw["out"], kw["in0"], kw["scalar"],
                                           kw["in1"], kw["op0"], kw["op1"])
            elif kind == "red":
                r = e.reduce_sum(kw["out"], kw["in_"], axis=AX.X,
                                 apply_absolute_value=kw.get("absv", False))
            elif kind == "recip":
                r = e.reciprocal(kw["out"], kw["in_"])
            else:
                raise ValueError(kind)
            if sem is not None:
                r.then_inc(sem.h, 1)
        self.q["vector"].append(f)
        return v

    def pop_red(self, out, in_, absv, sem=None):
        v = sem.inc(1) if sem is not None else None

        def f(e, out=out, in_=in_, absv=absv, sem=sem):
            r = e.reduce_sum(out, in_, axis=AX.XYZWC, apply_absolute_value=absv)
            if sem is not None:
                r.then_inc(sem.h, 1)
        self.q["gpsimd"].append(f)
        return v

    def dma(self, eng, out, in_, sem):
        v = sem.inc(16)

        def f(e, out=out, in_=in_, sem=sem):
            e.dma_start(out=out, in_=in_).then_inc(sem.h, 16)
        self.q[eng].append(f)
        return v

    def coll(self, kind, op, groups, ins, outs, sem):
        v = sem.inc(1)

        def f(e, kind=kind, op=op, groups=groups, ins=ins, outs=outs, sem=sem):
            e.collective_compute(kind, op, replica_groups=groups, ins=ins,
                                 outs=outs).then_inc(sem.h, 1)
        self.q["gpsimd"].append(f)
        return v


def build_nc(T, HF):
    NBLK = T // 512
    NTO = T // 4
    NGRP = NTO // 512
    HB = HF // 128
    WMX = max(D, HF)
    assert T % 2048 == 0 and HF % 128 == 0

    nc = bass.Bass("TRN2", target_bir_lowering=False, debug=False)

    xT = nc.dram_tensor("xT", [D, T], F32, kind="ExternalInput")
    xTo = nc.dram_tensor("xTo", [D, NTO], F32, kind="ExternalInput")
    wq_s = nc.dram_tensor("wq_s", [D, 512], F32, kind="ExternalInput")
    wk_s = nc.dram_tensor("wk_s", [D, 128], F32, kind="ExternalInput")
    wv_s = nc.dram_tensor("wv_s", [D, 128], F32, kind="ExternalInput")
    bq4 = nc.dram_tensor("bq4", [4, 128], F32, kind="ExternalInput")
    bk1 = nc.dram_tensor("bk1", [1, 128], F32, kind="ExternalInput")
    bv1 = nc.dram_tensor("bv1", [1, 128], F32, kind="ExternalInput")
    wo_d = nc.dram_tensor("wo", [512, D], F32, kind="ExternalInput")
    bo16 = nc.dram_tensor("bo16", [16, 128], F32, kind="ExternalInput")
    w1T = nc.dram_tensor("w1T", [D, HF], F32, kind="ExternalInput")
    w2T = nc.dram_tensor("w2T", [D, HF], F32, kind="ExternalInput")
    w3T = nc.dram_tensor("w3T", [HF, D], F32, kind="ExternalInput")
    m1t = nc.dram_tensor("m1t", [128, T], F32, kind="ExternalInput")
    m2t = nc.dram_tensor("m2t", [128, T], F32, kind="ExternalInput")
    atm = nc.dram_tensor("atm", [128, 128], BF16, kind="ExternalInput")
    btm = nc.dram_tensor("btm", [128, 128], BF16, kind="ExternalInput")
    idb = nc.dram_tensor("idb", [128, 128], BF16, kind="ExternalInput")
    idf = nc.dram_tensor("idf", [16, 16], F32, kind="ExternalInput")
    oneb = nc.dram_tensor("oneb", [128, 1], BF16, kind="ExternalInput")
    onef = nc.dram_tensor("onef", [128, 1], F32, kind="ExternalInput")
    onerow = nc.dram_tensor("onerow", [1, 128], F32, kind="ExternalInput")
    halfrow = nc.dram_tensor("halfrow", [1, 128], F32, kind="ExternalInput")
    mskd = nc.dram_tensor("mskd", [128, 2048], BF16, kind="ExternalInput")
    msel = nc.dram_tensor("msel", [128, 4], F32, kind="ExternalInput")
    cvals = nc.dram_tensor("cvals", [128, 4], F32, kind="ExternalInput")

    outT = nc.dram_tensor("outT", [D, NTO], F32, kind="ExternalOutput")

    ag_in = nc.dram_tensor("ag_in", [128, T], BF16)
    ag_out = nc.dram_tensor("ag_out", [512, T], BF16)
    w1q_h = nc.dram_tensor("w1q_h", [128, HB * D], BF16)
    w2q_h = nc.dram_tensor("w2q_h", [128, HB * D], BF16)
    w3q_h = nc.dram_tensor("w3q_h", [128, 16 * HF], BF16)

    P = Plan()
    NW = 2 * HB + 16 * ((HB + 15) // 16)

    with contextlib.ExitStack() as G:
        sems = {}
        for nm in ["tsem", "vsem", "ssem", "spre", "spreg", "sx0", "sx1", "sx2",
                   "sy0", "sy1", "scc", "psem", "swg0", "swg1", "swg2", "swg3", "sqz0", "sqz1", "sqz2", "sqz3",
                   "sqw0", "sqw1", "sq0", "sq1", "sq2", "sqb0", "sqb1", "sqb2", "sw30", "sw31", "sxf0", "sxf1",
                   "sg0", "sg1", "sxb0", "sxb1", "sout0", "sout1"]:
            s = Sem(nm)
            s.h = G.enter_context(nc.semaphore(nm))
            sems[nm] = s
        ts, vs, ss = sems["tsem"], sems["vsem"], sems["ssem"]
        sx = [sems["sx0"], sems["sx1"], sems["sx2"]]
        swg = [sems[f"swg{i}"] for i in range(4)]
        sqz = [sems[f"sqz{i}"] for i in range(4)]
        psm = sems["psem"]
        sq = [sems["sq0"], sems["sq1"], sems["sq2"]]
        sqb = [sems["sqb0"], sems["sqb1"], sems["sqb2"]]
        sw3 = [sems["sw30"], sems["sw31"]]
        sxf = [sems["sxf0"], sems["sxf1"]]
        sg = [sems["sg0"], sems["sg1"]]
        sxb = [sems["sxb0"], sems["sxb1"]]
        spre, spreg = sems["spre"], sems["spreg"]
        scc = sems["scc"]
        sy = [sems["sy0"], sems["sy1"]]
        sqw = [sems["sqw0"], sems["sqw1"]]
        sout = [sems["sout0"], sems["sout1"]]

        ps01 = G.enter_context(nc.psum_tensor("ps01", [128, 1024], F32))
        ps23 = G.enter_context(nc.psum_tensor("ps23", [128, 1024], F32))
        ps45 = G.enter_context(nc.psum_tensor("ps45", [128, 1024], F32))
        ps6 = G.enter_context(nc.psum_tensor("ps6", [128, 512], F32))
        ps7 = G.enter_context(nc.psum_tensor("ps7", [128, 512], F32))
        ps4r = ps45[0:1, 0:512]
        ps5r = ps45[0:1, 512:1024]

        sb = lambda n, shp, dt: G.enter_context(nc.sbuf_tensor(n, shp, dt))
        wq_eff = sb("wq_eff", [128, KD * 128], BF16)
        wk_sb = sb("wk_sb", [128, KD * 128], BF16)
        wv_sb = sb("wv_sb", [128, KD * 128], BF16)
        atm_sb = sb("atm_sb", [128, 128], BF16)
        btm_sb = sb("btm_sb", [128, 128], BF16)
        idb_sb = sb("idb_sb", [128, 128], BF16)
        idf_sb = sb("idf_sb", [16, 16], F32)
        oneb_sb = sb("oneb_sb", [128, 1], BF16)
        onef_sb = sb("onef_sb", [128, 1], F32)
        onerow_sb = sb("onerow_sb", [1, 128], F32)
        halfrow_sb = sb("halfrow_sb", [1, 128], F32)
        msel_sb = sb("msel_sb", [128, 4], F32)
        cv_sb = sb("cv_sb", [128, 4], F32)
        bq_col = sb("bq_col", [128, 1], F32)
        bk_col = sb("bk_col", [128, 1], F32)
        bv_col = sb("bv_col", [128, 1], F32)
        bo_sb = sb("bo_sb", [128, 16], F32)
        csq = sb("csq", [1, 128], BF16)
        csk = sb("csk", [1, 128], BF16)
        csv = sb("csv", [1, 128], BF16)
        mrow = sb("mrow", [1, 512], F32)
        srow = sb("srow", [1, 512], F32)
        m2row = sb("m2row", [1, 512], F32)
        varrow = sb("varrow", [1, 512], F32)
        lnrow = m2row
        rrow = sb("rrow", [1, 512], F32)
        rmrow = sb("rmrow", [1, 512], BF16)
        recrow = sb("recrow", [1, 512], F32)
        gred = sb("gred", [128, NW], F32)
        gcol = sb("gcol", [128, 3], F32)
        ge_row = sb("ge_row", [1, 1], F32)
        bpc = [sb(f"bp{m}", [128, 1], F32) for m in range(3)]
        bnc = [sb(f"bn{m}", [128, 1], F32) for m in range(3)]

        with contextlib.ExitStack() as PR:
            wq_st = PR.enter_context(nc.sbuf_tensor("wq_st", [128, KD * 512], F32))
            bq4_sb = PR.enter_context(nc.sbuf_tensor("bq4_sb", [4, 128], F32))
            bk1_sb = PR.enter_context(nc.sbuf_tensor("bk1_sb", [1, 128], F32))
            bv1_sb = PR.enter_context(nc.sbuf_tensor("bv1_sb", [1, 128], F32))
            bo16_sb = PR.enter_context(nc.sbuf_tensor("bo16_sb", [16, 128], F32))
            tqa = PR.enter_context(nc.sbuf_tensor("tqa", [128, 128], F32))
            tqb = PR.enter_context(nc.sbuf_tensor("tqb", [128, 128], F32))
            PRE_N = 0
            for dst, src in [(wq_st.ap().rearrange("p (k i) -> p k i", k=KD), wq_s.ap().rearrange("(k p) i -> p k i", p=128)),
                             (atm_sb[:, :], atm[:, :]), (btm_sb[:, :], btm[:, :]),
                             (idb_sb[:, :], idb[:, :]), (idf_sb[:, :], idf[:, :]),
                             (oneb_sb[:, :], oneb[:, :]), (onef_sb[:, :], onef[:, :]),
                             (onerow_sb[:, :], onerow[:, :]), (halfrow_sb[:, :], halfrow[:, :]),
                             (msel_sb[:, :], msel[:, :]),
                             (cv_sb[:, :], cvals[:, :]),
                             (bq4_sb[:, :], bq4[:, :]), (bk1_sb[:, :], bk1[:, :]),
                             (bv1_sb[:, :], bv1[:, :]), (bo16_sb[:, :], bo16[:, :])]:
                PRE_N = P.dma("sync", dst, src, spre)
            PREG_N = P.dma("gpsimd", wk_sb.ap().rearrange("p (k i) -> p k i", k=KD),
                           wk_s.ap().rearrange("(k p) i -> p k i", p=128), spreg)
            PREG_N = P.dma("gpsimd", wv_sb.ap().rearrange("p (k i) -> p k i", k=KD),
                           wv_s.ap().rearrange("(k p) i -> p k i", p=128), spreg)

            P.wait("vector", spre, PRE_N)
            V_WQE = 0
            for k in range(KD):
                c = k * 512
                if k > 0:
                    P.wait("vector", vs, V_WQE)
                P.vop("tt", out=tqa[:, :], in0=wq_st[:, c:c + 128],
                      in1=wq_st[:, c + 128:c + 256], op=ALU.add)
                P.vop("tt", out=tqb[:, :], in0=wq_st[:, c + 256:c + 384],
                      in1=wq_st[:, c + 384:c + 512], op=ALU.add, sem=vs)
                P.wait("vector", vs, vs.v)
                V_WQE = P.vop("tt", out=wq_eff[:, k * 128:(k + 1) * 128],
                              in0=tqa[:, :], in1=tqb[:, :], op=ALU.add, sem=vs)

            P.wait("tensor", vs, V_WQE)
            P.wait("tensor", spre, PRE_N)
            P.wait("tensor", spreg, PREG_N)
            T_CSQ = 0
            for k in range(KD):
                T_CSQ = P.mm(ps45[0:1, 0:128], oneb_sb[:, :],
                             wq_eff[:, k * 128:(k + 1) * 128],
                             start=(k == 0), stop=(k == KD - 1),
                             sem=ts if k == KD - 1 else None)
            for k in range(KD):
                P.mm(ps6[0:1, 0:128], oneb_sb[:, :], wk_sb[:, k * 128:(k + 1) * 128],
                     start=(k == 0), stop=(k == KD - 1))
            T_CSKV = 0
            for k in range(KD):
                T_CSKV = P.mm(ps7[0:1, 0:128], oneb_sb[:, :],
                              wv_sb[:, k * 128:(k + 1) * 128],
                              start=(k == 0), stop=(k == KD - 1),
                              sem=ts if k == KD - 1 else None)
            P.wait("scalar", ts, T_CSQ)
            P.act(csq[:, :], ps45[0:1, 0:128], AF.Copy, scale=-1.0, sem=ss)
            P.wait("scalar", ts, T_CSKV)
            P.act(csk[:, :], ps6[0:1, 0:128], AF.Copy, scale=-1.0, sem=ss)
            SS_CS = P.act(csv[:, :], ps7[0:1, 0:128], AF.Copy, scale=-1.0, sem=ss)
            P.wait("tensor", ss, SS_CS)
            P.mm(ps45[:, 0:1], bq4_sb[:, :], onef_sb[0:4, 0:1], True, True)
            P.mm(ps45[:, 512:513], bk1_sb[:, :], onef_sb[0:1, 0:1], True, True)
            P.mm(ps6[:, 0:1], bv1_sb[:, :], onef_sb[0:1, 0:1], True, True)
            T_BIAS = P.tr(ps7[:, 0:16], bo16_sb[:, :], idf_sb[:, :], sem=ts)
            P.wait("scalar", ts, T_BIAS)
            P.act(bq_col[:, :], ps45[:, 0:1], AF.Copy, sem=ss)
            P.act(bk_col[:, :], ps45[:, 512:513], AF.Copy, sem=ss)
            P.act(bv_col[:, :], ps6[:, 0:1], AF.Copy, sem=ss)
            SS_PRO = P.act(bo_sb[:, :], ps7[:, 0:16], AF.Copy, sem=ss)

        stk_p1c = contextlib.ExitStack()
        qT = stk_p1c.enter_context(nc.sbuf_tensor("qT", [128, T], BF16))
        kT = stk_p1c.enter_context(nc.sbuf_tensor("kT", [128, T], BF16))
        vS = stk_p1c.enter_context(nc.sbuf_tensor("vS", [128, T], BF16))

        stk_p1a = contextlib.ExitStack()
        xb = [stk_p1a.enter_context(nc.sbuf_tensor(f"xb{p}", [128, KD * 512], BF16))
              for p in range(3)]
        xsqh = stk_p1a.enter_context(nc.sbuf_tensor("xsqh", [128, 8 * 512], BF16))
        m1s = [stk_p1a.enter_context(nc.sbuf_tensor(f"m1s{p}", [128, 512], F32))
               for p in range(3)]
        m2s = [stk_p1a.enter_context(nc.sbuf_tensor(f"m2s{p}", [128, 512], F32))
               for p in range(3)]
        tq_sb = stk_p1a.enter_context(nc.sbuf_tensor("tq_sb", [128, 512], F32))
        tk_sb = stk_p1a.enter_context(nc.sbuf_tensor("tk_sb", [128, 512], F32))
        tv_sb = stk_p1a.enter_context(nc.sbuf_tensor("tv_sb", [128, 512], F32))
        rbc_sb = stk_p1a.enter_context(nc.sbuf_tensor("rbc_sb", [128, 512], F32))
        t1_sb = rbc_sb
        t2_sb = tq_sb
        uq_sb = stk_p1a.enter_context(nc.sbuf_tensor("uq_sb", [128, 512], BF16))
        uk_sb = stk_p1a.enter_context(nc.sbuf_tensor("uk_sb", [128, 512], BF16))
        vr_sb = stk_p1a.enter_context(nc.sbuf_tensor("vr_sb", [128, 512], BF16))

        T_XB = [0] * NBLK
        T_MS = [0] * NBLK
        T_SK = {}
        T_SKL = [0] * NBLK
        T_RANK = [0] * NBLK
        SS_SROW = [0] * NBLK
        SS_RBC = [0] * NBLK
        SS_VSB = [0] * NBLK
        V_SQ = [0] * NBLK
        V_QEV = [0] * NBLK
        V_KEV = [0] * NBLK
        V_TV = [0] * NBLK
        V_QT2 = [0] * NBLK
        V_KT2 = [0] * NBLK
        V_KROT = [0] * NBLK

        for i in range(NBLK):
            p = i % 3
            c0 = i * 512
            if i >= 3:
                P.wait("gpsimd", ts, T_XB[i - 3])
                P.wait("gpsimd", vs, V_KROT[i - 3])
            else:
                P.wait("gpsimd", ss, SS_PRO)
            P.dma("gpsimd", xb[p].ap().rearrange("p (k t) -> p k t", k=KD),
                  xT[:, c0:c0 + 512].rearrange("(k p) t -> p k t", p=128), sx[p])
            P.dma("gpsimd", m1s[p][:, :], m1t[:, c0:c0 + 512], sx[p])
            SX_N = P.dma("gpsimd", m2s[p][:, :], m2t[:, c0:c0 + 512], sx[p])

            assert SX_N == 48 * (i // 3 + 1)
            if i == 0:
                P.wait("tensor", sx[p], SX_N)
                P.wait("tensor", ss, SS_PRO)
                for k in range(KD):
                    P.mm(ps4r, oneb_sb[:, :], xb[p][:, k * 512:(k + 1) * 512],
                         start=(k == 0), stop=(k == KD - 1))
            P.wait("tensor", sx[p], SX_N)
            if i >= 1:
                P.wait("tensor", vs, V_QEV[i - 1])
            for k in range(KD):
                P.mm(ps01[:, 0:512], wq_eff[:, k * 128:(k + 1) * 128],
                     xb[p][:, k * 512:(k + 1) * 512], start=(k == 0), stop=False)

            P.wait("vector", sx[p], SX_N)
            if i >= 1:
                P.wait("vector", ts, T_MS[i - 1])
            V_FA = 0
            for k in range(8):
                V_FA = P.vop("tt", out=xsqh[:, k * 512:(k + 1) * 512],
                             in0=xb[p][:, k * 512:(k + 1) * 512],
                             in1=xb[p][:, k * 512:(k + 1) * 512],
                             op=ALU.mult, sem=vs if k == 7 else None)
            P.wait("tensor", vs, V_FA)
            T_SA = 0
            for k in range(8):
                T_SA = P.mm(ps5r, oneb_sb[:, :], xsqh[:, k * 512:(k + 1) * 512],
                            start=(k == 0), stop=False,
                            sem=ts if k == 7 else None)

            if i >= 1:
                P.wait("tensor", vs, V_KEV[i - 1])
            for k in range(KD):
                P.mm(ps01[:, 512:1024], wk_sb[:, k * 128:(k + 1) * 128],
                     xb[p][:, k * 512:(k + 1) * 512], start=(k == 0), stop=False)

            P.wait("vector", ts, T_SA)
            V_FB = 0
            for k in range(8, KD):
                V_FB = P.vop("tt", out=xsqh[:, (k - 8) * 512:(k - 7) * 512],
                             in0=xb[p][:, k * 512:(k + 1) * 512],
                             in1=xb[p][:, k * 512:(k + 1) * 512],
                             op=ALU.mult, sem=vs if k == KD - 1 else None)
            P.wait("tensor", vs, V_FB)
            for k in range(8, KD):
                T_MS[i] = P.mm(ps5r, oneb_sb[:, :],
                               xsqh[:, (k - 8) * 512:(k - 7) * 512],
                               start=False, stop=(k == KD - 1),
                               sem=ts if k == KD - 1 else None)

            P.wait("scalar", ts, T_MS[i])
            P.act(mrow[:, :], ps4r, AF.Copy, scale=1.0 / D, sem=ss)
            SS_SROW[i] = P.act(srow[:, :], ps5r, AF.Copy, scale=1.0 / D, sem=ss)
            P.wait("vector", ss, SS_SROW[i])
            P.vop("tt", out=m2row[:, :], in0=mrow[:, :], in1=mrow[:, :],
                  op=ALU.mult, sem=vs)
            P.wait("vector", vs, vs.v)
            V_VAR = P.vop("tt", out=varrow[:, :], in0=srow[:, :], in1=m2row[:, :],
                          op=ALU.subtract, sem=vs)
            P.wait("scalar", vs, V_VAR)
            P.act(lnrow[:, :], varrow[:, :], AF.Ln, bias=cv_sb[0:1, 1:2], sem=ss)
            P.wait("scalar", ss, ss.v)
            SS_R = P.act(rrow[:, :], lnrow[:, :], AF.Exp, scale=-0.5, sem=ss)
            P.wait("vector", ss, SS_R)
            V_RM = P.vop("tt", out=rmrow[:, :], in0=rrow[:, :], in1=mrow[:, :],
                         op=ALU.mult, sem=vs)

            if i >= 1:
                P.wait("tensor", vs, V_TV[i - 1])
            for k in range(KD):
                T_XB[i] = P.mm(ps23[:, 0:512], wv_sb[:, k * 128:(k + 1) * 128],
                               xb[p][:, k * 512:(k + 1) * 512],
                               start=(k == 0), stop=False,
                               sem=ts if k == KD - 1 else None)
            if i >= 1:
                P.wait("tensor", ss, SS_RBC[i - 1])
            P.wait("tensor", ss, SS_R)
            P.mm(ps7[:, :], onerow_sb[:, :], rrow[:, :], True, True)
            P.wait("tensor", vs, V_RM)
            P.mm(ps01[:, 0:512], csq[:, :], rmrow[:, :], start=False, stop=True)
            P.mm(ps01[:, 512:1024], csk[:, :], rmrow[:, :], start=False, stop=True)
            T_RANK[i] = P.mm(ps23[:, 0:512], csv[:, :], rmrow[:, :],
                             start=False, stop=True, sem=ts)

            def m_next(k0, k1, i=i):
                if i + 1 >= NBLK:
                    return
                p2 = (i + 1) % 3
                if k0 == 0:
                    P.wait("tensor", sx[p2], 48 * ((i + 1) // 3 + 1))
                    P.wait("tensor", ss, SS_SROW[i])
                for k in range(k0, k1):
                    P.mm(ps4r, oneb_sb[:, :], xb[p2][:, k * 512:(k + 1) * 512],
                         start=(k == 0), stop=(k == KD - 1))
            m_next(0, 8)

            P.wait("scalar", ts, T_RANK[i])
            SS_RBC[i] = P.act(rbc_sb[:, :], ps7[:, :], AF.Copy, sem=ss)

            P.wait("vector", ts, T_RANK[i])
            P.wait("vector", ss, SS_RBC[i])
            V_QEV[i] = P.vop("tt", out=tq_sb[:, :], in0=ps01[:, 0:512],
                             in1=rbc_sb[:, :], op=ALU.mult, sem=vs)
            V_KEV[i] = P.vop("tt", out=tk_sb[:, :], in0=ps01[:, 512:1024],
                             in1=rbc_sb[:, :], op=ALU.mult, sem=vs)
            V_TV[i] = P.vop("tt", out=tv_sb[:, :], in0=ps23[:, 0:512],
                            in1=rbc_sb[:, :], op=ALU.mult, sem=vs)
            P.wait("scalar", vs, V_QEV[i])
            SS_UQ = P.act(uq_sb[:, :], tq_sb[:, :], AF.Identity, bias=bq_col[:, 0:1], sem=ss)
            P.wait("scalar", vs, V_KEV[i])
            SS_UK = P.act(uk_sb[:, :], tk_sb[:, :], AF.Identity, bias=bk_col[:, 0:1], sem=ss)
            P.wait("scalar", vs, V_TV[i])
            SS_VR = P.act(vr_sb[:, :], tv_sb[:, :], AF.Identity, bias=bv_col[:, 0:1], sem=ss)

            P.wait("tensor", ss, SS_UQ)
            if i >= 1:
                P.wait("tensor", vs, V_KT2[i - 1])
                P.wait("tensor", ss, SS_VSB[i - 1])
            P.mm(ps23[:, 512:1024], atm_sb[:, :], uq_sb[:, :], True, True)
            T_QAB = P.mm(ps6[:, :], btm_sb[:, :], uq_sb[:, :], True, True, sem=ts)
            m_next(8, 12)
            P.wait("vector", ts, T_QAB)
            P.wait("vector", vs, V_TV[i])
            P.vop("tt", out=t1_sb[:, :], in0=ps23[:, 512:1024], in1=m1s[p][:, :],
                  op=ALU.mult, sem=vs)
            V_QT2[i] = P.vop("tt", out=t2_sb[:, :], in0=ps6[:, :], in1=m2s[p][:, :],
                             op=ALU.mult, sem=vs)
            P.wait("vector", vs, V_QT2[i])
            V_QROT = P.vop("tt", out=qT[:, c0:c0 + 512], in0=t1_sb[:, :],
                           in1=t2_sb[:, :], op=ALU.add, sem=vs)

            P.wait("tensor", ss, SS_UK)
            P.wait("tensor", vs, V_QT2[i])
            P.mm(ps23[:, 512:1024], atm_sb[:, :], uk_sb[:, :], True, True)
            T_KAB = P.mm(ps6[:, :], btm_sb[:, :], uk_sb[:, :], True, True, sem=ts)
            m_next(12, KD)
            P.wait("vector", ts, T_KAB)
            P.wait("vector", vs, V_QROT)
            P.vop("tt", out=t1_sb[:, :], in0=ps23[:, 512:1024], in1=m1s[p][:, :],
                  op=ALU.mult, sem=vs)
            V_KT2[i] = P.vop("tt", out=t2_sb[:, :], in0=ps6[:, :], in1=m2s[p][:, :],
                             op=ALU.mult, sem=vs)
            P.wait("vector", vs, V_KT2[i])
            V_KROT[i] = P.vop("tt", out=kT[:, c0:c0 + 512], in0=t1_sb[:, :],
                              in1=t2_sb[:, :], op=ALU.add, sem=vs)

            P.wait("tensor", ss, SS_VR)
            P.wait("tensor", vs, V_KT2[i])
            ps6b = ps6.ap().bitcast(BF16)
            T_VTR = 0
            for j in range(4):
                T_VTR = P.tr(ps6b[:, j * 128:(j + 1) * 128],
                             vr_sb[:, j * 128:(j + 1) * 128], idb_sb[:, :],
                             sem=ts if j == 3 else None)
            P.wait("scalar", ts, T_VTR)
            for j in range(4):
                SS_VSB[i] = P.act(vS[:, (4 * i + j) * 128:(4 * i + j + 1) * 128],
                                  ps6b[:, j * 128:(j + 1) * 128], AF.Copy,
                                  sem=ss if j == 3 else None)

        stk_p1a.close()

        stk_p1b = contextlib.ExitStack()
        attT = [stk_p1b.enter_context(nc.sbuf_tensor(f"attT{p}", [128, 1024], BF16))
                for p in range(6)]
        attTm = stk_p1b.enter_context(nc.sbuf_tensor("attTm", [128, 1024], BF16))
        ybuf = [stk_p1b.enter_context(nc.sbuf_tensor(f"ybuf{p}", [128, 512], BF16))
                for p in range(2)]
        bcc_sb = stk_p1b.enter_context(nc.sbuf_tensor("bcc_sb", [128, 512], F32))
        mskd_sb = stk_p1b.enter_context(nc.sbuf_tensor("mskd_sb", [128, 2048], BF16))
        wstr = [stk_p1b.enter_context(nc.sbuf_tensor(f"wstr{p}", [128, 2048], F32))
                for p in range(4)]
        sgA = [stk_p1b.enter_context(nc.sbuf_tensor(f"sgA{p}", [128, 2048], BF16))
               for p in range(2)]
        sgB = [stk_p1b.enter_context(nc.sbuf_tensor(f"sgB{p}", [128, 2048], BF16))
               for p in range(2)]
        qbk = [stk_p1b.enter_context(nc.sbuf_tensor(f"qbk{p}", [128, 2048], BF16))
               for p in range(2)]

        P_BAR_T = ts.v
        P_BAR_V = vs.v
        P_BAR_S = ss.v
        P.wait("sync", ts, P_BAR_T)
        P.wait("sync", vs, P_BAR_V)
        P.wait("sync", ss, P_BAR_S)
        PRE_MSK = P.dma("sync", mskd_sb[:, :], mskd[:, :], spre)
        P.wait("vector", spre, PRE_MSK)

        units = []
        for hb in range(HB):
            units.append((0, w1T[:, hb * 128:(hb + 1) * 128]
                          .rearrange("(k p) h -> p k h", p=128),
                          w1q_h[:, hb * D:(hb + 1) * D], D))
        for hb in range(HB):
            units.append((1, w2T[:, hb * 128:(hb + 1) * 128]
                          .rearrange("(k p) h -> p k h", p=128),
                          w2q_h[:, hb * D:(hb + 1) * D], D))
        for dt in range(16):
            w3v = w3T[:, dt * 128:(dt + 1) * 128].rearrange("(k p) d_ -> p k d_", p=128)
            for k0 in range(0, HB, 16):
                k1 = min(k0 + 16, HB)
                units.append((2, w3v[:, k0:k1, :],
                              w3q_h[:, dt * HF + k0 * 128:dt * HF + k1 * 128],
                              (k1 - k0) * 128))
        assert len(units) == NW

        G_GRED = [0] * NW
        GSWN = [0] * NW
        g_state = {"iss": 0, "drn": 0}

        def gamma_issue_one(j):
            pj = j % 4
            if j >= 4:
                P.wait("sync", vs, G_GRED[j - 4])
            m, gsrc, _, gw = units[j]
            GSWN[j] = P.dma("sync", wstr[pj][:, 0:gw]
                            .rearrange("p (k h) -> p k h", h=128), gsrc, swg[pj])

        def gamma_drain_one(j):
            if g_state["drn"] != j:
                return
            g_state["drn"] = j + 1
            pj = j % 4
            m, gsrc, _, gw = units[j]
            P.wait("vector", swg[pj], GSWN[j])
            G_GRED[j] = P.vop("red", out=gred[:, j:j + 1], in_=wstr[pj][:, 0:gw],
                              absv=True, sem=vs)

        def emit_gamma(n, final=False):
            for _ in range(n):
                j = g_state["iss"]
                if j >= NW:
                    break
                gamma_issue_one(j)
                g_state["iss"] = j + 1
                if j >= 3:
                    gamma_drain_one(j - 3)
            if final:
                for k in range(g_state["drn"], NW):
                    gamma_drain_one(k)

        V_SG = [0] * NW
        V_QB = [0] * NW
        QSQN = [0] * NW
        q_state = {"iss": 0, "drn": 0, "stored": 0, "V_GF": 0, "SS_GDONE": 0}

        def qz_issue_one(u):
            pu = u % 4
            m, src, dst, wdt = units[u]
            if u < 4:
                P.wait("sync", vs, G_GRED[NW - 1])
            else:
                P.wait("sync", vs, V_SG[u - 4])
            QSQN[u] = P.dma("sync", wstr[pu][:, 0:wdt]
                            .rearrange("p (k h) -> p k h", h=128), src, sqz[pu])

        def qz_drain_one(u):
            if q_state["drn"] != u:
                return
            q_state["drn"] = u + 1
            pu = u % 4
            ps2 = u & 1
            m, src, dst, wdt = units[u]
            P.wait("vector", sqz[pu], QSQN[u])
            if u == 0:
                P.wait("vector", ss, q_state["SS_GDONE"])
            if u >= 2:
                P.wait("vector", vs, V_QB[u - 2])
            P.vop("ts2", out=sgA[ps2][:, 0:wdt], in0=wstr[pu][:, 0:wdt],
                  s1=bpc[m][:, 0:1], s2=0.0, op0=ALU.add, op1=ALU.is_ge,
                  sem=vs)
            V_SG[u] = P.vop("ts2", out=sgB[ps2][:, 0:wdt], in0=wstr[pu][:, 0:wdt],
                            s1=bnc[m][:, 0:1], s2=0.0, op0=ALU.add,
                            op1=ALU.is_ge, sem=vs)
            P.wait("vector", vs, V_SG[u])
            if u >= 2:
                P.wait("vector", sqw[ps2], 16 * (u // 2))
            V_QB[u] = P.vop("stt", out=qbk[ps2][:, 0:wdt], in0=sgA[ps2][:, 0:wdt],
                            scalar=-1.0, in1=sgB[ps2][:, 0:wdt],
                            op0=ALU.add, op1=ALU.add, sem=vs)

        def qz_store_one(u):
            if q_state["stored"] != u:
                return
            q_state["stored"] = u + 1
            m, src, dst, wdt = units[u]
            P.wait("sync", vs, V_QB[u])
            P.dma("sync", dst, qbk[u & 1][:, 0:wdt], sqw[u & 1])

        def emit_qz(n, final=False):
            for _ in range(n):
                u = q_state["iss"]
                if u >= NW:
                    break
                qz_issue_one(u)
                q_state["iss"] = u + 1
                if u >= 3:
                    qz_drain_one(u - 3)
                    qz_store_one(u - 3)
            if final:
                for k in range(q_state["drn"], NW):
                    qz_drain_one(k)
                for k in range(q_state["stored"], NW):
                    qz_store_one(k)

        wides = [ps01, ps23, ps45]
        T_AV = {}
        T_SC = {}
        SS_EXP = {}
        V_MSK = {}
        V_Y = [0] * NBLK
        V_REC = [0] * NBLK
        SS_BCC = [0] * NBLK
        V_MSK_LAST = 0
        ps7_free = {"ss": P_BAR_S}
        FINB = max(1, min(NBLK - 2, (NBLK * 45) // 100))
        npre = sum(2 * (b + 1) for b in range(FINB))
        GPRATE = max(1, npre // (NW + 1))
        QHALF = (NW + 3 * (NBLK - 1 - FINB) - 1) // (3 * (NBLK - 1 - FINB))

        pairs = []
        for b in range(NBLK):
            npair = 2 * (b + 1)
            for pp in range(npair):
                pairs.append((b, pp, npair, b * 512))
        NP = len(pairs)

        def emit_scores(gp):
            b, pp, npair, q0 = pairs[gp]
            W = wides[gp % 3]
            if gp < 3:
                P.wait("tensor", vs, P_BAR_V)
                P.wait("tensor", ss, P_BAR_S)
            else:
                P.wait("tensor", ss, SS_EXP[gp - 3])
            st = 2 * pp
            P.mm(W[:, 0:512], kT[:, st * 128:(st + 1) * 128], qT[:, q0:q0 + 512],
                 True, True)
            T_SC[gp] = P.mm(W[:, 512:1024], kT[:, (st + 1) * 128:(st + 2) * 128],
                            qT[:, q0:q0 + 512], True, True, sem=ts)

        for gp in range(min(3, NP)):
            emit_scores(gp)

        for gp in range(NP):
            b, pp, npair, q0 = pairs[gp]
            W = wides[gp % 3]
            At = attT[gp % 6]
            masked = pp >= npair - 2
            st = 2 * pp
            P.wait("scalar", ts, T_SC[gp])
            if gp >= 6:
                P.wait("scalar", ts, T_AV[gp - 6])
            if masked and pp == npair - 1:
                P.wait("scalar", vs, V_MSK_LAST)
            tgt = attTm if masked else At
            SS_EXP[gp] = P.act(tgt[:, :], W[:, :], AF.Exp,
                               bias=cv_sb[:, 0:1], scale=SCALE, sem=ss)
            if masked:
                slot0 = st - 4 * b
                P.wait("vector", ss, SS_EXP[gp])
                P.vop("tt", out=At[:, 0:512], in0=attTm[:, 0:512],
                      in1=mskd_sb[:, slot0 * 512:(slot0 + 1) * 512],
                      op=ALU.mult, sem=vs)
                V_MSK_LAST = P.vop("tt", out=At[:, 512:1024],
                                   in0=attTm[:, 512:1024],
                                   in1=mskd_sb[:, (slot0 + 1) * 512:(slot0 + 2) * 512],
                                   op=ALU.mult, sem=vs)
            if masked:
                P.wait("tensor", vs, V_MSK_LAST)
            else:
                P.wait("tensor", ss, SS_EXP[gp])
            if pp == 0:
                if b >= 1:
                    P.wait("tensor", vs, V_Y[b - 1])
                P.wait("tensor", ss, ps7_free["ss"])
            P.mm(ps6[:, :], vS[:, st * 128:(st + 1) * 128], At[:, 0:512],
                 start=(pp == 0), stop=False)
            P.mm(ps6[:, :], vS[:, (st + 1) * 128:(st + 2) * 128], At[:, 512:1024],
                 start=False, stop=(pp == npair - 1))
            P.mm(ps7[0:1, :], oneb_sb[:, :], At[:, 0:512],
                 start=(pp == 0), stop=False)
            T_AV[gp] = P.mm(ps7[0:1, :], oneb_sb[:, :], At[:, 512:1024],
                            start=False, stop=(pp == npair - 1), sem=ts)
            if gp + 3 < NP:
                emit_scores(gp + 3)
            if b > FINB and (pp == 1 or pp == npair // 3 or pp == (2 * npair) // 3):
                emit_qz(QHALF)
            elif b < FINB and gp % GPRATE == 0:
                emit_gamma(1)
            if pp != npair - 1:
                continue
            P.wait("vector", ts, T_AV[gp])
            V_REC[b] = P.vop("recip", out=recrow[:, :], in_=ps7[0:1, :], sem=vs)
            P.wait("tensor", vs, V_REC[b])
            T_BC = P.mm(ps7[:, :], onerow_sb[:, :], recrow[:, :], True, True, sem=ts)
            P.wait("scalar", ts, T_BC)
            SS_BCC[b] = P.act(bcc_sb[:, :], ps7[:, :], AF.Copy, sem=ss)
            ps7_free["ss"] = SS_BCC[b]
            P.wait("vector", ss, SS_BCC[b])
            if b >= 2:
                P.wait("vector", sy[b & 1], 16 * (b // 2))
            V_Y[b] = P.vop("tt", out=ybuf[b & 1][:, :], in0=ps6[:, :],
                           in1=bcc_sb[:, :], op=ALU.mult, sem=vs)
            P.wait("gpsimd", vs, V_Y[b])
            P.dma("gpsimd", ag_in[:, q0:q0 + 512], ybuf[b & 1][:, :], sy[b & 1])

            if b == FINB:
                emit_gamma(NW, final=True)
                P.wait("vector", vs, G_GRED[NW - 1])
                P.vop("red", out=gcol[:, 0:1], in_=gred[:, 0:HB])
                P.vop("red", out=gcol[:, 1:2], in_=gred[:, HB:2 * HB])
                V_GF = P.vop("red", out=gcol[:, 2:3], in_=gred[:, 2 * HB:NW],
                             sem=vs)
                q_state["V_GF"] = V_GF
                SS_GDONE = 0
                for m in range(3):
                    P.wait("tensor", vs, V_GF)
                    P.wait("tensor", ss, ps7_free["ss"] if m == 0 else SS_GDONE)
                    T_G1 = P.mm(ps7[0:1, 0:1], gcol[:, m:m + 1], onef_sb[:, :],
                                True, True, sem=ts)
                    P.wait("scalar", ts, T_G1)
                    SS_GE = P.act(ge_row[:, :], ps7[0:1, 0:1], AF.Identity,
                                  scale=1.0 / float(HF * D), bias=cv_sb[0:1, 2:3],
                                  sem=ss)
                    P.wait("tensor", ss, SS_GE)
                    T_G2 = P.mm(ps7[:, 0:1], onerow_sb[:, :], ge_row[:, :],
                                True, True, sem=ts)
                    P.wait("scalar", ts, T_G2)
                    P.act(bpc[m][:, :], ps7[:, 0:1], AF.Copy, scale=0.5, sem=ss)
                    SS_GDONE = P.act(bnc[m][:, :], ps7[:, 0:1], AF.Copy,
                                     scale=-0.5, sem=ss)
                q_state["SS_GDONE"] = SS_GDONE
                ps7_free["ss"] = SS_GDONE
        emit_qz(NW, final=True)

        P.wait("gpsimd", sy[0], sy[0].v)
        P.wait("gpsimd", sy[1], sy[1].v)
        CC_N = P.coll("AllGather", ALU.bypass, [[0, 1, 2, 3], [4, 5, 6, 7]],
                      [ag_in.ap().opt()], [ag_out.ap().opt()], scc)

        TS_QZ = ts.v
        VS_QZ = vs.v
        SS_QZ = ss.v
        stk_p1b.close()
        stk_p1c.close()

        stk_p2 = contextlib.ExitStack()
        p2 = lambda n, shp, dt: stk_p2.enter_context(nc.sbuf_tensor(n, shp, dt))
        xbo = p2("xbo", [128, KD * 512], BF16)
        cmb = [p2(f"cmb{c}", [128, 512], BF16) for c in range(4)]
        yTg = [p2(f"yTg{q}", [128, 4 * 512], BF16) for q in range(2)]
        gld = p2("gld", [128, 3 * 512], BF16)
        xq2 = [p2(f"xq2{q}", [128, 512], BF16) for q in range(2)]
        wo_sb = p2("wo_sb", [128, 4 * D], BF16)
        cs1 = p2("cs1", [1, HF], BF16)
        cs2 = p2("cs2", [1, HF], BF16)
        hid = p2("hid", [128, HB * 512], BF16)
        w1qb = [p2(f"w1qb{q}", [128, D], BF16) for q in range(3)]
        w2qb = [p2(f"w2qb{q}", [128, D], BF16) for q in range(3)]
        w3qb = [p2(f"w3qb{q}", [128, HF], BF16) for q in range(2)]
        xf32 = [p2(f"xf32{q}", [128, 512], F32) for q in range(2)]
        x1t = p2("x1t", [128, 512], F32)
        x2t = p2("x2t", [128, 512], F32)
        sig_sb = p2("sig_sb", [128, 512], BF16)
        rbc2 = p2("rbc2", [128, 512], F32)
        osb = [p2(f"osb{q}", [128, 512], F32) for q in range(2)]


        T_GRP = [0] * NGRP
        V_OUT = {}
        SS_RBC2 = [0] * NGRP
        SS_SIG_LAST = 0
        T_X12L = [0] * NGRP
        V_CMBL = [0] * NGRP
        T_S2KL = [0] * NGRP
        SS_SR2L = [0] * NGRP
        V_HIDL = [0] * NGRP

        for g in range(NGRP):
            pg = g & 1
            gc = g * 512
            if g >= 1:
                P.wait("gpsimd", ts, T_X12L[g - 1])
                P.wait("gpsimd", vs, V_CMBL[g - 1])
            SXB_N = P.dma("gpsimd", xbo.ap().rearrange("p (k t) -> p k t", k=KD),
                  xTo[:, gc:gc + 512].rearrange("(k p) t -> p k t", p=128), sxb[pg])
            if g == 0:
                P.wait("gpsimd", scc, CC_N)
            P.wait("tensor", sxb[pg], SXB_N)
            if g == 0:
                P.wait("tensor", ss, SS_EXP[NP - 1])
            else:
                P.wait("tensor", ss, SS_SR2L[g - 1])
            for k in range(KD):
                P.mm(ps4r, oneb_sb[:, :], xbo[:, k * 512:(k + 1) * 512],
                     start=(k == 0), stop=(k == KD - 1))
            P.wait("vector", sxb[pg], SXB_N)
            T_S2K = [0] * KD
            T_MS2 = 0
            for k in range(KD):
                if k >= 2:
                    P.wait("vector", ts, T_S2K[k - 2])
                elif g >= 1:
                    P.wait("vector", ts, T_S2KL[g - 1])
                vsq = P.vop("tt", out=xq2[k & 1][:, :],
                            in0=xbo[:, k * 512:(k + 1) * 512],
                            in1=xbo[:, k * 512:(k + 1) * 512],
                            op=ALU.mult, sem=vs)
                P.wait("tensor", vs, vsq)
                T_MS2 = P.mm(ps5r, oneb_sb[:, :], xq2[k & 1][:, :],
                             start=(k == 0), stop=(k == KD - 1), sem=ts)
                T_S2K[k] = T_MS2
            T_S2KL[g] = T_S2K[KD - 1]
            P.wait("scalar", ts, T_MS2)
            P.act(mrow[:, :], ps4r, AF.Copy, scale=1.0 / D, sem=ss)
            SS_SR2 = P.act(srow[:, :], ps5r, AF.Copy, scale=1.0 / D, sem=ss)
            SS_SR2L[g] = SS_SR2
            P.wait("vector", ss, SS_SR2)
            P.vop("tt", out=m2row[:, :], in0=mrow[:, :], in1=mrow[:, :],
                  op=ALU.mult, sem=vs)
            P.wait("vector", vs, vs.v)
            V_VAR2 = P.vop("tt", out=varrow[:, :], in0=srow[:, :], in1=m2row[:, :],
                           op=ALU.subtract, sem=vs)
            P.wait("scalar", vs, V_VAR2)
            P.act(lnrow[:, :], varrow[:, :], AF.Ln, bias=cv_sb[0:1, 1:2], sem=ss)
            P.wait("scalar", ss, ss.v)
            SS_R2 = P.act(rrow[:, :], lnrow[:, :], AF.Exp, scale=-0.5, sem=ss)
            P.wait("vector", ss, SS_R2)
            V_RM2 = P.vop("tt", out=rmrow[:, :], in0=rrow[:, :], in1=mrow[:, :],
                          op=ALU.mult, sem=vs)
            P.wait("tensor", ss, SS_R2)
            if g >= 1:
                P.wait("tensor", ss, SS_RBC2[g - 1])
                P.wait("tensor", vs, V_OUT[(g - 1, 15)])
            T_BC2 = P.mm(ps7[:, :], onerow_sb[:, :], rrow[:, :], True, True, sem=ts)
            P.wait("scalar", ts, T_BC2)
            SS_RBC2[g] = P.act(rbc2[:, :], ps7[:, :], AF.Copy, sem=ss)

            V_CMB = 0
            SG_N = 0
            for rt in range(4):
                if rt >= 1:
                    P.wait("gpsimd", vs, V_CMB)
                for c in range(4):
                    SG_N = P.dma("gpsimd", cmb[c][:, :],
                                 ag_out[rt * 128:(rt + 1) * 128,
                                        c * NTO + gc:c * NTO + gc + 512], sg[pg])
                P.wait("vector", sg[pg], SG_N)
                if rt == 0 and g == 0:
                    P.wait("vector", vs, VS_QZ)
                if rt == 0 and g >= 2:
                    P.wait("vector", ts, T_GRP[g - 2])
                if rt >= 1:
                    P.wait("vector", vs, V_CMB)
                P.vop("ts", out=gld[:, 0:512], in0=cmb[0][:, :],
                      s1=msel_sb[:, 0:1], op0=ALU.mult, sem=vs)
                P.wait("vector", vs, vs.v)
                P.vop("stt", out=gld[:, 512:1024], in0=cmb[1][:, :],
                      scalar=msel_sb[:, 1:2], in1=gld[:, 0:512],
                      op0=ALU.mult, op1=ALU.add, sem=vs)
                P.wait("vector", vs, vs.v)
                P.vop("stt", out=gld[:, 1024:1536], in0=cmb[2][:, :],
                      scalar=msel_sb[:, 2:3], in1=gld[:, 512:1024],
                      op0=ALU.mult, op1=ALU.add, sem=vs)
                P.wait("vector", vs, vs.v)
                V_CMB = P.vop("stt", out=yTg[pg][:, rt * 512:(rt + 1) * 512],
                              in0=cmb[3][:, :], scalar=msel_sb[:, 3:4],
                              in1=gld[:, 1024:1536], op0=ALU.mult, op1=ALU.add,
                              sem=vs)
            V_CMBL[g] = V_CMB
            if g == 0:
                P.wait("gpsimd", ts, TS_QZ)
                P.wait("gpsimd", vs, VS_QZ)
                P.wait("gpsimd", ss, SS_QZ)
                P.wait("gpsimd", sqw[0], sqw[0].v)
                P.wait("gpsimd", sqw[1], sqw[1].v)
                PREG2 = P.dma("gpsimd", wo_sb.ap().rearrange("p (k n) -> p k n", k=4),
                              wo_d.ap().rearrange("(k p) n -> p k n", p=128), spreg)

            V_HID = [0] * HB
            T_X12 = [0] * HB
            SS_CS2 = [0] * HB
            for hb in range(HB):
                ph = hb % 3
                psA = ps01 if (hb & 1) == 0 else ps23
                if hb >= 3:
                    P.wait("sync", ts, T_X12[hb - 3])
                elif g >= 1:
                    P.wait("sync", ts, T_GRP[g - 1])
                else:
                    P.wait("sync", sqw[0], sqw[0].v)
                    P.wait("sync", sqw[1], sqw[1].v)
                    P.wait("sync", ts, TS_QZ)
                    P.wait("sync", vs, VS_QZ)
                sqn = P.dma("sync", w1qb[ph][:, :], w1q_h[:, hb * D:(hb + 1) * D],
                            sq[ph])
                sqbn = P.dma("sync", w2qb[ph][:, :], w2q_h[:, hb * D:(hb + 1) * D],
                             sqb[ph])
                P.wait("tensor", sq[ph], sqn)
                P.wait("tensor", sqb[ph], sqbn)
                if hb >= 2:
                    P.wait("tensor", vs, V_HID[hb - 2])
                elif g >= 1:
                    P.wait("tensor", vs, V_HIDL[g - 1])
                if hb == 0:
                    P.wait("tensor", vs, V_RM2)
                    P.wait("tensor", vs, V_CMB)
                if g == 0:
                    if hb >= 1:
                        P.wait("tensor", ss, SS_CS2[hb - 1])
                    for k in range(KD):
                        P.mm(ps45[0:1, 0:128], oneb_sb[:, :],
                             w1qb[ph][:, k * 128:(k + 1) * 128],
                             start=(k == 0), stop=(k == KD - 1))
                    for k in range(KD):
                        T_CSC = P.mm(ps45[0:1, 512:640], oneb_sb[:, :],
                                     w2qb[ph][:, k * 128:(k + 1) * 128],
                                     start=(k == 0), stop=(k == KD - 1),
                                     sem=ts if k == KD - 1 else None)
                    P.wait("scalar", ts, T_CSC)
                    P.act(cs1[:, hb * 128:(hb + 1) * 128], ps45[0:1, 0:128],
                          AF.Copy, scale=-1.0, sem=ss)
                    SS_CS2[hb] = P.act(cs2[:, hb * 128:(hb + 1) * 128],
                                       ps45[0:1, 512:640], AF.Copy, scale=-1.0,
                                       sem=ss)
                for k in range(KD):
                    P.mm(psA[:, 0:512], w1qb[ph][:, k * 128:(k + 1) * 128],
                         xbo[:, k * 512:(k + 1) * 512], start=(k == 0), stop=False)
                if g == 0:
                    P.wait("tensor", ss, SS_CS2[hb])
                P.mm(psA[:, 0:512], cs1[:, hb * 128:(hb + 1) * 128], rmrow[:, :],
                     start=False, stop=True)
                for k in range(KD):
                    P.mm(psA[:, 512:1024], w2qb[ph][:, k * 128:(k + 1) * 128],
                         xbo[:, k * 512:(k + 1) * 512], start=(k == 0), stop=False)
                T_X12[hb] = P.mm(psA[:, 512:1024], cs2[:, hb * 128:(hb + 1) * 128],
                                 rmrow[:, :], start=False, stop=True, sem=ts)
                P.wait("vector", ts, T_X12[hb])
                if g == 0 and hb == 0:
                    P.wait("vector", ss, SS_RBC2[g])
                P.wait("vector", ss, SS_SIG_LAST)
                if hb >= 1:
                    P.wait("vector", vs, V_HID[hb - 1])
                P.vop("tt", out=x2t[:, :], in0=psA[:, 512:1024], in1=rbc2[:, :],
                      op=ALU.mult, sem=vs)
                V_X1 = P.vop("tt", out=x1t[:, :], in0=psA[:, 0:512], in1=rbc2[:, :],
                             op=ALU.mult, sem=vs)
                P.wait("scalar", vs, V_X1)
                SS_SIG_LAST = P.act(sig_sb[:, :], x1t[:, :], AF.Sigmoid, sem=ss)
                P.wait("vector", ss, SS_SIG_LAST)
                V_HID[hb] = P.vop("tt", out=hid[:, hb * 512:(hb + 1) * 512],
                                  in0=x2t[:, :], in1=sig_sb[:, :],
                                  op=ALU.mult, sem=vs)
            T_X12L[g] = T_X12[HB - 1]
            V_HIDL[g] = V_HID[HB - 1]

            T_OUTL = [0] * 16
            pend_store = [None]
            for dt in range(16):
                pw = dt & 1
                psO = ps6 if pw == 0 else ps7
                if dt >= 2:
                    P.wait("sync", ts, T_OUTL[dt - 2])
                elif g >= 1:
                    P.wait("sync", ts, T_GRP[g - 1])
                SW3_N = P.dma("sync", w3qb[pw][:, :], w3q_h[:, dt * HF:(dt + 1) * HF],
                              sw3[pw])
                if pend_store[0] is not None:
                    pend_store[0]()
                    pend_store[0] = None
                if dt >= 2:
                    P.wait("scalar", vs, V_OUT[(g, dt - 2)])
                elif g >= 1:
                    P.wait("scalar", vs, V_OUT[(g - 1, 14 + dt)])
                else:
                    P.wait("scalar", vs, VS_QZ)
                SXF_N = P.dma("scalar", xf32[pw][:, :],
                              xTo[dt * 128:(dt + 1) * 128, gc:gc + 512], sxf[pw])
                P.wait("tensor", sw3[pw], SW3_N)
                if dt == 0:
                    if g == 0:
                        P.wait("tensor", spreg, PREG2)
                    P.wait("tensor", vs, V_HID[HB - 1])
                if dt == 1:
                    P.wait("tensor", ss, SS_RBC2[g])
                if dt >= 2:
                    P.wait("tensor", vs, V_OUT[(g, dt - 2)])
                elif g >= 1:
                    P.wait("tensor", vs, V_OUT[(g - 1, 14 + dt)])
                for k in range(4):
                    P.mm(psO[:, :], wo_sb[:, k * D + dt * 128:k * D + (dt + 1) * 128],
                         yTg[pg][:, k * 512:(k + 1) * 512], start=(k == 0), stop=False)
                T_OUT = 0
                for k in range(HB):
                    T_OUT = P.mm(psO[:, :], w3qb[pw][:, k * 128:(k + 1) * 128],
                                 hid[:, k * 512:(k + 1) * 512], start=False,
                                 stop=(k == HB - 1), sem=ts if k == HB - 1 else None)
                T_OUTL[dt] = T_OUT
                nglob = g * 16 + dt
                P.wait("vector", ts, T_OUT)
                P.wait("vector", sxf[pw], SXF_N)
                if nglob >= 2:
                    P.wait("vector", sout[pw], 16 * (nglob // 2))
                V_OUT[(g, dt)] = P.vop("stt", out=osb[pw][:, :], in0=psO[:, :],
                                       scalar=bo_sb[:, dt:dt + 1], in1=xf32[pw][:, :],
                                       op0=ALU.add, op1=ALU.add, sem=vs)
                def _store(g=g, dt=dt, pw=pw, gc=gc):
                    P.wait("sync", vs, V_OUT[(g, dt)])
                    P.dma("sync", outT[dt * 128:(dt + 1) * 128, gc:gc + 512],
                          osb[pw][:, :], sout[pw])
                if dt == 15:
                    _store()
                else:
                    pend_store[0] = _store
            T_GRP[g] = T_OUTL[15]

        stk_p2.close()

        P.wait("sync", sout[0], sout[0].v)
        P.wait("sync", sout[1], sout[1].v)

        with nc.Block() as block:
            @block.sync
            def _(e):
                for f in P.q["sync"]:
                    f(e)

            @block.gpsimd
            def _(e):
                for f in P.q["gpsimd"]:
                    f(e)

            @block.tensor
            def _(e):
                for f in P.q["tensor"]:
                    f(e)

            @block.vector
            def _(e):
                for f in P.q["vector"]:
                    f(e)

            @block.scalar
            def _(e):
                for f in P.q["scalar"]:
                    f(e)

    return nc


def _constants(T):
    j = np.arange(64, dtype=np.float64)
    inv = 10000.0 ** (-2.0 * j / 128.0)
    t = np.arange(T, dtype=np.float64)
    emb = np.outer(t, inv)
    cos = np.cos(emb).T
    sin = np.sin(emb).T
    m1 = np.concatenate([cos, sin], axis=0).astype(np.float32)
    m2 = np.concatenate([-sin, cos], axis=0).astype(np.float32)
    A = np.zeros((128, 128), np.float32)
    Bm = np.zeros((128, 128), np.float32)
    for i in range(64):
        A[i, 2 * i] = 1.0
        A[64 + i, 2 * i] = 1.0
        Bm[i, 2 * i + 1] = 1.0
        Bm[64 + i, 2 * i + 1] = 1.0
    mskd = np.zeros((128, 2048), np.float32)
    sl = np.arange(128)
    ql = np.arange(128)
    for i in range(4):
        for qt in range(4):
            blockm = ((i * 128 + sl)[:, None] <= (qt * 128 + ql)[None, :])
            mskd[:, i * 512 + qt * 128:i * 512 + (qt + 1) * 128] = blockm
    return {
        "m1t": m1, "m2t": m2,
        "atm": np.ascontiguousarray(A.T).astype(BFNP),
        "btm": np.ascontiguousarray(Bm.T).astype(BFNP),
        "idb": np.eye(128, dtype=np.float32).astype(BFNP),
        "idf": np.eye(16, dtype=np.float32),
        "oneb": np.ones((128, 1), np.float32).astype(BFNP),
        "onef": np.ones((128, 1), np.float32),
        "onerow": np.ones((1, 128), np.float32),
        "halfrow": np.full((1, 128), 0.5, np.float32),
        "mskd": mskd.astype(BFNP),
        "cvals": np.repeat(np.array([[ESHIFT, LNEPS, QEPS, 0.0]], np.float32), 128, 0),
    }


def _stage_core(c, x, Wq, bq, Wk, bk, Wv, bv, Wo, bo, w1T, w2T, w3T, consts, T):
    b, h = c // 4, c % 4
    NTO = T // 4
    d = dict(consts)
    d["xT"] = np.ascontiguousarray(x[b].T)
    d["xTo"] = np.ascontiguousarray(x[b, h * NTO:(h + 1) * NTO].T)
    d["wq_s"] = np.ascontiguousarray(Wq[:, h * 512:(h + 1) * 512])
    d["wk_s"] = np.ascontiguousarray(Wk[:, h * 128:(h + 1) * 128])
    d["wv_s"] = np.ascontiguousarray(Wv[:, h * 128:(h + 1) * 128])
    d["bq4"] = np.ascontiguousarray(bq[h * 512:(h + 1) * 512].reshape(4, 128))
    d["bk1"] = np.ascontiguousarray(bk[h * 128:(h + 1) * 128].reshape(1, 128))
    d["bv1"] = np.ascontiguousarray(bv[h * 128:(h + 1) * 128].reshape(1, 128))
    d["wo"] = np.ascontiguousarray(Wo)
    d["bo16"] = np.ascontiguousarray(bo.reshape(16, 128))
    d["w1T"] = w1T
    d["w2T"] = w2T
    d["w3T"] = w3T
    ms = np.zeros((128, 4), np.float32)
    ms[:, h] = 1.0
    d["msel"] = ms
    return d


def stage_all(x, Wq, bq, Wk, bk, Wv, bv, Wo, bo, w1, w2, w3):
    T = x.shape[1]
    w1T = np.ascontiguousarray(np.asarray(w1, dtype=np.float32).T)
    w2T = np.ascontiguousarray(np.asarray(w2, dtype=np.float32).T)
    w3T = np.ascontiguousarray(np.asarray(w3, dtype=np.float32).T)
    consts = _constants(T)
    return [_stage_core(c, np.asarray(x, dtype=np.float32),
                        np.asarray(Wq, dtype=np.float32), np.asarray(bq, dtype=np.float32),
                        np.asarray(Wk, dtype=np.float32), np.asarray(bk, dtype=np.float32),
                        np.asarray(Wv, dtype=np.float32), np.asarray(bv, dtype=np.float32),
                        np.asarray(Wo, dtype=np.float32), np.asarray(bo, dtype=np.float32),
                        w1T, w2T, w3T, consts, T)
            for c in range(NCORE)]


def kernel(L0_attn_x, L0_attn_Wq, L0_attn_bq, L0_attn_Wk, L0_attn_bk,
           L0_attn_Wv, L0_attn_bv, L0_attn_Wo, L0_attn_bo,
           L0_ffn_w1, L0_ffn_w2, L0_ffn_w3):
    T = L0_attn_x.shape[1]
    HF = L0_ffn_w1.shape[0]
    in_maps = stage_all(L0_attn_x, L0_attn_Wq, L0_attn_bq, L0_attn_Wk, L0_attn_bk,
                        L0_attn_Wv, L0_attn_bv, L0_attn_Wo, L0_attn_bo,
                        L0_ffn_w1, L0_ffn_w2, L0_ffn_w3)
    nc = build_nc(T, HF)
    res = run_bass_kernel_spmd(nc, in_maps, core_ids=list(range(NCORE)))
    out = np.empty((2, T, D), np.float32)
    NTO = T // 4
    for c in range(NCORE):
        b, h = c // 4, c % 4
        out[b, h * NTO:(h + 1) * NTO, :] = np.asarray(res.results[c]["outT"]).T
    return out


# version 36
# speedup vs baseline: 1.0564x; 1.0086x over previous
import contextlib
import numpy as np
import ml_dtypes

import concourse.bass as bass
import concourse.mybir as mybir
from concourse.bass_utils import run_bass_kernel_spmd

F32 = mybir.dt.float32
BF16 = mybir.dt.bfloat16
AF = mybir.ActivationFunctionType
ALU = mybir.AluOpType
AX = mybir.AxisListType
BFNP = ml_dtypes.bfloat16

D = 2048
KD = D // 128
NCORE = 8
SCALE = float(np.float64(1.0) / np.sqrt(np.float64(128.0)))
ESHIFT = -20.0
LNEPS = 1e-5
QEPS = 1e-8


class Sem:
    def __init__(self, name):
        self.name = name
        self.v = 0
        self.h = None

    def inc(self, n=1):
        self.v += n
        return self.v


class Plan:
    def __init__(self):
        self.q = {"sync": [], "gpsimd": [], "tensor": [], "vector": [], "scalar": []}

    def wait(self, eng, sem, val):
        if val and val > 0:
            self.q[eng].append(lambda e, s=sem, v=val: e.wait_ge(s.h, v))

    def mm(self, out, lhsT, rhs, start, stop, sem=None):
        v = sem.inc(1) if sem is not None else None

        def f(e, out=out, lhsT=lhsT, rhs=rhs, start=start, stop=stop, sem=sem):
            r = e.matmul(out, lhsT, rhs, start=start, stop=stop)
            if sem is not None:
                r.then_inc(sem.h, 1)
        self.q["tensor"].append(f)
        return v

    def tr(self, out, in_, ident, sem=None):
        v = sem.inc(1) if sem is not None else None

        def f(e, out=out, in_=in_, ident=ident, sem=sem):
            r = e.transpose(out, in_, ident)
            if sem is not None:
                r.then_inc(sem.h, 1)
        self.q["tensor"].append(f)
        return v

    def act(self, out, in_, func, bias=None, scale=None, sem=None):
        v = sem.inc(1) if sem is not None else None

        def f(e, out=out, in_=in_, func=func, bias=bias, scale=scale, sem=sem):
            kw = {}
            if bias is not None:
                kw["bias"] = bias
            if scale is not None:
                kw["scale"] = scale
            r = e.activation(out, in_, func, **kw)
            if sem is not None:
                r.then_inc(sem.h, 1)
        self.q["scalar"].append(f)
        return v

    def vop(self, kind, sem=None, **kw):
        v = sem.inc(1) if sem is not None else None

        def f(e, kind=kind, kw=kw, sem=sem):
            if kind == "tt":
                r = e.tensor_tensor(kw["out"], kw["in0"], kw["in1"], kw["op"])
            elif kind == "ts":
                r = e.tensor_scalar(kw["out"], kw["in0"], kw["s1"], None,
                                    kw["op0"])
            elif kind == "ts2":
                r = e.tensor_scalar(kw["out"], kw["in0"], kw["s1"], kw["s2"],
                                    kw["op0"], kw["op1"])
            elif kind == "stt":
                r = e.scalar_tensor_tensor(kw["out"], kw["in0"], kw["scalar"],
                                           kw["in1"], kw["op0"], kw["op1"])
            elif kind == "red":
                r = e.reduce_sum(kw["out"], kw["in_"], axis=AX.X,
                                 apply_absolute_value=kw.get("absv", False))
            elif kind == "recip":
                r = e.reciprocal(kw["out"], kw["in_"])
            else:
                raise ValueError(kind)
            if sem is not None:
                r.then_inc(sem.h, 1)
        self.q["vector"].append(f)
        return v

    def pop_red(self, out, in_, absv, sem=None):
        v = sem.inc(1) if sem is not None else None

        def f(e, out=out, in_=in_, absv=absv, sem=sem):
            r = e.reduce_sum(out, in_, axis=AX.XYZWC, apply_absolute_value=absv)
            if sem is not None:
                r.then_inc(sem.h, 1)
        self.q["gpsimd"].append(f)
        return v

    def dma(self, eng, out, in_, sem):
        v = sem.inc(16)

        def f(e, out=out, in_=in_, sem=sem):
            e.dma_start(out=out, in_=in_).then_inc(sem.h, 16)
        self.q[eng].append(f)
        return v

    def coll(self, kind, op, groups, ins, outs, sem):
        v = sem.inc(1)

        def f(e, kind=kind, op=op, groups=groups, ins=ins, outs=outs, sem=sem):
            e.collective_compute(kind, op, replica_groups=groups, ins=ins,
                                 outs=outs).then_inc(sem.h, 1)
        self.q["gpsimd"].append(f)
        return v


def build_nc(T, HF):
    NBLK = T // 512
    NTO = T // 4
    NGRP = NTO // 512
    HB = HF // 128
    WMX = max(D, HF)
    assert T % 2048 == 0 and HF % 128 == 0

    nc = bass.Bass("TRN2", target_bir_lowering=False, debug=False)

    xT = nc.dram_tensor("xT", [D, T], F32, kind="ExternalInput")
    xTo = nc.dram_tensor("xTo", [D, NTO], F32, kind="ExternalInput")
    wq_s = nc.dram_tensor("wq_s", [D, 512], F32, kind="ExternalInput")
    wk_s = nc.dram_tensor("wk_s", [D, 128], F32, kind="ExternalInput")
    wv_s = nc.dram_tensor("wv_s", [D, 128], F32, kind="ExternalInput")
    bq4 = nc.dram_tensor("bq4", [4, 128], F32, kind="ExternalInput")
    bk1 = nc.dram_tensor("bk1", [1, 128], F32, kind="ExternalInput")
    bv1 = nc.dram_tensor("bv1", [1, 128], F32, kind="ExternalInput")
    wo_d = nc.dram_tensor("wo", [512, D], F32, kind="ExternalInput")
    bo16 = nc.dram_tensor("bo16", [16, 128], F32, kind="ExternalInput")
    w1T = nc.dram_tensor("w1T", [D, HF], F32, kind="ExternalInput")
    w2T = nc.dram_tensor("w2T", [D, HF], F32, kind="ExternalInput")
    w3T = nc.dram_tensor("w3T", [HF, D], F32, kind="ExternalInput")
    m1t = nc.dram_tensor("m1t", [128, T], F32, kind="ExternalInput")
    m2t = nc.dram_tensor("m2t", [128, T], F32, kind="ExternalInput")
    atm = nc.dram_tensor("atm", [128, 128], BF16, kind="ExternalInput")
    btm = nc.dram_tensor("btm", [128, 128], BF16, kind="ExternalInput")
    idb = nc.dram_tensor("idb", [128, 128], BF16, kind="ExternalInput")
    idf = nc.dram_tensor("idf", [16, 16], F32, kind="ExternalInput")
    oneb = nc.dram_tensor("oneb", [128, 1], BF16, kind="ExternalInput")
    onef = nc.dram_tensor("onef", [128, 1], F32, kind="ExternalInput")
    onerow = nc.dram_tensor("onerow", [1, 128], F32, kind="ExternalInput")
    halfrow = nc.dram_tensor("halfrow", [1, 128], F32, kind="ExternalInput")
    mskd = nc.dram_tensor("mskd", [128, 2048], BF16, kind="ExternalInput")
    msel = nc.dram_tensor("msel", [128, 4], F32, kind="ExternalInput")
    cvals = nc.dram_tensor("cvals", [128, 4], F32, kind="ExternalInput")

    outT = nc.dram_tensor("outT", [D, NTO], F32, kind="ExternalOutput")

    ag_in = nc.dram_tensor("ag_in", [128, T], BF16)
    ag_out = nc.dram_tensor("ag_out", [512, T], BF16)
    w1q_h = nc.dram_tensor("w1q_h", [128, HB * D], BF16)
    w2q_h = nc.dram_tensor("w2q_h", [128, HB * D], BF16)
    w3q_h = nc.dram_tensor("w3q_h", [128, 16 * HF], BF16)

    P = Plan()
    NW = 2 * HB + 16 * ((HB + 15) // 16)

    with contextlib.ExitStack() as G:
        sems = {}
        for nm in ["tsem", "vsem", "ssem", "spre", "spreg", "sx0", "sx1", "sx2",
                   "sy0", "sy1", "scc", "psem", "swg0", "swg1", "swg2", "swg3", "sqz0", "sqz1", "sqz2", "sqz3",
                   "sqw0", "sqw1", "sq0", "sq1", "sq2", "sqb0", "sqb1", "sqb2", "sw30", "sw31", "sxf0", "sxf1",
                   "sg0", "sg1", "sxb0", "sxb1", "sout0", "sout1"]:
            s = Sem(nm)
            s.h = G.enter_context(nc.semaphore(nm))
            sems[nm] = s
        ts, vs, ss = sems["tsem"], sems["vsem"], sems["ssem"]
        sx = [sems["sx0"], sems["sx1"], sems["sx2"]]
        swg = [sems[f"swg{i}"] for i in range(4)]
        sqz = [sems[f"sqz{i}"] for i in range(4)]
        psm = sems["psem"]
        sq = [sems["sq0"], sems["sq1"], sems["sq2"]]
        sqb = [sems["sqb0"], sems["sqb1"], sems["sqb2"]]
        sw3 = [sems["sw30"], sems["sw31"]]
        sxf = [sems["sxf0"], sems["sxf1"]]
        sg = [sems["sg0"], sems["sg1"]]
        sxb = [sems["sxb0"], sems["sxb1"]]
        spre, spreg = sems["spre"], sems["spreg"]
        scc = sems["scc"]
        sy = [sems["sy0"], sems["sy1"]]
        sqw = [sems["sqw0"], sems["sqw1"]]
        sout = [sems["sout0"], sems["sout1"]]

        ps01 = G.enter_context(nc.psum_tensor("ps01", [128, 1024], F32))
        ps23 = G.enter_context(nc.psum_tensor("ps23", [128, 1024], F32))
        ps45 = G.enter_context(nc.psum_tensor("ps45", [128, 1024], F32))
        ps6 = G.enter_context(nc.psum_tensor("ps6", [128, 512], F32))
        ps7 = G.enter_context(nc.psum_tensor("ps7", [128, 512], F32))
        ps4r = ps45[0:1, 0:512]
        ps5r = ps45[0:1, 512:1024]

        sb = lambda n, shp, dt: G.enter_context(nc.sbuf_tensor(n, shp, dt))
        wq_eff = sb("wq_eff", [128, KD * 128], BF16)
        wk_sb = sb("wk_sb", [128, KD * 128], BF16)
        wv_sb = sb("wv_sb", [128, KD * 128], BF16)
        atm_sb = sb("atm_sb", [128, 128], BF16)
        btm_sb = sb("btm_sb", [128, 128], BF16)
        idb_sb = sb("idb_sb", [128, 128], BF16)
        idf_sb = sb("idf_sb", [16, 16], F32)
        oneb_sb = sb("oneb_sb", [128, 1], BF16)
        onef_sb = sb("onef_sb", [128, 1], F32)
        onerow_sb = sb("onerow_sb", [1, 128], F32)
        halfrow_sb = sb("halfrow_sb", [1, 128], F32)
        msel_sb = sb("msel_sb", [128, 4], F32)
        cv_sb = sb("cv_sb", [128, 4], F32)
        bq_col = sb("bq_col", [128, 1], F32)
        bk_col = sb("bk_col", [128, 1], F32)
        bv_col = sb("bv_col", [128, 1], F32)
        bo_sb = sb("bo_sb", [128, 16], F32)
        csq = sb("csq", [1, 128], BF16)
        csk = sb("csk", [1, 128], BF16)
        csv = sb("csv", [1, 128], BF16)
        mrow = sb("mrow", [1, 512], F32)
        srow = sb("srow", [1, 512], F32)
        m2row = sb("m2row", [1, 512], F32)
        varrow = sb("varrow", [1, 512], F32)
        lnrow = m2row
        rrow = sb("rrow", [1, 512], F32)
        rmrow = sb("rmrow", [1, 512], BF16)
        recrow = sb("recrow", [1, 512], F32)
        gred = sb("gred", [128, NW], F32)
        gcol = sb("gcol", [128, 3], F32)
        ge_row = sb("ge_row", [1, 1], F32)
        bpc = [sb(f"bp{m}", [128, 1], F32) for m in range(3)]
        bnc = [sb(f"bn{m}", [128, 1], F32) for m in range(3)]

        with contextlib.ExitStack() as PR:
            wq_st = PR.enter_context(nc.sbuf_tensor("wq_st", [128, KD * 512], F32))
            bq4_sb = PR.enter_context(nc.sbuf_tensor("bq4_sb", [4, 128], F32))
            bk1_sb = PR.enter_context(nc.sbuf_tensor("bk1_sb", [1, 128], F32))
            bv1_sb = PR.enter_context(nc.sbuf_tensor("bv1_sb", [1, 128], F32))
            bo16_sb = PR.enter_context(nc.sbuf_tensor("bo16_sb", [16, 128], F32))
            tqa = PR.enter_context(nc.sbuf_tensor("tqa", [128, 128], F32))
            tqb = PR.enter_context(nc.sbuf_tensor("tqb", [128, 128], F32))
            PRE_N = 0
            for dst, src in [(wq_st.ap().rearrange("p (k i) -> p k i", k=KD), wq_s.ap().rearrange("(k p) i -> p k i", p=128)),
                             (atm_sb[:, :], atm[:, :]), (btm_sb[:, :], btm[:, :]),
                             (idb_sb[:, :], idb[:, :]), (idf_sb[:, :], idf[:, :]),
                             (oneb_sb[:, :], oneb[:, :]), (onef_sb[:, :], onef[:, :]),
                             (onerow_sb[:, :], onerow[:, :]), (halfrow_sb[:, :], halfrow[:, :]),
                             (msel_sb[:, :], msel[:, :]),
                             (cv_sb[:, :], cvals[:, :]),
                             (bq4_sb[:, :], bq4[:, :]), (bk1_sb[:, :], bk1[:, :]),
                             (bv1_sb[:, :], bv1[:, :]), (bo16_sb[:, :], bo16[:, :])]:
                PRE_N = P.dma("sync", dst, src, spre)
            PREG_N = P.dma("gpsimd", wk_sb.ap().rearrange("p (k i) -> p k i", k=KD),
                           wk_s.ap().rearrange("(k p) i -> p k i", p=128), spreg)
            PREG_N = P.dma("gpsimd", wv_sb.ap().rearrange("p (k i) -> p k i", k=KD),
                           wv_s.ap().rearrange("(k p) i -> p k i", p=128), spreg)

            P.wait("vector", spre, PRE_N)
            V_WQE = 0
            for k in range(KD):
                c = k * 512
                if k > 0:
                    P.wait("vector", vs, V_WQE)
                P.vop("tt", out=tqa[:, :], in0=wq_st[:, c:c + 128],
                      in1=wq_st[:, c + 128:c + 256], op=ALU.add)
                P.vop("tt", out=tqb[:, :], in0=wq_st[:, c + 256:c + 384],
                      in1=wq_st[:, c + 384:c + 512], op=ALU.add, sem=vs)
                P.wait("vector", vs, vs.v)
                V_WQE = P.vop("tt", out=wq_eff[:, k * 128:(k + 1) * 128],
                              in0=tqa[:, :], in1=tqb[:, :], op=ALU.add, sem=vs)

            P.wait("tensor", vs, V_WQE)
            P.wait("tensor", spre, PRE_N)
            P.wait("tensor", spreg, PREG_N)
            T_CSQ = 0
            for k in range(KD):
                T_CSQ = P.mm(ps45[0:1, 0:128], oneb_sb[:, :],
                             wq_eff[:, k * 128:(k + 1) * 128],
                             start=(k == 0), stop=(k == KD - 1),
                             sem=ts if k == KD - 1 else None)
            for k in range(KD):
                P.mm(ps6[0:1, 0:128], oneb_sb[:, :], wk_sb[:, k * 128:(k + 1) * 128],
                     start=(k == 0), stop=(k == KD - 1))
            T_CSKV = 0
            for k in range(KD):
                T_CSKV = P.mm(ps7[0:1, 0:128], oneb_sb[:, :],
                              wv_sb[:, k * 128:(k + 1) * 128],
                              start=(k == 0), stop=(k == KD - 1),
                              sem=ts if k == KD - 1 else None)
            P.wait("scalar", ts, T_CSQ)
            P.act(csq[:, :], ps45[0:1, 0:128], AF.Copy, scale=-1.0, sem=ss)
            P.wait("scalar", ts, T_CSKV)
            P.act(csk[:, :], ps6[0:1, 0:128], AF.Copy, scale=-1.0, sem=ss)
            SS_CS = P.act(csv[:, :], ps7[0:1, 0:128], AF.Copy, scale=-1.0, sem=ss)
            P.wait("tensor", ss, SS_CS)
            P.mm(ps45[:, 0:1], bq4_sb[:, :], onef_sb[0:4, 0:1], True, True)
            P.mm(ps45[:, 512:513], bk1_sb[:, :], onef_sb[0:1, 0:1], True, True)
            P.mm(ps6[:, 0:1], bv1_sb[:, :], onef_sb[0:1, 0:1], True, True)
            T_BIAS = P.tr(ps7[:, 0:16], bo16_sb[:, :], idf_sb[:, :], sem=ts)
            P.wait("scalar", ts, T_BIAS)
            P.act(bq_col[:, :], ps45[:, 0:1], AF.Copy, sem=ss)
            P.act(bk_col[:, :], ps45[:, 512:513], AF.Copy, sem=ss)
            P.act(bv_col[:, :], ps6[:, 0:1], AF.Copy, sem=ss)
            SS_PRO = P.act(bo_sb[:, :], ps7[:, 0:16], AF.Copy, sem=ss)

        stk_p1c = contextlib.ExitStack()
        qT = stk_p1c.enter_context(nc.sbuf_tensor("qT", [128, T], BF16))
        kT = stk_p1c.enter_context(nc.sbuf_tensor("kT", [128, T], BF16))
        vS = stk_p1c.enter_context(nc.sbuf_tensor("vS", [128, T], BF16))

        stk_p1a = contextlib.ExitStack()
        xb = [stk_p1a.enter_context(nc.sbuf_tensor(f"xb{p}", [128, KD * 512], BF16))
              for p in range(3)]
        xsqh = stk_p1a.enter_context(nc.sbuf_tensor("xsqh", [128, 8 * 512], BF16))
        m1s = [stk_p1a.enter_context(nc.sbuf_tensor(f"m1s{p}", [128, 512], F32))
               for p in range(3)]
        m2s = [stk_p1a.enter_context(nc.sbuf_tensor(f"m2s{p}", [128, 512], F32))
               for p in range(3)]
        tq_sb = stk_p1a.enter_context(nc.sbuf_tensor("tq_sb", [128, 512], F32))
        tk_sb = stk_p1a.enter_context(nc.sbuf_tensor("tk_sb", [128, 512], F32))
        tv_sb = stk_p1a.enter_context(nc.sbuf_tensor("tv_sb", [128, 512], F32))
        rbc_sb = stk_p1a.enter_context(nc.sbuf_tensor("rbc_sb", [128, 512], F32))
        t1_sb = rbc_sb
        t2_sb = tq_sb
        uq_sb = stk_p1a.enter_context(nc.sbuf_tensor("uq_sb", [128, 512], BF16))
        uk_sb = stk_p1a.enter_context(nc.sbuf_tensor("uk_sb", [128, 512], BF16))
        vr_sb = stk_p1a.enter_context(nc.sbuf_tensor("vr_sb", [128, 512], BF16))

        T_XB = [0] * NBLK
        T_MS = [0] * NBLK
        T_SK = {}
        T_SKL = [0] * NBLK
        T_RANK = [0] * NBLK
        SS_SROW = [0] * NBLK
        SS_RBC = [0] * NBLK
        SS_VSB = [0] * NBLK
        V_SQ = [0] * NBLK
        V_QEV = [0] * NBLK
        V_KEV = [0] * NBLK
        V_TV = [0] * NBLK
        V_QT2 = [0] * NBLK
        V_KT2 = [0] * NBLK
        V_KROT = [0] * NBLK

        for i in range(NBLK):
            p = i % 3
            c0 = i * 512
            if i >= 3:
                P.wait("gpsimd", ts, T_XB[i - 3])
                P.wait("gpsimd", vs, V_KROT[i - 3])
            else:
                P.wait("gpsimd", ss, SS_PRO)
            P.dma("gpsimd", xb[p].ap().rearrange("p (k t) -> p k t", k=KD),
                  xT[:, c0:c0 + 512].rearrange("(k p) t -> p k t", p=128), sx[p])
            P.dma("gpsimd", m1s[p][:, :], m1t[:, c0:c0 + 512], sx[p])
            SX_N = P.dma("gpsimd", m2s[p][:, :], m2t[:, c0:c0 + 512], sx[p])

            assert SX_N == 48 * (i // 3 + 1)
            if i == 0:
                P.wait("tensor", sx[p], SX_N)
                P.wait("tensor", ss, SS_PRO)
                for k in range(KD):
                    P.mm(ps4r, oneb_sb[:, :], xb[p][:, k * 512:(k + 1) * 512],
                         start=(k == 0), stop=(k == KD - 1))
            P.wait("tensor", sx[p], SX_N)
            if i >= 1:
                P.wait("tensor", vs, V_QEV[i - 1])
            for k in range(KD):
                P.mm(ps01[:, 0:512], wq_eff[:, k * 128:(k + 1) * 128],
                     xb[p][:, k * 512:(k + 1) * 512], start=(k == 0), stop=False)

            P.wait("vector", sx[p], SX_N)
            if i >= 1:
                P.wait("vector", ts, T_MS[i - 1])
            V_FA = 0
            for k in range(8):
                V_FA = P.vop("tt", out=xsqh[:, k * 512:(k + 1) * 512],
                             in0=xb[p][:, k * 512:(k + 1) * 512],
                             in1=xb[p][:, k * 512:(k + 1) * 512],
                             op=ALU.mult, sem=vs if k == 7 else None)
            P.wait("tensor", vs, V_FA)
            T_SA = 0
            for k in range(8):
                T_SA = P.mm(ps5r, oneb_sb[:, :], xsqh[:, k * 512:(k + 1) * 512],
                            start=(k == 0), stop=False,
                            sem=ts if k == 7 else None)

            if i >= 1:
                P.wait("tensor", vs, V_KEV[i - 1])
            for k in range(KD):
                P.mm(ps01[:, 512:1024], wk_sb[:, k * 128:(k + 1) * 128],
                     xb[p][:, k * 512:(k + 1) * 512], start=(k == 0), stop=False)

            P.wait("vector", ts, T_SA)
            V_FB = 0
            for k in range(8, KD):
                V_FB = P.vop("tt", out=xsqh[:, (k - 8) * 512:(k - 7) * 512],
                             in0=xb[p][:, k * 512:(k + 1) * 512],
                             in1=xb[p][:, k * 512:(k + 1) * 512],
                             op=ALU.mult, sem=vs if k == KD - 1 else None)
            P.wait("tensor", vs, V_FB)
            for k in range(8, KD):
                T_MS[i] = P.mm(ps5r, oneb_sb[:, :],
                               xsqh[:, (k - 8) * 512:(k - 7) * 512],
                               start=False, stop=(k == KD - 1),
                               sem=ts if k == KD - 1 else None)

            P.wait("scalar", ts, T_MS[i])
            P.act(mrow[:, :], ps4r, AF.Copy, scale=1.0 / D, sem=ss)
            SS_SROW[i] = P.act(srow[:, :], ps5r, AF.Copy, scale=1.0 / D, sem=ss)
            P.wait("vector", ss, SS_SROW[i])
            P.vop("tt", out=m2row[:, :], in0=mrow[:, :], in1=mrow[:, :],
                  op=ALU.mult, sem=vs)
            P.wait("vector", vs, vs.v)
            V_VAR = P.vop("tt", out=varrow[:, :], in0=srow[:, :], in1=m2row[:, :],
                          op=ALU.subtract, sem=vs)
            P.wait("scalar", vs, V_VAR)
            P.act(lnrow[:, :], varrow[:, :], AF.Ln, bias=cv_sb[0:1, 1:2], sem=ss)
            P.wait("scalar", ss, ss.v)
            SS_R = P.act(rrow[:, :], lnrow[:, :], AF.Exp, scale=-0.5, sem=ss)
            P.wait("vector", ss, SS_R)
            V_RM = P.vop("tt", out=rmrow[:, :], in0=rrow[:, :], in1=mrow[:, :],
                         op=ALU.mult, sem=vs)

            if i >= 1:
                P.wait("tensor", vs, V_TV[i - 1])
            for k in range(KD):
                T_XB[i] = P.mm(ps23[:, 0:512], wv_sb[:, k * 128:(k + 1) * 128],
                               xb[p][:, k * 512:(k + 1) * 512],
                               start=(k == 0), stop=False,
                               sem=ts if k == KD - 1 else None)
            if i >= 1:
                P.wait("tensor", ss, SS_RBC[i - 1])
            P.wait("tensor", ss, SS_R)
            P.mm(ps7[:, :], onerow_sb[:, :], rrow[:, :], True, True)
            P.wait("tensor", vs, V_RM)
            P.mm(ps01[:, 0:512], csq[:, :], rmrow[:, :], start=False, stop=True)
            P.mm(ps01[:, 512:1024], csk[:, :], rmrow[:, :], start=False, stop=True)
            T_RANK[i] = P.mm(ps23[:, 0:512], csv[:, :], rmrow[:, :],
                             start=False, stop=True, sem=ts)

            def m_next(k0, k1, i=i):
                if i + 1 >= NBLK:
                    return
                p2 = (i + 1) % 3
                if k0 == 0:
                    P.wait("tensor", sx[p2], 48 * ((i + 1) // 3 + 1))
                    P.wait("tensor", ss, SS_SROW[i])
                for k in range(k0, k1):
                    P.mm(ps4r, oneb_sb[:, :], xb[p2][:, k * 512:(k + 1) * 512],
                         start=(k == 0), stop=(k == KD - 1))
            m_next(0, 8)

            P.wait("scalar", ts, T_RANK[i])
            SS_RBC[i] = P.act(rbc_sb[:, :], ps7[:, :], AF.Copy, sem=ss)

            P.wait("vector", ts, T_RANK[i])
            P.wait("vector", ss, SS_RBC[i])
            V_QEV[i] = P.vop("tt", out=tq_sb[:, :], in0=ps01[:, 0:512],
                             in1=rbc_sb[:, :], op=ALU.mult, sem=vs)
            V_KEV[i] = P.vop("tt", out=tk_sb[:, :], in0=ps01[:, 512:1024],
                             in1=rbc_sb[:, :], op=ALU.mult, sem=vs)
            V_TV[i] = P.vop("tt", out=tv_sb[:, :], in0=ps23[:, 0:512],
                            in1=rbc_sb[:, :], op=ALU.mult, sem=vs)
            P.wait("scalar", vs, V_QEV[i])
            SS_UQ = P.act(uq_sb[:, :], tq_sb[:, :], AF.Identity, bias=bq_col[:, 0:1], sem=ss)
            P.wait("scalar", vs, V_KEV[i])
            SS_UK = P.act(uk_sb[:, :], tk_sb[:, :], AF.Identity, bias=bk_col[:, 0:1], sem=ss)
            P.wait("scalar", vs, V_TV[i])
            SS_VR = P.act(vr_sb[:, :], tv_sb[:, :], AF.Identity, bias=bv_col[:, 0:1], sem=ss)

            P.wait("tensor", ss, SS_UQ)
            if i >= 1:
                P.wait("tensor", vs, V_KT2[i - 1])
                P.wait("tensor", ss, SS_VSB[i - 1])
            P.mm(ps23[:, 512:1024], atm_sb[:, :], uq_sb[:, :], True, True)
            T_QAB = P.mm(ps6[:, :], btm_sb[:, :], uq_sb[:, :], True, True, sem=ts)
            m_next(8, 12)
            P.wait("vector", ts, T_QAB)
            P.wait("vector", vs, V_TV[i])
            P.vop("tt", out=t1_sb[:, :], in0=ps23[:, 512:1024], in1=m1s[p][:, :],
                  op=ALU.mult, sem=vs)
            V_QT2[i] = P.vop("tt", out=t2_sb[:, :], in0=ps6[:, :], in1=m2s[p][:, :],
                             op=ALU.mult, sem=vs)
            P.wait("vector", vs, V_QT2[i])
            V_QROT = P.vop("tt", out=qT[:, c0:c0 + 512], in0=t1_sb[:, :],
                           in1=t2_sb[:, :], op=ALU.add, sem=vs)

            P.wait("tensor", ss, SS_UK)
            P.wait("tensor", vs, V_QT2[i])
            P.mm(ps23[:, 512:1024], atm_sb[:, :], uk_sb[:, :], True, True)
            T_KAB = P.mm(ps6[:, :], btm_sb[:, :], uk_sb[:, :], True, True, sem=ts)
            m_next(12, KD)
            P.wait("vector", ts, T_KAB)
            P.wait("vector", vs, V_QROT)
            P.vop("tt", out=t1_sb[:, :], in0=ps23[:, 512:1024], in1=m1s[p][:, :],
                  op=ALU.mult, sem=vs)
            V_KT2[i] = P.vop("tt", out=t2_sb[:, :], in0=ps6[:, :], in1=m2s[p][:, :],
                             op=ALU.mult, sem=vs)
            P.wait("vector", vs, V_KT2[i])
            V_KROT[i] = P.vop("tt", out=kT[:, c0:c0 + 512], in0=t1_sb[:, :],
                              in1=t2_sb[:, :], op=ALU.add, sem=vs)

            P.wait("tensor", ss, SS_VR)
            P.wait("tensor", vs, V_KT2[i])
            ps6b = ps6.ap().bitcast(BF16)
            T_VTR = 0
            for j in range(4):
                T_VTR = P.tr(ps6b[:, j * 128:(j + 1) * 128],
                             vr_sb[:, j * 128:(j + 1) * 128], idb_sb[:, :],
                             sem=ts if j == 3 else None)
            P.wait("scalar", ts, T_VTR)
            for j in range(4):
                SS_VSB[i] = P.act(vS[:, (4 * i + j) * 128:(4 * i + j + 1) * 128],
                                  ps6b[:, j * 128:(j + 1) * 128], AF.Copy,
                                  sem=ss if j == 3 else None)

        stk_p1a.close()

        stk_p1b = contextlib.ExitStack()
        attT = [stk_p1b.enter_context(nc.sbuf_tensor(f"attT{p}", [128, 1024], BF16))
                for p in range(6)]
        attTm = stk_p1b.enter_context(nc.sbuf_tensor("attTm", [128, 1024], BF16))
        ybuf = [stk_p1b.enter_context(nc.sbuf_tensor(f"ybuf{p}", [128, 512], BF16))
                for p in range(2)]
        bcc_sb = stk_p1b.enter_context(nc.sbuf_tensor("bcc_sb", [128, 512], F32))
        mskd_sb = stk_p1b.enter_context(nc.sbuf_tensor("mskd_sb", [128, 2048], BF16))
        wstr = [stk_p1b.enter_context(nc.sbuf_tensor(f"wstr{p}", [128, 2048], F32))
                for p in range(4)]
        sgA = [stk_p1b.enter_context(nc.sbuf_tensor(f"sgA{p}", [128, 2048], BF16))
               for p in range(2)]
        sgB = [stk_p1b.enter_context(nc.sbuf_tensor(f"sgB{p}", [128, 2048], BF16))
               for p in range(2)]
        qbk = [stk_p1b.enter_context(nc.sbuf_tensor(f"qbk{p}", [128, 2048], BF16))
               for p in range(2)]

        P_BAR_T = ts.v
        P_BAR_V = vs.v
        P_BAR_S = ss.v
        P.wait("sync", ts, P_BAR_T)
        P.wait("sync", vs, P_BAR_V)
        P.wait("sync", ss, P_BAR_S)
        PRE_MSK = P.dma("sync", mskd_sb[:, :], mskd[:, :], spre)
        P.wait("vector", spre, PRE_MSK)

        units = []
        for hb in range(HB):
            units.append((0, w1T[:, hb * 128:(hb + 1) * 128]
                          .rearrange("(k p) h -> p k h", p=128),
                          w1q_h[:, hb * D:(hb + 1) * D], D))
        for hb in range(HB):
            units.append((1, w2T[:, hb * 128:(hb + 1) * 128]
                          .rearrange("(k p) h -> p k h", p=128),
                          w2q_h[:, hb * D:(hb + 1) * D], D))
        for dt in range(16):
            w3v = w3T[:, dt * 128:(dt + 1) * 128].rearrange("(k p) d_ -> p k d_", p=128)
            for k0 in range(0, HB, 16):
                k1 = min(k0 + 16, HB)
                units.append((2, w3v[:, k0:k1, :],
                              w3q_h[:, dt * HF + k0 * 128:dt * HF + k1 * 128],
                              (k1 - k0) * 128))
        assert len(units) == NW

        G_GRED = [0] * NW
        GSWN = [0] * NW
        g_state = {"iss": 0, "drn": 0}

        def gamma_issue_one(j):
            pj = j % 4
            if j >= 4:
                P.wait("sync", vs, G_GRED[j - 4])
            m, gsrc, _, gw = units[j]
            GSWN[j] = P.dma("sync", wstr[pj][:, 0:gw]
                            .rearrange("p (k h) -> p k h", h=128), gsrc, swg[pj])

        def gamma_drain_one(j):
            if g_state["drn"] != j:
                return
            g_state["drn"] = j + 1
            pj = j % 4
            m, gsrc, _, gw = units[j]
            P.wait("vector", swg[pj], GSWN[j])
            G_GRED[j] = P.vop("red", out=gred[:, j:j + 1], in_=wstr[pj][:, 0:gw],
                              absv=True, sem=vs)

        def emit_gamma(n, final=False):
            for _ in range(n):
                j = g_state["iss"]
                if j >= NW:
                    break
                gamma_issue_one(j)
                g_state["iss"] = j + 1
                if j >= 3:
                    gamma_drain_one(j - 3)
            if final:
                for k in range(g_state["drn"], NW):
                    gamma_drain_one(k)

        V_SG = [0] * NW
        V_QB = [0] * NW
        QSQN = [0] * NW
        q_state = {"iss": 0, "drn": 0, "stored": 0, "V_GF": 0, "SS_GDONE": 0}

        def qz_issue_one(u):
            pu = u % 4
            m, src, dst, wdt = units[u]
            if u < 4:
                P.wait("sync", vs, G_GRED[NW - 1])
            else:
                P.wait("sync", vs, V_SG[u - 4])
            QSQN[u] = P.dma("sync", wstr[pu][:, 0:wdt]
                            .rearrange("p (k h) -> p k h", h=128), src, sqz[pu])

        def qz_drain_one(u):
            if q_state["drn"] != u:
                return
            q_state["drn"] = u + 1
            pu = u % 4
            ps2 = u & 1
            m, src, dst, wdt = units[u]
            P.wait("vector", sqz[pu], QSQN[u])
            if u == 0:
                P.wait("vector", ss, q_state["SS_GDONE"])
            if u >= 2:
                P.wait("vector", vs, V_QB[u - 2])
            P.vop("ts2", out=sgA[ps2][:, 0:wdt], in0=wstr[pu][:, 0:wdt],
                  s1=bpc[m][:, 0:1], s2=0.0, op0=ALU.add, op1=ALU.is_ge,
                  sem=vs)
            V_SG[u] = P.vop("ts2", out=sgB[ps2][:, 0:wdt], in0=wstr[pu][:, 0:wdt],
                            s1=bnc[m][:, 0:1], s2=0.0, op0=ALU.add,
                            op1=ALU.is_ge, sem=vs)
            P.wait("vector", vs, V_SG[u])
            if u >= 2:
                P.wait("vector", sqw[ps2], 16 * (u // 2))
            V_QB[u] = P.vop("stt", out=qbk[ps2][:, 0:wdt], in0=sgA[ps2][:, 0:wdt],
                            scalar=-1.0, in1=sgB[ps2][:, 0:wdt],
                            op0=ALU.add, op1=ALU.add, sem=vs)

        def qz_store_one(u):
            if q_state["stored"] != u:
                return
            q_state["stored"] = u + 1
            m, src, dst, wdt = units[u]
            P.wait("sync", vs, V_QB[u])
            P.dma("sync", dst, qbk[u & 1][:, 0:wdt], sqw[u & 1])

        def emit_qz(n, final=False):
            for _ in range(n):
                u = q_state["iss"]
                if u >= NW:
                    break
                qz_issue_one(u)
                q_state["iss"] = u + 1
                if u >= 3:
                    qz_drain_one(u - 3)
                    qz_store_one(u - 3)
            if final:
                for k in range(q_state["drn"], NW):
                    qz_drain_one(k)
                for k in range(q_state["stored"], NW):
                    qz_store_one(k)

        wides = [ps01, ps23, ps45]
        T_AV = {}
        T_SC = {}
        SS_EXP = {}
        V_MSK = {}
        V_Y = [0] * NBLK
        V_REC = [0] * NBLK
        SS_BCC = [0] * NBLK
        V_MSK_LAST = 0
        ps7_free = {"ss": P_BAR_S}
        FINB = max(1, min(NBLK - 2, (NBLK * 45) // 100))
        npre = sum(2 * (b + 1) for b in range(FINB))
        GPRATE = max(1, npre // (NW + 1))
        QPB = (NW + (NBLK - 2 - FINB)) // (NBLK - 1 - FINB)

        pairs = []
        for b in range(NBLK):
            npair = 2 * (b + 1)
            for pp in range(npair):
                pairs.append((b, pp, npair, b * 512))
        NP = len(pairs)

        def emit_scores(gp):
            b, pp, npair, q0 = pairs[gp]
            W = wides[gp % 3]
            if gp < 3:
                P.wait("tensor", vs, P_BAR_V)
                P.wait("tensor", ss, P_BAR_S)
            else:
                P.wait("tensor", ss, SS_EXP[gp - 3])
            st = 2 * pp
            P.mm(W[:, 0:512], kT[:, st * 128:(st + 1) * 128], qT[:, q0:q0 + 512],
                 True, True)
            T_SC[gp] = P.mm(W[:, 512:1024], kT[:, (st + 1) * 128:(st + 2) * 128],
                            qT[:, q0:q0 + 512], True, True, sem=ts)

        for gp in range(min(3, NP)):
            emit_scores(gp)

        for gp in range(NP):
            b, pp, npair, q0 = pairs[gp]
            W = wides[gp % 3]
            At = attT[gp % 6]
            masked = pp >= npair - 2
            st = 2 * pp
            P.wait("scalar", ts, T_SC[gp])
            if gp >= 6:
                P.wait("scalar", ts, T_AV[gp - 6])
            if masked and pp == npair - 1:
                P.wait("scalar", vs, V_MSK_LAST)
            tgt = attTm if masked else At
            SS_EXP[gp] = P.act(tgt[:, :], W[:, :], AF.Exp,
                               bias=cv_sb[:, 0:1], scale=SCALE, sem=ss)
            if masked:
                slot0 = st - 4 * b
                P.wait("vector", ss, SS_EXP[gp])
                P.vop("tt", out=At[:, 0:512], in0=attTm[:, 0:512],
                      in1=mskd_sb[:, slot0 * 512:(slot0 + 1) * 512],
                      op=ALU.mult, sem=vs)
                V_MSK_LAST = P.vop("tt", out=At[:, 512:1024],
                                   in0=attTm[:, 512:1024],
                                   in1=mskd_sb[:, (slot0 + 1) * 512:(slot0 + 2) * 512],
                                   op=ALU.mult, sem=vs)
            if masked:
                P.wait("tensor", vs, V_MSK_LAST)
            else:
                P.wait("tensor", ss, SS_EXP[gp])
            if pp == 0:
                if b >= 1:
                    P.wait("tensor", vs, V_Y[b - 1])
                P.wait("tensor", ss, ps7_free["ss"])
            P.mm(ps6[:, :], vS[:, st * 128:(st + 1) * 128], At[:, 0:512],
                 start=(pp == 0), stop=False)
            P.mm(ps6[:, :], vS[:, (st + 1) * 128:(st + 2) * 128], At[:, 512:1024],
                 start=False, stop=(pp == npair - 1))
            P.mm(ps7[0:1, :], oneb_sb[:, :], At[:, 0:512],
                 start=(pp == 0), stop=False)
            T_AV[gp] = P.mm(ps7[0:1, :], oneb_sb[:, :], At[:, 512:1024],
                            start=False, stop=(pp == npair - 1), sem=ts)
            if gp + 3 < NP:
                emit_scores(gp + 3)
            if b > FINB and pp % max(2, npair // (QPB + 1)) == 1:
                emit_qz(1)
            elif b < FINB and gp % GPRATE == 0:
                emit_gamma(1)
            if pp != npair - 1:
                continue
            P.wait("vector", ts, T_AV[gp])
            V_REC[b] = P.vop("recip", out=recrow[:, :], in_=ps7[0:1, :], sem=vs)
            P.wait("tensor", vs, V_REC[b])
            T_BC = P.mm(ps7[:, :], onerow_sb[:, :], recrow[:, :], True, True, sem=ts)
            P.wait("scalar", ts, T_BC)
            SS_BCC[b] = P.act(bcc_sb[:, :], ps7[:, :], AF.Copy, sem=ss)
            ps7_free["ss"] = SS_BCC[b]
            P.wait("vector", ss, SS_BCC[b])
            if b >= 2:
                P.wait("vector", sy[b & 1], 16 * (b // 2))
            V_Y[b] = P.vop("tt", out=ybuf[b & 1][:, :], in0=ps6[:, :],
                           in1=bcc_sb[:, :], op=ALU.mult, sem=vs)
            P.wait("gpsimd", vs, V_Y[b])
            P.dma("gpsimd", ag_in[:, q0:q0 + 512], ybuf[b & 1][:, :], sy[b & 1])

            if b == FINB:
                emit_gamma(NW, final=True)
                P.wait("vector", vs, G_GRED[NW - 1])
                P.vop("red", out=gcol[:, 0:1], in_=gred[:, 0:HB])
                P.vop("red", out=gcol[:, 1:2], in_=gred[:, HB:2 * HB])
                V_GF = P.vop("red", out=gcol[:, 2:3], in_=gred[:, 2 * HB:NW],
                             sem=vs)
                q_state["V_GF"] = V_GF
                SS_GDONE = 0
                for m in range(3):
                    P.wait("tensor", vs, V_GF)
                    P.wait("tensor", ss, ps7_free["ss"] if m == 0 else SS_GDONE)
                    T_G1 = P.mm(ps7[0:1, 0:1], gcol[:, m:m + 1], onef_sb[:, :],
                                True, True, sem=ts)
                    P.wait("scalar", ts, T_G1)
                    SS_GE = P.act(ge_row[:, :], ps7[0:1, 0:1], AF.Identity,
                                  scale=1.0 / float(HF * D), bias=cv_sb[0:1, 2:3],
                                  sem=ss)
                    P.wait("tensor", ss, SS_GE)
                    T_G2 = P.mm(ps7[:, 0:1], onerow_sb[:, :], ge_row[:, :],
                                True, True, sem=ts)
                    P.wait("scalar", ts, T_G2)
                    P.act(bpc[m][:, :], ps7[:, 0:1], AF.Copy, scale=0.5, sem=ss)
                    SS_GDONE = P.act(bnc[m][:, :], ps7[:, 0:1], AF.Copy,
                                     scale=-0.5, sem=ss)
                q_state["SS_GDONE"] = SS_GDONE
                ps7_free["ss"] = SS_GDONE
        emit_qz(NW, final=True)

        P.wait("gpsimd", sy[0], sy[0].v)
        P.wait("gpsimd", sy[1], sy[1].v)
        CC_N = P.coll("AllGather", ALU.bypass, [[0, 1, 2, 3], [4, 5, 6, 7]],
                      [ag_in.ap().opt()], [ag_out.ap().opt()], scc)

        TS_QZ = ts.v
        VS_QZ = vs.v
        SS_QZ = ss.v
        stk_p1b.close()
        stk_p1c.close()

        stk_p2 = contextlib.ExitStack()
        p2 = lambda n, shp, dt: stk_p2.enter_context(nc.sbuf_tensor(n, shp, dt))
        xbo = p2("xbo", [128, KD * 512], BF16)
        cmb = [p2(f"cmb{c}", [128, 512], BF16) for c in range(4)]
        yTg = [p2(f"yTg{q}", [128, 4 * 512], BF16) for q in range(2)]
        gld = p2("gld", [128, 3 * 512], BF16)
        xq2 = [p2(f"xq2{q}", [128, 512], BF16) for q in range(2)]
        wo_sb = p2("wo_sb", [128, 4 * D], BF16)
        cs1 = p2("cs1", [1, HF], BF16)
        cs2 = p2("cs2", [1, HF], BF16)
        hid = p2("hid", [128, HB * 512], BF16)
        w1qb = [p2(f"w1qb{q}", [128, D], BF16) for q in range(3)]
        w2qb = [p2(f"w2qb{q}", [128, D], BF16) for q in range(3)]
        w3qb = [p2(f"w3qb{q}", [128, HF], BF16) for q in range(2)]
        xf32 = [p2(f"xf32{q}", [128, 512], F32) for q in range(2)]
        x1t = p2("x1t", [128, 512], F32)
        x2t = p2("x2t", [128, 512], F32)
        sig_sb = p2("sig_sb", [128, 512], BF16)
        rbc2 = p2("rbc2", [128, 512], F32)
        osb = [p2(f"osb{q}", [128, 512], F32) for q in range(2)]


        T_GRP = [0] * NGRP
        V_OUT = {}
        SS_RBC2 = [0] * NGRP
        SS_SIG_LAST = 0
        T_X12L = [0] * NGRP
        V_CMBL = [0] * NGRP
        T_S2KL = [0] * NGRP
        SS_SR2L = [0] * NGRP
        V_HIDL = [0] * NGRP

        for g in range(NGRP):
            pg = g & 1
            gc = g * 512
            if g >= 1:
                P.wait("gpsimd", ts, T_X12L[g - 1])
                P.wait("gpsimd", vs, V_CMBL[g - 1])
            SXB_N = P.dma("gpsimd", xbo.ap().rearrange("p (k t) -> p k t", k=KD),
                  xTo[:, gc:gc + 512].rearrange("(k p) t -> p k t", p=128), sxb[pg])
            if g == 0:
                P.wait("gpsimd", scc, CC_N)
            P.wait("tensor", sxb[pg], SXB_N)
            if g == 0:
                P.wait("tensor", ss, SS_EXP[NP - 1])
            else:
                P.wait("tensor", ss, SS_SR2L[g - 1])
            for k in range(KD):
                P.mm(ps4r, oneb_sb[:, :], xbo[:, k * 512:(k + 1) * 512],
                     start=(k == 0), stop=(k == KD - 1))
            P.wait("vector", sxb[pg], SXB_N)
            T_S2K = [0] * KD
            T_MS2 = 0
            for k in range(KD):
                if k >= 2:
                    P.wait("vector", ts, T_S2K[k - 2])
                elif g >= 1:
                    P.wait("vector", ts, T_S2KL[g - 1])
                vsq = P.vop("tt", out=xq2[k & 1][:, :],
                            in0=xbo[:, k * 512:(k + 1) * 512],
                            in1=xbo[:, k * 512:(k + 1) * 512],
                            op=ALU.mult, sem=vs)
                P.wait("tensor", vs, vsq)
                T_MS2 = P.mm(ps5r, oneb_sb[:, :], xq2[k & 1][:, :],
                             start=(k == 0), stop=(k == KD - 1), sem=ts)
                T_S2K[k] = T_MS2
            T_S2KL[g] = T_S2K[KD - 1]
            P.wait("scalar", ts, T_MS2)
            P.act(mrow[:, :], ps4r, AF.Copy, scale=1.0 / D, sem=ss)
            SS_SR2 = P.act(srow[:, :], ps5r, AF.Copy, scale=1.0 / D, sem=ss)
            SS_SR2L[g] = SS_SR2
            P.wait("vector", ss, SS_SR2)
            P.vop("tt", out=m2row[:, :], in0=mrow[:, :], in1=mrow[:, :],
                  op=ALU.mult, sem=vs)
            P.wait("vector", vs, vs.v)
            V_VAR2 = P.vop("tt", out=varrow[:, :], in0=srow[:, :], in1=m2row[:, :],
                           op=ALU.subtract, sem=vs)
            P.wait("scalar", vs, V_VAR2)
            P.act(lnrow[:, :], varrow[:, :], AF.Ln, bias=cv_sb[0:1, 1:2], sem=ss)
            P.wait("scalar", ss, ss.v)
            SS_R2 = P.act(rrow[:, :], lnrow[:, :], AF.Exp, scale=-0.5, sem=ss)
            P.wait("vector", ss, SS_R2)
            V_RM2 = P.vop("tt", out=rmrow[:, :], in0=rrow[:, :], in1=mrow[:, :],
                          op=ALU.mult, sem=vs)
            P.wait("tensor", ss, SS_R2)
            if g >= 1:
                P.wait("tensor", ss, SS_RBC2[g - 1])
                P.wait("tensor", vs, V_OUT[(g - 1, 15)])
            T_BC2 = P.mm(ps7[:, :], onerow_sb[:, :], rrow[:, :], True, True, sem=ts)
            P.wait("scalar", ts, T_BC2)
            SS_RBC2[g] = P.act(rbc2[:, :], ps7[:, :], AF.Copy, sem=ss)

            V_CMB = 0
            SG_N = 0
            for rt in range(4):
                if rt >= 1:
                    P.wait("gpsimd", vs, V_CMB)
                for c in range(4):
                    SG_N = P.dma("gpsimd", cmb[c][:, :],
                                 ag_out[rt * 128:(rt + 1) * 128,
                                        c * NTO + gc:c * NTO + gc + 512], sg[pg])
                P.wait("vector", sg[pg], SG_N)
                if rt == 0 and g == 0:
                    P.wait("vector", vs, VS_QZ)
                if rt == 0 and g >= 2:
                    P.wait("vector", ts, T_GRP[g - 2])
                if rt >= 1:
                    P.wait("vector", vs, V_CMB)
                P.vop("ts", out=gld[:, 0:512], in0=cmb[0][:, :],
                      s1=msel_sb[:, 0:1], op0=ALU.mult, sem=vs)
                P.wait("vector", vs, vs.v)
                P.vop("stt", out=gld[:, 512:1024], in0=cmb[1][:, :],
                      scalar=msel_sb[:, 1:2], in1=gld[:, 0:512],
                      op0=ALU.mult, op1=ALU.add, sem=vs)
                P.wait("vector", vs, vs.v)
                P.vop("stt", out=gld[:, 1024:1536], in0=cmb[2][:, :],
                      scalar=msel_sb[:, 2:3], in1=gld[:, 512:1024],
                      op0=ALU.mult, op1=ALU.add, sem=vs)
                P.wait("vector", vs, vs.v)
                V_CMB = P.vop("stt", out=yTg[pg][:, rt * 512:(rt + 1) * 512],
                              in0=cmb[3][:, :], scalar=msel_sb[:, 3:4],
                              in1=gld[:, 1024:1536], op0=ALU.mult, op1=ALU.add,
                              sem=vs)
            V_CMBL[g] = V_CMB
            if g == 0:
                P.wait("gpsimd", ts, TS_QZ)
                P.wait("gpsimd", vs, VS_QZ)
                P.wait("gpsimd", ss, SS_QZ)
                P.wait("gpsimd", sqw[0], sqw[0].v)
                P.wait("gpsimd", sqw[1], sqw[1].v)
                PREG2 = P.dma("gpsimd", wo_sb.ap().rearrange("p (k n) -> p k n", k=4),
                              wo_d.ap().rearrange("(k p) n -> p k n", p=128), spreg)

            V_HID = [0] * HB
            T_X12 = [0] * HB
            SS_CS2 = [0] * HB
            for hb in range(HB):
                ph = hb % 3
                psA = ps01 if (hb & 1) == 0 else ps23
                if hb >= 3:
                    P.wait("sync", ts, T_X12[hb - 3])
                elif g >= 1:
                    P.wait("sync", ts, T_GRP[g - 1])
                else:
                    P.wait("sync", sqw[0], sqw[0].v)
                    P.wait("sync", sqw[1], sqw[1].v)
                    P.wait("sync", ts, TS_QZ)
                    P.wait("sync", vs, VS_QZ)
                sqn = P.dma("sync", w1qb[ph][:, :], w1q_h[:, hb * D:(hb + 1) * D],
                            sq[ph])
                sqbn = P.dma("sync", w2qb[ph][:, :], w2q_h[:, hb * D:(hb + 1) * D],
                             sqb[ph])
                P.wait("tensor", sq[ph], sqn)
                P.wait("tensor", sqb[ph], sqbn)
                if hb >= 2:
                    P.wait("tensor", vs, V_HID[hb - 2])
                elif g >= 1:
                    P.wait("tensor", vs, V_HIDL[g - 1])
                if hb == 0:
                    P.wait("tensor", vs, V_RM2)
                    P.wait("tensor", vs, V_CMB)
                if g == 0:
                    if hb >= 1:
                        P.wait("tensor", ss, SS_CS2[hb - 1])
                    for k in range(KD):
                        P.mm(ps45[0:1, 0:128], oneb_sb[:, :],
                             w1qb[ph][:, k * 128:(k + 1) * 128],
                             start=(k == 0), stop=(k == KD - 1))
                    for k in range(KD):
                        T_CSC = P.mm(ps45[0:1, 512:640], oneb_sb[:, :],
                                     w2qb[ph][:, k * 128:(k + 1) * 128],
                                     start=(k == 0), stop=(k == KD - 1),
                                     sem=ts if k == KD - 1 else None)
                    P.wait("scalar", ts, T_CSC)
                    P.act(cs1[:, hb * 128:(hb + 1) * 128], ps45[0:1, 0:128],
                          AF.Copy, scale=-1.0, sem=ss)
                    SS_CS2[hb] = P.act(cs2[:, hb * 128:(hb + 1) * 128],
                                       ps45[0:1, 512:640], AF.Copy, scale=-1.0,
                                       sem=ss)
                for k in range(KD):
                    P.mm(psA[:, 0:512], w1qb[ph][:, k * 128:(k + 1) * 128],
                         xbo[:, k * 512:(k + 1) * 512], start=(k == 0), stop=False)
                if g == 0:
                    P.wait("tensor", ss, SS_CS2[hb])
                P.mm(psA[:, 0:512], cs1[:, hb * 128:(hb + 1) * 128], rmrow[:, :],
                     start=False, stop=True)
                for k in range(KD):
                    P.mm(psA[:, 512:1024], w2qb[ph][:, k * 128:(k + 1) * 128],
                         xbo[:, k * 512:(k + 1) * 512], start=(k == 0), stop=False)
                T_X12[hb] = P.mm(psA[:, 512:1024], cs2[:, hb * 128:(hb + 1) * 128],
                                 rmrow[:, :], start=False, stop=True, sem=ts)
                P.wait("vector", ts, T_X12[hb])
                if g == 0 and hb == 0:
                    P.wait("vector", ss, SS_RBC2[g])
                P.wait("vector", ss, SS_SIG_LAST)
                if hb >= 1:
                    P.wait("vector", vs, V_HID[hb - 1])
                P.vop("tt", out=x2t[:, :], in0=psA[:, 512:1024], in1=rbc2[:, :],
                      op=ALU.mult, sem=vs)
                V_X1 = P.vop("tt", out=x1t[:, :], in0=psA[:, 0:512], in1=rbc2[:, :],
                             op=ALU.mult, sem=vs)
                P.wait("scalar", vs, V_X1)
                SS_SIG_LAST = P.act(sig_sb[:, :], x1t[:, :], AF.Sigmoid, sem=ss)
                P.wait("vector", ss, SS_SIG_LAST)
                V_HID[hb] = P.vop("tt", out=hid[:, hb * 512:(hb + 1) * 512],
                                  in0=x2t[:, :], in1=sig_sb[:, :],
                                  op=ALU.mult, sem=vs)
            T_X12L[g] = T_X12[HB - 1]
            V_HIDL[g] = V_HID[HB - 1]

            T_OUTL = [0] * 16
            pend_store = [None]
            for dt in range(16):
                pw = dt & 1
                psO = ps6 if pw == 0 else ps7
                if dt >= 2:
                    P.wait("sync", ts, T_OUTL[dt - 2])
                elif g >= 1:
                    P.wait("sync", ts, T_GRP[g - 1])
                SW3_N = P.dma("sync", w3qb[pw][:, :], w3q_h[:, dt * HF:(dt + 1) * HF],
                              sw3[pw])
                if pend_store[0] is not None:
                    pend_store[0]()
                    pend_store[0] = None
                if dt >= 2:
                    P.wait("scalar", vs, V_OUT[(g, dt - 2)])
                elif g >= 1:
                    P.wait("scalar", vs, V_OUT[(g - 1, 14 + dt)])
                else:
                    P.wait("scalar", vs, VS_QZ)
                SXF_N = P.dma("scalar", xf32[pw][:, :],
                              xTo[dt * 128:(dt + 1) * 128, gc:gc + 512], sxf[pw])
                P.wait("tensor", sw3[pw], SW3_N)
                if dt == 0:
                    if g == 0:
                        P.wait("tensor", spreg, PREG2)
                    P.wait("tensor", vs, V_HID[HB - 1])
                if dt == 1:
                    P.wait("tensor", ss, SS_RBC2[g])
                if dt >= 2:
                    P.wait("tensor", vs, V_OUT[(g, dt - 2)])
                elif g >= 1:
                    P.wait("tensor", vs, V_OUT[(g - 1, 14 + dt)])
                for k in range(4):
                    P.mm(psO[:, :], wo_sb[:, k * D + dt * 128:k * D + (dt + 1) * 128],
                         yTg[pg][:, k * 512:(k + 1) * 512], start=(k == 0), stop=False)
                T_OUT = 0
                for k in range(HB):
                    T_OUT = P.mm(psO[:, :], w3qb[pw][:, k * 128:(k + 1) * 128],
                                 hid[:, k * 512:(k + 1) * 512], start=False,
                                 stop=(k == HB - 1), sem=ts if k == HB - 1 else None)
                T_OUTL[dt] = T_OUT
                nglob = g * 16 + dt
                P.wait("vector", ts, T_OUT)
                P.wait("vector", sxf[pw], SXF_N)
                if nglob >= 2:
                    P.wait("vector", sout[pw], 16 * (nglob // 2))
                V_OUT[(g, dt)] = P.vop("stt", out=osb[pw][:, :], in0=psO[:, :],
                                       scalar=bo_sb[:, dt:dt + 1], in1=xf32[pw][:, :],
                                       op0=ALU.add, op1=ALU.add, sem=vs)
                def _store(g=g, dt=dt, pw=pw, gc=gc):
                    P.wait("sync", vs, V_OUT[(g, dt)])
                    P.dma("sync", outT[dt * 128:(dt + 1) * 128, gc:gc + 512],
                          osb[pw][:, :], sout[pw])
                if dt == 15:
                    _store()
                else:
                    pend_store[0] = _store
            T_GRP[g] = T_OUTL[15]

        stk_p2.close()

        P.wait("sync", sout[0], sout[0].v)
        P.wait("sync", sout[1], sout[1].v)

        with nc.Block() as block:
            @block.sync
            def _(e):
                for f in P.q["sync"]:
                    f(e)

            @block.gpsimd
            def _(e):
                for f in P.q["gpsimd"]:
                    f(e)

            @block.tensor
            def _(e):
                for f in P.q["tensor"]:
                    f(e)

            @block.vector
            def _(e):
                for f in P.q["vector"]:
                    f(e)

            @block.scalar
            def _(e):
                for f in P.q["scalar"]:
                    f(e)

    return nc


def _constants(T):
    j = np.arange(64, dtype=np.float64)
    inv = 10000.0 ** (-2.0 * j / 128.0)
    t = np.arange(T, dtype=np.float64)
    emb = np.outer(t, inv)
    cos = np.cos(emb).T
    sin = np.sin(emb).T
    m1 = np.concatenate([cos, sin], axis=0).astype(np.float32)
    m2 = np.concatenate([-sin, cos], axis=0).astype(np.float32)
    A = np.zeros((128, 128), np.float32)
    Bm = np.zeros((128, 128), np.float32)
    for i in range(64):
        A[i, 2 * i] = 1.0
        A[64 + i, 2 * i] = 1.0
        Bm[i, 2 * i + 1] = 1.0
        Bm[64 + i, 2 * i + 1] = 1.0
    mskd = np.zeros((128, 2048), np.float32)
    sl = np.arange(128)
    ql = np.arange(128)
    for i in range(4):
        for qt in range(4):
            blockm = ((i * 128 + sl)[:, None] <= (qt * 128 + ql)[None, :])
            mskd[:, i * 512 + qt * 128:i * 512 + (qt + 1) * 128] = blockm
    return {
        "m1t": m1, "m2t": m2,
        "atm": np.ascontiguousarray(A.T).astype(BFNP),
        "btm": np.ascontiguousarray(Bm.T).astype(BFNP),
        "idb": np.eye(128, dtype=np.float32).astype(BFNP),
        "idf": np.eye(16, dtype=np.float32),
        "oneb": np.ones((128, 1), np.float32).astype(BFNP),
        "onef": np.ones((128, 1), np.float32),
        "onerow": np.ones((1, 128), np.float32),
        "halfrow": np.full((1, 128), 0.5, np.float32),
        "mskd": mskd.astype(BFNP),
        "cvals": np.repeat(np.array([[ESHIFT, LNEPS, QEPS, 0.0]], np.float32), 128, 0),
    }


def _stage_core(c, x, Wq, bq, Wk, bk, Wv, bv, Wo, bo, w1T, w2T, w3T, consts, T):
    b, h = c // 4, c % 4
    NTO = T // 4
    d = dict(consts)
    d["xT"] = np.ascontiguousarray(x[b].T)
    d["xTo"] = np.ascontiguousarray(x[b, h * NTO:(h + 1) * NTO].T)
    d["wq_s"] = np.ascontiguousarray(Wq[:, h * 512:(h + 1) * 512])
    d["wk_s"] = np.ascontiguousarray(Wk[:, h * 128:(h + 1) * 128])
    d["wv_s"] = np.ascontiguousarray(Wv[:, h * 128:(h + 1) * 128])
    d["bq4"] = np.ascontiguousarray(bq[h * 512:(h + 1) * 512].reshape(4, 128))
    d["bk1"] = np.ascontiguousarray(bk[h * 128:(h + 1) * 128].reshape(1, 128))
    d["bv1"] = np.ascontiguousarray(bv[h * 128:(h + 1) * 128].reshape(1, 128))
    d["wo"] = np.ascontiguousarray(Wo)
    d["bo16"] = np.ascontiguousarray(bo.reshape(16, 128))
    d["w1T"] = w1T
    d["w2T"] = w2T
    d["w3T"] = w3T
    ms = np.zeros((128, 4), np.float32)
    ms[:, h] = 1.0
    d["msel"] = ms
    return d


def stage_all(x, Wq, bq, Wk, bk, Wv, bv, Wo, bo, w1, w2, w3):
    T = x.shape[1]
    w1T = np.ascontiguousarray(np.asarray(w1, dtype=np.float32).T)
    w2T = np.ascontiguousarray(np.asarray(w2, dtype=np.float32).T)
    w3T = np.ascontiguousarray(np.asarray(w3, dtype=np.float32).T)
    consts = _constants(T)
    return [_stage_core(c, np.asarray(x, dtype=np.float32),
                        np.asarray(Wq, dtype=np.float32), np.asarray(bq, dtype=np.float32),
                        np.asarray(Wk, dtype=np.float32), np.asarray(bk, dtype=np.float32),
                        np.asarray(Wv, dtype=np.float32), np.asarray(bv, dtype=np.float32),
                        np.asarray(Wo, dtype=np.float32), np.asarray(bo, dtype=np.float32),
                        w1T, w2T, w3T, consts, T)
            for c in range(NCORE)]


def kernel(L0_attn_x, L0_attn_Wq, L0_attn_bq, L0_attn_Wk, L0_attn_bk,
           L0_attn_Wv, L0_attn_bv, L0_attn_Wo, L0_attn_bo,
           L0_ffn_w1, L0_ffn_w2, L0_ffn_w3):
    T = L0_attn_x.shape[1]
    HF = L0_ffn_w1.shape[0]
    in_maps = stage_all(L0_attn_x, L0_attn_Wq, L0_attn_bq, L0_attn_Wk, L0_attn_bk,
                        L0_attn_Wv, L0_attn_bv, L0_attn_Wo, L0_attn_bo,
                        L0_ffn_w1, L0_ffn_w2, L0_ffn_w3)
    nc = build_nc(T, HF)
    res = run_bass_kernel_spmd(nc, in_maps, core_ids=list(range(NCORE)))
    out = np.empty((2, T, D), np.float32)
    NTO = T // 4
    for c in range(NCORE):
        b, h = c // 4, c % 4
        out[b, h * NTO:(h + 1) * NTO, :] = np.asarray(res.results[c]["outT"]).T
    return out
